# version 03
# baseline (speedup 1.0000x reference)
from contextlib import ExitStack

import numpy as np
import concourse.bass as bass
import concourse.mybir as mybir
from concourse.bass_utils import run_bass_kernel_spmd

F32 = mybir.dt.float32
BF16 = mybir.dt.bfloat16
I16 = mybir.dt.int16
U16 = mybir.dt.uint16
I8 = mybir.dt.int8
AF = mybir.ActivationFunctionType
ALU = mybir.AluOpType
AX = mybir.AxisListType.X

T = 2048
D = 1024
NH = 8
DH = 256
LR = 64
FF = 3072
DC = D // 128
FC = FF // 128
TB = T // 512

QMAX = 32767.0
WQMAX = 7.0
EPS_Q = 1e-12
EPS_LN = 1e-5


class Ctr:

    def __init__(self, sem):
        self.sem = sem
        self.v = 0

    def inc(self, n=1):
        self.v += n
        return self.v



def build_l2():
    nc = bass.Bass("TRN2", target_bir_lowering=False, debug=False)
    stream = nc.dram_tensor("stream", [128, DC * T], F32, kind="ExternalInput")
    maxx = nc.dram_tensor("maxx", [1, 8], F32, kind="ExternalInput")
    wup = nc.dram_tensor("wup", [FF, D], BF16, kind="ExternalInput")
    wgate = nc.dram_tensor("wgate", [FF, D], BF16, kind="ExternalInput")
    upout = nc.dram_tensor("upout", [FF, T], BF16, kind="ExternalOutput")
    gateout = nc.dram_tensor("gateout", [FF, T], BF16, kind="ExternalOutput")
    maxug = nc.dram_tensor("maxug", [2, 1], F32, kind="ExternalOutput")
    ones_f32 = nc.inline_tensor(np.ones((1, 128), np.float32), name="ones_f32")
    ident128 = nc.inline_tensor(np.eye(128, dtype=np.float32), name="ident128")

    NT = 2 * FC * TB

    with ExitStack() as es:
        sb = lambda *a: es.enter_context(nc.sbuf_tensor(*a))
        ps = lambda *a: es.enter_context(nc.psum_tensor(*a))
        sem = lambda n: es.enter_context(nc.semaphore(n))
        xq = sb("xq", [128, DC * T], BF16)
        sstage = sb("sstage", [128, 2 * T], F32)
        i16s = sb("i16s", [128, T], I16)
        warena = sb("warena", [128, 2 * DC * 512], BF16)
        stage = sb("stage", [128, 4 * 512], BF16)
        maxcol = sb("maxcol", [128, NT], F32)
        maxin = sb("maxin", [1, 8], F32)
        scal = sb("scal", [1, 2], F32)
        sxb = sb("sxb", [128, 2], F32)
        onesr = sb("onesr", [1, 128], F32)
        id128 = sb("id128", [128, 128], F32)
        mfold = sb("mfold", [128, 2], F32)
        mout = sb("mout", [2, 1], F32)
        ps_mm = ps("ps_mm", [128, 4 * 512], F32)
        ps_bc = ps("ps_bc", [128, 2], F32)
        ps_tr = ps("ps_tr", [2, 128], F32)
        _cfg, _scal, _bc, _q = (sem(n) for n in
                                ("s_cfg", "s_scal", "s_bc", "s_q"))
        _i0, _i1 = sem("s_i0"), sem("s_i1")
        _w0, _w1 = sem("s_w0"), sem("s_w1")
        _mm, _ev, _red = (sem(n) for n in ("s_mm", "s_ev", "s_red"))
        _stl = [sem(f"s_st{k}") for k in range(4)]
        _stf = sem("s_stf")
        block = es.enter_context(nc.Block())
        @block.gpsimd
        def _(g):
            g.dma_start(out=maxin[:, :], in_=maxx[:, :]).then_inc(_cfg, 16)
            g.dma_start(out=onesr[:, :], in_=ones_f32[:, :]).then_inc(_cfg, 16)
            g.dma_start(out=id128[:, :], in_=ident128[:, :]).then_inc(_cfg, 16)
            for c in range(DC):
                if c >= 2:
                    g.wait_ge(_q, 2 * (c - 1))
                g.dma_start(out=sstage[:, (c % 2) * T:(c % 2) * T + T],
                            in_=stream[:, c * T:(c + 1) * T]
                            ).then_inc(_i0 if c % 2 == 0 else _i1, 16)
            for k in range(4):
                g.wait_ge(_stl[k], 16 * (NT // 4))
            g.wait_ge(_stf, 16)

        @block.sync
        def _(sy):
            for gidx in range(12):
                ub, W = gidx // 2, gidx % 2
                wt = wup if W == 0 else wgate
                half = gidx % 2
                if gidx >= 2:
                    sy.wait_ge(_mm, 16 * (gidx - 1))
                for dc in range(DC):
                    base = half * DC * 512 + dc * 512
                    sy.dma_start(
                        out=warena[:, base:base + 512],
                        in_=wt[ub * 512:(ub + 1) * 512, dc * 128:(dc + 1) * 128],
                        transpose=True,
                    ).then_inc(_w0 if half == 0 else _w1, 16)

        @block.vector
        def _(v):
            v.wait_ge(_cfg, 48)
            v.tensor_reduce(scal[0:1, 0:1], maxin[0:1, 0:8], AX,
                            ALU.max).then_inc(_scal, 1)
            v.wait_ge(_scal, 1)
            v.tensor_scalar(scal[0:1, 0:1], scal[0:1, 0:1], 1.0 / QMAX, EPS_Q,
                            ALU.mult, ALU.max).then_inc(_scal, 1)
            v.wait_ge(_scal, 2)
            v.reciprocal(scal[0:1, 1:2], scal[0:1, 0:1]).then_inc(_scal, 1)
            v.wait_ge(_bc, 1)
            v.tensor_copy(sxb[:, :], ps_bc[:, :]).then_inc(_scal, 1)
            v.wait_ge(_scal, 4)
            for c in range(DC):
                v.wait_ge(_i0 if c % 2 == 0 else _i1, 16 * (c // 2 + 1))
                if c > 0:
                    v.wait_ge(_q, 2 * c)
                st_sl = sstage[:, (c % 2) * T:(c % 2) * T + T]
                v.tensor_scalar(i16s[:, :], st_sl, sxb[:, 1:2], None,
                                ALU.mult).then_inc(_q, 1)
                v.wait_ge(_q, 2 * c + 1)
                v.tensor_scalar(xq[:, c * T:(c + 1) * T], i16s[:, :],
                                sxb[:, 0:1], None, ALU.mult).then_inc(_q, 1)
            for t in range(NT):
                v.wait_ge(_ev, t + 1)
                gidx, sub = t // 16, t % 16
                col = (gidx % 2) * (NT // 2) + (gidx // 2) * 16 + sub
                v.tensor_reduce(maxcol[:, col:col + 1],
                                ps_mm[:, (t % 4) * 512:(t % 4) * 512 + 512],
                                AX, ALU.max,
                                apply_absolute_value=True).then_inc(_red, 1)
            v.wait_ge(_red, NT)
            v.tensor_reduce(mfold[:, 0:1], maxcol[:, 0:NT // 2], AX, ALU.max)
            v.tensor_reduce(mfold[:, 1:2], maxcol[:, NT // 2:NT], AX,
                            ALU.max).then_inc(_red, 1)
            v.wait_ge(_bc, 2)
            v.tensor_reduce(mout[0:2, 0:1], ps_tr[0:2, 0:128], AX,
                            ALU.max).then_inc(_red, 1)

        @block.tensor
        def _(t_):
            t_.wait_ge(_cfg, 48)
            t_.wait_ge(_scal, 3)
            t_.matmul(ps_bc[:, :], onesr[0:1, :], scal[0:1, 0:2], start=True,
                      stop=True).then_inc(_bc, 1)
            t_.wait_ge(_q, 2 * DC)
            for t in range(NT):
                gidx, sub = t // 16, t % 16
                half = gidx % 2
                tb, uc4 = sub // 4, sub % 4
                if sub == 0:
                    t_.wait_ge(_w0 if half == 0 else _w1,
                               16 * DC * (gidx // 2 + 1))
                if t >= 4:
                    t_.wait_ge(_ev, t - 3)
                    t_.wait_ge(_red, t - 3)
                ins = None
                for dc in range(DC):
                    base = half * DC * 512 + dc * 512 + uc4 * 128
                    ins = t_.matmul(
                        ps_mm[:, (t % 4) * 512:(t % 4) * 512 + 512],
                        warena[:, base:base + 128],
                        xq[:, dc * T + tb * 512:dc * T + tb * 512 + 512],
                        start=(dc == 0), stop=(dc == DC - 1),
                    )
                ins.then_inc(_mm, 1)
            t_.wait_ge(_red, NT + 1)
            t_.transpose(ps_tr[0:2, 0:128], mfold[0:128, 0:2],
                         id128[:, :]).then_inc(_bc, 1)

        @block.scalar
        def _(s):
            for t in range(NT):
                s.wait_ge(_mm, t + 1)
                if t >= 4:
                    s.wait_ge(_stl[t % 4], 16 * (t // 4))
                s.activation(stage[:, (t % 4) * 512:(t % 4) * 512 + 512],
                             ps_mm[:, (t % 4) * 512:(t % 4) * 512 + 512],
                             AF.Copy).then_inc(_ev, 1)
                s.wait_ge(_ev, t + 1)
                gidx, sub = t // 16, t % 16
                ub, W = gidx // 2, gidx % 2
                tb, uc4 = sub // 4, sub % 4
                uc = ub * 4 + uc4
                dst = upout if W == 0 else gateout
                s.dma_start(
                    out=dst[uc * 128:(uc + 1) * 128, tb * 512:(tb + 1) * 512],
                    in_=stage[:, (t % 4) * 512:(t % 4) * 512 + 512],
                ).then_inc(_stl[t % 4], 16)
            s.wait_ge(_red, NT + 2)
            s.dma_start(out=maxug[0:2, 0:1], in_=mout[0:2, 0:1]).then_inc(_stf, 16)

    return nc



def build_l0():
    nc = bass.Bass("TRN2", target_bir_lowering=False, debug=False)
    wu = nc.dram_tensor("wu", [384, D], F32, kind="ExternalInput")
    wg = nc.dram_tensor("wg", [384, D], F32, kind="ExternalInput")
    wd = nc.dram_tensor("wd", [128, FF], F32, kind="ExternalInput")
    wuo = nc.dram_tensor("wuo", [384, D], BF16, kind="ExternalOutput")
    wgo = nc.dram_tensor("wgo", [384, D], BF16, kind="ExternalOutput")
    wdo = nc.dram_tensor("wdo", [128, FF], BF16, kind="ExternalOutput")
    tiles = [(wu, wuo, 0, D), (wu, wuo, 1, D), (wu, wuo, 2, D),
             (wg, wgo, 0, D), (wg, wgo, 1, D), (wg, wgo, 2, D),
             (wd, wdo, 0, FF)]
    with ExitStack() as es:
        sb = lambda *a: es.enter_context(nc.sbuf_tensor(*a))
        sem = lambda n: es.enter_context(nc.semaphore(n))
        wstage = sb("wstage", [128, 2 * FF], F32)
        t32 = sb("t32", [128, FF], F32)
        q8 = sb("q8", [128, FF], I8)
        sraw = sb("sraw", [128, FF // 32], F32)
        sinv = sb("sinv", [128, FF // 32], F32)
        ostage = sb("ostage", [128, 2 * FF], BF16)
        _i0, _i1 = sem("s_i0"), sem("s_i1")
        _c = sem("s_c")
        _o0, _o1 = sem("s_o0"), sem("s_o1")
        block = es.enter_context(nc.Block())

        @block.gpsimd
        def _(g):
            for k, (src_t, _, r, I) in enumerate(tiles):
                par, cnt = (_i0, k // 2 + 1) if k % 2 == 0 else (_i1, k // 2 + 1)
                if k >= 2:
                    g.wait_ge(_c, 6 * (k - 1))
                g.dma_start(out=wstage[:, (k % 2) * FF:(k % 2) * FF + I],
                            in_=src_t[r * 128:(r + 1) * 128, 0:I]).then_inc(par, 16)
            g.wait_ge(_o0, 16 * 4)
            g.wait_ge(_o1, 16 * 3)

        @block.vector
        def _(v):
            for k, (_, _, r, I) in enumerate(tiles):
                par, cnt = (_i0, 16 * (k // 2 + 1)) if k % 2 == 0 else (_i1, 16 * (k // 2 + 1))
                v.wait_ge(par, cnt)
                if k >= 2:
                    spar, scnt = (_o0, 16 * (k // 2)) if k % 2 == 0 else (_o1, 16 * (k // 2))
                    v.wait_ge(spar, scnt)
                if k > 0:
                    v.wait_ge(_c, 6 * k)
                w_sl = wstage[:, (k % 2) * FF:(k % 2) * FF + I]
                nb = I // 32
                ww = w_sl.rearrange("p (b w) -> p b w", w=32)
                v.tensor_reduce(sraw[:, 0:nb], ww, AX, ALU.max,
                                apply_absolute_value=True).then_inc(_c, 1)
                v.wait_ge(_c, 6 * k + 1)
                v.tensor_scalar(sraw[:, 0:nb], sraw[:, 0:nb], 1.0 / WQMAX, EPS_Q,
                                ALU.mult, ALU.max).then_inc(_c, 1)
                v.wait_ge(_c, 6 * k + 2)
                v.reciprocal(sinv[:, 0:nb], sraw[:, 0:nb]).then_inc(_c, 1)
                v.wait_ge(_c, 6 * k + 3)
                sb_b = sraw[:, 0:nb].unsqueeze(2).broadcast_to([128, nb, 32])
                si_b = sinv[:, 0:nb].unsqueeze(2).broadcast_to([128, nb, 32])
                tw = t32[:, 0:I].rearrange("p (b w) -> p b w", w=32)
                v.tensor_tensor(tw, ww, si_b, ALU.mult).then_inc(_c, 1)
                v.wait_ge(_c, 6 * k + 4)
                v.tensor_copy(q8[:, 0:I], t32[:, 0:I]).then_inc(_c, 1)
                v.wait_ge(_c, 6 * k + 5)
                o_sl = ostage[:, (k % 2) * FF:(k % 2) * FF + I]
                ow = o_sl.rearrange("p (b w) -> p b w", w=32)
                qw = q8[:, 0:I].rearrange("p (b w) -> p b w", w=32)
                v.tensor_tensor(ow, qw, sb_b, ALU.mult).then_inc(_c, 1)

        @block.sync
        def _(sy):
            for k, (_, dst_t, r, I) in enumerate(tiles):
                sy.wait_ge(_c, 6 * (k + 1))
                par = _o0 if k % 2 == 0 else _o1
                sy.dma_start(out=dst_t[r * 128:(r + 1) * 128, 0:I],
                             in_=ostage[:, (k % 2) * FF:(k % 2) * FF + I]
                             ).then_inc(par, 16)
    return nc



def build_l3():
    nc = bass.Bass("TRN2", target_bir_lowering=False, debug=False)
    gatein = nc.dram_tensor("gatein", [FF, T], BF16, kind="ExternalInput")
    maxug8 = nc.dram_tensor("maxug8", [1, 16], F32, kind="ExternalInput")
    g2out = nc.dram_tensor("g2out", [FF, T], BF16, kind="ExternalOutput")
    maxg2 = nc.dram_tensor("maxg2", [1, 1], F32, kind="ExternalOutput")
    ones_f32 = nc.inline_tensor(np.ones((1, 128), np.float32), name="ones_f32")
    ident128 = nc.inline_tensor(np.eye(128, dtype=np.float32), name="ident128")
    with ExitStack() as es:
        sb = lambda *a: es.enter_context(nc.sbuf_tensor(*a))
        ps = lambda *a: es.enter_context(nc.psum_tensor(*a))
        sem = lambda n: es.enter_context(nc.semaphore(n))
        gstage = sb("gstage", [128, 2 * T], BF16)
        i16g = sb("i16g", [128, T], I16)
        sig = sb("sig", [128, T], F32)
        u16s = sb("u16s", [128, T], U16)
        g2f = sb("g2f", [128, T], F32)
        ostage = sb("ostage", [128, 2 * T], BF16)
        maxin = sb("maxin", [1, 16], F32)
        scal = sb("scal", [1, 4], F32)
        scb = sb("scb", [128, 4], F32)
        onesr = sb("onesr", [1, 128], F32)
        id128 = sb("id128", [128, 128], F32)
        maxcol = sb("maxcol", [128, FC], F32)
        mfold = sb("mfold", [128, 1], F32)
        mo = sb("mo", [1, 1], F32)
        ps_bc = ps("ps_bc", [128, 4], F32)
        ps_tr = ps("ps_tr", [1, 128], F32)
        _cfg, _scal, _bc = sem("s_cfg"), sem("s_scal"), sem("s_bc")
        _i0, _i1 = sem("s_i0"), sem("s_i1")
        _vq = sem("s_vq")
        _sg = sem("s_sg")
        _red = sem("s_red")
        _o0, _o1 = sem("s_o0"), sem("s_o1")
        _stf = sem("s_stf")
        block = es.enter_context(nc.Block())

        @block.gpsimd
        def _(g):
            g.dma_start(out=maxin[:, :], in_=maxug8[:, :]).then_inc(_cfg, 16)
            g.dma_start(out=onesr[:, :], in_=ones_f32[:, :]).then_inc(_cfg, 16)
            g.dma_start(out=id128[:, :], in_=ident128[:, :]).then_inc(_cfg, 16)
            for c in range(FC):
                if c >= 2:
                    g.wait_ge(_vq, 5 * (c - 1))
                g.dma_start(out=gstage[:, (c % 2) * T:(c % 2) * T + T],
                            in_=gatein[c * 128:(c + 1) * 128, :]
                            ).then_inc(_i0 if c % 2 == 0 else _i1, 16)
            g.wait_ge(_o0, 16 * (FC // 2))
            g.wait_ge(_o1, 16 * (FC // 2))
            g.wait_ge(_stf, 16)

        @block.vector
        def _(v):
            v.wait_ge(_cfg, 48)
            v.tensor_reduce(scal[0:1, 0:1], maxin[0:1, 8:16], AX,
                            ALU.max).then_inc(_scal, 1)
            v.wait_ge(_scal, 1)
            v.tensor_scalar(scal[0:1, 0:1], scal[0:1, 0:1], 1.0 / QMAX, EPS_Q,
                            ALU.mult, ALU.max).then_inc(_scal, 1)
            v.wait_ge(_scal, 2)
            v.reciprocal(scal[0:1, 1:2], scal[0:1, 0:1]).then_inc(_scal, 1)
            v.wait_ge(_scal, 3)
            v.tensor_scalar(scal[0:1, 2:3], scal[0:1, 0:1], 1.0 / 65536.0, None,
                            ALU.mult).then_inc(_scal, 1)
            v.wait_ge(_bc, 1)
            v.tensor_copy(scb[:, :], ps_bc[:, :]).then_inc(_scal, 1)
            v.wait_ge(_scal, 5)
            for c in range(FC):
                v.wait_ge(_i0 if c % 2 == 0 else _i1, 16 * (c // 2 + 1))
                if c >= 2:
                    v.wait_ge(_o0 if c % 2 == 0 else _o1, 16 * (c // 2))
                if c > 0:
                    v.wait_ge(_red, c)
                g_sl = gstage[:, (c % 2) * T:(c % 2) * T + T]
                v.tensor_scalar(i16g[:, :], g_sl, scb[:, 1:2], None,
                                ALU.mult).then_inc(_vq, 1)
                v.wait_ge(_sg, c + 1)
                v.tensor_scalar(u16s[:, :], sig[:, :], 65536.0, None,
                                ALU.mult).then_inc(_vq, 1)
                v.wait_ge(_vq, 5 * c + 2)
                v.scalar_tensor_tensor(g2f[:, :], i16g[:, :], scb[:, 2:3],
                                       u16s[:, :], ALU.mult,
                                       ALU.mult).then_inc(_vq, 1)
                v.wait_ge(_vq, 5 * c + 3)
                v.tensor_reduce(maxcol[:, c:c + 1], g2f[:, :], AX, ALU.max,
                                apply_absolute_value=True).then_inc(_red, 1)
                o_sl = ostage[:, (c % 2) * T:(c % 2) * T + T]
                v.tensor_copy(o_sl, g2f[:, :]).then_inc(_vq, 2)
            v.wait_ge(_red, FC)
            v.tensor_reduce(mfold[:, 0:1], maxcol[:, :], AX,
                            ALU.max).then_inc(_red, 1)
            v.wait_ge(_bc, 2)
            v.tensor_reduce(mo[0:1, 0:1], ps_tr[0:1, 0:128], AX,
                            ALU.max).then_inc(_red, 1)

        @block.scalar
        def _(s):
            s.wait_ge(_scal, 5)
            for c in range(FC):
                s.wait_ge(_vq, 5 * c + 1)
                s.activation(sig[:, :], i16g[:, :], AF.Sigmoid, bias=0.0,
                             scale=scb[:, 0:1]).then_inc(_sg, 1)
                if c < FC - 1:
                    s.wait_ge(_vq, 5 * c + 2)

        @block.tensor
        def _(t_):
            t_.wait_ge(_cfg, 48)
            t_.wait_ge(_scal, 4)
            t_.matmul(ps_bc[:, :], onesr[0:1, :], scal[0:1, 0:4], start=True,
                      stop=True).then_inc(_bc, 1)
            t_.wait_ge(_red, FC + 1)
            t_.transpose(ps_tr[0:1, 0:128], mfold[0:128, 0:1],
                         id128[:, :]).then_inc(_bc, 1)

        @block.sync
        def _(sy):
            for c in range(FC):
                sy.wait_ge(_vq, 5 * c + 5)
                sy.dma_start(out=g2out[c * 128:(c + 1) * 128, :],
                             in_=ostage[:, (c % 2) * T:(c % 2) * T + T]
                             ).then_inc(_o0 if c % 2 == 0 else _o1, 16)
            sy.wait_ge(_red, FC + 2)
            sy.dma_start(out=maxg2[0:1, 0:1], in_=mo[0:1, 0:1]).then_inc(_stf, 16)
    return nc



def build_l4():
    nc = bass.Bass("TRN2", target_bir_lowering=False, debug=False)
    g2in = nc.dram_tensor("g2in", [FF, T], BF16, kind="ExternalInput")
    upin = nc.dram_tensor("upin", [FF, T], BF16, kind="ExternalInput")
    maxg28 = nc.dram_tensor("maxg28", [1, 8], F32, kind="ExternalInput")
    maxug8 = nc.dram_tensor("maxug8", [1, 16], F32, kind="ExternalInput")
    oout = nc.dram_tensor("oout", [FF, T], BF16, kind="ExternalOutput")
    maxo = nc.dram_tensor("maxo", [1, 1], F32, kind="ExternalOutput")
    ones_f32 = nc.inline_tensor(np.ones((1, 128), np.float32), name="ones_f32")
    ident128 = nc.inline_tensor(np.eye(128, dtype=np.float32), name="ident128")
    with ExitStack() as es:
        sb = lambda *a: es.enter_context(nc.sbuf_tensor(*a))
        ps = lambda *a: es.enter_context(nc.psum_tensor(*a))
        sem = lambda n: es.enter_context(nc.semaphore(n))
        gstage = sb("gstage", [128, 2 * T], BF16)
        ustage = sb("ustage", [128, 2 * T], BF16)
        i16g = sb("i16g", [128, T], I16)
        i16u = sb("i16u", [128, T], I16)
        of = sb("of", [128, T], F32)
        ostage = sb("ostage", [128, 2 * T], BF16)
        maxing = sb("maxing", [1, 8], F32)
        maxinu = sb("maxinu", [1, 16], F32)
        scal = sb("scal", [1, 4], F32)
        scb = sb("scb", [128, 4], F32)
        onesr = sb("onesr", [1, 128], F32)
        id128 = sb("id128", [128, 128], F32)
        maxcol = sb("maxcol", [128, FC], F32)
        mfold = sb("mfold", [128, 1], F32)
        mo = sb("mo", [1, 1], F32)
        ps_bc = ps("ps_bc", [128, 4], F32)
        ps_tr = ps("ps_tr", [1, 128], F32)
        _cfg, _scal, _bc = sem("s_cfg"), sem("s_scal"), sem("s_bc")
        _ig0, _ig1 = sem("s_ig0"), sem("s_ig1")
        _iu0, _iu1 = sem("s_iu0"), sem("s_iu1")
        _vq = sem("s_vq")
        _red = sem("s_red")
        _o0, _o1 = sem("s_o0"), sem("s_o1")
        _stf = sem("s_stf")
        block = es.enter_context(nc.Block())

        @block.gpsimd
        def _(g):
            g.dma_start(out=maxing[:, :], in_=maxg28[:, :]).then_inc(_cfg, 16)
            g.dma_start(out=maxinu[:, :], in_=maxug8[:, :]).then_inc(_cfg, 16)
            g.dma_start(out=onesr[:, :], in_=ones_f32[:, :]).then_inc(_cfg, 16)
            g.dma_start(out=id128[:, :], in_=ident128[:, :]).then_inc(_cfg, 16)
            for c in range(FC):
                if c >= 2:
                    g.wait_ge(_vq, 5 * (c - 1))
                g.dma_start(out=gstage[:, (c % 2) * T:(c % 2) * T + T],
                            in_=g2in[c * 128:(c + 1) * 128, :]
                            ).then_inc(_ig0 if c % 2 == 0 else _ig1, 16)
            g.wait_ge(_o0, 16 * (FC // 2))
            g.wait_ge(_o1, 16 * (FC // 2))
            g.wait_ge(_stf, 16)

        @block.sync
        def _(sy):
            for c in range(FC):
                if c >= 2:
                    sy.wait_ge(_vq, 5 * (c - 1))
                sy.dma_start(out=ustage[:, (c % 2) * T:(c % 2) * T + T],
                             in_=upin[c * 128:(c + 1) * 128, :]
                             ).then_inc(_iu0 if c % 2 == 0 else _iu1, 16)

        @block.scalar
        def _(s):
            for c in range(FC):
                s.wait_ge(_vq, 5 * c + 5)
                s.dma_start(out=oout[c * 128:(c + 1) * 128, :],
                            in_=ostage[:, (c % 2) * T:(c % 2) * T + T]
                            ).then_inc(_o0 if c % 2 == 0 else _o1, 16)
            s.wait_ge(_red, FC + 2)
            s.dma_start(out=maxo[0:1, 0:1], in_=mo[0:1, 0:1]).then_inc(_stf, 16)

        @block.vector
        def _(v):
            v.wait_ge(_cfg, 64)
            v.tensor_reduce(scal[0:1, 0:1], maxing[0:1, 0:8], AX,
                            ALU.max).then_inc(_scal, 1)
            v.tensor_reduce(scal[0:1, 1:2], maxinu[0:1, 0:8], AX,
                            ALU.max).then_inc(_scal, 1)
            v.wait_ge(_scal, 2)
            v.tensor_scalar(scal[0:1, 0:1], scal[0:1, 0:1], 1.0 / QMAX, EPS_Q,
                            ALU.mult, ALU.max)
            v.tensor_scalar(scal[0:1, 1:2], scal[0:1, 1:2], 1.0 / QMAX, EPS_Q,
                            ALU.mult, ALU.max).then_inc(_scal, 1)
            v.wait_ge(_scal, 3)
            v.tensor_tensor(scal[0:1, 2:3], scal[0:1, 0:1], scal[0:1, 1:2],
                            ALU.mult)
            v.reciprocal(scal[0:1, 3:4], scal[0:1, 1:2]).then_inc(_scal, 1)
            v.wait_ge(_scal, 4)
            v.reciprocal(scal[0:1, 0:1], scal[0:1, 0:1]).then_inc(_scal, 1)
            v.wait_ge(_bc, 1)
            v.tensor_copy(scb[:, :], ps_bc[:, :]).then_inc(_scal, 1)
            v.wait_ge(_scal, 6)
            for c in range(FC):
                v.wait_ge(_ig0 if c % 2 == 0 else _ig1, 16 * (c // 2 + 1))
                v.wait_ge(_iu0 if c % 2 == 0 else _iu1, 16 * (c // 2 + 1))
                if c >= 2:
                    v.wait_ge(_o0 if c % 2 == 0 else _o1, 16 * (c // 2))
                if c > 0:
                    v.wait_ge(_red, c)
                g_sl = gstage[:, (c % 2) * T:(c % 2) * T + T]
                u_sl = ustage[:, (c % 2) * T:(c % 2) * T + T]
                v.tensor_scalar(i16g[:, :], g_sl, scb[:, 0:1], None, ALU.mult)
                v.tensor_scalar(i16u[:, :], u_sl, scb[:, 3:4], None,
                                ALU.mult).then_inc(_vq, 2)
                v.wait_ge(_vq, 5 * c + 2)
                v.scalar_tensor_tensor(of[:, :], i16g[:, :], scb[:, 2:3],
                                       i16u[:, :], ALU.mult,
                                       ALU.mult).then_inc(_vq, 1)
                v.wait_ge(_vq, 5 * c + 3)
                v.tensor_reduce(maxcol[:, c:c + 1], of[:, :], AX, ALU.max,
                                apply_absolute_value=True).then_inc(_red, 1)
                o_sl = ostage[:, (c % 2) * T:(c % 2) * T + T]
                v.tensor_copy(o_sl, of[:, :]).then_inc(_vq, 2)
            v.wait_ge(_red, FC)
            v.tensor_reduce(mfold[:, 0:1], maxcol[:, :], AX,
                            ALU.max).then_inc(_red, 1)
            v.wait_ge(_bc, 2)
            v.tensor_reduce(mo[0:1, 0:1], ps_tr[0:1, 0:128], AX,
                            ALU.max).then_inc(_red, 1)

        @block.tensor
        def _(t_):
            t_.wait_ge(_cfg, 64)
            t_.wait_ge(_scal, 5)
            t_.matmul(ps_bc[:, :], onesr[0:1, :], scal[0:1, 0:4], start=True,
                      stop=True).then_inc(_bc, 1)
            t_.wait_ge(_red, FC + 1)
            t_.transpose(ps_tr[0:1, 0:128], mfold[0:128, 0:1],
                         id128[:, :]).then_inc(_bc, 1)
    return nc



def build_l5():
    import ml_dtypes as _mld
    nc = bass.Bass("TRN2", target_bir_lowering=False, debug=False)
    oin = nc.dram_tensor("oin", [FF, T], BF16, kind="ExternalInput")
    maxo8 = nc.dram_tensor("maxo8", [1, 8], F32, kind="ExternalInput")
    streamin = nc.dram_tensor("streamin", [128, DC * T], F32, kind="ExternalInput")
    wdown = nc.dram_tensor("wdown", [D, FF], BF16, kind="ExternalInput")
    final = nc.dram_tensor("final", [128, DC * T], F32, kind="ExternalOutput")
    ones_f32 = nc.inline_tensor(np.ones((1, 128), np.float32), name="ones_f32")
    onescol_b = nc.inline_tensor(np.ones((128, 1), dtype=_mld.bfloat16),
                                 name="onescol_b")
    LNS = 22
    with ExitStack() as es:
        sb = lambda *a: es.enter_context(nc.sbuf_tensor(*a))
        ps = lambda *a: es.enter_context(nc.psum_tensor(*a))
        sem = lambda n: es.enter_context(nc.semaphore(n))
        stream = sb("stream", [128, DC * T], F32)
        oq = sb("oq", [128, 2 * FC * 512], BF16)
        warena = sb("warena", [128, FC * 1024], BF16)
        stage = sb("stage", [128, 4 * 512], BF16)
        i16o = sb("i16o", [128, 512], I16)
        maxin = sb("maxin", [1, 8], F32)
        scal = sb("scal", [1, 2], F32)
        scb = sb("scb", [128, 2], F32)
        onesr = sb("onesr", [1, 128], F32)
        onecb = sb("onecb", [128, 1], BF16)
        sbf = sb("sbf", [128, DC * 512], BF16)
        sqbf = sb("sqbf", [128, DC * 512], BF16)
        lrow = sb("lrow", [1, 5 * 512], F32)
        ps_mm = ps("ps_mm", [128, 4 * 512], F32)
        ps_ln = ps("ps_ln", [1, 1024], F32)
        ps_r = ps("ps_r", [128, 512], F32)
        ps_m = ps("ps_m", [128, 512], F32)
        _cfg, _scal, _bc, _q = (sem(n) for n in ("s_cfg", "s_scal", "s_bc", "s_q"))
        _ist = [sem(f"s_ist{k}") for k in range(4)]
        _w = sem("s_w")
        _si = sem("s_si")
        _mm, _ev = sem("s_mm"), sem("s_ev")
        _ln, _lnp, _lna = sem("s_ln"), sem("s_lnp"), sem("s_lna")
        _so = sem("s_so")
        block = es.enter_context(nc.Block())

        @block.gpsimd
        def _(g):
            g.dma_start(out=maxin[:, :], in_=maxo8[:, :]).then_inc(_cfg, 16)
            g.dma_start(out=onesr[:, :], in_=ones_f32[:, :]).then_inc(_cfg, 16)
            g.dma_start(out=onecb[:, :], in_=onescol_b[:, :]).then_inc(_cfg, 16)
            for c in range(DC):
                g.dma_start(out=stream[:, c * T:(c + 1) * T],
                            in_=streamin[:, c * T:(c + 1) * T]).then_inc(_si, 16)
            for tb in range(TB):
                for c in range(FC):
                    k = tb * FC + c
                    if k >= 4:
                        g.wait_ge(_q, 2 * (k - 3))
                    g.dma_start(out=stage[:, (k % 4) * 512:(k % 4) * 512 + 512],
                                in_=oin[c * 128:(c + 1) * 128,
                                        tb * 512:(tb + 1) * 512]
                                ).then_inc(_ist[k % 4], 16)
            g.wait_ge(_so, 16 * TB)

        @block.sync
        def _(sy):
            for ic in range(FC):
                for dh in range(2):
                    sy.dma_start(
                        out=warena[:, ic * 1024 + dh * 512:ic * 1024 + dh * 512 + 512],
                        in_=wdown[dh * 512:(dh + 1) * 512, ic * 128:(ic + 1) * 128],
                        transpose=True,
                    ).then_inc(_w, 16)

        @block.vector
        def _(v):
            v.wait_ge(_cfg, 48)
            v.tensor_reduce(scal[0:1, 0:1], maxin[0:1, 0:8], AX,
                            ALU.max).then_inc(_scal, 1)
            v.wait_ge(_scal, 1)
            v.tensor_scalar(scal[0:1, 0:1], scal[0:1, 0:1], 1.0 / QMAX, EPS_Q,
                            ALU.mult, ALU.max).then_inc(_scal, 1)
            v.wait_ge(_scal, 2)
            v.reciprocal(scal[0:1, 1:2], scal[0:1, 0:1]).then_inc(_scal, 1)
            v.wait_ge(_bc, 1)
            v.tensor_copy(scb[:, :], ps_r[:, 0:2]).then_inc(_scal, 1)
            v.wait_ge(_scal, 4)

            def _quant_tb(tb):
                for c in range(FC):
                    k = tb * FC + c
                    v.wait_ge(_ist[k % 4], 16 * (k // 4 + 1))
                    o_sl = stage[:, (k % 4) * 512:(k % 4) * 512 + 512]
                    v.tensor_scalar(i16o[:, :], o_sl, scb[:, 1:2], None,
                                    ALU.mult).then_inc(_q, 1)
                    v.wait_ge(_q, 2 * k + 1)
                    v.tensor_scalar(
                        oq[:, (tb % 2) * FC * 512 + c * 512:(tb % 2) * FC * 512 + c * 512 + 512],
                        i16o[:, :], scb[:, 0:1], None, ALU.mult).then_inc(_q, 1)

            _quant_tb(0)
            _quant_tb(1)
            v.wait_ge(_si, 16 * DC)
            for tt in range(32):
                tb, dc = tt // 8, tt % 8
                v.wait_ge(_mm, tt + 1)
                sl = stream[:, dc * T + tb * 512:dc * T + tb * 512 + 512]
                v.tensor_tensor(sl, sl,
                                ps_mm[:, (tt % 4) * 512:(tt % 4) * 512 + 512],
                                ALU.add).then_inc(_ev, 1)
                if dc == 7 and tb + 2 < TB:
                    _quant_tb(tb + 2)
                if dc == 7:
                    v.wait_ge(_ev, tt + 1)
                    b = LNS * tb
                    sv = stream[:, :].rearrange("p (c t) -> p c t", t=T)[:, :, tb * 512:tb * 512 + 512]
                    v.tensor_copy(sbf[:, :].rearrange("p (c t) -> p c t", t=512),
                                  sv).then_inc(_ln, 1)
                    v.wait_ge(_lnp, 2 * tb + 1)
                    v.tensor_scalar(lrow[0:1, 0:512], ps_ln[0:1, 0:512],
                                    1.0 / D, None, ALU.mult)
                    v.tensor_scalar(lrow[0:1, 512:1024], ps_ln[0:1, 512:1024],
                                    1.0 / D, None, ALU.mult).then_inc(_ln, 1)
                    v.wait_ge(_ln, b + 2)
                    v.tensor_tensor(lrow[0:1, 1024:1536], lrow[0:1, 0:512],
                                    lrow[0:1, 0:512], ALU.mult).then_inc(_ln, 1)
                    v.wait_ge(_ln, b + 3)
                    v.tensor_tensor(lrow[0:1, 1024:1536], lrow[0:1, 512:1024],
                                    lrow[0:1, 1024:1536],
                                    ALU.subtract).then_inc(_ln, 1)
                    v.wait_ge(_ln, b + 4)
                    v.tensor_scalar(lrow[0:1, 1024:1536], lrow[0:1, 1024:1536],
                                    EPS_LN, None, ALU.add).then_inc(_ln, 1)
                    v.wait_ge(_lna, 3 * tb + 3)
                    v.tensor_tensor(lrow[0:1, 2048:2560], lrow[0:1, 0:512],
                                    lrow[0:1, 1536:2048], ALU.mult).then_inc(_ln, 1)
                    v.wait_ge(_lnp, 2 * tb + 2)
                    for dcn in range(DC):
                        sl2 = stream[:, dcn * T + tb * 512:dcn * T + tb * 512 + 512]
                        v.tensor_tensor(sl2, sl2, ps_r[:, 0:512],
                                        ALU.mult).then_inc(_ln, 1)
                        v.wait_ge(_ln, b + 6 + 2 * dcn + 1)
                        v.tensor_tensor(sl2, sl2, ps_m[:, 0:512],
                                        ALU.subtract).then_inc(_ln, 1)

        @block.scalar
        def _(s):
            for tb in range(TB):
                b = LNS * tb
                s.wait_ge(_ln, b + 1)
                s.activation(sqbf[:, :], sbf[:, :], AF.Square).then_inc(_lna, 1)
                s.wait_ge(_ln, b + 5)
                s.activation(lrow[0:1, 1536:2048], lrow[0:1, 1024:1536],
                             AF.Ln).then_inc(_lna, 1)
                s.wait_ge(_lna, 3 * tb + 2)
                s.activation(lrow[0:1, 1536:2048], lrow[0:1, 1536:2048], AF.Exp,
                             bias=0.0, scale=-0.5).then_inc(_lna, 1)
                s.wait_ge(_ln, LNS * (tb + 1))
                sv = final[:, :].rearrange("p (c t) -> p c t", t=T)[:, :, tb * 512:tb * 512 + 512]
                s.dma_start(out=sv,
                            in_=stream[:, :].rearrange("p (c t) -> p c t", t=T)[:, :, tb * 512:tb * 512 + 512]
                            ).then_inc(_so, 16)

        @block.tensor
        def _(t_):
            t_.wait_ge(_cfg, 48)
            t_.wait_ge(_scal, 3)
            t_.matmul(ps_r[:, 0:2], onesr[0:1, :], scal[0:1, 0:2], start=True,
                      stop=True).then_inc(_bc, 1)
            t_.wait_ge(_w, 16 * FC * 2)
            for tt in range(32):
                tb, dc = tt // 8, tt % 8
                t_.wait_ge(_q, 2 * FC * (tb + 1))
                if tt >= 4:
                    t_.wait_ge(_ev, tt - 3)
                if tb >= 1 and dc == 0:
                    pass
                ins = None
                for ic in range(FC):
                    base = ic * 1024 + dc * 128
                    ins = t_.matmul(
                        ps_mm[:, (tt % 4) * 512:(tt % 4) * 512 + 512],
                        warena[:, base:base + 128],
                        oq[:, (tb % 2) * FC * 512 + ic * 512:(tb % 2) * FC * 512 + ic * 512 + 512],
                        start=(ic == 0), stop=(ic == FC - 1))
                ins.then_inc(_mm, 1)
                if dc == 7:
                    b = LNS * tb
                    t_.wait_ge(_ln, b + 1)
                    t_.wait_ge(_lna, 3 * tb + 1)
                    for dcn in range(DC):
                        t_.matmul(ps_ln[0:1, 0:512], onecb[:, :],
                                  sbf[:, dcn * 512:(dcn + 1) * 512],
                                  start=(dcn == 0), stop=(dcn == DC - 1))
                    ins = None
                    for dcn in range(DC):
                        ins = t_.matmul(ps_ln[0:1, 512:1024], onecb[:, :],
                                        sqbf[:, dcn * 512:(dcn + 1) * 512],
                                        start=(dcn == 0), stop=(dcn == DC - 1))
                    ins.then_inc(_lnp, 1)
                    t_.wait_ge(_ln, b + 6)
                    t_.matmul(ps_r[:, 0:512], onesr[0:1, :],
                              lrow[0:1, 1536:2048], start=True, stop=True)
                    ins = t_.matmul(ps_m[:, 0:512], onesr[0:1, :],
                                    lrow[0:1, 2048:2560], start=True, stop=True)
                    ins.then_inc(_lnp, 1)
    return nc



def build_l1():
    import ml_dtypes as _mld
    nc = bass.Bass("TRN2", target_bir_lowering=False, debug=False)
    xT = nc.dram_tensor("xT", [128, DC * T], F32, kind="ExternalInput")
    wqT = nc.dram_tensor("wqT", [D, NH * DH], F32, kind="ExternalInput")
    wkvT = nc.dram_tensor("wkvT", [D, LR], F32, kind="ExternalInput")
    wkuT = nc.dram_tensor("wkuT", [LR, NH * DH], F32, kind="ExternalInput")
    wvuT = nc.dram_tensor("wvuT", [LR, NH * DH], F32, kind="ExternalInput")
    woT = nc.dram_tensor("woT", [NH * DH, D], F32, kind="ExternalInput")
    streamout = nc.dram_tensor("streamout", [128, DC * T], F32, kind="ExternalOutput")
    maxx = nc.dram_tensor("maxx", [1, 1], F32, kind="ExternalOutput")
    attnout = nc.dram_tensor("attnout", [T, NH * DH], BF16)
    ones_f32 = nc.inline_tensor(np.ones((1, 128), np.float32), name="ones_f32")
    onescol_b = nc.inline_tensor(np.ones((128, 1), dtype=_mld.bfloat16),
                                 name="onescol_b")
    ident128 = nc.inline_tensor(np.eye(128, dtype=np.float32), name="ident128")
    LNS = 22
    with ExitStack() as es:
        sb = lambda *a: es.enter_context(nc.sbuf_tensor(*a))
        sem = lambda n: es.enter_context(nc.semaphore(n))
        stream = sb("stream", [128, DC * T], F32)
        xbf = sb("xbf", [128, DC * T], BF16)
        latbf = sb("latbf", [64, T], BF16)
        kvw = sb("kvw", [128, DC * LR], BF16)
        kup = sb("kup", [64, NH * DH], BF16)
        vup = sb("vup", [64, NH * DH], BF16)
        wq_sb = sb("wq_sb", [128, 2 * DC * 256], BF16)
        u1 = sb("u1", [128, 16384], BF16)
        probsT = u1[:, 0:8192]
        qbf = u1[:, 8192:12288]
        kbf = u1[:, 12288:16384]
        aoT = u1[:, 0:16384]
        vbf = sb("vbf", [128, 16 * 257], BF16)
        sqs = sb("sqs", [128, 2 * 512], BF16)
        qn2 = sb("qn2", [1, T], F32)
        kn2 = sb("kn2", [1, T], F32)
        csc = sb("csc", [1, 8], F32)
        negcb = sb("negcb", [128, 1], F32)
        rcp = sb("rcp", [128, 2], F32)
        aostage = sb("aostage", [128, 2 * 256], BF16)
        wo_sb = sb("wo_sb", [128, 8 * 1024], BF16)
        onesr = sb("onesr", [1, 128], F32)
        onecb = sb("onecb", [128, 1], BF16)
        id128 = sb("id128", [128, 128], F32)
        lrow = sb("lrow", [1, 5 * 512], F32)
        maxcol = sb("maxcol", [128, 8], F32)
        mfold = sb("mfold", [128, 1], F32)
        mxo = sb("mxo", [1, 1], F32)
        psum = es.enter_context(nc.psum_tensor("psum", [128, 4096], F32))
        B0 = psum[:, 0:512]
        B1 = psum[:, 512:1024]
        B2 = psum[:, 1024:1536]
        B23 = psum[0:1, 1024:2048]
        B3c = psum[:, 1536:1537]
        B4 = psum[:, 2048:2560]
        B5 = psum[:, 2560:3072]
        B6 = psum[:, 3072:3584]
        B7 = psum[:, 3584:4096]
        _cfg = sem("s_cfg")
        _vb = sem("s_vb")
        _xs, _xb = sem("s_xs"), sem("s_xb")
        _wq0, _wq1 = sem("s_wq0"), sem("s_wq1")
        _lat, _late = sem("s_lat"), sem("s_late")
        _mmq, _evq = sem("s_mmq"), sem("s_evq")
        _nmm, _nev = sem("s_nmm"), sem("s_nev")
        _mmv, _evv = sem("s_mmv"), sem("s_evv")
        _ch, _cha, _chb = sem("s_ch"), sem("s_cha"), sem("s_chb")
        _smm, _sex = sem("s_smm"), sem("s_sex")
        _amm, _arc, _aev = sem("s_amm"), sem("s_arc"), sem("s_aev")
        _as0, _as1 = sem("s_as0"), sem("s_as1")
        _ao0, _ao1 = sem("s_ao0"), sem("s_ao1")
        _wo0, _wo1 = sem("s_wo0"), sem("s_wo1")
        _pmm, _pev = sem("s_pmm"), sem("s_pev")
        _ln, _lnp, _lna = sem("s_ln"), sem("s_lnp"), sem("s_lna")
        _mx, _mxp = sem("s_mx"), sem("s_mxp")
        _so = sem("s_so")
        block = es.enter_context(nc.Block())

        @block.gpsimd
        def _(g):
            g.dma_start(out=onesr[:, :], in_=ones_f32[:, :]).then_inc(_cfg, 16)
            g.dma_start(out=onecb[:, :], in_=onescol_b[:, :]).then_inc(_cfg, 16)
            g.dma_start(out=id128[:, :], in_=ident128[:, :]).then_inc(_cfg, 16)
            g.dma_start(out=kvw[:, :].rearrange("p (c n) -> p c n", n=LR),
                        in_=wkvT[:, :].rearrange("(c p) n -> p c n", p=128)
                        ).then_inc(_cfg, 16)
            g.dma_start(out=kup[:, :], in_=wkuT[:, :]).then_inc(_cfg, 16)
            g.dma_start(out=vup[:, :], in_=wvuT[:, :]).then_inc(_cfg, 16)
            for tc in range(16):
                g.memset(vbf[:, tc * 257 + 256:tc * 257 + 257], 1.0)
            g.sem_inc(_vb, 1)
            for c in range(DC):
                g.dma_start(out=xbf[:, c * T:(c + 1) * T],
                            in_=xT[:, c * T:(c + 1) * T]).then_inc(_xb, 16)
            for h in range(NH):
                if h >= 2:
                    g.wait_ge(_mmq, (h - 2) * 16 + 8)
                g.dma_start(
                    out=wq_sb[:, (h % 2) * DC * 256:(h % 2 + 1) * DC * 256]
                        .rearrange("p (c n) -> p c n", n=256),
                    in_=wqT[:, h * 256:(h + 1) * 256]
                        .rearrange("(c p) n -> p c n", p=128)
                ).then_inc(_wq0 if h % 2 == 0 else _wq1, 16)
            for H in range(2):
                if H == 1:
                    g.wait_ge(_pmm, 32)
                for c8 in range(8):
                    cc = H * 8 + c8
                    g.dma_start(out=wo_sb[:, c8 * 1024:(c8 + 1) * 1024],
                                in_=woT[cc * 128:(cc + 1) * 128, :]
                                ).then_inc(_wo0 if H == 0 else _wo1, 16)
            g.wait_ge(_so, 16 * (TB + 1))

        @block.sync
        def _(sy):
            for c in range(DC):
                sy.dma_start(out=stream[:, c * T:(c + 1) * T],
                             in_=xT[:, c * T:(c + 1) * T]).then_inc(_xs, 16)
            for h in range(NH):
                for qb in range(TB):
                    for qt in range(4):
                        aidx = (h * TB + qb) * 4 + qt
                        sy.wait_ge(_aev, aidx + 1)
                        sy.dma_start(
                            out=attnout[qb * 512 + qt * 128:qb * 512 + qt * 128 + 128,
                                        h * 256:(h + 1) * 256],
                            in_=aostage[:, (aidx % 2) * 256:(aidx % 2) * 256 + 256],
                        ).then_inc(_as0 if aidx % 2 == 0 else _as1, 16)
            for tb in range(TB):
                sy.wait_ge(_ln, LNS * (tb + 1))
                sv = streamout[:, :].rearrange("p (c t) -> p c t", t=T)[:, :, tb * 512:tb * 512 + 512]
                sy.dma_start(out=sv,
                             in_=stream[:, :].rearrange("p (c t) -> p c t", t=T)[:, :, tb * 512:tb * 512 + 512]
                             ).then_inc(_so, 16)
            sy.wait_ge(_mx, 10)
            sy.dma_start(out=maxx[0:1, 0:1], in_=mxo[0:1, 0:1]).then_inc(_so, 16)

        @block.scalar
        def _(s):
            for tb in range(TB):
                s.wait_ge(_lat, tb + 1)
                s.activation(latbf[0:64, tb * 512:(tb + 1) * 512],
                             B0[0:64, :], AF.Copy).then_inc(_late, 1)
            for h in range(NH):
                for ph in range(2):
                    for tb in range(TB):
                        for j in range(2):
                            gidx = h * 16 + ph * 8 + tb * 2 + j
                            s.wait_ge(_mmq, gidx + 1)
                            if gidx >= 2:
                                pc = h * 16 + ph * 8 + tb * 2 + j - 2
                                s.wait_ge(_nmm, pc // 2 + 1)
                            if ph == 0 and tb == 0 and j == 0 and h > 0:
                                s.wait_ge(_smm, h * 64)
                            dst = qbf if ph == 0 else kbf
                            sc = (1.0 / 16.0) if ph == 0 else 1.0
                            s.activation(dst[:, j * T + tb * 512:j * T + tb * 512 + 512],
                                         B0[:, :], AF.Copy, bias=0.0,
                                         scale=sc).then_inc(_evq, 1)
                            s.activation(sqs[:, (gidx % 2) * 512:(gidx % 2) * 512 + 512],
                                         B0[:, :], AF.Square, bias=0.0,
                                         scale=sc).then_inc(_evq, 1)
                for tc in range(16):
                    vidx = h * 16 + tc
                    s.wait_ge(_mmv, vidx + 1)
                    if h > 0 and tc == 0:
                        s.wait_ge(_amm, h * 16)
                    s.activation(vbf[:, tc * 257:tc * 257 + 256],
                                 psum[:, 2048 + (vidx % 2) * 512:2048 + (vidx % 2) * 512 + 256],
                                 AF.Copy).then_inc(_evv, 1)
                s.wait_ge(_ch, h * 5 + 3)
                s.activation(csc[0:1, 3:4], csc[0:1, 2:3], AF.Ln).then_inc(_cha, 1)
                s.wait_ge(_cha, h * 2 + 1)
                s.activation(csc[0:1, 4:5], csc[0:1, 3:4], AF.Exp, bias=0.0,
                             scale=0.5).then_inc(_cha, 1)
                for qb in range(TB):
                    for kt in range(16):
                        sidx = (h * TB + qb) * 16 + kt
                        s.wait_ge(_smm, sidx + 1)
                        s.wait_ge(_ch, h * 5 + 5)
                        if qb > 0 and kt == 0:
                            s.wait_ge(_amm, (h * TB + qb) * 4)
                        elif h > 0 and qb == 0 and kt == 0:
                            s.wait_ge(_amm, h * TB * 4)
                        s.activation(probsT[:, kt * 512:(kt + 1) * 512],
                                     psum[:, 2048 + (kt % 2) * 512:2048 + (kt % 2) * 512 + 512],
                                     AF.Exp, bias=negcb[:, 0:1],
                                     scale=1.0).then_inc(_sex, 1)
                    for qt in range(4):
                        aidx = (h * TB + qb) * 4 + qt
                        s.wait_ge(_arc, aidx + 1)
                        spar, scnt = (_as0, 16 * (aidx // 2)) if aidx % 2 == 0 \
                            else (_as1, 16 * (aidx // 2))
                        if aidx >= 2:
                            s.wait_ge(spar, scnt)
                        s.activation(aostage[:, (aidx % 2) * 256:(aidx % 2) * 256 + 256],
                                     psum[:, 3072 + (aidx % 2) * 512:3072 + (aidx % 2) * 512 + 256],
                                     AF.Copy, bias=0.0,
                                     scale=rcp[:, aidx % 2:aidx % 2 + 1]
                                     ).then_inc(_aev, 1)
            s.wait_ge(_as0, 16 * 64)
            s.wait_ge(_as1, 16 * 64)
            for H in range(2):
                for qb in range(TB):
                    g8 = H * 4 + qb
                    if g8 >= 2:
                        s.wait_ge(_pmm, 8 * (g8 - 1))
                    for c8 in range(8):
                        cc = H * 8 + c8
                        s.dma_start(
                            out=aoT[:, (g8 % 2) * 16 * 512 + c8 * 512:(g8 % 2) * 16 * 512 + c8 * 512 + 512],
                            in_=attnout[qb * 512:(qb + 1) * 512, cc * 128:(cc + 1) * 128],
                            transpose=True,
                        ).then_inc(_ao0 if g8 % 2 == 0 else _ao1, 16)
            for tb in range(TB):
                b = LNS * tb
                s.wait_ge(_ln, b + 1)
                s.activation(xbf[:, DC * 512:2 * DC * 512], xbf[:, 0:DC * 512],
                             AF.Square).then_inc(_lna, 1)
                s.wait_ge(_ln, b + 5)
                s.activation(lrow[0:1, 1536:2048], lrow[0:1, 1024:1536],
                             AF.Ln).then_inc(_lna, 1)
                s.wait_ge(_lna, 3 * tb + 2)
                s.activation(lrow[0:1, 1536:2048], lrow[0:1, 1536:2048], AF.Exp,
                             bias=0.0, scale=-0.5).then_inc(_lna, 1)

        @block.vector
        def _(v):
            for h in range(NH):
                for ph in range(2):
                    for tb in range(TB):
                        v.wait_ge(_nmm, h * 8 + ph * 4 + tb + 1)
                        dst = qn2 if ph == 0 else kn2
                        v.tensor_copy(dst[0:1, tb * 512:(tb + 1) * 512],
                                      B2[0:1, :]).then_inc(_nev, 1)
                v.wait_ge(_nev, h * 8 + 4)
                v.tensor_reduce(csc[0:1, 0:1], qn2[0:1, :], AX,
                                ALU.max).then_inc(_ch, 1)
                v.wait_ge(_nev, h * 8 + 8)
                v.tensor_reduce(csc[0:1, 1:2], kn2[0:1, :], AX,
                                ALU.max).then_inc(_ch, 1)
                v.wait_ge(_ch, h * 5 + 2)
                v.tensor_tensor(csc[0:1, 2:3], csc[0:1, 0:1], csc[0:1, 1:2],
                                ALU.mult).then_inc(_ch, 1)
                v.wait_ge(_cha, h * 2 + 2)
                v.tensor_scalar(csc[0:1, 5:6], csc[0:1, 4:5], -1.0, None,
                                ALU.mult).then_inc(_ch, 1)
                v.wait_ge(_chb, h + 1)
                if h > 0:
                    v.wait_ge(_sex, h * 64)
                v.tensor_copy(negcb[:, 0:1], B3c[:, 0:1]).then_inc(_ch, 1)
                for qb in range(TB):
                    for qt in range(4):
                        aidx = (h * TB + qb) * 4 + qt
                        v.wait_ge(_amm, aidx + 1)
                        if aidx >= 2:
                            v.wait_ge(_aev, aidx - 1)
                        v.reciprocal(rcp[:, aidx % 2:aidx % 2 + 1],
                                     psum[:, 3072 + (aidx % 2) * 512 + 256:3072 + (aidx % 2) * 512 + 257]
                                     ).then_inc(_arc, 1)
            v.wait_ge(_xs, 16 * DC)
            for H in range(2):
                for qb in range(TB):
                    for dc in range(DC):
                        pidx = H * 32 + qb * 8 + dc
                        v.wait_ge(_pmm, pidx + 1)
                        sl = stream[:, dc * T + qb * 512:dc * T + qb * 512 + 512]
                        v.tensor_tensor(sl, sl,
                                        psum[:, (pidx % 2) * 512:(pidx % 2) * 512 + 512],
                                        ALU.add).then_inc(_pev, 1)
            for tb in range(TB):
                b = LNS * tb
                v.wait_ge(_pev, 32 + 8 * (tb + 1))
                sv = stream[:, :].rearrange("p (c t) -> p c t", t=T)[:, :, tb * 512:tb * 512 + 512]
                v.tensor_copy(xbf[:, 0:DC * 512].rearrange("p (c t) -> p c t", t=512),
                              sv).then_inc(_ln, 1)
                v.wait_ge(_lnp, 2 * tb + 1)
                v.tensor_scalar(lrow[0:1, 0:512], B23[0:1, 0:512], 1.0 / D,
                                None, ALU.mult)
                v.tensor_scalar(lrow[0:1, 512:1024], B23[0:1, 512:1024],
                                1.0 / D, None, ALU.mult).then_inc(_ln, 1)
                v.wait_ge(_ln, b + 2)
                v.tensor_tensor(lrow[0:1, 1024:1536], lrow[0:1, 0:512],
                                lrow[0:1, 0:512], ALU.mult).then_inc(_ln, 1)
                v.wait_ge(_ln, b + 3)
                v.tensor_tensor(lrow[0:1, 1024:1536], lrow[0:1, 512:1024],
                                lrow[0:1, 1024:1536], ALU.subtract).then_inc(_ln, 1)
                v.wait_ge(_ln, b + 4)
                v.tensor_scalar(lrow[0:1, 1024:1536], lrow[0:1, 1024:1536],
                                EPS_LN, None, ALU.add).then_inc(_ln, 1)
                v.wait_ge(_lna, 3 * tb + 3)
                v.tensor_tensor(lrow[0:1, 2048:2560], lrow[0:1, 0:512],
                                lrow[0:1, 1536:2048], ALU.mult).then_inc(_ln, 1)
                v.wait_ge(_lnp, 2 * tb + 2)
                for dcn in range(DC):
                    sl2 = stream[:, dcn * T + tb * 512:dcn * T + tb * 512 + 512]
                    v.tensor_tensor(sl2, sl2, B6[:, :], ALU.mult).then_inc(_ln, 1)
                    v.wait_ge(_ln, b + 6 + 2 * dcn + 1)
                    v.tensor_tensor(sl2, sl2, B7[:, :],
                                    ALU.subtract).then_inc(_ln, 1)
            v.wait_ge(_ln, TB * LNS)
            for dc in range(DC):
                v.tensor_reduce(maxcol[:, dc:dc + 1], stream[:, dc * T:(dc + 1) * T],
                                AX, ALU.max, apply_absolute_value=True).then_inc(_mx, 1)
            v.wait_ge(_mx, 8)
            v.tensor_reduce(mfold[:, 0:1], maxcol[:, 0:8], AX,
                            ALU.max).then_inc(_mx, 1)
            v.wait_ge(_mxp, 1)
            v.tensor_reduce(mxo[0:1, 0:1], B4[0:1, 0:128], AX,
                            ALU.max).then_inc(_mx, 1)

        @block.tensor
        def _(t_):
            t_.wait_ge(_cfg, 96)
            t_.wait_ge(_vb, 1)
            t_.wait_ge(_xb, 16 * DC)
            for tb in range(TB):
                if tb > 0:
                    t_.wait_ge(_late, tb)
                ins = None
                for dc in range(DC):
                    ins = t_.matmul(B0[0:64, :], kvw[:, dc * LR:(dc + 1) * LR],
                                    xbf[:, dc * T + tb * 512:dc * T + tb * 512 + 512],
                                    start=(dc == 0), stop=(dc == DC - 1))
                ins.then_inc(_lat, 1)
            t_.wait_ge(_late, TB)
            for h in range(NH):
                t_.wait_ge(_wq0 if h % 2 == 0 else _wq1, 16 * (h // 2 + 1))
                for ph in range(2):
                    for tb in range(TB):
                        for j in range(2):
                            gidx = h * 16 + ph * 8 + tb * 2 + j
                            t_.wait_ge(_evq, 2 * gidx)
                            ins = None
                            if ph == 0:
                                for dc in range(DC):
                                    base = (h % 2) * DC * 256 + dc * 256 + j * 128
                                    ins = t_.matmul(
                                        B0[:, :], wq_sb[:, base:base + 128],
                                        xbf[:, dc * T + tb * 512:dc * T + tb * 512 + 512],
                                        start=(dc == 0), stop=(dc == DC - 1))
                            else:
                                ins = t_.matmul(
                                    B0[:, :],
                                    kup[0:64, h * 256 + j * 128:h * 256 + j * 128 + 128],
                                    latbf[0:64, tb * 512:(tb + 1) * 512],
                                    start=True, stop=True)
                            ins.then_inc(_mmq, 1)
                            t_.wait_ge(_evq, 2 * gidx + 2)
                            if j == 0 and not (h == 0 and ph == 0 and tb == 0):
                                t_.wait_ge(_nev, h * 8 + ph * 4 + tb)
                            ins2 = t_.matmul(B2[0:1, :], onecb[:, :],
                                             sqs[:, (gidx % 2) * 512:(gidx % 2) * 512 + 512],
                                             start=(j == 0), stop=(j == 1))
                            if j == 1:
                                ins2.then_inc(_nmm, 1)
                for tc in range(16):
                    vidx = h * 16 + tc
                    if vidx >= 2:
                        t_.wait_ge(_evv, vidx - 1)
                    ins = t_.matmul(psum[:, 2048 + (vidx % 2) * 512:2048 + (vidx % 2) * 512 + 256],
                                    latbf[0:64, tc * 128:(tc + 1) * 128],
                                    vup[0:64, h * 256:(h + 1) * 256],
                                    start=True, stop=True)
                    ins.then_inc(_mmv, 1)
                t_.wait_ge(_ch, h * 5 + 4)
                t_.matmul(B3c[:, 0:1], onesr[0:1, :], csc[0:1, 5:6],
                          start=True, stop=True).then_inc(_chb, 1)
                for qb in range(TB):
                    if qb == 0:
                        t_.wait_ge(_evv, h * 16 + 16)
                    for kt in range(16):
                        sidx = (h * TB + qb) * 16 + kt
                        if sidx >= 2:
                            t_.wait_ge(_sex, sidx - 1)
                        ins = None
                        for j in range(2):
                            ins = t_.matmul(
                                psum[:, 2048 + (kt % 2) * 512:2048 + (kt % 2) * 512 + 512],
                                kbf[:, j * T + kt * 128:j * T + kt * 128 + 128],
                                qbf[:, j * T + qb * 512:j * T + qb * 512 + 512],
                                start=(j == 0), stop=(j == 1))
                        ins.then_inc(_smm, 1)
                    t_.wait_ge(_sex, (h * TB + qb) * 16 + 16)
                    for qt in range(4):
                        aidx = (h * TB + qb) * 4 + qt
                        if aidx >= 2:
                            t_.wait_ge(_aev, aidx - 1)
                        ins = None
                        for kc in range(16):
                            ins = t_.matmul(
                                psum[:, 3072 + (aidx % 2) * 512:3072 + (aidx % 2) * 512 + 257],
                                probsT[:, kc * 512 + qt * 128:kc * 512 + qt * 128 + 128],
                                vbf[:, kc * 257:(kc + 1) * 257],
                                start=(kc == 0), stop=(kc == 15))
                        ins.then_inc(_amm, 1)
            for H in range(2):
                t_.wait_ge(_wo0 if H == 0 else _wo1, 16 * 8)
                for qb in range(TB):
                    g8 = H * 4 + qb
                    t_.wait_ge(_ao0 if g8 % 2 == 0 else _ao1, 16 * 8 * (g8 // 2 + 1))
                    for dc in range(DC):
                        pidx = H * 32 + qb * 8 + dc
                        if pidx >= 2:
                            t_.wait_ge(_pev, pidx - 1)
                        ins = None
                        for c8 in range(8):
                            ins = t_.matmul(
                                psum[:, (pidx % 2) * 512:(pidx % 2) * 512 + 512],
                                wo_sb[:, c8 * 1024 + dc * 128:c8 * 1024 + dc * 128 + 128],
                                aoT[:, (g8 % 2) * 16 * 512 + c8 * 512:(g8 % 2) * 16 * 512 + c8 * 512 + 512],
                                start=(c8 == 0), stop=(c8 == 7))
                        ins.then_inc(_pmm, 1)
            for tb in range(TB):
                b = LNS * tb
                t_.wait_ge(_ln, b + 1)
                t_.wait_ge(_lna, 3 * tb + 1)
                for dcn in range(DC):
                    t_.matmul(B23[0:1, 0:512], onecb[:, :],
                              xbf[:, dcn * 512:(dcn + 1) * 512],
                              start=(dcn == 0), stop=(dcn == DC - 1))
                ins = None
                for dcn in range(DC):
                    ins = t_.matmul(B23[0:1, 512:1024], onecb[:, :],
                                    xbf[:, DC * 512 + dcn * 512:DC * 512 + (dcn + 1) * 512],
                                    start=(dcn == 0), stop=(dcn == DC - 1))
                ins.then_inc(_lnp, 1)
                t_.wait_ge(_ln, b + 6)
                t_.matmul(B6[:, :], onesr[0:1, :], lrow[0:1, 1536:2048],
                          start=True, stop=True)
                ins = t_.matmul(B7[:, :], onesr[0:1, :], lrow[0:1, 2048:2560],
                                start=True, stop=True)
                ins.then_inc(_lnp, 1)
            t_.wait_ge(_mx, 9)
            t_.transpose(B4[0:1, 0:128], mfold[0:128, 0:1],
                         id128[:, :]).then_inc(_mxp, 1)
    return nc


_NC_CACHE = {}


def _nc(name):
    if name not in _NC_CACHE:
        _NC_CACHE[name] = globals()["build_" + name]()
    return _NC_CACHE[name]


def _to_fm(a):
    return np.ascontiguousarray(
        a.T.reshape(DC, 128, T).transpose(1, 0, 2).reshape(128, DC * T))


def _from_fm(a):
    return a.reshape(128, DC, T).transpose(1, 0, 2).reshape(D, T).T


def kernel(L0_attn_x, L0_attn_Wq, L0_attn_Wkv_down, L0_attn_Wk_up,
           L0_attn_Wv_up, L0_attn_Wo, L0_ffn_w_gate, L0_ffn_w_up,
           L0_ffn_w_down):
    cores = list(range(8))
    x = np.ascontiguousarray(np.asarray(L0_attn_x, np.float32))
    w_up = np.ascontiguousarray(np.asarray(L0_ffn_w_up, np.float32))
    w_gate = np.ascontiguousarray(np.asarray(L0_ffn_w_gate, np.float32))
    w_down = np.ascontiguousarray(np.asarray(L0_ffn_w_down, np.float32))

    in0 = [{"wu": np.ascontiguousarray(w_up[c * 384:(c + 1) * 384]),
            "wg": np.ascontiguousarray(w_gate[c * 384:(c + 1) * 384]),
            "wd": np.ascontiguousarray(w_down[c * 128:(c + 1) * 128])}
           for c in cores]
    r0 = run_bass_kernel_spmd(_nc("l0"), in0, core_ids=cores).results
    wupdq = np.concatenate([np.asarray(r0[c]["wuo"]) for c in cores], 0)
    wgatedq = np.concatenate([np.asarray(r0[c]["wgo"]) for c in cores], 0)
    wdowndq = np.concatenate([np.asarray(r0[c]["wdo"]) for c in cores], 0)

    wqT = np.ascontiguousarray(np.asarray(L0_attn_Wq, np.float32).T)
    wkvT = np.ascontiguousarray(np.asarray(L0_attn_Wkv_down, np.float32).T)
    wkuT = np.ascontiguousarray(np.asarray(L0_attn_Wk_up, np.float32).T)
    wvuT = np.ascontiguousarray(np.asarray(L0_attn_Wv_up, np.float32).T)
    woT = np.ascontiguousarray(np.asarray(L0_attn_Wo, np.float32).T)
    in1 = [{"xT": _to_fm(x[c]), "wqT": wqT, "wkvT": wkvT, "wkuT": wkuT,
            "wvuT": wvuT, "woT": woT} for c in cores]
    r1 = run_bass_kernel_spmd(_nc("l1"), in1, core_ids=cores).results
    streams = [np.asarray(r1[c]["streamout"]) for c in cores]
    maxx8 = np.concatenate([np.asarray(r1[c]["maxx"]) for c in cores],
                           axis=1)

    in2 = [{"stream": streams[c], "maxx": maxx8, "wup": wupdq,
            "wgate": wgatedq} for c in cores]
    r2 = run_bass_kernel_spmd(_nc("l2"), in2, core_ids=cores).results
    maxug8 = np.concatenate(
        [np.asarray(r2[c]["maxug"])[0:1, :] for c in cores] +
        [np.asarray(r2[c]["maxug"])[1:2, :] for c in cores], axis=0).reshape(1, 16)

    in3 = [{"gatein": np.asarray(r2[c]["gateout"]), "maxug8": maxug8}
           for c in cores]
    r3 = run_bass_kernel_spmd(_nc("l3"), in3, core_ids=cores).results
    maxg28 = np.concatenate([np.asarray(r3[c]["maxg2"]) for c in cores],
                            axis=1)

    in4 = [{"g2in": np.asarray(r3[c]["g2out"]),
            "upin": np.asarray(r2[c]["upout"]),
            "maxg28": maxg28, "maxug8": maxug8} for c in cores]
    r4 = run_bass_kernel_spmd(_nc("l4"), in4, core_ids=cores).results
    maxo8 = np.concatenate([np.asarray(r4[c]["maxo"]) for c in cores], axis=1)

    in5 = [{"oin": np.asarray(r4[c]["oout"]), "maxo8": maxo8,
            "streamin": streams[c], "wdown": wdowndq} for c in cores]
    r5 = run_bass_kernel_spmd(_nc("l5"), in5, core_ids=cores).results

    out = np.empty((8, T, D), np.float32)
    for c in cores:
        out[c] = _from_fm(np.asarray(r5[c]["final"]))
    return out


# version 04
# speedup vs baseline: 1.0880x; 1.0880x over previous
from contextlib import ExitStack

import numpy as np
import concourse.bass as bass
import concourse.mybir as mybir
from concourse.bass_utils import run_bass_kernel_spmd

F32 = mybir.dt.float32
BF16 = mybir.dt.bfloat16
I16 = mybir.dt.int16
U16 = mybir.dt.uint16
I8 = mybir.dt.int8
AF = mybir.ActivationFunctionType
ALU = mybir.AluOpType
AX = mybir.AxisListType.X

T = 2048
D = 1024
NH = 8
DH = 256
LR = 64
FF = 3072
DC = D // 128
FC = FF // 128
TB = T // 512

QMAX = 32767.0
WQMAX = 7.0
EPS_Q = 1e-12
EPS_LN = 1e-5


class Ctr:

    def __init__(self, sem):
        self.sem = sem
        self.v = 0

    def inc(self, n=1):
        self.v += n
        return self.v



def build_l2():
    nc = bass.Bass("TRN2", target_bir_lowering=False, debug=False)
    stream = nc.dram_tensor("stream", [128, DC * T], F32, kind="ExternalInput")
    maxx = nc.dram_tensor("maxx", [1, 8], F32, kind="ExternalInput")
    wup = nc.dram_tensor("wup", [FF, D], BF16, kind="ExternalInput")
    wgate = nc.dram_tensor("wgate", [FF, D], BF16, kind="ExternalInput")
    upout = nc.dram_tensor("upout", [FF, T], BF16, kind="ExternalOutput")
    gateout = nc.dram_tensor("gateout", [FF, T], BF16, kind="ExternalOutput")
    maxug = nc.dram_tensor("maxug", [2, 1], F32, kind="ExternalOutput")
    ones_f32 = nc.inline_tensor(np.ones((1, 128), np.float32), name="ones_f32")
    ident128 = nc.inline_tensor(np.eye(128, dtype=np.float32), name="ident128")

    NT = 2 * FC * TB

    with ExitStack() as es:
        sb = lambda *a: es.enter_context(nc.sbuf_tensor(*a))
        ps = lambda *a: es.enter_context(nc.psum_tensor(*a))
        sem = lambda n: es.enter_context(nc.semaphore(n))
        xq = sb("xq", [128, DC * T], BF16)
        sstage = sb("sstage", [128, 2 * T], F32)
        i16s = sb("i16s", [128, T], I16)
        warena = sb("warena", [128, 2 * DC * 512], BF16)
        stage = sb("stage", [128, 4 * 512], BF16)
        maxcol = sb("maxcol", [128, NT], F32)
        maxin = sb("maxin", [1, 8], F32)
        scal = sb("scal", [1, 2], F32)
        sxb = sb("sxb", [128, 2], F32)
        onesr = sb("onesr", [1, 128], F32)
        id128 = sb("id128", [128, 128], F32)
        mfold = sb("mfold", [128, 2], F32)
        mout = sb("mout", [2, 1], F32)
        ps_mm = ps("ps_mm", [128, 4 * 512], F32)
        ps_bc = ps("ps_bc", [128, 2], F32)
        ps_tr = ps("ps_tr", [2, 128], F32)
        _cfg, _scal, _bc, _q = (sem(n) for n in
                                ("s_cfg", "s_scal", "s_bc", "s_q"))
        _i0, _i1 = sem("s_i0"), sem("s_i1")
        _w0, _w1 = sem("s_w0"), sem("s_w1")
        _mm, _ev, _red = (sem(n) for n in ("s_mm", "s_ev", "s_red"))
        _stl = [sem(f"s_st{k}") for k in range(4)]
        _stf = sem("s_stf")
        block = es.enter_context(nc.Block())
        @block.gpsimd
        def _(g):
            g.dma_start(out=maxin[:, :], in_=maxx[:, :]).then_inc(_cfg, 16)
            g.dma_start(out=onesr[:, :], in_=ones_f32[:, :]).then_inc(_cfg, 16)
            g.dma_start(out=id128[:, :], in_=ident128[:, :]).then_inc(_cfg, 16)
            for c in range(DC):
                if c >= 2:
                    g.wait_ge(_q, 2 * (c - 1))
                g.dma_start(out=sstage[:, (c % 2) * T:(c % 2) * T + T],
                            in_=stream[:, c * T:(c + 1) * T]
                            ).then_inc(_i0 if c % 2 == 0 else _i1, 16)
            for k in range(4):
                g.wait_ge(_stl[k], 16 * (NT // 4))
            g.wait_ge(_stf, 16)

        @block.sync
        def _(sy):
            for gidx in range(12):
                ub, W = gidx // 2, gidx % 2
                wt = wup if W == 0 else wgate
                half = gidx % 2
                if gidx >= 2:
                    sy.wait_ge(_mm, 16 * (gidx - 1))
                for dc in range(DC):
                    base = half * DC * 512 + dc * 512
                    sy.dma_start(
                        out=warena[:, base:base + 512],
                        in_=wt[ub * 512:(ub + 1) * 512, dc * 128:(dc + 1) * 128],
                        transpose=True,
                    ).then_inc(_w0 if half == 0 else _w1, 16)

        @block.vector
        def _(v):
            v.wait_ge(_cfg, 48)
            v.tensor_reduce(scal[0:1, 0:1], maxin[0:1, 0:8], AX,
                            ALU.max).then_inc(_scal, 1)
            v.wait_ge(_scal, 1)
            v.tensor_scalar(scal[0:1, 0:1], scal[0:1, 0:1], 1.0 / QMAX, EPS_Q,
                            ALU.mult, ALU.max).then_inc(_scal, 1)
            v.wait_ge(_scal, 2)
            v.reciprocal(scal[0:1, 1:2], scal[0:1, 0:1]).then_inc(_scal, 1)
            v.wait_ge(_bc, 1)
            v.tensor_copy(sxb[:, :], ps_bc[:, :]).then_inc(_scal, 1)
            v.wait_ge(_scal, 4)
            for c in range(DC):
                v.wait_ge(_i0 if c % 2 == 0 else _i1, 16 * (c // 2 + 1))
                if c > 0:
                    v.wait_ge(_q, 2 * c)
                st_sl = sstage[:, (c % 2) * T:(c % 2) * T + T]
                v.tensor_scalar(i16s[:, :], st_sl, sxb[:, 1:2], None,
                                ALU.mult).then_inc(_q, 1)
                v.wait_ge(_q, 2 * c + 1)
                v.tensor_scalar(xq[:, c * T:(c + 1) * T], i16s[:, :],
                                sxb[:, 0:1], None, ALU.mult).then_inc(_q, 1)
            for t in range(NT):
                v.wait_ge(_ev, t + 1)
                gidx, sub = t // 16, t % 16
                col = (gidx % 2) * (NT // 2) + (gidx // 2) * 16 + sub
                v.tensor_reduce(maxcol[:, col:col + 1],
                                ps_mm[:, (t % 4) * 512:(t % 4) * 512 + 512],
                                AX, ALU.max,
                                apply_absolute_value=True).then_inc(_red, 1)
            v.wait_ge(_red, NT)
            v.tensor_reduce(mfold[:, 0:1], maxcol[:, 0:NT // 2], AX, ALU.max)
            v.tensor_reduce(mfold[:, 1:2], maxcol[:, NT // 2:NT], AX,
                            ALU.max).then_inc(_red, 1)
            v.wait_ge(_bc, 2)
            v.tensor_reduce(mout[0:2, 0:1], ps_tr[0:2, 0:128], AX,
                            ALU.max).then_inc(_red, 1)

        @block.tensor
        def _(t_):
            t_.wait_ge(_cfg, 48)
            t_.wait_ge(_scal, 3)
            t_.matmul(ps_bc[:, :], onesr[0:1, :], scal[0:1, 0:2], start=True,
                      stop=True).then_inc(_bc, 1)
            t_.wait_ge(_q, 2 * DC)
            for t in range(NT):
                gidx, sub = t // 16, t % 16
                half = gidx % 2
                tb, uc4 = sub // 4, sub % 4
                if sub == 0:
                    t_.wait_ge(_w0 if half == 0 else _w1,
                               16 * DC * (gidx // 2 + 1))
                if t >= 4:
                    t_.wait_ge(_ev, t - 3)
                    t_.wait_ge(_red, t - 3)
                ins = None
                for dc in range(DC):
                    base = half * DC * 512 + dc * 512 + uc4 * 128
                    ins = t_.matmul(
                        ps_mm[:, (t % 4) * 512:(t % 4) * 512 + 512],
                        warena[:, base:base + 128],
                        xq[:, dc * T + tb * 512:dc * T + tb * 512 + 512],
                        start=(dc == 0), stop=(dc == DC - 1),
                    )
                ins.then_inc(_mm, 1)
            t_.wait_ge(_red, NT + 1)
            t_.transpose(ps_tr[0:2, 0:128], mfold[0:128, 0:2],
                         id128[:, :]).then_inc(_bc, 1)

        @block.scalar
        def _(s):
            for t in range(NT):
                s.wait_ge(_mm, t + 1)
                if t >= 4:
                    s.wait_ge(_stl[t % 4], 16 * (t // 4))
                s.activation(stage[:, (t % 4) * 512:(t % 4) * 512 + 512],
                             ps_mm[:, (t % 4) * 512:(t % 4) * 512 + 512],
                             AF.Copy).then_inc(_ev, 1)
                s.wait_ge(_ev, t + 1)
                gidx, sub = t // 16, t % 16
                ub, W = gidx // 2, gidx % 2
                tb, uc4 = sub // 4, sub % 4
                uc = ub * 4 + uc4
                dst = upout if W == 0 else gateout
                s.dma_start(
                    out=dst[uc * 128:(uc + 1) * 128, tb * 512:(tb + 1) * 512],
                    in_=stage[:, (t % 4) * 512:(t % 4) * 512 + 512],
                ).then_inc(_stl[t % 4], 16)
            s.wait_ge(_red, NT + 2)
            s.dma_start(out=maxug[0:2, 0:1], in_=mout[0:2, 0:1]).then_inc(_stf, 16)

    return nc



def build_l0():
    nc = bass.Bass("TRN2", target_bir_lowering=False, debug=False)
    wu = nc.dram_tensor("wu", [384, D], F32, kind="ExternalInput")
    wg = nc.dram_tensor("wg", [384, D], F32, kind="ExternalInput")
    wd = nc.dram_tensor("wd", [128, FF], F32, kind="ExternalInput")
    wuo = nc.dram_tensor("wuo", [384, D], BF16, kind="ExternalOutput")
    wgo = nc.dram_tensor("wgo", [384, D], BF16, kind="ExternalOutput")
    wdo = nc.dram_tensor("wdo", [128, FF], BF16, kind="ExternalOutput")
    tiles = [(wu, wuo, 0, D), (wu, wuo, 1, D), (wu, wuo, 2, D),
             (wg, wgo, 0, D), (wg, wgo, 1, D), (wg, wgo, 2, D),
             (wd, wdo, 0, FF)]
    with ExitStack() as es:
        sb = lambda *a: es.enter_context(nc.sbuf_tensor(*a))
        sem = lambda n: es.enter_context(nc.semaphore(n))
        wstage = sb("wstage", [128, 2 * FF], F32)
        t32 = sb("t32", [128, FF], F32)
        q8 = sb("q8", [128, FF], I8)
        sraw = sb("sraw", [128, FF // 32], F32)
        sinv = sb("sinv", [128, FF // 32], F32)
        ostage = sb("ostage", [128, 2 * FF], BF16)
        _i0, _i1 = sem("s_i0"), sem("s_i1")
        _c = sem("s_c")
        _o0, _o1 = sem("s_o0"), sem("s_o1")
        block = es.enter_context(nc.Block())

        @block.gpsimd
        def _(g):
            for k, (src_t, _, r, I) in enumerate(tiles):
                par, cnt = (_i0, k // 2 + 1) if k % 2 == 0 else (_i1, k // 2 + 1)
                if k >= 2:
                    g.wait_ge(_c, 6 * (k - 1))
                g.dma_start(out=wstage[:, (k % 2) * FF:(k % 2) * FF + I],
                            in_=src_t[r * 128:(r + 1) * 128, 0:I]).then_inc(par, 16)
            g.wait_ge(_o0, 16 * 4)
            g.wait_ge(_o1, 16 * 3)

        @block.vector
        def _(v):
            for k, (_, _, r, I) in enumerate(tiles):
                par, cnt = (_i0, 16 * (k // 2 + 1)) if k % 2 == 0 else (_i1, 16 * (k // 2 + 1))
                v.wait_ge(par, cnt)
                if k >= 2:
                    spar, scnt = (_o0, 16 * (k // 2)) if k % 2 == 0 else (_o1, 16 * (k // 2))
                    v.wait_ge(spar, scnt)
                if k > 0:
                    v.wait_ge(_c, 6 * k)
                w_sl = wstage[:, (k % 2) * FF:(k % 2) * FF + I]
                nb = I // 32
                ww = w_sl.rearrange("p (b w) -> p b w", w=32)
                v.tensor_reduce(sraw[:, 0:nb], ww, AX, ALU.max,
                                apply_absolute_value=True).then_inc(_c, 1)
                v.wait_ge(_c, 6 * k + 1)
                v.tensor_scalar(sraw[:, 0:nb], sraw[:, 0:nb], 1.0 / WQMAX, EPS_Q,
                                ALU.mult, ALU.max).then_inc(_c, 1)
                v.wait_ge(_c, 6 * k + 2)
                v.reciprocal(sinv[:, 0:nb], sraw[:, 0:nb]).then_inc(_c, 1)
                v.wait_ge(_c, 6 * k + 3)
                sb_b = sraw[:, 0:nb].unsqueeze(2).broadcast_to([128, nb, 32])
                si_b = sinv[:, 0:nb].unsqueeze(2).broadcast_to([128, nb, 32])
                tw = t32[:, 0:I].rearrange("p (b w) -> p b w", w=32)
                v.tensor_tensor(tw, ww, si_b, ALU.mult).then_inc(_c, 1)
                v.wait_ge(_c, 6 * k + 4)
                v.tensor_copy(q8[:, 0:I], t32[:, 0:I]).then_inc(_c, 1)
                v.wait_ge(_c, 6 * k + 5)
                o_sl = ostage[:, (k % 2) * FF:(k % 2) * FF + I]
                ow = o_sl.rearrange("p (b w) -> p b w", w=32)
                qw = q8[:, 0:I].rearrange("p (b w) -> p b w", w=32)
                v.tensor_tensor(ow, qw, sb_b, ALU.mult).then_inc(_c, 1)

        @block.sync
        def _(sy):
            for k, (_, dst_t, r, I) in enumerate(tiles):
                sy.wait_ge(_c, 6 * (k + 1))
                par = _o0 if k % 2 == 0 else _o1
                sy.dma_start(out=dst_t[r * 128:(r + 1) * 128, 0:I],
                             in_=ostage[:, (k % 2) * FF:(k % 2) * FF + I]
                             ).then_inc(par, 16)
    return nc



def build_l3():
    nc = bass.Bass("TRN2", target_bir_lowering=False, debug=False)
    gatein = nc.dram_tensor("gatein", [FF, T], BF16, kind="ExternalInput")
    maxug8 = nc.dram_tensor("maxug8", [1, 16], F32, kind="ExternalInput")
    g2out = nc.dram_tensor("g2out", [FF, T], BF16, kind="ExternalOutput")
    maxg2 = nc.dram_tensor("maxg2", [1, 1], F32, kind="ExternalOutput")
    ones_f32 = nc.inline_tensor(np.ones((1, 128), np.float32), name="ones_f32")
    ident128 = nc.inline_tensor(np.eye(128, dtype=np.float32), name="ident128")
    with ExitStack() as es:
        sb = lambda *a: es.enter_context(nc.sbuf_tensor(*a))
        ps = lambda *a: es.enter_context(nc.psum_tensor(*a))
        sem = lambda n: es.enter_context(nc.semaphore(n))
        gstage = sb("gstage", [128, 2 * T], BF16)
        i16g = sb("i16g", [128, T], I16)
        sig = sb("sig", [128, T], F32)
        u16s = sb("u16s", [128, T], U16)
        g2f = sb("g2f", [128, T], F32)
        ostage = sb("ostage", [128, 2 * T], BF16)
        maxin = sb("maxin", [1, 16], F32)
        scal = sb("scal", [1, 4], F32)
        scb = sb("scb", [128, 4], F32)
        onesr = sb("onesr", [1, 128], F32)
        id128 = sb("id128", [128, 128], F32)
        maxcol = sb("maxcol", [128, FC], F32)
        mfold = sb("mfold", [128, 1], F32)
        mo = sb("mo", [1, 1], F32)
        ps_bc = ps("ps_bc", [128, 4], F32)
        ps_tr = ps("ps_tr", [1, 128], F32)
        _cfg, _scal, _bc = sem("s_cfg"), sem("s_scal"), sem("s_bc")
        _i0, _i1 = sem("s_i0"), sem("s_i1")
        _vq = sem("s_vq")
        _sg = sem("s_sg")
        _red = sem("s_red")
        _o0, _o1 = sem("s_o0"), sem("s_o1")
        _stf = sem("s_stf")
        block = es.enter_context(nc.Block())

        @block.gpsimd
        def _(g):
            g.dma_start(out=maxin[:, :], in_=maxug8[:, :]).then_inc(_cfg, 16)
            g.dma_start(out=onesr[:, :], in_=ones_f32[:, :]).then_inc(_cfg, 16)
            g.dma_start(out=id128[:, :], in_=ident128[:, :]).then_inc(_cfg, 16)
            for c in range(FC):
                if c >= 2:
                    g.wait_ge(_vq, 5 * (c - 1))
                g.dma_start(out=gstage[:, (c % 2) * T:(c % 2) * T + T],
                            in_=gatein[c * 128:(c + 1) * 128, :]
                            ).then_inc(_i0 if c % 2 == 0 else _i1, 16)
            g.wait_ge(_o0, 16 * (FC // 2))
            g.wait_ge(_o1, 16 * (FC // 2))
            g.wait_ge(_stf, 16)

        @block.vector
        def _(v):
            v.wait_ge(_cfg, 48)
            v.tensor_reduce(scal[0:1, 0:1], maxin[0:1, 8:16], AX,
                            ALU.max).then_inc(_scal, 1)
            v.wait_ge(_scal, 1)
            v.tensor_scalar(scal[0:1, 0:1], scal[0:1, 0:1], 1.0 / QMAX, EPS_Q,
                            ALU.mult, ALU.max).then_inc(_scal, 1)
            v.wait_ge(_scal, 2)
            v.reciprocal(scal[0:1, 1:2], scal[0:1, 0:1]).then_inc(_scal, 1)
            v.wait_ge(_scal, 3)
            v.tensor_scalar(scal[0:1, 2:3], scal[0:1, 0:1], 1.0 / 65536.0, None,
                            ALU.mult).then_inc(_scal, 1)
            v.wait_ge(_bc, 1)
            v.tensor_copy(scb[:, :], ps_bc[:, :]).then_inc(_scal, 1)
            v.wait_ge(_scal, 5)
            for c in range(FC):
                v.wait_ge(_i0 if c % 2 == 0 else _i1, 16 * (c // 2 + 1))
                if c >= 2:
                    v.wait_ge(_o0 if c % 2 == 0 else _o1, 16 * (c // 2))
                if c > 0:
                    v.wait_ge(_red, c)
                g_sl = gstage[:, (c % 2) * T:(c % 2) * T + T]
                v.tensor_scalar(i16g[:, :], g_sl, scb[:, 1:2], None,
                                ALU.mult).then_inc(_vq, 1)
                v.wait_ge(_sg, c + 1)
                v.tensor_scalar(u16s[:, :], sig[:, :], 65536.0, None,
                                ALU.mult).then_inc(_vq, 1)
                v.wait_ge(_vq, 5 * c + 2)
                v.scalar_tensor_tensor(g2f[:, :], i16g[:, :], scb[:, 2:3],
                                       u16s[:, :], ALU.mult,
                                       ALU.mult).then_inc(_vq, 1)
                v.wait_ge(_vq, 5 * c + 3)
                v.tensor_reduce(maxcol[:, c:c + 1], g2f[:, :], AX, ALU.max,
                                apply_absolute_value=True).then_inc(_red, 1)
                o_sl = ostage[:, (c % 2) * T:(c % 2) * T + T]
                v.tensor_copy(o_sl, g2f[:, :]).then_inc(_vq, 2)
            v.wait_ge(_red, FC)
            v.tensor_reduce(mfold[:, 0:1], maxcol[:, :], AX,
                            ALU.max).then_inc(_red, 1)
            v.wait_ge(_bc, 2)
            v.tensor_reduce(mo[0:1, 0:1], ps_tr[0:1, 0:128], AX,
                            ALU.max).then_inc(_red, 1)

        @block.scalar
        def _(s):
            s.wait_ge(_scal, 5)
            for c in range(FC):
                s.wait_ge(_vq, 5 * c + 1)
                s.activation(sig[:, :], i16g[:, :], AF.Sigmoid, bias=0.0,
                             scale=scb[:, 0:1]).then_inc(_sg, 1)
                if c < FC - 1:
                    s.wait_ge(_vq, 5 * c + 2)

        @block.tensor
        def _(t_):
            t_.wait_ge(_cfg, 48)
            t_.wait_ge(_scal, 4)
            t_.matmul(ps_bc[:, :], onesr[0:1, :], scal[0:1, 0:4], start=True,
                      stop=True).then_inc(_bc, 1)
            t_.wait_ge(_red, FC + 1)
            t_.transpose(ps_tr[0:1, 0:128], mfold[0:128, 0:1],
                         id128[:, :]).then_inc(_bc, 1)

        @block.sync
        def _(sy):
            for c in range(FC):
                sy.wait_ge(_vq, 5 * c + 5)
                sy.dma_start(out=g2out[c * 128:(c + 1) * 128, :],
                             in_=ostage[:, (c % 2) * T:(c % 2) * T + T]
                             ).then_inc(_o0 if c % 2 == 0 else _o1, 16)
            sy.wait_ge(_red, FC + 2)
            sy.dma_start(out=maxg2[0:1, 0:1], in_=mo[0:1, 0:1]).then_inc(_stf, 16)
    return nc



def build_l4():
    nc = bass.Bass("TRN2", target_bir_lowering=False, debug=False)
    g2in = nc.dram_tensor("g2in", [FF, T], BF16, kind="ExternalInput")
    upin = nc.dram_tensor("upin", [FF, T], BF16, kind="ExternalInput")
    maxg28 = nc.dram_tensor("maxg28", [1, 8], F32, kind="ExternalInput")
    maxug8 = nc.dram_tensor("maxug8", [1, 16], F32, kind="ExternalInput")
    oout = nc.dram_tensor("oout", [FF, T], BF16, kind="ExternalOutput")
    maxo = nc.dram_tensor("maxo", [1, 1], F32, kind="ExternalOutput")
    ones_f32 = nc.inline_tensor(np.ones((1, 128), np.float32), name="ones_f32")
    ident128 = nc.inline_tensor(np.eye(128, dtype=np.float32), name="ident128")
    with ExitStack() as es:
        sb = lambda *a: es.enter_context(nc.sbuf_tensor(*a))
        ps = lambda *a: es.enter_context(nc.psum_tensor(*a))
        sem = lambda n: es.enter_context(nc.semaphore(n))
        gstage = sb("gstage", [128, 2 * T], BF16)
        ustage = sb("ustage", [128, 2 * T], BF16)
        i16g = sb("i16g", [128, T], I16)
        i16u = sb("i16u", [128, T], I16)
        of = sb("of", [128, T], F32)
        ostage = sb("ostage", [128, 2 * T], BF16)
        maxing = sb("maxing", [1, 8], F32)
        maxinu = sb("maxinu", [1, 16], F32)
        scal = sb("scal", [1, 4], F32)
        scb = sb("scb", [128, 4], F32)
        onesr = sb("onesr", [1, 128], F32)
        id128 = sb("id128", [128, 128], F32)
        maxcol = sb("maxcol", [128, FC], F32)
        mfold = sb("mfold", [128, 1], F32)
        mo = sb("mo", [1, 1], F32)
        ps_bc = ps("ps_bc", [128, 4], F32)
        ps_tr = ps("ps_tr", [1, 128], F32)
        _cfg, _scal, _bc = sem("s_cfg"), sem("s_scal"), sem("s_bc")
        _ig0, _ig1 = sem("s_ig0"), sem("s_ig1")
        _iu0, _iu1 = sem("s_iu0"), sem("s_iu1")
        _vq = sem("s_vq")
        _red = sem("s_red")
        _o0, _o1 = sem("s_o0"), sem("s_o1")
        _stf = sem("s_stf")
        block = es.enter_context(nc.Block())

        @block.gpsimd
        def _(g):
            g.dma_start(out=maxing[:, :], in_=maxg28[:, :]).then_inc(_cfg, 16)
            g.dma_start(out=maxinu[:, :], in_=maxug8[:, :]).then_inc(_cfg, 16)
            g.dma_start(out=onesr[:, :], in_=ones_f32[:, :]).then_inc(_cfg, 16)
            g.dma_start(out=id128[:, :], in_=ident128[:, :]).then_inc(_cfg, 16)
            for c in range(FC):
                if c >= 2:
                    g.wait_ge(_vq, 5 * (c - 1))
                g.dma_start(out=gstage[:, (c % 2) * T:(c % 2) * T + T],
                            in_=g2in[c * 128:(c + 1) * 128, :]
                            ).then_inc(_ig0 if c % 2 == 0 else _ig1, 16)
            g.wait_ge(_o0, 16 * (FC // 2))
            g.wait_ge(_o1, 16 * (FC // 2))
            g.wait_ge(_stf, 16)

        @block.sync
        def _(sy):
            for c in range(FC):
                if c >= 2:
                    sy.wait_ge(_vq, 5 * (c - 1))
                sy.dma_start(out=ustage[:, (c % 2) * T:(c % 2) * T + T],
                             in_=upin[c * 128:(c + 1) * 128, :]
                             ).then_inc(_iu0 if c % 2 == 0 else _iu1, 16)

        @block.scalar
        def _(s):
            for c in range(FC):
                s.wait_ge(_vq, 5 * c + 5)
                s.dma_start(out=oout[c * 128:(c + 1) * 128, :],
                            in_=ostage[:, (c % 2) * T:(c % 2) * T + T]
                            ).then_inc(_o0 if c % 2 == 0 else _o1, 16)
            s.wait_ge(_red, FC + 2)
            s.dma_start(out=maxo[0:1, 0:1], in_=mo[0:1, 0:1]).then_inc(_stf, 16)

        @block.vector
        def _(v):
            v.wait_ge(_cfg, 64)
            v.tensor_reduce(scal[0:1, 0:1], maxing[0:1, 0:8], AX,
                            ALU.max).then_inc(_scal, 1)
            v.tensor_reduce(scal[0:1, 1:2], maxinu[0:1, 0:8], AX,
                            ALU.max).then_inc(_scal, 1)
            v.wait_ge(_scal, 2)
            v.tensor_scalar(scal[0:1, 0:1], scal[0:1, 0:1], 1.0 / QMAX, EPS_Q,
                            ALU.mult, ALU.max)
            v.tensor_scalar(scal[0:1, 1:2], scal[0:1, 1:2], 1.0 / QMAX, EPS_Q,
                            ALU.mult, ALU.max).then_inc(_scal, 1)
            v.wait_ge(_scal, 3)
            v.tensor_tensor(scal[0:1, 2:3], scal[0:1, 0:1], scal[0:1, 1:2],
                            ALU.mult)
            v.reciprocal(scal[0:1, 3:4], scal[0:1, 1:2]).then_inc(_scal, 1)
            v.wait_ge(_scal, 4)
            v.reciprocal(scal[0:1, 0:1], scal[0:1, 0:1]).then_inc(_scal, 1)
            v.wait_ge(_bc, 1)
            v.tensor_copy(scb[:, :], ps_bc[:, :]).then_inc(_scal, 1)
            v.wait_ge(_scal, 6)
            for c in range(FC):
                v.wait_ge(_ig0 if c % 2 == 0 else _ig1, 16 * (c // 2 + 1))
                v.wait_ge(_iu0 if c % 2 == 0 else _iu1, 16 * (c // 2 + 1))
                if c >= 2:
                    v.wait_ge(_o0 if c % 2 == 0 else _o1, 16 * (c // 2))
                if c > 0:
                    v.wait_ge(_red, c)
                g_sl = gstage[:, (c % 2) * T:(c % 2) * T + T]
                u_sl = ustage[:, (c % 2) * T:(c % 2) * T + T]
                v.tensor_scalar(i16g[:, :], g_sl, scb[:, 0:1], None, ALU.mult)
                v.tensor_scalar(i16u[:, :], u_sl, scb[:, 3:4], None,
                                ALU.mult).then_inc(_vq, 2)
                v.wait_ge(_vq, 5 * c + 2)
                v.scalar_tensor_tensor(of[:, :], i16g[:, :], scb[:, 2:3],
                                       i16u[:, :], ALU.mult,
                                       ALU.mult).then_inc(_vq, 1)
                v.wait_ge(_vq, 5 * c + 3)
                v.tensor_reduce(maxcol[:, c:c + 1], of[:, :], AX, ALU.max,
                                apply_absolute_value=True).then_inc(_red, 1)
                o_sl = ostage[:, (c % 2) * T:(c % 2) * T + T]
                v.tensor_copy(o_sl, of[:, :]).then_inc(_vq, 2)
            v.wait_ge(_red, FC)
            v.tensor_reduce(mfold[:, 0:1], maxcol[:, :], AX,
                            ALU.max).then_inc(_red, 1)
            v.wait_ge(_bc, 2)
            v.tensor_reduce(mo[0:1, 0:1], ps_tr[0:1, 0:128], AX,
                            ALU.max).then_inc(_red, 1)

        @block.tensor
        def _(t_):
            t_.wait_ge(_cfg, 64)
            t_.wait_ge(_scal, 5)
            t_.matmul(ps_bc[:, :], onesr[0:1, :], scal[0:1, 0:4], start=True,
                      stop=True).then_inc(_bc, 1)
            t_.wait_ge(_red, FC + 1)
            t_.transpose(ps_tr[0:1, 0:128], mfold[0:128, 0:1],
                         id128[:, :]).then_inc(_bc, 1)
    return nc



def build_l5():
    import ml_dtypes as _mld
    nc = bass.Bass("TRN2", target_bir_lowering=False, debug=False)
    oin = nc.dram_tensor("oin", [FF, T], BF16, kind="ExternalInput")
    maxo8 = nc.dram_tensor("maxo8", [1, 8], F32, kind="ExternalInput")
    streamin = nc.dram_tensor("streamin", [128, DC * T], F32, kind="ExternalInput")
    wdown = nc.dram_tensor("wdown", [D, FF], BF16, kind="ExternalInput")
    final = nc.dram_tensor("final", [128, DC * T], F32, kind="ExternalOutput")
    ones_f32 = nc.inline_tensor(np.ones((1, 128), np.float32), name="ones_f32")
    onescol_b = nc.inline_tensor(np.ones((128, 1), dtype=_mld.bfloat16),
                                 name="onescol_b")
    LNS = 22
    with ExitStack() as es:
        sb = lambda *a: es.enter_context(nc.sbuf_tensor(*a))
        ps = lambda *a: es.enter_context(nc.psum_tensor(*a))
        sem = lambda n: es.enter_context(nc.semaphore(n))
        stream = sb("stream", [128, DC * T], F32)
        oq = sb("oq", [128, 2 * FC * 512], BF16)
        warena = sb("warena", [128, FC * 1024], BF16)
        stage = sb("stage", [128, 4 * 512], BF16)
        i16o = sb("i16o", [128, 512], I16)
        maxin = sb("maxin", [1, 8], F32)
        scal = sb("scal", [1, 2], F32)
        scb = sb("scb", [128, 2], F32)
        onesr = sb("onesr", [1, 128], F32)
        onecb = sb("onecb", [128, 1], BF16)
        sbf = sb("sbf", [128, DC * 512], BF16)
        sqbf = sb("sqbf", [128, DC * 512], BF16)
        lrow = sb("lrow", [1, 5 * 512], F32)
        ps_mm = ps("ps_mm", [128, 4 * 512], F32)
        ps_ln = ps("ps_ln", [1, 1024], F32)
        ps_r = ps("ps_r", [128, 512], F32)
        ps_m = ps("ps_m", [128, 512], F32)
        _cfg, _scal, _bc, _q = (sem(n) for n in ("s_cfg", "s_scal", "s_bc", "s_q"))
        _ist = [sem(f"s_ist{k}") for k in range(4)]
        _w = sem("s_w")
        _si = sem("s_si")
        _mm, _ev = sem("s_mm"), sem("s_ev")
        _ln, _lnp, _lna = sem("s_ln"), sem("s_lnp"), sem("s_lna")
        _so = sem("s_so")
        block = es.enter_context(nc.Block())

        @block.gpsimd
        def _(g):
            g.dma_start(out=maxin[:, :], in_=maxo8[:, :]).then_inc(_cfg, 16)
            g.dma_start(out=onesr[:, :], in_=ones_f32[:, :]).then_inc(_cfg, 16)
            g.dma_start(out=onecb[:, :], in_=onescol_b[:, :]).then_inc(_cfg, 16)
            for c in range(DC):
                g.dma_start(out=stream[:, c * T:(c + 1) * T],
                            in_=streamin[:, c * T:(c + 1) * T]).then_inc(_si, 16)
            for tb in range(TB):
                for c in range(FC):
                    k = tb * FC + c
                    if k >= 4:
                        g.wait_ge(_q, 2 * (k - 3))
                    g.dma_start(out=stage[:, (k % 4) * 512:(k % 4) * 512 + 512],
                                in_=oin[c * 128:(c + 1) * 128,
                                        tb * 512:(tb + 1) * 512]
                                ).then_inc(_ist[k % 4], 16)
            g.wait_ge(_so, 16 * TB)

        @block.sync
        def _(sy):
            for ic in range(FC):
                for dh in range(2):
                    sy.dma_start(
                        out=warena[:, ic * 1024 + dh * 512:ic * 1024 + dh * 512 + 512],
                        in_=wdown[dh * 512:(dh + 1) * 512, ic * 128:(ic + 1) * 128],
                        transpose=True,
                    ).then_inc(_w, 16)

        @block.vector
        def _(v):
            v.wait_ge(_cfg, 48)
            v.tensor_reduce(scal[0:1, 0:1], maxin[0:1, 0:8], AX,
                            ALU.max).then_inc(_scal, 1)
            v.wait_ge(_scal, 1)
            v.tensor_scalar(scal[0:1, 0:1], scal[0:1, 0:1], 1.0 / QMAX, EPS_Q,
                            ALU.mult, ALU.max).then_inc(_scal, 1)
            v.wait_ge(_scal, 2)
            v.reciprocal(scal[0:1, 1:2], scal[0:1, 0:1]).then_inc(_scal, 1)
            v.wait_ge(_bc, 1)
            v.tensor_copy(scb[:, :], ps_r[:, 0:2]).then_inc(_scal, 1)
            v.wait_ge(_scal, 4)

            def _quant_tb(tb):
                for c in range(FC):
                    k = tb * FC + c
                    v.wait_ge(_ist[k % 4], 16 * (k // 4 + 1))
                    o_sl = stage[:, (k % 4) * 512:(k % 4) * 512 + 512]
                    v.tensor_scalar(i16o[:, :], o_sl, scb[:, 1:2], None,
                                    ALU.mult).then_inc(_q, 1)
                    v.wait_ge(_q, 2 * k + 1)
                    v.tensor_scalar(
                        oq[:, (tb % 2) * FC * 512 + c * 512:(tb % 2) * FC * 512 + c * 512 + 512],
                        i16o[:, :], scb[:, 0:1], None, ALU.mult).then_inc(_q, 1)

            _quant_tb(0)
            _quant_tb(1)
            v.wait_ge(_si, 16 * DC)
            for tt in range(32):
                tb, dc = tt // 8, tt % 8
                v.wait_ge(_mm, tt + 1)
                sl = stream[:, dc * T + tb * 512:dc * T + tb * 512 + 512]
                v.tensor_tensor(sl, sl,
                                ps_mm[:, (tt % 4) * 512:(tt % 4) * 512 + 512],
                                ALU.add).then_inc(_ev, 1)
                if dc == 7 and tb + 2 < TB:
                    _quant_tb(tb + 2)
                if dc == 7:
                    v.wait_ge(_ev, tt + 1)
                    b = LNS * tb
                    sv = stream[:, :].rearrange("p (c t) -> p c t", t=T)[:, :, tb * 512:tb * 512 + 512]
                    v.tensor_copy(sbf[:, :].rearrange("p (c t) -> p c t", t=512),
                                  sv).then_inc(_ln, 1)
                    v.wait_ge(_lnp, 2 * tb + 1)
                    v.tensor_scalar(lrow[0:1, 0:512], ps_ln[0:1, 0:512],
                                    1.0 / D, None, ALU.mult)
                    v.tensor_scalar(lrow[0:1, 512:1024], ps_ln[0:1, 512:1024],
                                    1.0 / D, None, ALU.mult).then_inc(_ln, 1)
                    v.wait_ge(_ln, b + 2)
                    v.tensor_tensor(lrow[0:1, 1024:1536], lrow[0:1, 0:512],
                                    lrow[0:1, 0:512], ALU.mult).then_inc(_ln, 1)
                    v.wait_ge(_ln, b + 3)
                    v.tensor_tensor(lrow[0:1, 1024:1536], lrow[0:1, 512:1024],
                                    lrow[0:1, 1024:1536],
                                    ALU.subtract).then_inc(_ln, 1)
                    v.wait_ge(_ln, b + 4)
                    v.tensor_scalar(lrow[0:1, 1024:1536], lrow[0:1, 1024:1536],
                                    EPS_LN, None, ALU.add).then_inc(_ln, 1)
                    v.wait_ge(_lna, 3 * tb + 3)
                    v.tensor_tensor(lrow[0:1, 2048:2560], lrow[0:1, 0:512],
                                    lrow[0:1, 1536:2048], ALU.mult).then_inc(_ln, 1)
                    v.wait_ge(_lnp, 2 * tb + 2)
                    for dcn in range(DC):
                        sl2 = stream[:, dcn * T + tb * 512:dcn * T + tb * 512 + 512]
                        v.tensor_tensor(sl2, sl2, ps_r[:, 0:512],
                                        ALU.mult).then_inc(_ln, 1)
                        v.wait_ge(_ln, b + 6 + 2 * dcn + 1)
                        v.tensor_tensor(sl2, sl2, ps_m[:, 0:512],
                                        ALU.subtract).then_inc(_ln, 1)

        @block.scalar
        def _(s):
            for tb in range(TB):
                b = LNS * tb
                s.wait_ge(_ln, b + 1)
                s.activation(sqbf[:, :], sbf[:, :], AF.Square).then_inc(_lna, 1)
                s.wait_ge(_ln, b + 5)
                s.activation(lrow[0:1, 1536:2048], lrow[0:1, 1024:1536],
                             AF.Ln).then_inc(_lna, 1)
                s.wait_ge(_lna, 3 * tb + 2)
                s.activation(lrow[0:1, 1536:2048], lrow[0:1, 1536:2048], AF.Exp,
                             bias=0.0, scale=-0.5).then_inc(_lna, 1)
                s.wait_ge(_ln, LNS * (tb + 1))
                sv = final[:, :].rearrange("p (c t) -> p c t", t=T)[:, :, tb * 512:tb * 512 + 512]
                s.dma_start(out=sv,
                            in_=stream[:, :].rearrange("p (c t) -> p c t", t=T)[:, :, tb * 512:tb * 512 + 512]
                            ).then_inc(_so, 16)

        @block.tensor
        def _(t_):
            t_.wait_ge(_cfg, 48)
            t_.wait_ge(_scal, 3)
            t_.matmul(ps_r[:, 0:2], onesr[0:1, :], scal[0:1, 0:2], start=True,
                      stop=True).then_inc(_bc, 1)
            t_.wait_ge(_w, 16 * FC * 2)
            for tt in range(32):
                tb, dc = tt // 8, tt % 8
                t_.wait_ge(_q, 2 * FC * (tb + 1))
                if tt >= 4:
                    t_.wait_ge(_ev, tt - 3)
                if tb >= 1 and dc == 0:
                    pass
                ins = None
                for ic in range(FC):
                    base = ic * 1024 + dc * 128
                    ins = t_.matmul(
                        ps_mm[:, (tt % 4) * 512:(tt % 4) * 512 + 512],
                        warena[:, base:base + 128],
                        oq[:, (tb % 2) * FC * 512 + ic * 512:(tb % 2) * FC * 512 + ic * 512 + 512],
                        start=(ic == 0), stop=(ic == FC - 1))
                ins.then_inc(_mm, 1)
                if dc == 7:
                    b = LNS * tb
                    t_.wait_ge(_ln, b + 1)
                    t_.wait_ge(_lna, 3 * tb + 1)
                    for dcn in range(DC):
                        t_.matmul(ps_ln[0:1, 0:512], onecb[:, :],
                                  sbf[:, dcn * 512:(dcn + 1) * 512],
                                  start=(dcn == 0), stop=(dcn == DC - 1))
                    ins = None
                    for dcn in range(DC):
                        ins = t_.matmul(ps_ln[0:1, 512:1024], onecb[:, :],
                                        sqbf[:, dcn * 512:(dcn + 1) * 512],
                                        start=(dcn == 0), stop=(dcn == DC - 1))
                    ins.then_inc(_lnp, 1)
                    t_.wait_ge(_ln, b + 6)
                    t_.matmul(ps_r[:, 0:512], onesr[0:1, :],
                              lrow[0:1, 1536:2048], start=True, stop=True)
                    ins = t_.matmul(ps_m[:, 0:512], onesr[0:1, :],
                                    lrow[0:1, 2048:2560], start=True, stop=True)
                    ins.then_inc(_lnp, 1)
    return nc



def build_l1():
    import ml_dtypes as _mld
    nc = bass.Bass("TRN2", target_bir_lowering=False, debug=False)
    xT = nc.dram_tensor("xT", [128, DC * T], F32, kind="ExternalInput")
    wqT = nc.dram_tensor("wqT", [D, NH * DH], F32, kind="ExternalInput")
    wkvT = nc.dram_tensor("wkvT", [D, LR], F32, kind="ExternalInput")
    wkuT = nc.dram_tensor("wkuT", [LR, NH * DH], F32, kind="ExternalInput")
    wvuT = nc.dram_tensor("wvuT", [LR, NH * DH], F32, kind="ExternalInput")
    woT = nc.dram_tensor("woT", [NH * DH, D], F32, kind="ExternalInput")
    streamout = nc.dram_tensor("streamout", [128, DC * T], F32, kind="ExternalOutput")
    maxx = nc.dram_tensor("maxx", [1, 1], F32, kind="ExternalOutput")
    attnout = nc.dram_tensor("attnout", [T, NH * DH], BF16)
    ones_f32 = nc.inline_tensor(np.ones((1, 128), np.float32), name="ones_f32")
    onescol_b = nc.inline_tensor(np.ones((128, 1), dtype=_mld.bfloat16),
                                 name="onescol_b")
    ident128 = nc.inline_tensor(np.eye(128, dtype=np.float32), name="ident128")
    LNS = 22
    with ExitStack() as es:
        sb = lambda *a: es.enter_context(nc.sbuf_tensor(*a))
        sem = lambda n: es.enter_context(nc.semaphore(n))
        stream = sb("stream", [128, DC * T], F32)
        xbf = sb("xbf", [128, DC * T], BF16)
        latbf = sb("latbf", [64, T], BF16)
        kvw = sb("kvw", [128, DC * LR], BF16)
        kup = sb("kup", [64, NH * DH], BF16)
        vup = sb("vup", [64, NH * DH], BF16)
        wq_sb = sb("wq_sb", [128, 2 * DC * 256], BF16)
        u1 = sb("u1", [128, 16384], BF16)
        probsT = u1[:, 0:8192]
        qbf = u1[:, 8192:12288]
        kbf = u1[:, 12288:16384]
        aoT = u1[:, 0:16384]
        vbf = sb("vbf", [128, 16 * 257], BF16)
        sqs = sb("sqs", [128, 4 * 512], BF16)
        qn2 = sb("qn2", [1, T], F32)
        kn2 = sb("kn2", [1, T], F32)
        csc = sb("csc", [1, 8], F32)
        negcb = sb("negcb", [128, 1], F32)
        rcp = sb("rcp", [128, 2], F32)
        aostage = sb("aostage", [128, 2 * 256], BF16)
        wo_sb = sb("wo_sb", [128, 8 * 1024], BF16)
        onesr = sb("onesr", [1, 128], F32)
        onecb = sb("onecb", [128, 1], BF16)
        id128 = sb("id128", [128, 128], F32)
        lrow = sb("lrow", [1, 5 * 512], F32)
        maxcol = sb("maxcol", [128, 8], F32)
        mfold = sb("mfold", [128, 1], F32)
        mxo = sb("mxo", [1, 1], F32)
        psum = es.enter_context(nc.psum_tensor("psum", [128, 4096], F32))
        B0 = psum[:, 0:512]
        B1 = psum[:, 512:1024]
        B2 = psum[:, 1024:1536]
        B23 = psum[0:1, 1024:2048]
        B3c = psum[:, 1536:1537]
        B4 = psum[:, 2048:2560]
        B5 = psum[:, 2560:3072]
        B6 = psum[:, 3072:3584]
        B7 = psum[:, 3584:4096]
        _cfg = sem("s_cfg")
        _vb = sem("s_vb")
        _xs, _xb = sem("s_xs"), sem("s_xb")
        _wq0, _wq1 = sem("s_wq0"), sem("s_wq1")
        _lat, _late = sem("s_lat"), sem("s_late")
        _mmq, _evq = sem("s_mmq"), sem("s_evq")
        _nmm, _nev = sem("s_nmm"), sem("s_nev")
        _mmv, _evv = sem("s_mmv"), sem("s_evv")
        _ch, _cha, _chb = sem("s_ch"), sem("s_cha"), sem("s_chb")
        _smm, _sex = sem("s_smm"), sem("s_sex")
        _sq = sem("s_sq")
        _sexp = sem("s_sexp")
        _amm, _arc, _aev = sem("s_amm"), sem("s_arc"), sem("s_aev")
        _as0, _as1 = sem("s_as0"), sem("s_as1")
        _ao0, _ao1 = sem("s_ao0"), sem("s_ao1")
        _wo0, _wo1 = sem("s_wo0"), sem("s_wo1")
        _pmm, _pev = sem("s_pmm"), sem("s_pev")
        _ln, _lnp, _lna = sem("s_ln"), sem("s_lnp"), sem("s_lna")
        _mx, _mxp = sem("s_mx"), sem("s_mxp")
        _so = sem("s_so")
        block = es.enter_context(nc.Block())

        @block.gpsimd
        def _(g):
            g.dma_start(out=onesr[:, :], in_=ones_f32[:, :]).then_inc(_cfg, 16)
            g.dma_start(out=onecb[:, :], in_=onescol_b[:, :]).then_inc(_cfg, 16)
            g.dma_start(out=id128[:, :], in_=ident128[:, :]).then_inc(_cfg, 16)
            g.dma_start(out=kvw[:, :].rearrange("p (c n) -> p c n", n=LR),
                        in_=wkvT[:, :].rearrange("(c p) n -> p c n", p=128)
                        ).then_inc(_cfg, 16)
            g.dma_start(out=kup[:, :], in_=wkuT[:, :]).then_inc(_cfg, 16)
            g.dma_start(out=vup[:, :], in_=wvuT[:, :]).then_inc(_cfg, 16)
            for tc in range(16):
                g.memset(vbf[:, tc * 257 + 256:tc * 257 + 257], 1.0)
            g.sem_inc(_vb, 1)
            for c in range(DC):
                g.dma_start(out=xbf[:, c * T:(c + 1) * T],
                            in_=xT[:, c * T:(c + 1) * T]).then_inc(_xb, 16)
            for h in range(2):
                g.dma_start(
                    out=wq_sb[:, (h % 2) * DC * 256:(h % 2 + 1) * DC * 256]
                        .rearrange("p (c n) -> p c n", n=256),
                    in_=wqT[:, h * 256:(h + 1) * 256]
                        .rearrange("(c p) n -> p c n", p=128)
                ).then_inc(_wq0 if h % 2 == 0 else _wq1, 16)
            for h in range(NH):
                if h + 2 < NH:
                    g.wait_ge(_mmq, h * 16 + 8)
                    hh = h + 2
                    g.dma_start(
                        out=wq_sb[:, (hh % 2) * DC * 256:(hh % 2 + 1) * DC * 256]
                            .rearrange("p (c n) -> p c n", n=256),
                        in_=wqT[:, hh * 256:(hh + 1) * 256]
                            .rearrange("(c p) n -> p c n", p=128)
                    ).then_inc(_wq0 if hh % 2 == 0 else _wq1, 16)
            for H in range(2):
                if H == 1:
                    g.wait_ge(_pmm, 32)
                for c8 in range(8):
                    cc = H * 8 + c8
                    g.dma_start(out=wo_sb[:, c8 * 1024:(c8 + 1) * 1024],
                                in_=woT[cc * 128:(cc + 1) * 128, :]
                                ).then_inc(_wo0 if H == 0 else _wo1, 16)
            g.wait_ge(_so, 16 * (TB + 1))

        @block.sync
        def _(sy):
            for c in range(DC):
                sy.dma_start(out=stream[:, c * T:(c + 1) * T],
                             in_=xT[:, c * T:(c + 1) * T]).then_inc(_xs, 16)
            for h in range(NH):
                for qb in range(TB):
                    for qt in range(4):
                        aidx = (h * TB + qb) * 4 + qt
                        sy.wait_ge(_aev, aidx + 1)
                        sy.dma_start(
                            out=attnout[qb * 512 + qt * 128:qb * 512 + qt * 128 + 128,
                                        h * 256:(h + 1) * 256],
                            in_=aostage[:, (aidx % 2) * 256:(aidx % 2) * 256 + 256],
                        ).then_inc(_as0 if aidx % 2 == 0 else _as1, 16)
            for tb in range(TB):
                sy.wait_ge(_ln, LNS * (tb + 1))
                sv = streamout[:, :].rearrange("p (c t) -> p c t", t=T)[:, :, tb * 512:tb * 512 + 512]
                sy.dma_start(out=sv,
                             in_=stream[:, :].rearrange("p (c t) -> p c t", t=T)[:, :, tb * 512:tb * 512 + 512]
                             ).then_inc(_so, 16)
            sy.wait_ge(_mx, 10)
            sy.dma_start(out=maxx[0:1, 0:1], in_=mxo[0:1, 0:1]).then_inc(_so, 16)

        @block.scalar
        def _(s):
            for tb in range(TB):
                s.wait_ge(_lat, tb + 1)
                s.activation(latbf[0:64, tb * 512:(tb + 1) * 512],
                             B0[0:64, :], AF.Copy).then_inc(_late, 1)
            for h in range(NH):
                for ph in range(2):
                    for tbj in range(8):
                        gidx = h * 16 + ph * 8 + tbj
                        tb, j = tbj // 2, tbj % 2
                        s.wait_ge(_evq, gidx + 1)
                        if gidx >= 4:
                            s.wait_ge(_nmm, (gidx - 4) // 2 + 1)
                        srcb = qbf if ph == 0 else kbf
                        ssl = sqs[:, (tbj % 4) * 512:(tbj % 4) * 512 + 512]
                        dsl = srcb[:, j * T + tb * 512:j * T + tb * 512 + 512]
                        s.activation(ssl, dsl, AF.Square).then_inc(_sq, 1)
                for tc in range(16):
                    vidx = h * 16 + tc
                    s.wait_ge(_mmv, vidx + 1)
                    if h > 0 and tc == 0:
                        s.wait_ge(_amm, h * 16)
                    s.activation(vbf[:, tc * 257:tc * 257 + 256],
                                 psum[:, 2048 + (vidx % 2) * 512:2048 + (vidx % 2) * 512 + 256],
                                 AF.Copy).then_inc(_evv, 1)
                s.wait_ge(_ch, h * 5 + 3)
                s.activation(csc[0:1, 3:4], csc[0:1, 2:3], AF.Ln).then_inc(_cha, 1)
                s.wait_ge(_cha, h * 2 + 1)
                s.activation(csc[0:1, 4:5], csc[0:1, 3:4], AF.Exp, bias=0.0,
                             scale=0.5).then_inc(_cha, 1)
                for qb in range(TB):
                    for pr in range(8):
                        pidx2 = (h * TB + qb) * 8 + pr
                        s.wait_ge(_smm, (h * TB + qb) * 16 + 2 * pr + 2)
                        s.wait_ge(_ch, h * 5 + 5)
                        if pr == 0 and (qb > 0 or h > 0):
                            s.wait_ge(_amm, (h * TB + qb) * 4)
                        off = 2048 if pr % 2 == 0 else 0
                        s.activation(probsT[:, pr * 1024:(pr + 1) * 1024],
                                     psum[:, off:off + 1024],
                                     AF.Exp, bias=negcb[:, 0:1],
                                     scale=1.0).then_inc(_sexp, 1)
                    for qt in range(4):
                        aidx = (h * TB + qb) * 4 + qt
                        s.wait_ge(_arc, aidx + 1)
                        spar, scnt = (_as0, 16 * (aidx // 2)) if aidx % 2 == 0 \
                            else (_as1, 16 * (aidx // 2))
                        if aidx >= 2:
                            s.wait_ge(spar, scnt)
                        s.activation(aostage[:, (aidx % 2) * 256:(aidx % 2) * 256 + 256],
                                     psum[:, 3072 + (aidx % 2) * 512:3072 + (aidx % 2) * 512 + 256],
                                     AF.Copy, bias=0.0,
                                     scale=rcp[:, aidx % 2:aidx % 2 + 1]
                                     ).then_inc(_aev, 1)
            s.wait_ge(_as0, 16 * 64)
            s.wait_ge(_as1, 16 * 64)
            for H in range(2):
                for qb in range(TB):
                    g8 = H * 4 + qb
                    if g8 >= 2:
                        s.wait_ge(_pmm, 8 * (g8 - 1))
                    for c8 in range(8):
                        cc = H * 8 + c8
                        s.dma_start(
                            out=aoT[:, (g8 % 2) * 16 * 512 + c8 * 512:(g8 % 2) * 16 * 512 + c8 * 512 + 512],
                            in_=attnout[qb * 512:(qb + 1) * 512, cc * 128:(cc + 1) * 128],
                            transpose=True,
                        ).then_inc(_ao0 if g8 % 2 == 0 else _ao1, 16)
            for tb in range(TB):
                b = LNS * tb
                s.wait_ge(_ln, b + 1)
                s.activation(xbf[:, DC * 512:2 * DC * 512], xbf[:, 0:DC * 512],
                             AF.Square).then_inc(_lna, 1)
                s.wait_ge(_ln, b + 5)
                s.activation(lrow[0:1, 1536:2048], lrow[0:1, 1024:1536],
                             AF.Ln).then_inc(_lna, 1)
                s.wait_ge(_lna, 3 * tb + 2)
                s.activation(lrow[0:1, 1536:2048], lrow[0:1, 1536:2048], AF.Exp,
                             bias=0.0, scale=-0.5).then_inc(_lna, 1)

        @block.vector
        def _(v):
            for h in range(NH):
                for ph in range(2):
                    for tbj in range(8):
                        gidx = h * 16 + ph * 8 + tbj
                        tb, j = tbj // 2, tbj % 2
                        v.wait_ge(_mmq, gidx + 1)
                        if ph == 0 and tbj == 0 and h > 0:
                            v.wait_ge(_smm, h * 64)
                        dst = qbf if ph == 0 else kbf
                        sc = (1.0 / 16.0) if ph == 0 else 1.0
                        v.tensor_scalar(dst[:, j * T + tb * 512:j * T + tb * 512 + 512],
                                        psum[:, (gidx % 2) * 512:(gidx % 2) * 512 + 512],
                                        sc, None, ALU.mult).then_inc(_evq, 1)
                    for tb in range(TB):
                        v.wait_ge(_nmm, h * 8 + ph * 4 + tb + 1)
                        dst = qn2 if ph == 0 else kn2
                        v.tensor_copy(dst[0:1, tb * 512:(tb + 1) * 512],
                                      B2[0:1, :]).then_inc(_nev, 1)
                v.wait_ge(_nev, h * 8 + 4)
                v.tensor_reduce(csc[0:1, 0:1], qn2[0:1, :], AX,
                                ALU.max).then_inc(_ch, 1)
                v.wait_ge(_nev, h * 8 + 8)
                v.tensor_reduce(csc[0:1, 1:2], kn2[0:1, :], AX,
                                ALU.max).then_inc(_ch, 1)
                v.wait_ge(_ch, h * 5 + 2)
                v.tensor_tensor(csc[0:1, 2:3], csc[0:1, 0:1], csc[0:1, 1:2],
                                ALU.mult).then_inc(_ch, 1)
                v.wait_ge(_cha, h * 2 + 2)
                v.tensor_scalar(csc[0:1, 5:6], csc[0:1, 4:5], -1.01, None,
                                ALU.mult).then_inc(_ch, 1)
                v.wait_ge(_chb, h + 1)
                if h > 0:
                    v.wait_ge(_sexp, h * 32)
                v.tensor_copy(negcb[:, 0:1], B3c[:, 0:1]).then_inc(_ch, 1)
                for qb in range(TB):
                    for qt in range(4):
                        aidx = (h * TB + qb) * 4 + qt
                        v.wait_ge(_amm, aidx + 1)
                        if aidx >= 2:
                            v.wait_ge(_aev, aidx - 1)
                        v.reciprocal(rcp[:, aidx % 2:aidx % 2 + 1],
                                     psum[:, 3072 + (aidx % 2) * 512 + 256:3072 + (aidx % 2) * 512 + 257]
                                     ).then_inc(_arc, 1)
            v.wait_ge(_xs, 16 * DC)
            for H in range(2):
                for qb in range(TB):
                    for dc in range(DC):
                        pidx = H * 32 + qb * 8 + dc
                        v.wait_ge(_pmm, pidx + 1)
                        sl = stream[:, dc * T + qb * 512:dc * T + qb * 512 + 512]
                        v.tensor_tensor(sl, sl,
                                        psum[:, (pidx % 2) * 512:(pidx % 2) * 512 + 512],
                                        ALU.add).then_inc(_pev, 1)
            for tb in range(TB):
                b = LNS * tb
                v.wait_ge(_pev, 32 + 8 * (tb + 1))
                sv = stream[:, :].rearrange("p (c t) -> p c t", t=T)[:, :, tb * 512:tb * 512 + 512]
                v.tensor_copy(xbf[:, 0:DC * 512].rearrange("p (c t) -> p c t", t=512),
                              sv).then_inc(_ln, 1)
                v.wait_ge(_lnp, 2 * tb + 1)
                v.tensor_scalar(lrow[0:1, 0:512], B23[0:1, 0:512], 1.0 / D,
                                None, ALU.mult)
                v.tensor_scalar(lrow[0:1, 512:1024], B23[0:1, 512:1024],
                                1.0 / D, None, ALU.mult).then_inc(_ln, 1)
                v.wait_ge(_ln, b + 2)
                v.tensor_tensor(lrow[0:1, 1024:1536], lrow[0:1, 0:512],
                                lrow[0:1, 0:512], ALU.mult).then_inc(_ln, 1)
                v.wait_ge(_ln, b + 3)
                v.tensor_tensor(lrow[0:1, 1024:1536], lrow[0:1, 512:1024],
                                lrow[0:1, 1024:1536], ALU.subtract).then_inc(_ln, 1)
                v.wait_ge(_ln, b + 4)
                v.tensor_scalar(lrow[0:1, 1024:1536], lrow[0:1, 1024:1536],
                                EPS_LN, None, ALU.add).then_inc(_ln, 1)
                v.wait_ge(_lna, 3 * tb + 3)
                v.tensor_tensor(lrow[0:1, 2048:2560], lrow[0:1, 0:512],
                                lrow[0:1, 1536:2048], ALU.mult).then_inc(_ln, 1)
                v.wait_ge(_lnp, 2 * tb + 2)
                for dcn in range(DC):
                    sl2 = stream[:, dcn * T + tb * 512:dcn * T + tb * 512 + 512]
                    v.tensor_tensor(sl2, sl2, B6[:, :], ALU.mult).then_inc(_ln, 1)
                    v.wait_ge(_ln, b + 6 + 2 * dcn + 1)
                    v.tensor_tensor(sl2, sl2, B7[:, :],
                                    ALU.subtract).then_inc(_ln, 1)
            v.wait_ge(_ln, TB * LNS)
            for dc in range(DC):
                v.tensor_reduce(maxcol[:, dc:dc + 1], stream[:, dc * T:(dc + 1) * T],
                                AX, ALU.max, apply_absolute_value=True).then_inc(_mx, 1)
            v.wait_ge(_mx, 8)
            v.tensor_reduce(mfold[:, 0:1], maxcol[:, 0:8], AX,
                            ALU.max).then_inc(_mx, 1)
            v.wait_ge(_mxp, 1)
            v.tensor_reduce(mxo[0:1, 0:1], B4[0:1, 0:128], AX,
                            ALU.max).then_inc(_mx, 1)

        @block.tensor
        def _(t_):
            t_.wait_ge(_cfg, 96)
            t_.wait_ge(_vb, 1)
            t_.wait_ge(_xb, 16 * DC)
            for tb in range(TB):
                if tb > 0:
                    t_.wait_ge(_late, tb)
                ins = None
                for dc in range(DC):
                    ins = t_.matmul(B0[0:64, :], kvw[:, dc * LR:(dc + 1) * LR],
                                    xbf[:, dc * T + tb * 512:dc * T + tb * 512 + 512],
                                    start=(dc == 0), stop=(dc == DC - 1))
                ins.then_inc(_lat, 1)
            t_.wait_ge(_late, TB)
            for h in range(NH):
                t_.wait_ge(_wq0 if h % 2 == 0 else _wq1, 16 * (h // 2 + 1))
                for ph in range(2):
                    for tbj in range(8):
                        gidx = h * 16 + ph * 8 + tbj
                        tb, j = tbj // 2, tbj % 2
                        if gidx >= 2:
                            t_.wait_ge(_evq, gidx - 1)
                        ins = None
                        if ph == 0:
                            for dc in range(DC):
                                base = (h % 2) * DC * 256 + dc * 256 + j * 128
                                ins = t_.matmul(
                                    psum[:, (gidx % 2) * 512:(gidx % 2) * 512 + 512],
                                    wq_sb[:, base:base + 128],
                                    xbf[:, dc * T + tb * 512:dc * T + tb * 512 + 512],
                                    start=(dc == 0), stop=(dc == DC - 1))
                        else:
                            ins = t_.matmul(
                                psum[:, (gidx % 2) * 512:(gidx % 2) * 512 + 512],
                                kup[0:64, h * 256 + j * 128:h * 256 + j * 128 + 128],
                                latbf[0:64, tb * 512:(tb + 1) * 512],
                                start=True, stop=True)
                        ins.then_inc(_mmq, 1)
                    for tbj in range(8):
                        gidx = h * 16 + ph * 8 + tbj
                        tb, j = tbj // 2, tbj % 2
                        t_.wait_ge(_sq, gidx + 1)
                        if j == 0 and not (h == 0 and ph == 0 and tb == 0):
                            t_.wait_ge(_nev, h * 8 + ph * 4 + tb)
                        ins2 = t_.matmul(B2[0:1, :], onecb[:, :],
                                         sqs[:, (tbj % 4) * 512:(tbj % 4) * 512 + 512],
                                         start=(j == 0), stop=(j == 1))
                        if j == 1:
                            ins2.then_inc(_nmm, 1)
                for tc in range(16):
                    vidx = h * 16 + tc
                    if vidx >= 2:
                        t_.wait_ge(_evv, vidx - 1)
                    ins = t_.matmul(psum[:, 2048 + (vidx % 2) * 512:2048 + (vidx % 2) * 512 + 256],
                                    latbf[0:64, tc * 128:(tc + 1) * 128],
                                    vup[0:64, h * 256:(h + 1) * 256],
                                    start=True, stop=True)
                    ins.then_inc(_mmv, 1)
                t_.wait_ge(_ch, h * 5 + 4)
                t_.matmul(B3c[:, 0:1], onesr[0:1, :], csc[0:1, 5:6],
                          start=True, stop=True).then_inc(_chb, 1)
                for qb in range(TB):
                    if qb == 0:
                        t_.wait_ge(_evv, h * 16 + 16)
                        t_.wait_ge(_evq, h * 16 + 16)
                    for kt in range(16):
                        sidx = (h * TB + qb) * 16 + kt
                        prg = (h * TB + qb) * 8 + kt // 2
                        if prg >= 2:
                            t_.wait_ge(_sexp, prg - 1)
                        off = (2048 if (kt // 2) % 2 == 0 else 0) + (kt % 2) * 512
                        ins = None
                        for j in range(2):
                            ins = t_.matmul(
                                psum[:, off:off + 512],
                                kbf[:, j * T + kt * 128:j * T + kt * 128 + 128],
                                qbf[:, j * T + qb * 512:j * T + qb * 512 + 512],
                                start=(j == 0), stop=(j == 1))
                        ins.then_inc(_smm, 1)
                    t_.wait_ge(_sexp, (h * TB + qb) * 8 + 8)
                    for qt in range(4):
                        aidx = (h * TB + qb) * 4 + qt
                        if aidx >= 2:
                            t_.wait_ge(_aev, aidx - 1)
                        ins = None
                        for kc in range(16):
                            ins = t_.matmul(
                                psum[:, 3072 + (aidx % 2) * 512:3072 + (aidx % 2) * 512 + 257],
                                probsT[:, kc * 512 + qt * 128:kc * 512 + qt * 128 + 128],
                                vbf[:, kc * 257:(kc + 1) * 257],
                                start=(kc == 0), stop=(kc == 15))
                        ins.then_inc(_amm, 1)
            for H in range(2):
                t_.wait_ge(_wo0 if H == 0 else _wo1, 16 * 8)
                for qb in range(TB):
                    g8 = H * 4 + qb
                    t_.wait_ge(_ao0 if g8 % 2 == 0 else _ao1, 16 * 8 * (g8 // 2 + 1))
                    for dc in range(DC):
                        pidx = H * 32 + qb * 8 + dc
                        if pidx >= 2:
                            t_.wait_ge(_pev, pidx - 1)
                        ins = None
                        for c8 in range(8):
                            ins = t_.matmul(
                                psum[:, (pidx % 2) * 512:(pidx % 2) * 512 + 512],
                                wo_sb[:, c8 * 1024 + dc * 128:c8 * 1024 + dc * 128 + 128],
                                aoT[:, (g8 % 2) * 16 * 512 + c8 * 512:(g8 % 2) * 16 * 512 + c8 * 512 + 512],
                                start=(c8 == 0), stop=(c8 == 7))
                        ins.then_inc(_pmm, 1)
            for tb in range(TB):
                b = LNS * tb
                t_.wait_ge(_ln, b + 1)
                t_.wait_ge(_lna, 3 * tb + 1)
                for dcn in range(DC):
                    t_.matmul(B23[0:1, 0:512], onecb[:, :],
                              xbf[:, dcn * 512:(dcn + 1) * 512],
                              start=(dcn == 0), stop=(dcn == DC - 1))
                ins = None
                for dcn in range(DC):
                    ins = t_.matmul(B23[0:1, 512:1024], onecb[:, :],
                                    xbf[:, DC * 512 + dcn * 512:DC * 512 + (dcn + 1) * 512],
                                    start=(dcn == 0), stop=(dcn == DC - 1))
                ins.then_inc(_lnp, 1)
                t_.wait_ge(_ln, b + 6)
                t_.matmul(B6[:, :], onesr[0:1, :], lrow[0:1, 1536:2048],
                          start=True, stop=True)
                ins = t_.matmul(B7[:, :], onesr[0:1, :], lrow[0:1, 2048:2560],
                                start=True, stop=True)
                ins.then_inc(_lnp, 1)
            t_.wait_ge(_mx, 9)
            t_.transpose(B4[0:1, 0:128], mfold[0:128, 0:1],
                         id128[:, :]).then_inc(_mxp, 1)
    return nc


_NC_CACHE = {}


def _nc(name):
    if name not in _NC_CACHE:
        _NC_CACHE[name] = globals()["build_" + name]()
    return _NC_CACHE[name]


def _to_fm(a):
    return np.ascontiguousarray(
        a.T.reshape(DC, 128, T).transpose(1, 0, 2).reshape(128, DC * T))


def _from_fm(a):
    return a.reshape(128, DC, T).transpose(1, 0, 2).reshape(D, T).T


def kernel(L0_attn_x, L0_attn_Wq, L0_attn_Wkv_down, L0_attn_Wk_up,
           L0_attn_Wv_up, L0_attn_Wo, L0_ffn_w_gate, L0_ffn_w_up,
           L0_ffn_w_down):
    cores = list(range(8))
    x = np.ascontiguousarray(np.asarray(L0_attn_x, np.float32))
    w_up = np.ascontiguousarray(np.asarray(L0_ffn_w_up, np.float32))
    w_gate = np.ascontiguousarray(np.asarray(L0_ffn_w_gate, np.float32))
    w_down = np.ascontiguousarray(np.asarray(L0_ffn_w_down, np.float32))

    in0 = [{"wu": np.ascontiguousarray(w_up[c * 384:(c + 1) * 384]),
            "wg": np.ascontiguousarray(w_gate[c * 384:(c + 1) * 384]),
            "wd": np.ascontiguousarray(w_down[c * 128:(c + 1) * 128])}
           for c in cores]
    r0 = run_bass_kernel_spmd(_nc("l0"), in0, core_ids=cores).results
    wupdq = np.concatenate([np.asarray(r0[c]["wuo"]) for c in cores], 0)
    wgatedq = np.concatenate([np.asarray(r0[c]["wgo"]) for c in cores], 0)
    wdowndq = np.concatenate([np.asarray(r0[c]["wdo"]) for c in cores], 0)

    wqT = np.ascontiguousarray(np.asarray(L0_attn_Wq, np.float32).T)
    wkvT = np.ascontiguousarray(np.asarray(L0_attn_Wkv_down, np.float32).T)
    wkuT = np.ascontiguousarray(np.asarray(L0_attn_Wk_up, np.float32).T)
    wvuT = np.ascontiguousarray(np.asarray(L0_attn_Wv_up, np.float32).T)
    woT = np.ascontiguousarray(np.asarray(L0_attn_Wo, np.float32).T)
    in1 = [{"xT": _to_fm(x[c]), "wqT": wqT, "wkvT": wkvT, "wkuT": wkuT,
            "wvuT": wvuT, "woT": woT} for c in cores]
    r1 = run_bass_kernel_spmd(_nc("l1"), in1, core_ids=cores).results
    streams = [np.asarray(r1[c]["streamout"]) for c in cores]
    maxx8 = np.concatenate([np.asarray(r1[c]["maxx"]) for c in cores],
                           axis=1)

    in2 = [{"stream": streams[c], "maxx": maxx8, "wup": wupdq,
            "wgate": wgatedq} for c in cores]
    r2 = run_bass_kernel_spmd(_nc("l2"), in2, core_ids=cores).results
    maxug8 = np.concatenate(
        [np.asarray(r2[c]["maxug"])[0:1, :] for c in cores] +
        [np.asarray(r2[c]["maxug"])[1:2, :] for c in cores], axis=0).reshape(1, 16)

    in3 = [{"gatein": np.asarray(r2[c]["gateout"]), "maxug8": maxug8}
           for c in cores]
    r3 = run_bass_kernel_spmd(_nc("l3"), in3, core_ids=cores).results
    maxg28 = np.concatenate([np.asarray(r3[c]["maxg2"]) for c in cores],
                            axis=1)

    in4 = [{"g2in": np.asarray(r3[c]["g2out"]),
            "upin": np.asarray(r2[c]["upout"]),
            "maxg28": maxg28, "maxug8": maxug8} for c in cores]
    r4 = run_bass_kernel_spmd(_nc("l4"), in4, core_ids=cores).results
    maxo8 = np.concatenate([np.asarray(r4[c]["maxo"]) for c in cores], axis=1)

    in5 = [{"oin": np.asarray(r4[c]["oout"]), "maxo8": maxo8,
            "streamin": streams[c], "wdown": wdowndq} for c in cores]
    r5 = run_bass_kernel_spmd(_nc("l5"), in5, core_ids=cores).results

    out = np.empty((8, T, D), np.float32)
    for c in cores:
        out[c] = _from_fm(np.asarray(r5[c]["final"]))
    return out


# version 06
# speedup vs baseline: 1.1309x; 1.0394x over previous
from contextlib import ExitStack

import numpy as np
import concourse.bass as bass
import concourse.mybir as mybir
from concourse.bass_utils import run_bass_kernel_spmd

F32 = mybir.dt.float32
BF16 = mybir.dt.bfloat16
I16 = mybir.dt.int16
U16 = mybir.dt.uint16
I8 = mybir.dt.int8
AF = mybir.ActivationFunctionType
ALU = mybir.AluOpType
AX = mybir.AxisListType.X

T = 2048
D = 1024
NH = 8
DH = 256
LR = 64
FF = 3072
DC = D // 128
FC = FF // 128
TB = T // 512

QMAX = 32767.0
WQMAX = 7.0
EPS_Q = 1e-12
EPS_LN = 1e-5


class Ctr:

    def __init__(self, sem):
        self.sem = sem
        self.v = 0

    def inc(self, n=1):
        self.v += n
        return self.v



def build_l2():
    nc = bass.Bass("TRN2", target_bir_lowering=False, debug=False)
    stream = nc.dram_tensor("stream", [128, DC * T], F32, kind="ExternalInput")
    maxx = nc.dram_tensor("maxx", [1, 8], F32, kind="ExternalInput")
    wup = nc.dram_tensor("wup", [FF, D], BF16, kind="ExternalInput")
    wgate = nc.dram_tensor("wgate", [FF, D], BF16, kind="ExternalInput")
    upout = nc.dram_tensor("upout", [FF, T], BF16, kind="ExternalOutput")
    gateout = nc.dram_tensor("gateout", [FF, T], BF16, kind="ExternalOutput")
    maxug = nc.dram_tensor("maxug", [2, 1], F32, kind="ExternalOutput")
    ones_f32 = nc.inline_tensor(np.ones((1, 128), np.float32), name="ones_f32")
    ident128 = nc.inline_tensor(np.eye(128, dtype=np.float32), name="ident128")

    NT = 2 * FC * TB

    with ExitStack() as es:
        sb = lambda *a: es.enter_context(nc.sbuf_tensor(*a))
        ps = lambda *a: es.enter_context(nc.psum_tensor(*a))
        sem = lambda n: es.enter_context(nc.semaphore(n))
        xq = sb("xq", [128, DC * T], BF16)
        sstage = sb("sstage", [128, 2 * T], F32)
        i16s = sb("i16s", [128, T], I16)
        warena = sb("warena", [128, 2 * DC * 512], BF16)
        stage = sb("stage", [128, 4 * 512], BF16)
        maxcol = sb("maxcol", [128, NT], F32)
        maxin = sb("maxin", [1, 8], F32)
        scal = sb("scal", [1, 2], F32)
        sxb = sb("sxb", [128, 2], F32)
        onesr = sb("onesr", [1, 128], F32)
        id128 = sb("id128", [128, 128], F32)
        mfold = sb("mfold", [128, 2], F32)
        mout = sb("mout", [2, 1], F32)
        ps_mm = ps("ps_mm", [128, 4 * 512], F32)
        ps_bc = ps("ps_bc", [128, 2], F32)
        ps_tr = ps("ps_tr", [2, 128], F32)
        _cfg, _scal, _bc, _q = (sem(n) for n in
                                ("s_cfg", "s_scal", "s_bc", "s_q"))
        _i0, _i1 = sem("s_i0"), sem("s_i1")
        _w0, _w1 = sem("s_w0"), sem("s_w1")
        _mm, _ev, _red = (sem(n) for n in ("s_mm", "s_ev", "s_red"))
        _stl = [sem(f"s_st{k}") for k in range(4)]
        _stf = sem("s_stf")
        block = es.enter_context(nc.Block())
        @block.gpsimd
        def _(g):
            g.dma_start(out=maxin[:, :], in_=maxx[:, :]).then_inc(_cfg, 16)
            g.dma_start(out=onesr[:, :], in_=ones_f32[:, :]).then_inc(_cfg, 16)
            g.dma_start(out=id128[:, :], in_=ident128[:, :]).then_inc(_cfg, 16)
            for c in range(DC):
                if c >= 2:
                    g.wait_ge(_q, 2 * (c - 1))
                g.dma_start(out=sstage[:, (c % 2) * T:(c % 2) * T + T],
                            in_=stream[:, c * T:(c + 1) * T]
                            ).then_inc(_i0 if c % 2 == 0 else _i1, 16)
            for k in range(4):
                g.wait_ge(_stl[k], 16 * (NT // 4))
            g.wait_ge(_stf, 16)

        @block.sync
        def _(sy):
            for gidx in range(12):
                ub, W = gidx // 2, gidx % 2
                wt = wup if W == 0 else wgate
                half = gidx % 2
                if gidx >= 2:
                    sy.wait_ge(_mm, 16 * (gidx - 1))
                for dc in range(DC):
                    base = half * DC * 512 + dc * 512
                    sy.dma_start(
                        out=warena[:, base:base + 512],
                        in_=wt[ub * 512:(ub + 1) * 512, dc * 128:(dc + 1) * 128],
                        transpose=True,
                    ).then_inc(_w0 if half == 0 else _w1, 16)

        @block.vector
        def _(v):
            v.wait_ge(_cfg, 48)
            v.tensor_reduce(scal[0:1, 0:1], maxin[0:1, 0:8], AX,
                            ALU.max).then_inc(_scal, 1)
            v.wait_ge(_scal, 1)
            v.tensor_scalar(scal[0:1, 0:1], scal[0:1, 0:1], 1.0 / QMAX, EPS_Q,
                            ALU.mult, ALU.max).then_inc(_scal, 1)
            v.wait_ge(_scal, 2)
            v.reciprocal(scal[0:1, 1:2], scal[0:1, 0:1]).then_inc(_scal, 1)
            v.wait_ge(_bc, 1)
            v.tensor_copy(sxb[:, :], ps_bc[:, :]).then_inc(_scal, 1)
            v.wait_ge(_scal, 4)
            for c in range(DC):
                v.wait_ge(_i0 if c % 2 == 0 else _i1, 16 * (c // 2 + 1))
                if c > 0:
                    v.wait_ge(_q, 2 * c)
                st_sl = sstage[:, (c % 2) * T:(c % 2) * T + T]
                v.tensor_scalar(i16s[:, :], st_sl, sxb[:, 1:2], None,
                                ALU.mult).then_inc(_q, 1)
                v.wait_ge(_q, 2 * c + 1)
                v.tensor_scalar(xq[:, c * T:(c + 1) * T], i16s[:, :],
                                sxb[:, 0:1], None, ALU.mult).then_inc(_q, 1)
            for t in range(NT):
                v.wait_ge(_ev, t + 1)
                gidx, sub = t // 16, t % 16
                col = (gidx % 2) * (NT // 2) + (gidx // 2) * 16 + sub
                v.tensor_reduce(maxcol[:, col:col + 1],
                                ps_mm[:, (t % 4) * 512:(t % 4) * 512 + 512],
                                AX, ALU.max,
                                apply_absolute_value=True).then_inc(_red, 1)
            v.wait_ge(_red, NT)
            v.tensor_reduce(mfold[:, 0:1], maxcol[:, 0:NT // 2], AX, ALU.max)
            v.tensor_reduce(mfold[:, 1:2], maxcol[:, NT // 2:NT], AX,
                            ALU.max).then_inc(_red, 1)
            v.wait_ge(_bc, 2)
            v.tensor_reduce(mout[0:2, 0:1], ps_tr[0:2, 0:128], AX,
                            ALU.max).then_inc(_red, 1)

        @block.tensor
        def _(t_):
            t_.wait_ge(_cfg, 48)
            t_.wait_ge(_scal, 3)
            t_.matmul(ps_bc[:, :], onesr[0:1, :], scal[0:1, 0:2], start=True,
                      stop=True).then_inc(_bc, 1)
            t_.wait_ge(_q, 2 * DC)
            for t in range(NT):
                gidx, sub = t // 16, t % 16
                half = gidx % 2
                tb, uc4 = sub // 4, sub % 4
                if sub == 0:
                    t_.wait_ge(_w0 if half == 0 else _w1,
                               16 * DC * (gidx // 2 + 1))
                if t >= 4:
                    t_.wait_ge(_ev, t - 3)
                    t_.wait_ge(_red, t - 3)
                ins = None
                for dc in range(DC):
                    base = half * DC * 512 + dc * 512 + uc4 * 128
                    ins = t_.matmul(
                        ps_mm[:, (t % 4) * 512:(t % 4) * 512 + 512],
                        warena[:, base:base + 128],
                        xq[:, dc * T + tb * 512:dc * T + tb * 512 + 512],
                        start=(dc == 0), stop=(dc == DC - 1),
                    )
                ins.then_inc(_mm, 1)
            t_.wait_ge(_red, NT + 1)
            t_.transpose(ps_tr[0:2, 0:128], mfold[0:128, 0:2],
                         id128[:, :]).then_inc(_bc, 1)

        @block.scalar
        def _(s):
            for t in range(NT):
                s.wait_ge(_mm, t + 1)
                if t >= 4:
                    s.wait_ge(_stl[t % 4], 16 * (t // 4))
                s.activation(stage[:, (t % 4) * 512:(t % 4) * 512 + 512],
                             ps_mm[:, (t % 4) * 512:(t % 4) * 512 + 512],
                             AF.Copy).then_inc(_ev, 1)
                s.wait_ge(_ev, t + 1)
                gidx, sub = t // 16, t % 16
                ub, W = gidx // 2, gidx % 2
                tb, uc4 = sub // 4, sub % 4
                uc = ub * 4 + uc4
                dst = upout if W == 0 else gateout
                s.dma_start(
                    out=dst[uc * 128:(uc + 1) * 128, tb * 512:(tb + 1) * 512],
                    in_=stage[:, (t % 4) * 512:(t % 4) * 512 + 512],
                ).then_inc(_stl[t % 4], 16)
            s.wait_ge(_red, NT + 2)
            s.dma_start(out=maxug[0:2, 0:1], in_=mout[0:2, 0:1]).then_inc(_stf, 16)

    return nc



def build_l0():
    nc = bass.Bass("TRN2", target_bir_lowering=False, debug=False)
    wu = nc.dram_tensor("wu", [384, D], F32, kind="ExternalInput")
    wg = nc.dram_tensor("wg", [384, D], F32, kind="ExternalInput")
    wd = nc.dram_tensor("wd", [128, FF], F32, kind="ExternalInput")
    wuo = nc.dram_tensor("wuo", [384, D], BF16, kind="ExternalOutput")
    wgo = nc.dram_tensor("wgo", [384, D], BF16, kind="ExternalOutput")
    wdo = nc.dram_tensor("wdo", [128, FF], BF16, kind="ExternalOutput")
    tiles = [(wu, wuo, 0, D), (wu, wuo, 1, D), (wu, wuo, 2, D),
             (wg, wgo, 0, D), (wg, wgo, 1, D), (wg, wgo, 2, D),
             (wd, wdo, 0, FF)]
    with ExitStack() as es:
        sb = lambda *a: es.enter_context(nc.sbuf_tensor(*a))
        sem = lambda n: es.enter_context(nc.semaphore(n))
        wstage = sb("wstage", [128, 2 * FF], F32)
        t32 = sb("t32", [128, FF], F32)
        q8 = sb("q8", [128, FF], I8)
        sraw = sb("sraw", [128, FF // 32], F32)
        sinv = sb("sinv", [128, FF // 32], F32)
        ostage = sb("ostage", [128, 2 * FF], BF16)
        _i0, _i1 = sem("s_i0"), sem("s_i1")
        _c = sem("s_c")
        _o0, _o1 = sem("s_o0"), sem("s_o1")
        block = es.enter_context(nc.Block())

        @block.gpsimd
        def _(g):
            for k, (src_t, _, r, I) in enumerate(tiles):
                par, cnt = (_i0, k // 2 + 1) if k % 2 == 0 else (_i1, k // 2 + 1)
                if k >= 2:
                    g.wait_ge(_c, 6 * (k - 1))
                g.dma_start(out=wstage[:, (k % 2) * FF:(k % 2) * FF + I],
                            in_=src_t[r * 128:(r + 1) * 128, 0:I]).then_inc(par, 16)
            g.wait_ge(_o0, 16 * 4)
            g.wait_ge(_o1, 16 * 3)

        @block.vector
        def _(v):
            for k, (_, _, r, I) in enumerate(tiles):
                par, cnt = (_i0, 16 * (k // 2 + 1)) if k % 2 == 0 else (_i1, 16 * (k // 2 + 1))
                v.wait_ge(par, cnt)
                if k >= 2:
                    spar, scnt = (_o0, 16 * (k // 2)) if k % 2 == 0 else (_o1, 16 * (k // 2))
                    v.wait_ge(spar, scnt)
                if k > 0:
                    v.wait_ge(_c, 6 * k)
                w_sl = wstage[:, (k % 2) * FF:(k % 2) * FF + I]
                nb = I // 32
                ww = w_sl.rearrange("p (b w) -> p b w", w=32)
                v.tensor_reduce(sraw[:, 0:nb], ww, AX, ALU.max,
                                apply_absolute_value=True).then_inc(_c, 1)
                v.wait_ge(_c, 6 * k + 1)
                v.tensor_scalar(sraw[:, 0:nb], sraw[:, 0:nb], 1.0 / WQMAX, EPS_Q,
                                ALU.mult, ALU.max).then_inc(_c, 1)
                v.wait_ge(_c, 6 * k + 2)
                v.reciprocal(sinv[:, 0:nb], sraw[:, 0:nb]).then_inc(_c, 1)
                v.wait_ge(_c, 6 * k + 3)
                sb_b = sraw[:, 0:nb].unsqueeze(2).broadcast_to([128, nb, 32])
                si_b = sinv[:, 0:nb].unsqueeze(2).broadcast_to([128, nb, 32])
                tw = t32[:, 0:I].rearrange("p (b w) -> p b w", w=32)
                v.tensor_tensor(tw, ww, si_b, ALU.mult).then_inc(_c, 1)
                v.wait_ge(_c, 6 * k + 4)
                v.tensor_copy(q8[:, 0:I], t32[:, 0:I]).then_inc(_c, 1)
                v.wait_ge(_c, 6 * k + 5)
                o_sl = ostage[:, (k % 2) * FF:(k % 2) * FF + I]
                ow = o_sl.rearrange("p (b w) -> p b w", w=32)
                qw = q8[:, 0:I].rearrange("p (b w) -> p b w", w=32)
                v.tensor_tensor(ow, qw, sb_b, ALU.mult).then_inc(_c, 1)

        @block.sync
        def _(sy):
            for k, (_, dst_t, r, I) in enumerate(tiles):
                sy.wait_ge(_c, 6 * (k + 1))
                par = _o0 if k % 2 == 0 else _o1
                sy.dma_start(out=dst_t[r * 128:(r + 1) * 128, 0:I],
                             in_=ostage[:, (k % 2) * FF:(k % 2) * FF + I]
                             ).then_inc(par, 16)
    return nc



def build_l3():
    nc = bass.Bass("TRN2", target_bir_lowering=False, debug=False)
    gatein = nc.dram_tensor("gatein", [FF, T], BF16, kind="ExternalInput")
    maxug8 = nc.dram_tensor("maxug8", [1, 16], F32, kind="ExternalInput")
    g2out = nc.dram_tensor("g2out", [FF, T], BF16, kind="ExternalOutput")
    maxg2 = nc.dram_tensor("maxg2", [1, 1], F32, kind="ExternalOutput")
    ones_f32 = nc.inline_tensor(np.ones((1, 128), np.float32), name="ones_f32")
    ident128 = nc.inline_tensor(np.eye(128, dtype=np.float32), name="ident128")
    with ExitStack() as es:
        sb = lambda *a: es.enter_context(nc.sbuf_tensor(*a))
        ps = lambda *a: es.enter_context(nc.psum_tensor(*a))
        sem = lambda n: es.enter_context(nc.semaphore(n))
        gstage = sb("gstage", [128, 2 * T], BF16)
        i16g = sb("i16g", [128, T], I16)
        sig = sb("sig", [128, T], F32)
        u16s = sb("u16s", [128, T], U16)
        g2f = sb("g2f", [128, T], F32)
        ostage = sb("ostage", [128, 2 * T], BF16)
        maxin = sb("maxin", [1, 16], F32)
        scal = sb("scal", [1, 4], F32)
        scb = sb("scb", [128, 4], F32)
        onesr = sb("onesr", [1, 128], F32)
        id128 = sb("id128", [128, 128], F32)
        maxcol = sb("maxcol", [128, FC], F32)
        mfold = sb("mfold", [128, 1], F32)
        mo = sb("mo", [1, 1], F32)
        ps_bc = ps("ps_bc", [128, 4], F32)
        ps_tr = ps("ps_tr", [1, 128], F32)
        _cfg, _scal, _bc = sem("s_cfg"), sem("s_scal"), sem("s_bc")
        _i0, _i1 = sem("s_i0"), sem("s_i1")
        _vq = sem("s_vq")
        _sg = sem("s_sg")
        _red = sem("s_red")
        _o0, _o1 = sem("s_o0"), sem("s_o1")
        _stf = sem("s_stf")
        block = es.enter_context(nc.Block())

        @block.gpsimd
        def _(g):
            g.dma_start(out=maxin[:, :], in_=maxug8[:, :]).then_inc(_cfg, 16)
            g.dma_start(out=onesr[:, :], in_=ones_f32[:, :]).then_inc(_cfg, 16)
            g.dma_start(out=id128[:, :], in_=ident128[:, :]).then_inc(_cfg, 16)
            for c in range(FC):
                if c >= 2:
                    g.wait_ge(_vq, 5 * (c - 1))
                g.dma_start(out=gstage[:, (c % 2) * T:(c % 2) * T + T],
                            in_=gatein[c * 128:(c + 1) * 128, :]
                            ).then_inc(_i0 if c % 2 == 0 else _i1, 16)
            g.wait_ge(_o0, 16 * (FC // 2))
            g.wait_ge(_o1, 16 * (FC // 2))
            g.wait_ge(_stf, 16)

        @block.vector
        def _(v):
            v.wait_ge(_cfg, 48)
            v.tensor_reduce(scal[0:1, 0:1], maxin[0:1, 8:16], AX,
                            ALU.max).then_inc(_scal, 1)
            v.wait_ge(_scal, 1)
            v.tensor_scalar(scal[0:1, 0:1], scal[0:1, 0:1], 1.0 / QMAX, EPS_Q,
                            ALU.mult, ALU.max).then_inc(_scal, 1)
            v.wait_ge(_scal, 2)
            v.reciprocal(scal[0:1, 1:2], scal[0:1, 0:1]).then_inc(_scal, 1)
            v.wait_ge(_scal, 3)
            v.tensor_scalar(scal[0:1, 2:3], scal[0:1, 0:1], 1.0 / 65536.0, None,
                            ALU.mult).then_inc(_scal, 1)
            v.wait_ge(_bc, 1)
            v.tensor_copy(scb[:, :], ps_bc[:, :]).then_inc(_scal, 1)
            v.wait_ge(_scal, 5)
            for c in range(FC):
                v.wait_ge(_i0 if c % 2 == 0 else _i1, 16 * (c // 2 + 1))
                if c >= 2:
                    v.wait_ge(_o0 if c % 2 == 0 else _o1, 16 * (c // 2))
                if c > 0:
                    v.wait_ge(_red, c)
                g_sl = gstage[:, (c % 2) * T:(c % 2) * T + T]
                v.tensor_scalar(i16g[:, :], g_sl, scb[:, 1:2], None,
                                ALU.mult).then_inc(_vq, 1)
                v.wait_ge(_sg, c + 1)
                v.tensor_scalar(u16s[:, :], sig[:, :], 65536.0, None,
                                ALU.mult).then_inc(_vq, 1)
                v.wait_ge(_vq, 5 * c + 2)
                v.scalar_tensor_tensor(g2f[:, :], i16g[:, :], scb[:, 2:3],
                                       u16s[:, :], ALU.mult,
                                       ALU.mult).then_inc(_vq, 1)
                v.wait_ge(_vq, 5 * c + 3)
                v.tensor_reduce(maxcol[:, c:c + 1], g2f[:, :], AX, ALU.max,
                                apply_absolute_value=True).then_inc(_red, 1)
                o_sl = ostage[:, (c % 2) * T:(c % 2) * T + T]
                v.tensor_copy(o_sl, g2f[:, :]).then_inc(_vq, 2)
            v.wait_ge(_red, FC)
            v.tensor_reduce(mfold[:, 0:1], maxcol[:, :], AX,
                            ALU.max).then_inc(_red, 1)
            v.wait_ge(_bc, 2)
            v.tensor_reduce(mo[0:1, 0:1], ps_tr[0:1, 0:128], AX,
                            ALU.max).then_inc(_red, 1)

        @block.scalar
        def _(s):
            s.wait_ge(_scal, 5)
            for c in range(FC):
                s.wait_ge(_vq, 5 * c + 1)
                s.activation(sig[:, :], i16g[:, :], AF.Sigmoid, bias=0.0,
                             scale=scb[:, 0:1]).then_inc(_sg, 1)
                if c < FC - 1:
                    s.wait_ge(_vq, 5 * c + 2)

        @block.tensor
        def _(t_):
            t_.wait_ge(_cfg, 48)
            t_.wait_ge(_scal, 4)
            t_.matmul(ps_bc[:, :], onesr[0:1, :], scal[0:1, 0:4], start=True,
                      stop=True).then_inc(_bc, 1)
            t_.wait_ge(_red, FC + 1)
            t_.transpose(ps_tr[0:1, 0:128], mfold[0:128, 0:1],
                         id128[:, :]).then_inc(_bc, 1)

        @block.sync
        def _(sy):
            for c in range(FC):
                sy.wait_ge(_vq, 5 * c + 5)
                sy.dma_start(out=g2out[c * 128:(c + 1) * 128, :],
                             in_=ostage[:, (c % 2) * T:(c % 2) * T + T]
                             ).then_inc(_o0 if c % 2 == 0 else _o1, 16)
            sy.wait_ge(_red, FC + 2)
            sy.dma_start(out=maxg2[0:1, 0:1], in_=mo[0:1, 0:1]).then_inc(_stf, 16)
    return nc



def build_l4():
    nc = bass.Bass("TRN2", target_bir_lowering=False, debug=False)
    g2in = nc.dram_tensor("g2in", [FF, T], BF16, kind="ExternalInput")
    upin = nc.dram_tensor("upin", [FF, T], BF16, kind="ExternalInput")
    maxg28 = nc.dram_tensor("maxg28", [1, 8], F32, kind="ExternalInput")
    maxug8 = nc.dram_tensor("maxug8", [1, 16], F32, kind="ExternalInput")
    oout = nc.dram_tensor("oout", [FF, T], BF16, kind="ExternalOutput")
    maxo = nc.dram_tensor("maxo", [1, 1], F32, kind="ExternalOutput")
    ones_f32 = nc.inline_tensor(np.ones((1, 128), np.float32), name="ones_f32")
    ident128 = nc.inline_tensor(np.eye(128, dtype=np.float32), name="ident128")
    with ExitStack() as es:
        sb = lambda *a: es.enter_context(nc.sbuf_tensor(*a))
        ps = lambda *a: es.enter_context(nc.psum_tensor(*a))
        sem = lambda n: es.enter_context(nc.semaphore(n))
        gstage = sb("gstage", [128, 2 * T], BF16)
        ustage = sb("ustage", [128, 2 * T], BF16)
        i16g = sb("i16g", [128, T], I16)
        i16u = sb("i16u", [128, T], I16)
        of = sb("of", [128, T], F32)
        ostage = sb("ostage", [128, 2 * T], BF16)
        maxing = sb("maxing", [1, 8], F32)
        maxinu = sb("maxinu", [1, 16], F32)
        scal = sb("scal", [1, 4], F32)
        scb = sb("scb", [128, 4], F32)
        onesr = sb("onesr", [1, 128], F32)
        id128 = sb("id128", [128, 128], F32)
        maxcol = sb("maxcol", [128, FC], F32)
        mfold = sb("mfold", [128, 1], F32)
        mo = sb("mo", [1, 1], F32)
        ps_bc = ps("ps_bc", [128, 4], F32)
        ps_tr = ps("ps_tr", [1, 128], F32)
        _cfg, _scal, _bc = sem("s_cfg"), sem("s_scal"), sem("s_bc")
        _ig0, _ig1 = sem("s_ig0"), sem("s_ig1")
        _iu0, _iu1 = sem("s_iu0"), sem("s_iu1")
        _vq = sem("s_vq")
        _red = sem("s_red")
        _o0, _o1 = sem("s_o0"), sem("s_o1")
        _stf = sem("s_stf")
        block = es.enter_context(nc.Block())

        @block.gpsimd
        def _(g):
            g.dma_start(out=maxing[:, :], in_=maxg28[:, :]).then_inc(_cfg, 16)
            g.dma_start(out=maxinu[:, :], in_=maxug8[:, :]).then_inc(_cfg, 16)
            g.dma_start(out=onesr[:, :], in_=ones_f32[:, :]).then_inc(_cfg, 16)
            g.dma_start(out=id128[:, :], in_=ident128[:, :]).then_inc(_cfg, 16)
            for c in range(FC):
                if c >= 2:
                    g.wait_ge(_vq, 5 * (c - 1))
                g.dma_start(out=gstage[:, (c % 2) * T:(c % 2) * T + T],
                            in_=g2in[c * 128:(c + 1) * 128, :]
                            ).then_inc(_ig0 if c % 2 == 0 else _ig1, 16)
            g.wait_ge(_o0, 16 * (FC // 2))
            g.wait_ge(_o1, 16 * (FC // 2))
            g.wait_ge(_stf, 16)

        @block.sync
        def _(sy):
            for c in range(FC):
                if c >= 2:
                    sy.wait_ge(_vq, 5 * (c - 1))
                sy.dma_start(out=ustage[:, (c % 2) * T:(c % 2) * T + T],
                             in_=upin[c * 128:(c + 1) * 128, :]
                             ).then_inc(_iu0 if c % 2 == 0 else _iu1, 16)

        @block.scalar
        def _(s):
            for c in range(FC):
                s.wait_ge(_vq, 5 * c + 5)
                s.dma_start(out=oout[c * 128:(c + 1) * 128, :],
                            in_=ostage[:, (c % 2) * T:(c % 2) * T + T]
                            ).then_inc(_o0 if c % 2 == 0 else _o1, 16)
            s.wait_ge(_red, FC + 2)
            s.dma_start(out=maxo[0:1, 0:1], in_=mo[0:1, 0:1]).then_inc(_stf, 16)

        @block.vector
        def _(v):
            v.wait_ge(_cfg, 64)
            v.tensor_reduce(scal[0:1, 0:1], maxing[0:1, 0:8], AX,
                            ALU.max).then_inc(_scal, 1)
            v.tensor_reduce(scal[0:1, 1:2], maxinu[0:1, 0:8], AX,
                            ALU.max).then_inc(_scal, 1)
            v.wait_ge(_scal, 2)
            v.tensor_scalar(scal[0:1, 0:1], scal[0:1, 0:1], 1.0 / QMAX, EPS_Q,
                            ALU.mult, ALU.max)
            v.tensor_scalar(scal[0:1, 1:2], scal[0:1, 1:2], 1.0 / QMAX, EPS_Q,
                            ALU.mult, ALU.max).then_inc(_scal, 1)
            v.wait_ge(_scal, 3)
            v.tensor_tensor(scal[0:1, 2:3], scal[0:1, 0:1], scal[0:1, 1:2],
                            ALU.mult)
            v.reciprocal(scal[0:1, 3:4], scal[0:1, 1:2]).then_inc(_scal, 1)
            v.wait_ge(_scal, 4)
            v.reciprocal(scal[0:1, 0:1], scal[0:1, 0:1]).then_inc(_scal, 1)
            v.wait_ge(_bc, 1)
            v.tensor_copy(scb[:, :], ps_bc[:, :]).then_inc(_scal, 1)
            v.wait_ge(_scal, 6)
            for c in range(FC):
                v.wait_ge(_ig0 if c % 2 == 0 else _ig1, 16 * (c // 2 + 1))
                v.wait_ge(_iu0 if c % 2 == 0 else _iu1, 16 * (c // 2 + 1))
                if c >= 2:
                    v.wait_ge(_o0 if c % 2 == 0 else _o1, 16 * (c // 2))
                if c > 0:
                    v.wait_ge(_red, c)
                g_sl = gstage[:, (c % 2) * T:(c % 2) * T + T]
                u_sl = ustage[:, (c % 2) * T:(c % 2) * T + T]
                v.tensor_scalar(i16g[:, :], g_sl, scb[:, 0:1], None, ALU.mult)
                v.tensor_scalar(i16u[:, :], u_sl, scb[:, 3:4], None,
                                ALU.mult).then_inc(_vq, 2)
                v.wait_ge(_vq, 5 * c + 2)
                v.scalar_tensor_tensor(of[:, :], i16g[:, :], scb[:, 2:3],
                                       i16u[:, :], ALU.mult,
                                       ALU.mult).then_inc(_vq, 1)
                v.wait_ge(_vq, 5 * c + 3)
                v.tensor_reduce(maxcol[:, c:c + 1], of[:, :], AX, ALU.max,
                                apply_absolute_value=True).then_inc(_red, 1)
                o_sl = ostage[:, (c % 2) * T:(c % 2) * T + T]
                v.tensor_copy(o_sl, of[:, :]).then_inc(_vq, 2)
            v.wait_ge(_red, FC)
            v.tensor_reduce(mfold[:, 0:1], maxcol[:, :], AX,
                            ALU.max).then_inc(_red, 1)
            v.wait_ge(_bc, 2)
            v.tensor_reduce(mo[0:1, 0:1], ps_tr[0:1, 0:128], AX,
                            ALU.max).then_inc(_red, 1)

        @block.tensor
        def _(t_):
            t_.wait_ge(_cfg, 64)
            t_.wait_ge(_scal, 5)
            t_.matmul(ps_bc[:, :], onesr[0:1, :], scal[0:1, 0:4], start=True,
                      stop=True).then_inc(_bc, 1)
            t_.wait_ge(_red, FC + 1)
            t_.transpose(ps_tr[0:1, 0:128], mfold[0:128, 0:1],
                         id128[:, :]).then_inc(_bc, 1)
    return nc



def build_l5():
    import ml_dtypes as _mld
    nc = bass.Bass("TRN2", target_bir_lowering=False, debug=False)
    oin = nc.dram_tensor("oin", [FF, T], BF16, kind="ExternalInput")
    maxo8 = nc.dram_tensor("maxo8", [1, 8], F32, kind="ExternalInput")
    streamin = nc.dram_tensor("streamin", [128, DC * T], F32, kind="ExternalInput")
    wdown = nc.dram_tensor("wdown", [D, FF], BF16, kind="ExternalInput")
    final = nc.dram_tensor("final", [128, DC * T], F32, kind="ExternalOutput")
    ones_f32 = nc.inline_tensor(np.ones((1, 128), np.float32), name="ones_f32")
    onescol_b = nc.inline_tensor(np.ones((128, 1), dtype=_mld.bfloat16),
                                 name="onescol_b")
    LNS = 22
    with ExitStack() as es:
        sb = lambda *a: es.enter_context(nc.sbuf_tensor(*a))
        ps = lambda *a: es.enter_context(nc.psum_tensor(*a))
        sem = lambda n: es.enter_context(nc.semaphore(n))
        stream = sb("stream", [128, DC * T], F32)
        oq = sb("oq", [128, 2 * FC * 512], BF16)
        warena = sb("warena", [128, FC * 1024], BF16)
        stage = sb("stage", [128, 4 * 512], BF16)
        i16o = sb("i16o", [128, 512], I16)
        maxin = sb("maxin", [1, 8], F32)
        scal = sb("scal", [1, 2], F32)
        scb = sb("scb", [128, 2], F32)
        onesr = sb("onesr", [1, 128], F32)
        onecb = sb("onecb", [128, 1], BF16)
        sbf = sb("sbf", [128, DC * 512], BF16)
        sqbf = sb("sqbf", [128, DC * 512], BF16)
        lrow = sb("lrow", [1, 5 * 512], F32)
        ps_mm = ps("ps_mm", [128, 4 * 512], F32)
        ps_ln = ps("ps_ln", [1, 1024], F32)
        ps_r = ps("ps_r", [128, 512], F32)
        ps_m = ps("ps_m", [128, 512], F32)
        _cfg, _scal, _bc, _q = (sem(n) for n in ("s_cfg", "s_scal", "s_bc", "s_q"))
        _ist = [sem(f"s_ist{k}") for k in range(4)]
        _w = sem("s_w")
        _si = sem("s_si")
        _mm, _ev = sem("s_mm"), sem("s_ev")
        _ln, _lnp, _lna = sem("s_ln"), sem("s_lnp"), sem("s_lna")
        _so = sem("s_so")
        block = es.enter_context(nc.Block())

        @block.gpsimd
        def _(g):
            g.dma_start(out=maxin[:, :], in_=maxo8[:, :]).then_inc(_cfg, 16)
            g.dma_start(out=onesr[:, :], in_=ones_f32[:, :]).then_inc(_cfg, 16)
            g.dma_start(out=onecb[:, :], in_=onescol_b[:, :]).then_inc(_cfg, 16)
            for c in range(DC):
                g.dma_start(out=stream[:, c * T:(c + 1) * T],
                            in_=streamin[:, c * T:(c + 1) * T]).then_inc(_si, 16)
            for tb in range(TB):
                for c in range(FC):
                    k = tb * FC + c
                    if k >= 4:
                        g.wait_ge(_q, 2 * (k - 3))
                    g.dma_start(out=stage[:, (k % 4) * 512:(k % 4) * 512 + 512],
                                in_=oin[c * 128:(c + 1) * 128,
                                        tb * 512:(tb + 1) * 512]
                                ).then_inc(_ist[k % 4], 16)
            g.wait_ge(_so, 16 * TB)

        @block.sync
        def _(sy):
            for ic in range(FC):
                for dh in range(2):
                    sy.dma_start(
                        out=warena[:, ic * 1024 + dh * 512:ic * 1024 + dh * 512 + 512],
                        in_=wdown[dh * 512:(dh + 1) * 512, ic * 128:(ic + 1) * 128],
                        transpose=True,
                    ).then_inc(_w, 16)

        @block.vector
        def _(v):
            v.wait_ge(_cfg, 48)
            v.tensor_reduce(scal[0:1, 0:1], maxin[0:1, 0:8], AX,
                            ALU.max).then_inc(_scal, 1)
            v.wait_ge(_scal, 1)
            v.tensor_scalar(scal[0:1, 0:1], scal[0:1, 0:1], 1.0 / QMAX, EPS_Q,
                            ALU.mult, ALU.max).then_inc(_scal, 1)
            v.wait_ge(_scal, 2)
            v.reciprocal(scal[0:1, 1:2], scal[0:1, 0:1]).then_inc(_scal, 1)
            v.wait_ge(_bc, 1)
            v.tensor_copy(scb[:, :], ps_r[:, 0:2]).then_inc(_scal, 1)
            v.wait_ge(_scal, 4)

            def _quant_tb(tb):
                for c in range(FC):
                    k = tb * FC + c
                    v.wait_ge(_ist[k % 4], 16 * (k // 4 + 1))
                    o_sl = stage[:, (k % 4) * 512:(k % 4) * 512 + 512]
                    v.tensor_scalar(i16o[:, :], o_sl, scb[:, 1:2], None,
                                    ALU.mult).then_inc(_q, 1)
                    v.wait_ge(_q, 2 * k + 1)
                    v.tensor_scalar(
                        oq[:, (tb % 2) * FC * 512 + c * 512:(tb % 2) * FC * 512 + c * 512 + 512],
                        i16o[:, :], scb[:, 0:1], None, ALU.mult).then_inc(_q, 1)

            _quant_tb(0)
            _quant_tb(1)
            v.wait_ge(_si, 16 * DC)
            for tt in range(32):
                tb, dc = tt // 8, tt % 8
                v.wait_ge(_mm, tt + 1)
                sl = stream[:, dc * T + tb * 512:dc * T + tb * 512 + 512]
                v.tensor_tensor(sl, sl,
                                ps_mm[:, (tt % 4) * 512:(tt % 4) * 512 + 512],
                                ALU.add).then_inc(_ev, 1)
                if dc == 7 and tb + 2 < TB:
                    _quant_tb(tb + 2)
                if dc == 7:
                    v.wait_ge(_ev, tt + 1)
                    b = LNS * tb
                    sv = stream[:, :].rearrange("p (c t) -> p c t", t=T)[:, :, tb * 512:tb * 512 + 512]
                    v.tensor_copy(sbf[:, :].rearrange("p (c t) -> p c t", t=512),
                                  sv).then_inc(_ln, 1)
                    v.wait_ge(_lnp, 2 * tb + 1)
                    v.tensor_scalar(lrow[0:1, 0:512], ps_ln[0:1, 0:512],
                                    1.0 / D, None, ALU.mult)
                    v.tensor_scalar(lrow[0:1, 512:1024], ps_ln[0:1, 512:1024],
                                    1.0 / D, None, ALU.mult).then_inc(_ln, 1)
                    v.wait_ge(_ln, b + 2)
                    v.tensor_tensor(lrow[0:1, 1024:1536], lrow[0:1, 0:512],
                                    lrow[0:1, 0:512], ALU.mult).then_inc(_ln, 1)
                    v.wait_ge(_ln, b + 3)
                    v.tensor_tensor(lrow[0:1, 1024:1536], lrow[0:1, 512:1024],
                                    lrow[0:1, 1024:1536],
                                    ALU.subtract).then_inc(_ln, 1)
                    v.wait_ge(_ln, b + 4)
                    v.tensor_scalar(lrow[0:1, 1024:1536], lrow[0:1, 1024:1536],
                                    EPS_LN, None, ALU.add).then_inc(_ln, 1)
                    v.wait_ge(_lna, 3 * tb + 3)
                    v.tensor_tensor(lrow[0:1, 2048:2560], lrow[0:1, 0:512],
                                    lrow[0:1, 1536:2048], ALU.mult).then_inc(_ln, 1)
                    v.wait_ge(_lnp, 2 * tb + 2)
                    for dcn in range(DC):
                        sl2 = stream[:, dcn * T + tb * 512:dcn * T + tb * 512 + 512]
                        v.tensor_tensor(sl2, sl2, ps_r[:, 0:512],
                                        ALU.mult).then_inc(_ln, 1)
                        v.wait_ge(_ln, b + 6 + 2 * dcn + 1)
                        v.tensor_tensor(sl2, sl2, ps_m[:, 0:512],
                                        ALU.subtract).then_inc(_ln, 1)

        @block.scalar
        def _(s):
            for tb in range(TB):
                b = LNS * tb
                s.wait_ge(_ln, b + 1)
                s.activation(sqbf[:, :], sbf[:, :], AF.Square).then_inc(_lna, 1)
                s.wait_ge(_ln, b + 5)
                s.activation(lrow[0:1, 1536:2048], lrow[0:1, 1024:1536],
                             AF.Ln).then_inc(_lna, 1)
                s.wait_ge(_lna, 3 * tb + 2)
                s.activation(lrow[0:1, 1536:2048], lrow[0:1, 1536:2048], AF.Exp,
                             bias=0.0, scale=-0.5).then_inc(_lna, 1)
                s.wait_ge(_ln, LNS * (tb + 1))
                sv = final[:, :].rearrange("p (c t) -> p c t", t=T)[:, :, tb * 512:tb * 512 + 512]
                s.dma_start(out=sv,
                            in_=stream[:, :].rearrange("p (c t) -> p c t", t=T)[:, :, tb * 512:tb * 512 + 512]
                            ).then_inc(_so, 16)

        @block.tensor
        def _(t_):
            t_.wait_ge(_cfg, 48)
            t_.wait_ge(_scal, 3)
            t_.matmul(ps_r[:, 0:2], onesr[0:1, :], scal[0:1, 0:2], start=True,
                      stop=True).then_inc(_bc, 1)
            t_.wait_ge(_w, 16 * FC * 2)
            for tt in range(32):
                tb, dc = tt // 8, tt % 8
                t_.wait_ge(_q, 2 * FC * (tb + 1))
                if tt >= 4:
                    t_.wait_ge(_ev, tt - 3)
                if tb >= 1 and dc == 0:
                    pass
                ins = None
                for ic in range(FC):
                    base = ic * 1024 + dc * 128
                    ins = t_.matmul(
                        ps_mm[:, (tt % 4) * 512:(tt % 4) * 512 + 512],
                        warena[:, base:base + 128],
                        oq[:, (tb % 2) * FC * 512 + ic * 512:(tb % 2) * FC * 512 + ic * 512 + 512],
                        start=(ic == 0), stop=(ic == FC - 1))
                ins.then_inc(_mm, 1)
                if dc == 7:
                    b = LNS * tb
                    t_.wait_ge(_ln, b + 1)
                    t_.wait_ge(_lna, 3 * tb + 1)
                    for dcn in range(DC):
                        t_.matmul(ps_ln[0:1, 0:512], onecb[:, :],
                                  sbf[:, dcn * 512:(dcn + 1) * 512],
                                  start=(dcn == 0), stop=(dcn == DC - 1))
                    ins = None
                    for dcn in range(DC):
                        ins = t_.matmul(ps_ln[0:1, 512:1024], onecb[:, :],
                                        sqbf[:, dcn * 512:(dcn + 1) * 512],
                                        start=(dcn == 0), stop=(dcn == DC - 1))
                    ins.then_inc(_lnp, 1)
                    t_.wait_ge(_ln, b + 6)
                    t_.matmul(ps_r[:, 0:512], onesr[0:1, :],
                              lrow[0:1, 1536:2048], start=True, stop=True)
                    ins = t_.matmul(ps_m[:, 0:512], onesr[0:1, :],
                                    lrow[0:1, 2048:2560], start=True, stop=True)
                    ins.then_inc(_lnp, 1)
    return nc



def build_l1():
    import ml_dtypes as _mld
    nc = bass.Bass("TRN2", target_bir_lowering=False, debug=False)
    xT = nc.dram_tensor("xT", [128, DC * T], F32, kind="ExternalInput")
    wqn = nc.dram_tensor("wqn", [NH * DH, D], F32, kind="ExternalInput")
    wkvT = nc.dram_tensor("wkvT", [D, LR], F32, kind="ExternalInput")
    wku = nc.dram_tensor("wku", [NH * DH, LR], F32, kind="ExternalInput")
    wvuT = nc.dram_tensor("wvuT", [LR, NH * DH], F32, kind="ExternalInput")
    woT = nc.dram_tensor("woT", [NH * DH, D], F32, kind="ExternalInput")
    streamout = nc.dram_tensor("streamout", [128, DC * T], F32, kind="ExternalOutput")
    maxx = nc.dram_tensor("maxx", [1, 1], F32, kind="ExternalOutput")
    attnout = nc.dram_tensor("attnout", [T, NH * DH], BF16)
    ones_f32 = nc.inline_tensor(np.ones((1, 128), np.float32), name="ones_f32")
    onesm_b = nc.inline_tensor(np.ones((128, 128), dtype=_mld.bfloat16),
                               name="onesm_b")
    identb = nc.inline_tensor(np.eye(128, dtype=_mld.bfloat16), name="identb")
    ident128 = nc.inline_tensor(np.eye(128, dtype=np.float32), name="ident128")
    LNS = 22
    with ExitStack() as es:
        sb = lambda *a: es.enter_context(nc.sbuf_tensor(*a))
        sem = lambda n: es.enter_context(nc.semaphore(n))
        stream = sb("stream", [128, DC * T], F32)
        xbf = sb("xbf", [128, DC * T], BF16)
        latbf = sb("latbf", [64, T], BF16)
        lattok = sb("lattok", [128, 16 * 65], BF16)
        kvw = sb("kvw", [128, DC * LR], BF16)
        wku_sb = sb("wku_sb", [128, 16 * LR], BF16)
        vup = sb("vup", [64, NH * DH], BF16)
        wqn_sb = sb("wqn_sb", [128, 2 * 2 * D], BF16)
        wqk = sb("wqk", [128, DC * LR], BF16)
        qhat = sb("qhat", [64, T], BF16)
        sqs = sb("sqs", [64, T], BF16)
        qn2 = sb("qn2", [1, T], F32)
        latn2 = sb("latn2", [1, T], F32)
        csc = sb("csc", [1, 8], F32)
        negcb = sb("negcb", [128, 1], F32)
        rcp = sb("rcp", [128, 4], F32)
        aostage = sb("aostage", [128, 4 * 256], BF16)
        u1 = sb("u1", [128, 16384], BF16)
        probsT = u1[:, 0:8192]
        obar = u1[:, 8192:8704]
        aoT = u1[:, 0:16384]
        wo_sb = sb("wo_sb", [128, 8 * 1024], BF16)
        onesr = sb("onesr", [1, 128], F32)
        onemb = sb("onemb", [128, 128], BF16)
        idb = sb("idb", [128, 128], BF16)
        id128 = sb("id128", [128, 128], F32)
        lrow = sb("lrow", [1, 5 * 512], F32)
        maxcol = sb("maxcol", [128, 8], F32)
        mfold = sb("mfold", [128, 1], F32)
        mxo = sb("mxo", [1, 1], F32)
        psum = es.enter_context(nc.psum_tensor("psum", [128, 4096], F32))
        B2 = psum[:, 1024:1536]
        B3c = psum[:, 1536:1537]
        _cfg = sem("s_cfg")
        _vb = sem("s_vb")
        _xs, _xb = sem("s_xs"), sem("s_xb")
        _wq0, _wq1 = sem("s_wq0"), sem("s_wq1")
        _lat, _late = sem("s_lat"), sem("s_late")
        _ltr, _lte = sem("s_ltr"), sem("s_lte")
        _mmq, _ewq = sem("s_mmq"), sem("s_ewq")
        _eqh, _sq = sem("s_eqh"), sem("s_sq")
        _nmm, _nev = sem("s_nmm"), sem("s_nev")
        _lm = sem("s_lm")
        _ch, _cha, _chb = sem("s_ch"), sem("s_cha"), sem("s_chb")
        _smm, _sexp = sem("s_smm"), sem("s_sexp")
        _amm, _oev = sem("s_amm"), sem("s_oev")
        _otr, _arc = sem("s_otr"), sem("s_arc")
        _avm, _aev = sem("s_avm"), sem("s_aev")
        _asl = [sem(f"s_as{k}") for k in range(4)]
        _ao0, _ao1 = sem("s_ao0"), sem("s_ao1")
        _wo0, _wo1 = sem("s_wo0"), sem("s_wo1")
        _pmm, _pev = sem("s_pmm"), sem("s_pev")
        _ln, _lnp, _lna = sem("s_ln"), sem("s_lnp"), sem("s_lna")
        _mx, _mxp = sem("s_mx"), sem("s_mxp")
        _so = sem("s_so")
        block = es.enter_context(nc.Block())

        @block.gpsimd
        def _(g):
            g.dma_start(out=onesr[:, :], in_=ones_f32[:, :]).then_inc(_cfg, 16)
            g.dma_start(out=onemb[:, :], in_=onesm_b[:, :]).then_inc(_cfg, 16)
            g.dma_start(out=idb[:, :], in_=identb[:, :]).then_inc(_cfg, 16)
            g.dma_start(out=id128[:, :], in_=ident128[:, :]).then_inc(_cfg, 16)
            g.dma_start(out=kvw[:, :].rearrange("p (c n) -> p c n", n=LR),
                        in_=wkvT[:, :].rearrange("(c p) n -> p c n", p=128)
                        ).then_inc(_cfg, 16)
            g.dma_start(out=wku_sb[:, :].rearrange("p (c n) -> p c n", n=LR),
                        in_=wku[:, :].rearrange("(c p) n -> p c n", p=128)
                        ).then_inc(_cfg, 16)
            g.dma_start(out=vup[:, :], in_=wvuT[:, :]).then_inc(_cfg, 16)
            for tc in range(16):
                g.memset(lattok[:, tc * 65 + 64:tc * 65 + 65], 1.0)
            g.sem_inc(_vb, 1)
            for c in range(DC):
                g.dma_start(out=xbf[:, c * T:(c + 1) * T],
                            in_=xT[:, c * T:(c + 1) * T]).then_inc(_xb, 16)
            for h in range(NH):
                if h >= 2:
                    g.wait_ge(_mmq, (h - 2) * 5 + 1)
                g.dma_start(
                    out=wqn_sb[:, (h % 2) * 2 * D:(h % 2) * 2 * D + 2 * D]
                        .rearrange("p (c n) -> p c n", n=D),
                    in_=wqn[h * 256:(h + 1) * 256, :]
                        .rearrange("(c p) n -> p c n", p=128)
                ).then_inc(_wq0 if h % 2 == 0 else _wq1, 16)
            for H in range(2):
                if H == 1:
                    g.wait_ge(_pmm, 32)
                for c8 in range(8):
                    cc = H * 8 + c8
                    g.dma_start(out=wo_sb[:, c8 * 1024:(c8 + 1) * 1024],
                                in_=woT[cc * 128:(cc + 1) * 128, :]
                                ).then_inc(_wo0 if H == 0 else _wo1, 16)
            g.wait_ge(_so, 16 * (TB + 1))

        @block.sync
        def _(sy):
            for c in range(DC):
                sy.dma_start(out=stream[:, c * T:(c + 1) * T],
                             in_=xT[:, c * T:(c + 1) * T]).then_inc(_xs, 16)
            for h in range(NH):
                for qb in range(TB):
                    for qt in range(4):
                        aidx = (h * TB + qb) * 4 + qt
                        sy.wait_ge(_aev, aidx + 1)
                        sy.dma_start(
                            out=attnout[qb * 512 + qt * 128:qb * 512 + qt * 128 + 128,
                                        h * 256:(h + 1) * 256],
                            in_=aostage[:, (aidx % 4) * 256:(aidx % 4) * 256 + 256],
                        ).then_inc(_asl[aidx % 4], 16)
            for k in range(4):
                sy.wait_ge(_asl[k], 16 * 32)
            for H in range(2):
                for qb in range(TB):
                    g8 = H * 4 + qb
                    if g8 >= 2:
                        sy.wait_ge(_pmm, 8 * (g8 - 1))
                    for c8 in range(8):
                        cc = H * 8 + c8
                        sy.dma_start(
                            out=aoT[:, (g8 % 2) * 16 * 512 + c8 * 512:(g8 % 2) * 16 * 512 + c8 * 512 + 512],
                            in_=attnout[qb * 512:(qb + 1) * 512, cc * 128:(cc + 1) * 128],
                            transpose=True,
                        ).then_inc(_ao0 if g8 % 2 == 0 else _ao1, 16)
            for tb in range(TB):
                sy.wait_ge(_ln, LNS * (tb + 1))
                sv = streamout[:, :].rearrange("p (c t) -> p c t", t=T)[:, :, tb * 512:tb * 512 + 512]
                sy.dma_start(out=sv,
                             in_=stream[:, :].rearrange("p (c t) -> p c t", t=T)[:, :, tb * 512:tb * 512 + 512]
                             ).then_inc(_so, 16)
            sy.wait_ge(_mx, 10)
            sy.dma_start(out=maxx[0:1, 0:1], in_=mxo[0:1, 0:1]).then_inc(_so, 16)

        @block.scalar
        def _(s):
            for tb in range(TB):
                s.wait_ge(_lat, tb + 1)
                s.activation(latbf[0:64, tb * 512:(tb + 1) * 512],
                             psum[0:64, 0:512], AF.Copy).then_inc(_late, 1)
            for h in range(NH):
                s.wait_ge(_ch, h * 4 + 2)
                s.activation(csc[0:1, 3:4], csc[0:1, 2:3], AF.Ln).then_inc(_cha, 1)
                s.wait_ge(_cha, h * 2 + 1)
                s.activation(csc[0:1, 4:5], csc[0:1, 3:4], AF.Exp, bias=0.0,
                             scale=0.5).then_inc(_cha, 1)
                for qb in range(TB):
                    for pr in range(8):
                        s.wait_ge(_smm, (h * TB + qb) * 16 + 2 * pr + 2)
                        s.wait_ge(_ch, h * 4 + 4)
                        if pr == 0 and (qb > 0 or h > 0):
                            s.wait_ge(_amm, h * TB + qb)
                        off = 2048 if pr % 2 == 0 else 0
                        s.activation(probsT[:, pr * 1024:(pr + 1) * 1024],
                                     psum[:, off:off + 1024],
                                     AF.Exp, bias=negcb[:, 0:1],
                                     scale=1.0).then_inc(_sexp, 1)
            for tb in range(TB):
                b = LNS * tb
                s.wait_ge(_ln, b + 1)
                s.activation(xbf[:, DC * 512:2 * DC * 512], xbf[:, 0:DC * 512],
                             AF.Square).then_inc(_lna, 1)
                s.wait_ge(_ln, b + 5)
                s.activation(lrow[0:1, 1536:2048], lrow[0:1, 1024:1536],
                             AF.Ln).then_inc(_lna, 1)
                s.wait_ge(_lna, 3 * tb + 2)
                s.activation(lrow[0:1, 1536:2048], lrow[0:1, 1536:2048], AF.Exp,
                             bias=0.0, scale=-0.5).then_inc(_lna, 1)

        @block.vector
        def _(v):
            for tc in range(16):
                v.wait_ge(_ltr, tc + 1)
                pb = psum[:, 2048 + (tc % 2) * 512:2048 + (tc % 2) * 512 + 32].bitcast(BF16)
                v.tensor_copy(lattok[:, tc * 65:tc * 65 + 64],
                              pb[:, 0:64]).then_inc(_lte, 1)
            for i in range(4):
                v.wait_ge(_late, i + 1)
                v.tensor_tensor(sqs[0:64, i * 512:(i + 1) * 512],
                                latbf[0:64, i * 512:(i + 1) * 512],
                                latbf[0:64, i * 512:(i + 1) * 512],
                                ALU.mult).then_inc(_sq, 1)
            for i in range(4):
                v.wait_ge(_nmm, i + 1)
                v.tensor_copy(latn2[0:1, i * 512:(i + 1) * 512],
                              B2[0:1, :]).then_inc(_nev, 1)
            v.wait_ge(_nev, 4)
            v.tensor_reduce(csc[0:1, 1:2], latn2[0:1, :], AX,
                            ALU.max).then_inc(_lm, 1)
            for h in range(NH):
                v.wait_ge(_mmq, h * 5 + 1)
                v.tensor_copy(wqk[:, :], psum[:, 0:512]).then_inc(_ewq, 1)
                for i in range(4):
                    v.wait_ge(_mmq, h * 5 + 2 + i)
                    v.tensor_scalar(qhat[0:64, i * 512:(i + 1) * 512],
                                    psum[0:64, (i % 2) * 512:(i % 2) * 512 + 512],
                                    1.0 / 16.0, None, ALU.mult).then_inc(_eqh, 1)
                    v.wait_ge(_eqh, h * 4 + i + 1)
                    v.tensor_tensor(sqs[0:64, i * 512:(i + 1) * 512],
                                    qhat[0:64, i * 512:(i + 1) * 512],
                                    qhat[0:64, i * 512:(i + 1) * 512],
                                    ALU.mult).then_inc(_sq, 1)
                for i in range(4):
                    v.wait_ge(_nmm, 4 + h * 4 + i + 1)
                    v.tensor_copy(qn2[0:1, i * 512:(i + 1) * 512],
                                  B2[0:1, :]).then_inc(_nev, 1)
                v.wait_ge(_nev, 4 + h * 4 + 4)
                v.tensor_reduce(csc[0:1, 0:1], qn2[0:1, :], AX,
                                ALU.max).then_inc(_ch, 1)
                v.wait_ge(_ch, h * 4 + 1)
                v.wait_ge(_lm, 1)
                v.tensor_tensor(csc[0:1, 2:3], csc[0:1, 0:1], csc[0:1, 1:2],
                                ALU.mult).then_inc(_ch, 1)
                v.wait_ge(_cha, h * 2 + 2)
                v.tensor_scalar(csc[0:1, 5:6], csc[0:1, 4:5], -1.01, None,
                                ALU.mult).then_inc(_ch, 1)
                v.wait_ge(_chb, h + 1)
                if h > 0:
                    v.wait_ge(_sexp, h * 32)
                v.tensor_copy(negcb[:, 0:1], B3c[:, 0:1]).then_inc(_ch, 1)
                for qb in range(TB):
                    gq = h * TB + qb
                    v.wait_ge(_amm, gq + 1)
                    v.tensor_copy(obar[0:65, :],
                                  psum[0:65, 1024:1536]).then_inc(_oev, 1)
                    v.wait_ge(_otr, gq + 1)
                    if gq > 0:
                        v.wait_ge(_aev, gq * 4)
                    v.reciprocal(rcp[:, :],
                                 psum[:, 1540:1544].bitcast(BF16)[:, 0:8:2]
                                 ).then_inc(_arc, 1)
                    for qt in range(4):
                        aidx = gq * 4 + qt
                        v.wait_ge(_avm, aidx + 1)
                        v.wait_ge(_arc, gq + 1)
                        if aidx >= 4:
                            v.wait_ge(_asl[aidx % 4], 16 * (aidx // 4))
                        v.tensor_scalar(
                            aostage[:, (aidx % 4) * 256:(aidx % 4) * 256 + 256],
                            psum[:, 3072 + (aidx % 2) * 512:3072 + (aidx % 2) * 512 + 256],
                            rcp[:, qt:qt + 1], None, ALU.mult).then_inc(_aev, 1)
            v.wait_ge(_xs, 16 * DC)
            for H in range(2):
                for qb in range(TB):
                    for dc in range(DC):
                        pidx = H * 32 + qb * 8 + dc
                        v.wait_ge(_pmm, pidx + 1)
                        sl = stream[:, dc * T + qb * 512:dc * T + qb * 512 + 512]
                        v.tensor_tensor(sl, sl,
                                        psum[:, (pidx % 2) * 512:(pidx % 2) * 512 + 512],
                                        ALU.add).then_inc(_pev, 1)
            for tb in range(TB):
                b = LNS * tb
                v.wait_ge(_pev, 32 + 8 * (tb + 1))
                sv = stream[:, :].rearrange("p (c t) -> p c t", t=T)[:, :, tb * 512:tb * 512 + 512]
                v.tensor_copy(xbf[:, 0:DC * 512].rearrange("p (c t) -> p c t", t=512),
                              sv).then_inc(_ln, 1)
                v.wait_ge(_lnp, 2 * tb + 1)
                v.tensor_scalar(lrow[0:1, 0:512], psum[0:1, 1024:1536], 1.0 / D,
                                None, ALU.mult)
                v.tensor_scalar(lrow[0:1, 512:1024], psum[0:1, 1536:2048],
                                1.0 / D, None, ALU.mult).then_inc(_ln, 1)
                v.wait_ge(_ln, b + 2)
                v.tensor_tensor(lrow[0:1, 1024:1536], lrow[0:1, 0:512],
                                lrow[0:1, 0:512], ALU.mult).then_inc(_ln, 1)
                v.wait_ge(_ln, b + 3)
                v.tensor_tensor(lrow[0:1, 1024:1536], lrow[0:1, 512:1024],
                                lrow[0:1, 1024:1536], ALU.subtract).then_inc(_ln, 1)
                v.wait_ge(_ln, b + 4)
                v.tensor_scalar(lrow[0:1, 1024:1536], lrow[0:1, 1024:1536],
                                EPS_LN, None, ALU.add).then_inc(_ln, 1)
                v.wait_ge(_lna, 3 * tb + 3)
                v.tensor_tensor(lrow[0:1, 2048:2560], lrow[0:1, 0:512],
                                lrow[0:1, 1536:2048], ALU.mult).then_inc(_ln, 1)
                v.wait_ge(_lnp, 2 * tb + 2)
                for dcn in range(DC):
                    sl2 = stream[:, dcn * T + tb * 512:dcn * T + tb * 512 + 512]
                    v.tensor_tensor(sl2, sl2, psum[:, 3072:3584], ALU.mult).then_inc(_ln, 1)
                    v.wait_ge(_ln, b + 6 + 2 * dcn + 1)
                    v.tensor_tensor(sl2, sl2, psum[:, 3584:4096],
                                    ALU.subtract).then_inc(_ln, 1)
            v.wait_ge(_ln, TB * LNS)
            for dc in range(DC):
                v.tensor_reduce(maxcol[:, dc:dc + 1], stream[:, dc * T:(dc + 1) * T],
                                AX, ALU.max, apply_absolute_value=True).then_inc(_mx, 1)
            v.wait_ge(_mx, 8)
            v.tensor_reduce(mfold[:, 0:1], maxcol[:, 0:8], AX,
                            ALU.max).then_inc(_mx, 1)
            v.wait_ge(_mxp, 1)
            v.tensor_reduce(mxo[0:1, 0:1], psum[0:1, 2048:2176], AX,
                            ALU.max).then_inc(_mx, 1)

        @block.tensor
        def _(t_):
            t_.wait_ge(_cfg, 112)
            t_.wait_ge(_vb, 1)
            t_.wait_ge(_xb, 16 * DC)
            for tb in range(TB):
                if tb > 0:
                    t_.wait_ge(_late, tb)
                ins = None
                for dc in range(DC):
                    ins = t_.matmul(psum[0:64, 0:512], kvw[:, dc * LR:(dc + 1) * LR],
                                    xbf[:, dc * T + tb * 512:dc * T + tb * 512 + 512],
                                    start=(dc == 0), stop=(dc == DC - 1))
                ins.then_inc(_lat, 1)
            t_.wait_ge(_late, TB)
            for tc in range(16):
                if tc >= 2:
                    t_.wait_ge(_lte, tc - 1)
                pb = psum[:, 2048 + (tc % 2) * 512:2048 + (tc % 2) * 512 + 32].bitcast(BF16)
                t_.transpose(pb[:, 0:64], latbf[0:64, tc * 128:(tc + 1) * 128],
                             idb[0:64, 0:64]).then_inc(_ltr, 1)
            for i in range(4):
                t_.wait_ge(_sq, i + 1)
                if i > 0:
                    t_.wait_ge(_nev, i)
                t_.matmul(B2[:, :], onemb[0:64, :], sqs[0:64, i * 512:(i + 1) * 512],
                          start=True, stop=True).then_inc(_nmm, 1)
            for h in range(NH):
                t_.wait_ge(_wq0 if h % 2 == 0 else _wq1, 16 * (h // 2 + 1))
                t_.wait_ge(_eqh, h * 4)
                if h > 0:
                    t_.wait_ge(_ewq, h)
                ins = None
                for dc in range(DC):
                    for jj in range(2):
                        ins = t_.matmul(
                            psum[:, dc * 64:dc * 64 + 64],
                            wqn_sb[:, (h % 2) * 2 * D + jj * D + dc * 128:(h % 2) * 2 * D + jj * D + dc * 128 + 128],
                            wku_sb[:, (h * 2 + jj) * LR:(h * 2 + jj) * LR + LR],
                            start=(jj == 0), stop=(jj == 1))
                ins.then_inc(_mmq, 1)
                for i in range(4):
                    t_.wait_ge(_ewq, h + 1)
                    if i >= 2:
                        t_.wait_ge(_eqh, h * 4 + i - 1)
                    ins = None
                    for dc in range(DC):
                        ins = t_.matmul(
                            psum[0:64, (i % 2) * 512:(i % 2) * 512 + 512],
                            wqk[:, dc * LR:(dc + 1) * LR],
                            xbf[:, dc * T + i * 512:dc * T + i * 512 + 512],
                            start=(dc == 0), stop=(dc == DC - 1))
                    ins.then_inc(_mmq, 1)
                for i in range(4):
                    t_.wait_ge(_sq, 4 + h * 4 + i + 1)
                    t_.wait_ge(_nev, 4 + h * 4 + i)
                    t_.matmul(B2[:, :], onemb[0:64, :],
                              sqs[0:64, i * 512:(i + 1) * 512],
                              start=True, stop=True).then_inc(_nmm, 1)
                t_.wait_ge(_ch, h * 4 + 3)
                t_.matmul(B3c[:, 0:1], onesr[0:1, :], csc[0:1, 5:6],
                          start=True, stop=True).then_inc(_chb, 1)
                for qb in range(TB):
                    if qb == 0:
                        t_.wait_ge(_eqh, h * 4 + 4)
                    for kt in range(16):
                        sidx = (h * TB + qb) * 16 + kt
                        prg = (h * TB + qb) * 8 + kt // 2
                        if prg >= 2:
                            t_.wait_ge(_sexp, prg - 1)
                        off = (2048 if (kt // 2) % 2 == 0 else 0) + (kt % 2) * 512
                        t_.matmul(psum[:, off:off + 512],
                                  latbf[0:64, kt * 128:(kt + 1) * 128],
                                  qhat[0:64, qb * 512:(qb + 1) * 512],
                                  start=True, stop=True).then_inc(_smm, 1)
                    gq = h * TB + qb
                    t_.wait_ge(_sexp, gq * 8 + 8)
                    t_.wait_ge(_oev, gq)
                    if qb == 0:
                        t_.wait_ge(_nev, 4 + h * 4 + 4)
                    ins = None
                    for kc in range(16):
                        ins = t_.matmul(psum[0:65, 1024:1536],
                                        lattok[:, kc * 65:(kc + 1) * 65],
                                        probsT[:, kc * 512:(kc + 1) * 512],
                                        start=(kc == 0), stop=(kc == 15))
                    ins.then_inc(_amm, 1)
                    t_.wait_ge(_oev, gq + 1)
                    if gq > 0:
                        t_.wait_ge(_arc, gq)
                    ins = None
                    for qt in range(4):
                        pb2 = psum[:, 1540 + qt:1541 + qt].bitcast(BF16)
                        ins = t_.transpose(pb2[:, 0:1],
                                           obar[64:65, qt * 128:(qt + 1) * 128],
                                           idb[64:65, 64:65])
                    ins.then_inc(_otr, 1)
                    for qt in range(4):
                        aidx = gq * 4 + qt
                        if aidx >= 2:
                            t_.wait_ge(_aev, aidx - 1)
                        ins = t_.matmul(
                            psum[:, 3072 + (aidx % 2) * 512:3072 + (aidx % 2) * 512 + 256],
                            obar[0:64, qt * 128:(qt + 1) * 128],
                            vup[0:64, h * 256:(h + 1) * 256],
                            start=True, stop=True)
                        ins.then_inc(_avm, 1)
            for H in range(2):
                t_.wait_ge(_wo0 if H == 0 else _wo1, 16 * 8)
                for qb in range(TB):
                    g8 = H * 4 + qb
                    t_.wait_ge(_ao0 if g8 % 2 == 0 else _ao1, 16 * 8 * (g8 // 2 + 1))
                    for dc in range(DC):
                        pidx = H * 32 + qb * 8 + dc
                        if pidx >= 2:
                            t_.wait_ge(_pev, pidx - 1)
                        ins = None
                        for c8 in range(8):
                            ins = t_.matmul(
                                psum[:, (pidx % 2) * 512:(pidx % 2) * 512 + 512],
                                wo_sb[:, c8 * 1024 + dc * 128:c8 * 1024 + dc * 128 + 128],
                                aoT[:, (g8 % 2) * 16 * 512 + c8 * 512:(g8 % 2) * 16 * 512 + c8 * 512 + 512],
                                start=(c8 == 0), stop=(c8 == 7))
                        ins.then_inc(_pmm, 1)
            for tb in range(TB):
                b = LNS * tb
                t_.wait_ge(_ln, b + 1)
                t_.wait_ge(_lna, 3 * tb + 1)
                for dcn in range(DC):
                    t_.matmul(psum[:, 1024:1536], onemb[:, :],
                              xbf[:, dcn * 512:(dcn + 1) * 512],
                              start=(dcn == 0), stop=(dcn == DC - 1))
                ins = None
                for dcn in range(DC):
                    ins = t_.matmul(psum[:, 1536:2048], onemb[:, :],
                                    xbf[:, DC * 512 + dcn * 512:DC * 512 + (dcn + 1) * 512],
                                    start=(dcn == 0), stop=(dcn == DC - 1))
                ins.then_inc(_lnp, 1)
                t_.wait_ge(_ln, b + 6)
                t_.matmul(psum[:, 3072:3584], onesr[0:1, :], lrow[0:1, 1536:2048],
                          start=True, stop=True)
                ins = t_.matmul(psum[:, 3584:4096], onesr[0:1, :],
                                lrow[0:1, 2048:2560], start=True, stop=True)
                ins.then_inc(_lnp, 1)
            t_.wait_ge(_mx, 9)
            t_.transpose(psum[0:1, 2048:2176], mfold[0:128, 0:1],
                         id128[:, :]).then_inc(_mxp, 1)
    return nc


_NC_CACHE = {}


def _nc(name):
    if name not in _NC_CACHE:
        _NC_CACHE[name] = globals()["build_" + name]()
    return _NC_CACHE[name]


def _to_fm(a):
    return np.ascontiguousarray(
        a.T.reshape(DC, 128, T).transpose(1, 0, 2).reshape(128, DC * T))


def _from_fm(a):
    return a.reshape(128, DC, T).transpose(1, 0, 2).reshape(D, T).T


def kernel(L0_attn_x, L0_attn_Wq, L0_attn_Wkv_down, L0_attn_Wk_up,
           L0_attn_Wv_up, L0_attn_Wo, L0_ffn_w_gate, L0_ffn_w_up,
           L0_ffn_w_down):
    cores = list(range(8))
    x = np.ascontiguousarray(np.asarray(L0_attn_x, np.float32))
    w_up = np.ascontiguousarray(np.asarray(L0_ffn_w_up, np.float32))
    w_gate = np.ascontiguousarray(np.asarray(L0_ffn_w_gate, np.float32))
    w_down = np.ascontiguousarray(np.asarray(L0_ffn_w_down, np.float32))

    in0 = [{"wu": np.ascontiguousarray(w_up[c * 384:(c + 1) * 384]),
            "wg": np.ascontiguousarray(w_gate[c * 384:(c + 1) * 384]),
            "wd": np.ascontiguousarray(w_down[c * 128:(c + 1) * 128])}
           for c in cores]
    r0 = run_bass_kernel_spmd(_nc("l0"), in0, core_ids=cores).results
    wupdq = np.concatenate([np.asarray(r0[c]["wuo"]) for c in cores], 0)
    wgatedq = np.concatenate([np.asarray(r0[c]["wgo"]) for c in cores], 0)
    wdowndq = np.concatenate([np.asarray(r0[c]["wdo"]) for c in cores], 0)

    wqn = np.ascontiguousarray(np.asarray(L0_attn_Wq, np.float32))
    wkvT = np.ascontiguousarray(np.asarray(L0_attn_Wkv_down, np.float32).T)
    wku = np.ascontiguousarray(np.asarray(L0_attn_Wk_up, np.float32))
    wvuT = np.ascontiguousarray(np.asarray(L0_attn_Wv_up, np.float32).T)
    woT = np.ascontiguousarray(np.asarray(L0_attn_Wo, np.float32).T)
    in1 = [{"xT": _to_fm(x[c]), "wqn": wqn, "wkvT": wkvT, "wku": wku,
            "wvuT": wvuT, "woT": woT} for c in cores]
    r1 = run_bass_kernel_spmd(_nc("l1"), in1, core_ids=cores).results
    streams = [np.asarray(r1[c]["streamout"]) for c in cores]
    maxx8 = np.concatenate([np.asarray(r1[c]["maxx"]) for c in cores],
                           axis=1)

    in2 = [{"stream": streams[c], "maxx": maxx8, "wup": wupdq,
            "wgate": wgatedq} for c in cores]
    r2 = run_bass_kernel_spmd(_nc("l2"), in2, core_ids=cores).results
    maxug8 = np.concatenate(
        [np.asarray(r2[c]["maxug"])[0:1, :] for c in cores] +
        [np.asarray(r2[c]["maxug"])[1:2, :] for c in cores], axis=0).reshape(1, 16)

    in3 = [{"gatein": np.asarray(r2[c]["gateout"]), "maxug8": maxug8}
           for c in cores]
    r3 = run_bass_kernel_spmd(_nc("l3"), in3, core_ids=cores).results
    maxg28 = np.concatenate([np.asarray(r3[c]["maxg2"]) for c in cores],
                            axis=1)

    in4 = [{"g2in": np.asarray(r3[c]["g2out"]),
            "upin": np.asarray(r2[c]["upout"]),
            "maxg28": maxg28, "maxug8": maxug8} for c in cores]
    r4 = run_bass_kernel_spmd(_nc("l4"), in4, core_ids=cores).results
    maxo8 = np.concatenate([np.asarray(r4[c]["maxo"]) for c in cores], axis=1)

    in5 = [{"oin": np.asarray(r4[c]["oout"]), "maxo8": maxo8,
            "streamin": streams[c], "wdown": wdowndq} for c in cores]
    r5 = run_bass_kernel_spmd(_nc("l5"), in5, core_ids=cores).results

    out = np.empty((8, T, D), np.float32)
    for c in cores:
        out[c] = _from_fm(np.asarray(r5[c]["final"]))
    return out


# version 07
# speedup vs baseline: 1.1940x; 1.0558x over previous
from contextlib import ExitStack

import numpy as np
import concourse.bass as bass
import concourse.mybir as mybir
from concourse.bass_utils import run_bass_kernel_spmd

F32 = mybir.dt.float32
BF16 = mybir.dt.bfloat16
I16 = mybir.dt.int16
U16 = mybir.dt.uint16
I8 = mybir.dt.int8
AF = mybir.ActivationFunctionType
ALU = mybir.AluOpType
AX = mybir.AxisListType.X

T = 2048
D = 1024
NH = 8
DH = 256
LR = 64
FF = 3072
DC = D // 128
FC = FF // 128
TB = T // 512

QMAX = 32767.0
WQMAX = 7.0
EPS_Q = 1e-12
EPS_LN = 1e-5


class Ctr:

    def __init__(self, sem):
        self.sem = sem
        self.v = 0

    def inc(self, n=1):
        self.v += n
        return self.v



def build_l2():
    nc = bass.Bass("TRN2", target_bir_lowering=False, debug=False)
    stream = nc.dram_tensor("stream", [128, DC * T], F32, kind="ExternalInput")
    maxx = nc.dram_tensor("maxx", [1, 8], F32, kind="ExternalInput")
    wup = nc.dram_tensor("wup", [FF, D], BF16, kind="ExternalInput")
    wgate = nc.dram_tensor("wgate", [FF, D], BF16, kind="ExternalInput")
    upout = nc.dram_tensor("upout", [FF, T], BF16, kind="ExternalOutput")
    gateout = nc.dram_tensor("gateout", [FF, T], BF16, kind="ExternalOutput")
    maxug = nc.dram_tensor("maxug", [2, 1], F32, kind="ExternalOutput")
    ones_f32 = nc.inline_tensor(np.ones((1, 128), np.float32), name="ones_f32")
    ident128 = nc.inline_tensor(np.eye(128, dtype=np.float32), name="ident128")

    NT = 2 * FC * TB

    with ExitStack() as es:
        sb = lambda *a: es.enter_context(nc.sbuf_tensor(*a))
        ps = lambda *a: es.enter_context(nc.psum_tensor(*a))
        sem = lambda n: es.enter_context(nc.semaphore(n))
        xq = sb("xq", [128, DC * T], BF16)
        sstage = sb("sstage", [128, 2 * T], F32)
        i16s = sb("i16s", [128, T], I16)
        warena = sb("warena", [128, 2 * DC * 512], BF16)
        stage = sb("stage", [128, 4 * 512], BF16)
        maxcol = sb("maxcol", [128, NT], F32)
        maxin = sb("maxin", [1, 8], F32)
        scal = sb("scal", [1, 2], F32)
        sxb = sb("sxb", [128, 2], F32)
        onesr = sb("onesr", [1, 128], F32)
        id128 = sb("id128", [128, 128], F32)
        mfold = sb("mfold", [128, 2], F32)
        mout = sb("mout", [2, 1], F32)
        ps_mm = ps("ps_mm", [128, 4 * 512], F32)
        ps_bc = ps("ps_bc", [128, 2], F32)
        ps_tr = ps("ps_tr", [2, 128], F32)
        _cfg, _scal, _bc, _q = (sem(n) for n in
                                ("s_cfg", "s_scal", "s_bc", "s_q"))
        _i0, _i1 = sem("s_i0"), sem("s_i1")
        _w0, _w1 = sem("s_w0"), sem("s_w1")
        _mm, _ev, _red = (sem(n) for n in ("s_mm", "s_ev", "s_red"))
        _stl = [sem(f"s_st{k}") for k in range(4)]
        _stf = sem("s_stf")
        block = es.enter_context(nc.Block())
        @block.gpsimd
        def _(g):
            g.dma_start(out=maxin[:, :], in_=maxx[:, :]).then_inc(_cfg, 16)
            g.dma_start(out=onesr[:, :], in_=ones_f32[:, :]).then_inc(_cfg, 16)
            g.dma_start(out=id128[:, :], in_=ident128[:, :]).then_inc(_cfg, 16)
            for c in range(DC):
                if c >= 2:
                    g.wait_ge(_q, 2 * (c - 1))
                g.dma_start(out=sstage[:, (c % 2) * T:(c % 2) * T + T],
                            in_=stream[:, c * T:(c + 1) * T]
                            ).then_inc(_i0 if c % 2 == 0 else _i1, 16)
            for k in range(4):
                g.wait_ge(_stl[k], 16 * (NT // 4))
            g.wait_ge(_stf, 16)

        @block.sync
        def _(sy):
            for gidx in range(12):
                ub, W = gidx // 2, gidx % 2
                wt = wup if W == 0 else wgate
                half = gidx % 2
                if gidx >= 2:
                    sy.wait_ge(_mm, 16 * (gidx - 1))
                for dc in range(DC):
                    base = half * DC * 512 + dc * 512
                    sy.dma_start(
                        out=warena[:, base:base + 512],
                        in_=wt[ub * 512:(ub + 1) * 512, dc * 128:(dc + 1) * 128],
                        transpose=True,
                    ).then_inc(_w0 if half == 0 else _w1, 16)

        @block.vector
        def _(v):
            v.wait_ge(_cfg, 48)
            v.tensor_reduce(scal[0:1, 0:1], maxin[0:1, 0:8], AX,
                            ALU.max).then_inc(_scal, 1)
            v.wait_ge(_scal, 1)
            v.tensor_scalar(scal[0:1, 0:1], scal[0:1, 0:1], 1.0 / QMAX, EPS_Q,
                            ALU.mult, ALU.max).then_inc(_scal, 1)
            v.wait_ge(_scal, 2)
            v.reciprocal(scal[0:1, 1:2], scal[0:1, 0:1]).then_inc(_scal, 1)
            v.wait_ge(_bc, 1)
            v.tensor_copy(sxb[:, :], ps_bc[:, :]).then_inc(_scal, 1)
            v.wait_ge(_scal, 4)
            for c in range(DC):
                v.wait_ge(_i0 if c % 2 == 0 else _i1, 16 * (c // 2 + 1))
                if c > 0:
                    v.wait_ge(_q, 2 * c)
                st_sl = sstage[:, (c % 2) * T:(c % 2) * T + T]
                v.tensor_scalar(i16s[:, :], st_sl, sxb[:, 1:2], None,
                                ALU.mult).then_inc(_q, 1)
                v.wait_ge(_q, 2 * c + 1)
                v.tensor_scalar(xq[:, c * T:(c + 1) * T], i16s[:, :],
                                sxb[:, 0:1], None, ALU.mult).then_inc(_q, 1)
            for t in range(NT):
                v.wait_ge(_ev, t + 1)
                gidx, sub = t // 16, t % 16
                col = (gidx % 2) * (NT // 2) + (gidx // 2) * 16 + sub
                v.tensor_reduce(maxcol[:, col:col + 1],
                                ps_mm[:, (t % 4) * 512:(t % 4) * 512 + 512],
                                AX, ALU.max,
                                apply_absolute_value=True).then_inc(_red, 1)
            v.wait_ge(_red, NT)
            v.tensor_reduce(mfold[:, 0:1], maxcol[:, 0:NT // 2], AX, ALU.max)
            v.tensor_reduce(mfold[:, 1:2], maxcol[:, NT // 2:NT], AX,
                            ALU.max).then_inc(_red, 1)
            v.wait_ge(_bc, 2)
            v.tensor_reduce(mout[0:2, 0:1], ps_tr[0:2, 0:128], AX,
                            ALU.max).then_inc(_red, 1)

        @block.tensor
        def _(t_):
            t_.wait_ge(_cfg, 48)
            t_.wait_ge(_scal, 3)
            t_.matmul(ps_bc[:, :], onesr[0:1, :], scal[0:1, 0:2], start=True,
                      stop=True).then_inc(_bc, 1)
            t_.wait_ge(_q, 2 * DC)
            for t in range(NT):
                gidx, sub = t // 16, t % 16
                half = gidx % 2
                tb, uc4 = sub // 4, sub % 4
                if sub == 0:
                    t_.wait_ge(_w0 if half == 0 else _w1,
                               16 * DC * (gidx // 2 + 1))
                if t >= 4:
                    t_.wait_ge(_ev, t - 3)
                    t_.wait_ge(_red, t - 3)
                ins = None
                for dc in range(DC):
                    base = half * DC * 512 + dc * 512 + uc4 * 128
                    ins = t_.matmul(
                        ps_mm[:, (t % 4) * 512:(t % 4) * 512 + 512],
                        warena[:, base:base + 128],
                        xq[:, dc * T + tb * 512:dc * T + tb * 512 + 512],
                        start=(dc == 0), stop=(dc == DC - 1),
                    )
                ins.then_inc(_mm, 1)
            t_.wait_ge(_red, NT + 1)
            t_.transpose(ps_tr[0:2, 0:128], mfold[0:128, 0:2],
                         id128[:, :]).then_inc(_bc, 1)

        @block.scalar
        def _(s):
            for t in range(NT):
                s.wait_ge(_mm, t + 1)
                if t >= 4:
                    s.wait_ge(_stl[t % 4], 16 * (t // 4))
                s.activation(stage[:, (t % 4) * 512:(t % 4) * 512 + 512],
                             ps_mm[:, (t % 4) * 512:(t % 4) * 512 + 512],
                             AF.Copy).then_inc(_ev, 1)
                s.wait_ge(_ev, t + 1)
                gidx, sub = t // 16, t % 16
                ub, W = gidx // 2, gidx % 2
                tb, uc4 = sub // 4, sub % 4
                uc = ub * 4 + uc4
                dst = upout if W == 0 else gateout
                s.dma_start(
                    out=dst[uc * 128:(uc + 1) * 128, tb * 512:(tb + 1) * 512],
                    in_=stage[:, (t % 4) * 512:(t % 4) * 512 + 512],
                ).then_inc(_stl[t % 4], 16)
            s.wait_ge(_red, NT + 2)
            s.dma_start(out=maxug[0:2, 0:1], in_=mout[0:2, 0:1]).then_inc(_stf, 16)

    return nc



def build_l0():
    nc = bass.Bass("TRN2", target_bir_lowering=False, debug=False)
    wu = nc.dram_tensor("wu", [384, D], F32, kind="ExternalInput")
    wg = nc.dram_tensor("wg", [384, D], F32, kind="ExternalInput")
    wd = nc.dram_tensor("wd", [128, FF], F32, kind="ExternalInput")
    wuo = nc.dram_tensor("wuo", [384, D], BF16, kind="ExternalOutput")
    wgo = nc.dram_tensor("wgo", [384, D], BF16, kind="ExternalOutput")
    wdo = nc.dram_tensor("wdo", [128, FF], BF16, kind="ExternalOutput")
    tiles = [(wu, wuo, 0, D), (wu, wuo, 1, D), (wu, wuo, 2, D),
             (wg, wgo, 0, D), (wg, wgo, 1, D), (wg, wgo, 2, D),
             (wd, wdo, 0, FF)]
    with ExitStack() as es:
        sb = lambda *a: es.enter_context(nc.sbuf_tensor(*a))
        sem = lambda n: es.enter_context(nc.semaphore(n))
        wstage = sb("wstage", [128, 2 * FF], F32)
        t32 = sb("t32", [128, FF], F32)
        q8 = sb("q8", [128, FF], I8)
        sraw = sb("sraw", [128, FF // 32], F32)
        sinv = sb("sinv", [128, FF // 32], F32)
        ostage = sb("ostage", [128, 2 * FF], BF16)
        _i0, _i1 = sem("s_i0"), sem("s_i1")
        _c = sem("s_c")
        _o0, _o1 = sem("s_o0"), sem("s_o1")
        block = es.enter_context(nc.Block())

        @block.gpsimd
        def _(g):
            for k, (src_t, _, r, I) in enumerate(tiles):
                par, cnt = (_i0, k // 2 + 1) if k % 2 == 0 else (_i1, k // 2 + 1)
                if k >= 2:
                    g.wait_ge(_c, 6 * (k - 1))
                g.dma_start(out=wstage[:, (k % 2) * FF:(k % 2) * FF + I],
                            in_=src_t[r * 128:(r + 1) * 128, 0:I]).then_inc(par, 16)
            g.wait_ge(_o0, 16 * 4)
            g.wait_ge(_o1, 16 * 3)

        @block.vector
        def _(v):
            for k, (_, _, r, I) in enumerate(tiles):
                par, cnt = (_i0, 16 * (k // 2 + 1)) if k % 2 == 0 else (_i1, 16 * (k // 2 + 1))
                v.wait_ge(par, cnt)
                if k >= 2:
                    spar, scnt = (_o0, 16 * (k // 2)) if k % 2 == 0 else (_o1, 16 * (k // 2))
                    v.wait_ge(spar, scnt)
                if k > 0:
                    v.wait_ge(_c, 6 * k)
                w_sl = wstage[:, (k % 2) * FF:(k % 2) * FF + I]
                nb = I // 32
                ww = w_sl.rearrange("p (b w) -> p b w", w=32)
                v.tensor_reduce(sraw[:, 0:nb], ww, AX, ALU.max,
                                apply_absolute_value=True).then_inc(_c, 1)
                v.wait_ge(_c, 6 * k + 1)
                v.tensor_scalar(sraw[:, 0:nb], sraw[:, 0:nb], 1.0 / WQMAX, EPS_Q,
                                ALU.mult, ALU.max).then_inc(_c, 1)
                v.wait_ge(_c, 6 * k + 2)
                v.reciprocal(sinv[:, 0:nb], sraw[:, 0:nb]).then_inc(_c, 1)
                v.wait_ge(_c, 6 * k + 3)
                sb_b = sraw[:, 0:nb].unsqueeze(2).broadcast_to([128, nb, 32])
                si_b = sinv[:, 0:nb].unsqueeze(2).broadcast_to([128, nb, 32])
                tw = t32[:, 0:I].rearrange("p (b w) -> p b w", w=32)
                v.tensor_tensor(tw, ww, si_b, ALU.mult).then_inc(_c, 1)
                v.wait_ge(_c, 6 * k + 4)
                v.tensor_copy(q8[:, 0:I], t32[:, 0:I]).then_inc(_c, 1)
                v.wait_ge(_c, 6 * k + 5)
                o_sl = ostage[:, (k % 2) * FF:(k % 2) * FF + I]
                ow = o_sl.rearrange("p (b w) -> p b w", w=32)
                qw = q8[:, 0:I].rearrange("p (b w) -> p b w", w=32)
                v.tensor_tensor(ow, qw, sb_b, ALU.mult).then_inc(_c, 1)

        @block.sync
        def _(sy):
            for k, (_, dst_t, r, I) in enumerate(tiles):
                sy.wait_ge(_c, 6 * (k + 1))
                par = _o0 if k % 2 == 0 else _o1
                sy.dma_start(out=dst_t[r * 128:(r + 1) * 128, 0:I],
                             in_=ostage[:, (k % 2) * FF:(k % 2) * FF + I]
                             ).then_inc(par, 16)
    return nc



def build_l3():
    nc = bass.Bass("TRN2", target_bir_lowering=False, debug=False)
    gatein = nc.dram_tensor("gatein", [FF, T], BF16, kind="ExternalInput")
    maxug8 = nc.dram_tensor("maxug8", [1, 16], F32, kind="ExternalInput")
    g2out = nc.dram_tensor("g2out", [FF, T], BF16, kind="ExternalOutput")
    maxg2 = nc.dram_tensor("maxg2", [1, 1], F32, kind="ExternalOutput")
    ones_f32 = nc.inline_tensor(np.ones((1, 128), np.float32), name="ones_f32")
    ident128 = nc.inline_tensor(np.eye(128, dtype=np.float32), name="ident128")
    with ExitStack() as es:
        sb = lambda *a: es.enter_context(nc.sbuf_tensor(*a))
        ps = lambda *a: es.enter_context(nc.psum_tensor(*a))
        sem = lambda n: es.enter_context(nc.semaphore(n))
        gstage = sb("gstage", [128, 2 * T], BF16)
        i16g = sb("i16g", [128, T], I16)
        sig = sb("sig", [128, T], F32)
        u16s = sb("u16s", [128, T], U16)
        g2f = sb("g2f", [128, T], F32)
        ostage = sb("ostage", [128, 2 * T], BF16)
        maxin = sb("maxin", [1, 16], F32)
        scal = sb("scal", [1, 4], F32)
        scb = sb("scb", [128, 4], F32)
        onesr = sb("onesr", [1, 128], F32)
        id128 = sb("id128", [128, 128], F32)
        maxcol = sb("maxcol", [128, FC], F32)
        mfold = sb("mfold", [128, 1], F32)
        mo = sb("mo", [1, 1], F32)
        ps_bc = ps("ps_bc", [128, 4], F32)
        ps_tr = ps("ps_tr", [1, 128], F32)
        _cfg, _scal, _bc = sem("s_cfg"), sem("s_scal"), sem("s_bc")
        _i0, _i1 = sem("s_i0"), sem("s_i1")
        _vq = sem("s_vq")
        _sg = sem("s_sg")
        _red = sem("s_red")
        _o0, _o1 = sem("s_o0"), sem("s_o1")
        _stf = sem("s_stf")
        block = es.enter_context(nc.Block())

        @block.gpsimd
        def _(g):
            g.dma_start(out=maxin[:, :], in_=maxug8[:, :]).then_inc(_cfg, 16)
            g.dma_start(out=onesr[:, :], in_=ones_f32[:, :]).then_inc(_cfg, 16)
            g.dma_start(out=id128[:, :], in_=ident128[:, :]).then_inc(_cfg, 16)
            for c in range(FC):
                if c >= 2:
                    g.wait_ge(_vq, 5 * (c - 1))
                g.dma_start(out=gstage[:, (c % 2) * T:(c % 2) * T + T],
                            in_=gatein[c * 128:(c + 1) * 128, :]
                            ).then_inc(_i0 if c % 2 == 0 else _i1, 16)
            g.wait_ge(_o0, 16 * (FC // 2))
            g.wait_ge(_o1, 16 * (FC // 2))
            g.wait_ge(_stf, 16)

        @block.vector
        def _(v):
            v.wait_ge(_cfg, 48)
            v.tensor_reduce(scal[0:1, 0:1], maxin[0:1, 8:16], AX,
                            ALU.max).then_inc(_scal, 1)
            v.wait_ge(_scal, 1)
            v.tensor_scalar(scal[0:1, 0:1], scal[0:1, 0:1], 1.0 / QMAX, EPS_Q,
                            ALU.mult, ALU.max).then_inc(_scal, 1)
            v.wait_ge(_scal, 2)
            v.reciprocal(scal[0:1, 1:2], scal[0:1, 0:1]).then_inc(_scal, 1)
            v.wait_ge(_scal, 3)
            v.tensor_scalar(scal[0:1, 2:3], scal[0:1, 0:1], 1.0 / 65536.0, None,
                            ALU.mult).then_inc(_scal, 1)
            v.wait_ge(_bc, 1)
            v.tensor_copy(scb[:, :], ps_bc[:, :]).then_inc(_scal, 1)
            v.wait_ge(_scal, 5)
            for c in range(FC):
                v.wait_ge(_i0 if c % 2 == 0 else _i1, 16 * (c // 2 + 1))
                if c >= 2:
                    v.wait_ge(_o0 if c % 2 == 0 else _o1, 16 * (c // 2))
                if c > 0:
                    v.wait_ge(_red, c)
                g_sl = gstage[:, (c % 2) * T:(c % 2) * T + T]
                v.tensor_scalar(i16g[:, :], g_sl, scb[:, 1:2], None,
                                ALU.mult).then_inc(_vq, 1)
                v.wait_ge(_sg, c + 1)
                v.tensor_scalar(u16s[:, :], sig[:, :], 65536.0, None,
                                ALU.mult).then_inc(_vq, 1)
                v.wait_ge(_vq, 5 * c + 2)
                v.scalar_tensor_tensor(g2f[:, :], i16g[:, :], scb[:, 2:3],
                                       u16s[:, :], ALU.mult,
                                       ALU.mult).then_inc(_vq, 1)
                v.wait_ge(_vq, 5 * c + 3)
                v.tensor_reduce(maxcol[:, c:c + 1], g2f[:, :], AX, ALU.max,
                                apply_absolute_value=True).then_inc(_red, 1)
                o_sl = ostage[:, (c % 2) * T:(c % 2) * T + T]
                v.tensor_copy(o_sl, g2f[:, :]).then_inc(_vq, 2)
            v.wait_ge(_red, FC)
            v.tensor_reduce(mfold[:, 0:1], maxcol[:, :], AX,
                            ALU.max).then_inc(_red, 1)
            v.wait_ge(_bc, 2)
            v.tensor_reduce(mo[0:1, 0:1], ps_tr[0:1, 0:128], AX,
                            ALU.max).then_inc(_red, 1)

        @block.scalar
        def _(s):
            s.wait_ge(_scal, 5)
            for c in range(FC):
                s.wait_ge(_vq, 5 * c + 1)
                s.activation(sig[:, :], i16g[:, :], AF.Sigmoid, bias=0.0,
                             scale=scb[:, 0:1]).then_inc(_sg, 1)
                if c < FC - 1:
                    s.wait_ge(_vq, 5 * c + 2)

        @block.tensor
        def _(t_):
            t_.wait_ge(_cfg, 48)
            t_.wait_ge(_scal, 4)
            t_.matmul(ps_bc[:, :], onesr[0:1, :], scal[0:1, 0:4], start=True,
                      stop=True).then_inc(_bc, 1)
            t_.wait_ge(_red, FC + 1)
            t_.transpose(ps_tr[0:1, 0:128], mfold[0:128, 0:1],
                         id128[:, :]).then_inc(_bc, 1)

        @block.sync
        def _(sy):
            for c in range(FC):
                sy.wait_ge(_vq, 5 * c + 5)
                sy.dma_start(out=g2out[c * 128:(c + 1) * 128, :],
                             in_=ostage[:, (c % 2) * T:(c % 2) * T + T]
                             ).then_inc(_o0 if c % 2 == 0 else _o1, 16)
            sy.wait_ge(_red, FC + 2)
            sy.dma_start(out=maxg2[0:1, 0:1], in_=mo[0:1, 0:1]).then_inc(_stf, 16)
    return nc



def build_l4():
    nc = bass.Bass("TRN2", target_bir_lowering=False, debug=False)
    g2in = nc.dram_tensor("g2in", [FF, T], BF16, kind="ExternalInput")
    upin = nc.dram_tensor("upin", [FF, T], BF16, kind="ExternalInput")
    maxg28 = nc.dram_tensor("maxg28", [1, 8], F32, kind="ExternalInput")
    maxug8 = nc.dram_tensor("maxug8", [1, 16], F32, kind="ExternalInput")
    oout = nc.dram_tensor("oout", [FF, T], BF16, kind="ExternalOutput")
    maxo = nc.dram_tensor("maxo", [1, 1], F32, kind="ExternalOutput")
    ones_f32 = nc.inline_tensor(np.ones((1, 128), np.float32), name="ones_f32")
    ident128 = nc.inline_tensor(np.eye(128, dtype=np.float32), name="ident128")
    with ExitStack() as es:
        sb = lambda *a: es.enter_context(nc.sbuf_tensor(*a))
        ps = lambda *a: es.enter_context(nc.psum_tensor(*a))
        sem = lambda n: es.enter_context(nc.semaphore(n))
        gstage = sb("gstage", [128, 2 * T], BF16)
        ustage = sb("ustage", [128, 2 * T], BF16)
        i16g = sb("i16g", [128, T], I16)
        i16u = sb("i16u", [128, T], I16)
        of = sb("of", [128, T], F32)
        ostage = sb("ostage", [128, 2 * T], BF16)
        maxing = sb("maxing", [1, 8], F32)
        maxinu = sb("maxinu", [1, 16], F32)
        scal = sb("scal", [1, 4], F32)
        scb = sb("scb", [128, 4], F32)
        onesr = sb("onesr", [1, 128], F32)
        id128 = sb("id128", [128, 128], F32)
        maxcol = sb("maxcol", [128, FC], F32)
        mfold = sb("mfold", [128, 1], F32)
        mo = sb("mo", [1, 1], F32)
        ps_bc = ps("ps_bc", [128, 4], F32)
        ps_tr = ps("ps_tr", [1, 128], F32)
        _cfg, _scal, _bc = sem("s_cfg"), sem("s_scal"), sem("s_bc")
        _ig0, _ig1 = sem("s_ig0"), sem("s_ig1")
        _iu0, _iu1 = sem("s_iu0"), sem("s_iu1")
        _vq = sem("s_vq")
        _red = sem("s_red")
        _o0, _o1 = sem("s_o0"), sem("s_o1")
        _stf = sem("s_stf")
        block = es.enter_context(nc.Block())

        @block.gpsimd
        def _(g):
            g.dma_start(out=maxing[:, :], in_=maxg28[:, :]).then_inc(_cfg, 16)
            g.dma_start(out=maxinu[:, :], in_=maxug8[:, :]).then_inc(_cfg, 16)
            g.dma_start(out=onesr[:, :], in_=ones_f32[:, :]).then_inc(_cfg, 16)
            g.dma_start(out=id128[:, :], in_=ident128[:, :]).then_inc(_cfg, 16)
            for c in range(FC):
                if c >= 2:
                    g.wait_ge(_vq, 5 * (c - 1))
                g.dma_start(out=gstage[:, (c % 2) * T:(c % 2) * T + T],
                            in_=g2in[c * 128:(c + 1) * 128, :]
                            ).then_inc(_ig0 if c % 2 == 0 else _ig1, 16)
            g.wait_ge(_o0, 16 * (FC // 2))
            g.wait_ge(_o1, 16 * (FC // 2))
            g.wait_ge(_stf, 16)

        @block.sync
        def _(sy):
            for c in range(FC):
                if c >= 2:
                    sy.wait_ge(_vq, 5 * (c - 1))
                sy.dma_start(out=ustage[:, (c % 2) * T:(c % 2) * T + T],
                             in_=upin[c * 128:(c + 1) * 128, :]
                             ).then_inc(_iu0 if c % 2 == 0 else _iu1, 16)

        @block.scalar
        def _(s):
            for c in range(FC):
                s.wait_ge(_vq, 5 * c + 5)
                s.dma_start(out=oout[c * 128:(c + 1) * 128, :],
                            in_=ostage[:, (c % 2) * T:(c % 2) * T + T]
                            ).then_inc(_o0 if c % 2 == 0 else _o1, 16)
            s.wait_ge(_red, FC + 2)
            s.dma_start(out=maxo[0:1, 0:1], in_=mo[0:1, 0:1]).then_inc(_stf, 16)

        @block.vector
        def _(v):
            v.wait_ge(_cfg, 64)
            v.tensor_reduce(scal[0:1, 0:1], maxing[0:1, 0:8], AX,
                            ALU.max).then_inc(_scal, 1)
            v.tensor_reduce(scal[0:1, 1:2], maxinu[0:1, 0:8], AX,
                            ALU.max).then_inc(_scal, 1)
            v.wait_ge(_scal, 2)
            v.tensor_scalar(scal[0:1, 0:1], scal[0:1, 0:1], 1.0 / QMAX, EPS_Q,
                            ALU.mult, ALU.max)
            v.tensor_scalar(scal[0:1, 1:2], scal[0:1, 1:2], 1.0 / QMAX, EPS_Q,
                            ALU.mult, ALU.max).then_inc(_scal, 1)
            v.wait_ge(_scal, 3)
            v.tensor_tensor(scal[0:1, 2:3], scal[0:1, 0:1], scal[0:1, 1:2],
                            ALU.mult)
            v.reciprocal(scal[0:1, 3:4], scal[0:1, 1:2]).then_inc(_scal, 1)
            v.wait_ge(_scal, 4)
            v.reciprocal(scal[0:1, 0:1], scal[0:1, 0:1]).then_inc(_scal, 1)
            v.wait_ge(_bc, 1)
            v.tensor_copy(scb[:, :], ps_bc[:, :]).then_inc(_scal, 1)
            v.wait_ge(_scal, 6)
            for c in range(FC):
                v.wait_ge(_ig0 if c % 2 == 0 else _ig1, 16 * (c // 2 + 1))
                v.wait_ge(_iu0 if c % 2 == 0 else _iu1, 16 * (c // 2 + 1))
                if c >= 2:
                    v.wait_ge(_o0 if c % 2 == 0 else _o1, 16 * (c // 2))
                if c > 0:
                    v.wait_ge(_red, c)
                g_sl = gstage[:, (c % 2) * T:(c % 2) * T + T]
                u_sl = ustage[:, (c % 2) * T:(c % 2) * T + T]
                v.tensor_scalar(i16g[:, :], g_sl, scb[:, 0:1], None, ALU.mult)
                v.tensor_scalar(i16u[:, :], u_sl, scb[:, 3:4], None,
                                ALU.mult).then_inc(_vq, 2)
                v.wait_ge(_vq, 5 * c + 2)
                v.scalar_tensor_tensor(of[:, :], i16g[:, :], scb[:, 2:3],
                                       i16u[:, :], ALU.mult,
                                       ALU.mult).then_inc(_vq, 1)
                v.wait_ge(_vq, 5 * c + 3)
                v.tensor_reduce(maxcol[:, c:c + 1], of[:, :], AX, ALU.max,
                                apply_absolute_value=True).then_inc(_red, 1)
                o_sl = ostage[:, (c % 2) * T:(c % 2) * T + T]
                v.tensor_copy(o_sl, of[:, :]).then_inc(_vq, 2)
            v.wait_ge(_red, FC)
            v.tensor_reduce(mfold[:, 0:1], maxcol[:, :], AX,
                            ALU.max).then_inc(_red, 1)
            v.wait_ge(_bc, 2)
            v.tensor_reduce(mo[0:1, 0:1], ps_tr[0:1, 0:128], AX,
                            ALU.max).then_inc(_red, 1)

        @block.tensor
        def _(t_):
            t_.wait_ge(_cfg, 64)
            t_.wait_ge(_scal, 5)
            t_.matmul(ps_bc[:, :], onesr[0:1, :], scal[0:1, 0:4], start=True,
                      stop=True).then_inc(_bc, 1)
            t_.wait_ge(_red, FC + 1)
            t_.transpose(ps_tr[0:1, 0:128], mfold[0:128, 0:1],
                         id128[:, :]).then_inc(_bc, 1)
    return nc



def build_l5():
    import ml_dtypes as _mld
    nc = bass.Bass("TRN2", target_bir_lowering=False, debug=False)
    oin = nc.dram_tensor("oin", [FF, T], BF16, kind="ExternalInput")
    maxo8 = nc.dram_tensor("maxo8", [128, 8], F32, kind="ExternalInput")
    streamin = nc.dram_tensor("streamin", [128, DC * T], F32, kind="ExternalInput")
    wdown = nc.dram_tensor("wdown", [128, FC * 1024], BF16, kind="ExternalInput")
    final = nc.dram_tensor("final", [128, DC * T], F32, kind="ExternalOutput")
    ones_f32 = nc.inline_tensor(np.ones((1, 128), np.float32), name="ones_f32")
    onescol_b = nc.inline_tensor(np.ones((128, 1), dtype=_mld.bfloat16),
                                 name="onescol_b")
    SUMC = [1, 3, 5, 7]
    BCC = [2, 4, 6, 8]

    TAILSEQ = [("m", 0), ("m", 1), ("s", 0), ("m", 2), ("s", 1), ("m", 3),
               ("s", 2), ("m", 4), ("s", 3), ("m", 5), ("s", 4), ("m", 6),
               ("s", 5), ("m", 7), ("s", 6), ("s", 7)]
    TAILV = {}
    for _i, (_o, _d) in enumerate(TAILSEQ):
        if _o == "s":
            TAILV[_d] = _i + 1
    TSTORE = [max(TAILV[2 * j], TAILV[2 * j + 1]) for j in range(4)]

    with ExitStack() as es:
        sb = lambda *a: es.enter_context(nc.sbuf_tensor(*a))
        ps = lambda *a: es.enter_context(nc.psum_tensor(*a))
        sem = lambda n: es.enter_context(nc.semaphore(n))
        stream = sb("stream", [128, DC * T], F32)
        oq = sb("oq", [128, 2 * FC * 512], BF16)
        warena = sb("warena", [128, FC * 1024], BF16)
        i16o = sb("i16o", [128, 2 * 512], I16)
        maxin = sb("maxin", [128, 8], F32)
        scb = sb("scb", [128, 2], F32)
        onesr = sb("onesr", [1, 128], F32)
        onecb = sb("onecb", [128, 1], BF16)
        sbf = sb("sbf", [128, DC * 512], BF16)
        sqbf = sb("sqbf", [128, DC * 512], BF16)
        lrow = sb("lrow", [1, 5 * 512], F32)
        ps_mm = ps("ps_mm", [128, 4 * 512], F32)
        ps_ln = ps("ps_ln", [1, 1024], F32)
        ps_r = ps("ps_r", [128, 512], F32)
        ps_m = ps("ps_m", [128, 512], F32)
        _cfgm, _cfgo = sem("s_cfgm"), sem("s_cfgo")
        _scal = sem("s_scal")
        _q, _qi = sem("s_q"), sem("s_qi")
        _ldg = [sem(f"s_ldg{k}") for k in range(24)]
        _wd = [sem(f"s_wd{k}") for k in range(DC)]
        _sid = [sem(f"s_sid{k}") for k in range(DC)]
        _mm, _ev, _sb = sem("s_mm"), sem("s_ev"), sem("s_sb")
        _ln, _lnp, _lna = sem("s_ln"), sem("s_lnp"), sem("s_lna")
        _lnn = sem("s_lnn")
        _so = sem("s_so")
        block = es.enter_context(nc.Block())

        def _oqsl(k):
            tb, c = k // FC, k % FC
            return oq[:, (tb % 2) * FC * 512 + c * 512:(tb % 2) * FC * 512 + c * 512 + 512]

        def _oin_grp(eng, tb, g):
            if tb >= 2:
                eng.wait_ge(_mm, 8 * (tb - 2) + 8)
            base = (tb % 2) * FC * 512 + g * 4 * 512
            eng.dma_start(
                out=oq[:, base:base + 2048].rearrange("p (c t) -> p c t", t=512),
                in_=oin[g * 512:(g + 1) * 512,
                        tb * 512:(tb + 1) * 512].rearrange(
                            "(c p) t -> p c t", p=128)
            ).then_inc(_ldg[tb * 6 + g], 16)

        def _wd_dma(eng, dc):
            eng.dma_start(
                out=warena[:, dc * FC * 128:(dc + 1) * FC * 128],
                in_=wdown[:, dc * FC * 128:(dc + 1) * FC * 128],
            ).then_inc(_wd[dc], 16)

        def _sin_dma(eng, c):
            eng.dma_start(out=stream[:, c * T:(c + 1) * T],
                          in_=streamin[:, c * T:(c + 1) * T]
                          ).then_inc(_sid[c], 16)

        def _q_i16(s, k):
            if k % 4 == 0:
                s.wait_ge(_ldg[k // 4], 16)
            if k >= 2:
                s.wait_ge(_q, k - 1)
            s.activation(i16o[:, (k % 2) * 512:(k % 2) * 512 + 512],
                         _oqsl(k), AF.Copy, scale=scb[:, 1:2]).then_inc(_qi, 1)

        def _q_oq(v, k):
            v.wait_ge(_qi, k + 1)
            v.tensor_scalar(_oqsl(k), i16o[:, (k % 2) * 512:(k % 2) * 512 + 512],
                            scb[:, 0:1], None, ALU.mult).then_inc(_q, 1)

        def _store(eng, tbn, j):
            if tbn == TB - 1:
                eng.wait_ge(_lnn, 16 * tbn + TSTORE[j])
            else:
                eng.wait_ge(_lnn, 16 * tbn + 4 * (j + 1))
            sv = final[:, :].rearrange("p (c t) -> p c t", t=T)[
                :, 2 * j:2 * j + 2, tbn * 512:tbn * 512 + 512]
            eng.dma_start(out=sv,
                          in_=stream[:, :].rearrange("p (c t) -> p c t", t=T)[
                              :, 2 * j:2 * j + 2, tbn * 512:tbn * 512 + 512]
                          ).then_inc(_so, 16)

        @block.gpsimd
        def _(g):
            _sin_dma(g, 0)
            _sin_dma(g, 4)
            _sin_dma(g, 6)
            for grp in (1, 3, 5):
                _oin_grp(g, 1, grp)
            g.wait_ge(_so, 16 * 16)

        @block.sync
        def _(sy):
            sy.dma_start(out=onesr[:, :], in_=ones_f32[:, :]).then_inc(_cfgo, 16)
            sy.dma_start(out=onecb[:, :], in_=onescol_b[:, :]).then_inc(_cfgo, 16)
            _wd_dma(sy, 0)
            for grp in (0, 2, 4):
                _oin_grp(sy, 0, grp)
            _wd_dma(sy, 1)
            _sin_dma(sy, 2)
            _wd_dma(sy, 3)
            for grp in (0, 2, 4):
                _oin_grp(sy, 1, grp)
            _wd_dma(sy, 5)
            _sin_dma(sy, 5)
            _wd_dma(sy, 7)
            _sin_dma(sy, 7)
            for grp in range(6):
                _oin_grp(sy, 2, grp)
            _store(sy, 0, 1)
            _store(sy, 0, 3)
            for grp in range(6):
                _oin_grp(sy, 3, grp)
            _store(sy, 1, 1)
            _store(sy, 1, 3)
            for tbn in range(2, TB):
                _store(sy, tbn, 1)
                _store(sy, tbn, 3)

        @block.scalar
        def _(s):
            s.dma_start(out=maxin[:, :], in_=maxo8[:, :]).then_inc(_cfgm, 16)
            _wd_dma(s, 2)
            for grp in (1, 3, 5):
                _oin_grp(s, 0, grp)
            _wd_dma(s, 4)
            _sin_dma(s, 1)
            _wd_dma(s, 6)
            _sin_dma(s, 3)

            s.wait_ge(_scal, 3)
            for k in range(2 * FC):
                _q_i16(s, k)
            for tt in range(32):
                tb, dc = tt // 8, tt % 8
                s.wait_ge(_sb, tt + 1)
                s.activation(sqbf[:, dc * 512:dc * 512 + 512],
                             sbf[:, dc * 512:dc * 512 + 512],
                             AF.Square).then_inc(_lna, 1)
                if 1 <= dc <= 6 and tb + 1 in (2, 3):
                    for j in range(4):
                        _q_i16(s, (tb + 1) * FC + 4 * (dc - 1) + j)
                if tb > 0 and dc == 3:
                    _store(s, tb - 1, 0)
                if tb > 0 and dc == 6:
                    _store(s, tb - 1, 2)
                if dc == 7:
                    s.wait_ge(_ln, 5 * tb + 4)
                    if tb > 0:
                        s.wait_ge(_lnp, BCC[tb - 1])
                    s.activation(lrow[0:1, 1536:2048], lrow[0:1, 1024:1536],
                                 AF.Ln).then_inc(_lna, 1)
                    s.wait_ge(_lna, 10 * tb + 9)
                    s.activation(lrow[0:1, 1536:2048], lrow[0:1, 1536:2048],
                                 AF.Exp, bias=0.0,
                                 scale=-0.5).then_inc(_lna, 1)
            _store(s, TB - 1, 0)
            _store(s, TB - 1, 2)

        @block.vector
        def _(v):
            v.wait_ge(_cfgm, 16)
            v.tensor_reduce(scb[:, 0:1], maxin[:, :], AX,
                            ALU.max).then_inc(_scal, 1)
            v.wait_ge(_scal, 1)
            v.tensor_scalar(scb[:, 0:1], scb[:, 0:1], 1.0 / QMAX, EPS_Q,
                            ALU.mult, ALU.max).then_inc(_scal, 1)
            v.wait_ge(_scal, 2)
            v.reciprocal(scb[:, 1:2], scb[:, 0:1]).then_inc(_scal, 1)
            v.wait_ge(_scal, 3)
            for k in range(2 * FC):
                _q_oq(v, k)
            for tt in range(32):
                tb, dc = tt // 8, tt % 8
                if tb == 0:
                    v.wait_ge(_sid[dc], 16)
                v.wait_ge(_mm, tt + 1)
                sl = stream[:, dc * T + tb * 512:dc * T + tb * 512 + 512]
                v.tensor_tensor(sl, sl,
                                ps_mm[:, (tt % 4) * 512:(tt % 4) * 512 + 512],
                                ALU.add).then_inc(_ev, 1)
                v.wait_ge(_ev, tt + 1)
                v.tensor_copy(sbf[:, dc * 512:dc * 512 + 512],
                              sl).then_inc(_sb, 1)
                if 1 <= dc <= 6 and tb + 1 in (2, 3):
                    for j in range(4):
                        _q_oq(v, (tb + 1) * FC + 4 * (dc - 1) + j)

                def _norm_chunk(tbn, dcn, first):
                    if first:
                        v.wait_ge(_lnp, BCC[tbn])
                    sl2 = stream[:, dcn * T + tbn * 512:dcn * T + tbn * 512 + 512]
                    v.tensor_tensor(sl2, sl2, ps_r[:, 0:512],
                                    ALU.mult).then_inc(_lnn, 1)
                    v.wait_ge(_lnn, 16 * tbn + 2 * dcn + 1)
                    v.tensor_tensor(sl2, sl2, ps_m[:, 0:512],
                                    ALU.subtract).then_inc(_lnn, 1)

                if tb > 0 and dc >= 1:
                    if dc == 1:
                        _norm_chunk(tb - 1, 0, True)
                        _norm_chunk(tb - 1, 1, False)
                    else:
                        _norm_chunk(tb - 1, dc, False)
                if dc == 7:
                    b = 5 * tb
                    v.wait_ge(_lnp, SUMC[tb])
                    v.tensor_scalar(lrow[0:1, 0:512], ps_ln[0:1, 0:512],
                                    1.0 / D, None, ALU.mult)
                    v.tensor_scalar(lrow[0:1, 512:1024], ps_ln[0:1, 512:1024],
                                    1.0 / D, None, ALU.mult).then_inc(_ln, 1)
                    v.wait_ge(_ln, b + 1)
                    v.tensor_tensor(lrow[0:1, 1024:1536], lrow[0:1, 0:512],
                                    lrow[0:1, 0:512], ALU.mult).then_inc(_ln, 1)
                    v.wait_ge(_ln, b + 2)
                    v.tensor_tensor(lrow[0:1, 1024:1536], lrow[0:1, 512:1024],
                                    lrow[0:1, 1024:1536],
                                    ALU.subtract).then_inc(_ln, 1)
                    v.wait_ge(_ln, b + 3)
                    v.tensor_scalar(lrow[0:1, 1024:1536], lrow[0:1, 1024:1536],
                                    EPS_LN, None, ALU.add).then_inc(_ln, 1)
                    v.wait_ge(_lna, 10 * tb + 10)
                    if tb > 0:
                        v.wait_ge(_lnp, BCC[tb - 1])
                    v.tensor_tensor(lrow[0:1, 2048:2560], lrow[0:1, 0:512],
                                    lrow[0:1, 1536:2048], ALU.mult).then_inc(_ln, 1)
                    if tb == TB - 1:
                        v.wait_ge(_lnp, BCC[tb])
                        for op, dcn in TAILSEQ:
                            sl2 = stream[:, dcn * T + tb * 512:dcn * T + tb * 512 + 512]
                            if op == "m":
                                v.tensor_tensor(sl2, sl2, ps_r[:, 0:512],
                                                ALU.mult).then_inc(_lnn, 1)
                            else:
                                v.wait_ge(_lnn, 16 * tb + TAILSEQ.index(("m", dcn)) + 1)
                                v.tensor_tensor(sl2, sl2, ps_m[:, 0:512],
                                                ALU.subtract).then_inc(_lnn, 1)

        @block.tensor
        def _(t_):
            for tt in range(32):
                tb, dc = tt // 8, tt % 8
                if tb == 0:
                    t_.wait_ge(_wd[dc], 16)
                if dc != 0:
                    t_.wait_ge(_q, FC * (tb + 1))
                if tt >= 4:
                    t_.wait_ge(_ev, tt - 3)
                ins = None
                for ic in range(FC):
                    base = (dc * FC + ic) * 128
                    if dc == 0:
                        t_.wait_ge(_q, tb * FC + ic + 1)
                    ins = t_.matmul(
                        ps_mm[:, (tt % 4) * 512:(tt % 4) * 512 + 512],
                        warena[:, base:base + 128],
                        _oqsl(tb * FC + ic),
                        start=(ic == 0), stop=(ic == FC - 1))
                ins.then_inc(_mm, 1)

                def _sum_slice(dcn, stop):
                    t_.wait_ge(_sb, 8 * tb + dcn + 1)
                    t_.wait_ge(_lna, 10 * tb + dcn + 1)
                    if dcn == 0:
                        t_.wait_ge(_cfgo, 32)
                        if tb > 0:
                            t_.wait_ge(_ln, 5 * (tb - 1) + 1)
                    t_.matmul(ps_ln[0:1, 0:512], onecb[:, :],
                              sbf[:, dcn * 512:(dcn + 1) * 512],
                              start=(dcn == 0), stop=stop)
                    ins2 = t_.matmul(ps_ln[0:1, 512:1024], onecb[:, :],
                                     sqbf[:, dcn * 512:(dcn + 1) * 512],
                                     start=(dcn == 0), stop=stop)
                    if stop:
                        ins2.then_inc(_lnp, 1)

                def _bcast(tbn):
                    t_.wait_ge(_ln, 5 * tbn + 5)
                    if tbn > 0:
                        t_.wait_ge(_lnn, 16 * tbn)
                    t_.matmul(ps_r[:, 0:512], onesr[0:1, :],
                              lrow[0:1, 1536:2048], start=True, stop=True)
                    insb = t_.matmul(ps_m[:, 0:512], onesr[0:1, :],
                                     lrow[0:1, 2048:2560], start=True,
                                     stop=True)
                    insb.then_inc(_lnp, 1)

                if tb > 0 and dc == 0:
                    _bcast(tb - 1)
                if dc >= 1:
                    _sum_slice(dc - 1, False)
                if dc == 7:
                    _sum_slice(7, True)
                    if tt == 31:
                        _bcast(tb)
    return nc



def build_l1():
    import ml_dtypes as _mld
    nc = bass.Bass("TRN2", target_bir_lowering=False, debug=False)
    xT = nc.dram_tensor("xT", [128, DC * T], F32, kind="ExternalInput")
    wqn = nc.dram_tensor("wqn", [NH * DH, D], F32, kind="ExternalInput")
    wkvT = nc.dram_tensor("wkvT", [D, LR], F32, kind="ExternalInput")
    wku = nc.dram_tensor("wku", [NH * DH, LR], F32, kind="ExternalInput")
    wvuT = nc.dram_tensor("wvuT", [LR, NH * DH], F32, kind="ExternalInput")
    woT = nc.dram_tensor("woT", [NH * DH, D], F32, kind="ExternalInput")
    streamout = nc.dram_tensor("streamout", [128, DC * T], F32, kind="ExternalOutput")
    maxx = nc.dram_tensor("maxx", [1, 1], F32, kind="ExternalOutput")
    attnout = nc.dram_tensor("attnout", [T, NH * DH], BF16)
    ones_f32 = nc.inline_tensor(np.ones((1, 128), np.float32), name="ones_f32")
    onesm_b = nc.inline_tensor(np.ones((128, 128), dtype=_mld.bfloat16),
                               name="onesm_b")
    identb = nc.inline_tensor(np.eye(128, dtype=_mld.bfloat16), name="identb")
    ident128 = nc.inline_tensor(np.eye(128, dtype=np.float32), name="ident128")
    LNS = 22
    with ExitStack() as es:
        sb = lambda *a: es.enter_context(nc.sbuf_tensor(*a))
        sem = lambda n: es.enter_context(nc.semaphore(n))
        stream = sb("stream", [128, DC * T], F32)
        xbf = sb("xbf", [128, DC * T], BF16)
        latbf = sb("latbf", [64, T], BF16)
        lattok = sb("lattok", [128, 16 * 65], BF16)
        kvw = sb("kvw", [128, DC * LR], BF16)
        wku_sb = sb("wku_sb", [128, 16 * LR], BF16)
        vup = sb("vup", [64, NH * DH], BF16)
        wqn_sb = sb("wqn_sb", [128, 2 * 2 * D], BF16)
        wqk = sb("wqk", [128, DC * LR], BF16)
        qhat = sb("qhat", [64, T], BF16)
        sqs = sb("sqs", [64, T], BF16)
        qn2 = sb("qn2", [1, T], F32)
        latn2 = sb("latn2", [1, T], F32)
        csc = sb("csc", [1, 8], F32)
        negcb = sb("negcb", [128, 1], F32)
        rcp = sb("rcp", [128, 4], F32)
        aostage = sb("aostage", [128, 4 * 256], BF16)
        u1 = sb("u1", [128, 16384], BF16)
        probsT = u1[:, 0:8192]
        obar = u1[:, 8192:8704]
        aoT = u1[:, 0:16384]
        wo_sb = sb("wo_sb", [128, 8 * 1024], BF16)
        onesr = sb("onesr", [1, 128], F32)
        onemb = sb("onemb", [128, 128], BF16)
        idb = sb("idb", [128, 128], BF16)
        id128 = sb("id128", [128, 128], F32)
        lrow = sb("lrow", [1, 5 * 512], F32)
        maxcol = sb("maxcol", [128, 8], F32)
        mfold = sb("mfold", [128, 1], F32)
        mxo = sb("mxo", [1, 1], F32)
        psum = es.enter_context(nc.psum_tensor("psum", [128, 4096], F32))
        B2 = psum[:, 1024:1536]
        B3c = psum[:, 1536:1537]
        _cfg = sem("s_cfg")
        _vb = sem("s_vb")
        _xs, _xb = sem("s_xs"), sem("s_xb")
        _wq0, _wq1 = sem("s_wq0"), sem("s_wq1")
        _lat, _late = sem("s_lat"), sem("s_late")
        _ltr, _lte = sem("s_ltr"), sem("s_lte")
        _mmq, _ewq = sem("s_mmq"), sem("s_ewq")
        _eqh, _sq = sem("s_eqh"), sem("s_sq")
        _nmm, _nev = sem("s_nmm"), sem("s_nev")
        _lm = sem("s_lm")
        _ch, _cha, _chb = sem("s_ch"), sem("s_cha"), sem("s_chb")
        _smm, _sexp = sem("s_smm"), sem("s_sexp")
        _amm, _oev = sem("s_amm"), sem("s_oev")
        _otr, _arc = sem("s_otr"), sem("s_arc")
        _avm, _aev = sem("s_avm"), sem("s_aev")
        _asl = [sem(f"s_as{k}") for k in range(4)]
        _ao0, _ao1 = sem("s_ao0"), sem("s_ao1")
        _wo0, _wo1 = sem("s_wo0"), sem("s_wo1")
        _pmm, _pev = sem("s_pmm"), sem("s_pev")
        _ln, _lnp, _lna = sem("s_ln"), sem("s_lnp"), sem("s_lna")
        _mx, _mxp = sem("s_mx"), sem("s_mxp")
        _so = sem("s_so")
        block = es.enter_context(nc.Block())

        @block.gpsimd
        def _(g):
            g.dma_start(out=onesr[:, :], in_=ones_f32[:, :]).then_inc(_cfg, 16)
            g.dma_start(out=onemb[:, :], in_=onesm_b[:, :]).then_inc(_cfg, 16)
            g.dma_start(out=idb[:, :], in_=identb[:, :]).then_inc(_cfg, 16)
            g.dma_start(out=id128[:, :], in_=ident128[:, :]).then_inc(_cfg, 16)
            g.dma_start(out=kvw[:, :].rearrange("p (c n) -> p c n", n=LR),
                        in_=wkvT[:, :].rearrange("(c p) n -> p c n", p=128)
                        ).then_inc(_cfg, 16)
            g.dma_start(out=wku_sb[:, :].rearrange("p (c n) -> p c n", n=LR),
                        in_=wku[:, :].rearrange("(c p) n -> p c n", p=128)
                        ).then_inc(_cfg, 16)
            g.dma_start(out=vup[:, :], in_=wvuT[:, :]).then_inc(_cfg, 16)
            for tc in range(16):
                g.memset(lattok[:, tc * 65 + 64:tc * 65 + 65], 1.0)
            g.sem_inc(_vb, 1)
            for c in range(DC):
                g.dma_start(out=xbf[:, c * T:(c + 1) * T],
                            in_=xT[:, c * T:(c + 1) * T]).then_inc(_xb, 16)
            for h in range(NH):
                if h >= 2:
                    g.wait_ge(_mmq, (h - 2) * 5 + 1)
                g.dma_start(
                    out=wqn_sb[:, (h % 2) * 2 * D:(h % 2) * 2 * D + 2 * D]
                        .rearrange("p (c n) -> p c n", n=D),
                    in_=wqn[h * 256:(h + 1) * 256, :]
                        .rearrange("(c p) n -> p c n", p=128)
                ).then_inc(_wq0 if h % 2 == 0 else _wq1, 16)
            for H in range(2):
                if H == 1:
                    g.wait_ge(_pmm, 32)
                for c8 in range(8):
                    cc = H * 8 + c8
                    g.dma_start(out=wo_sb[:, c8 * 1024:(c8 + 1) * 1024],
                                in_=woT[cc * 128:(cc + 1) * 128, :]
                                ).then_inc(_wo0 if H == 0 else _wo1, 16)
            g.wait_ge(_so, 16 * (TB + 1))

        @block.sync
        def _(sy):
            for c in range(DC):
                sy.dma_start(out=stream[:, c * T:(c + 1) * T],
                             in_=xT[:, c * T:(c + 1) * T]).then_inc(_xs, 16)
            for h in range(NH):
                for qb in range(TB):
                    for qt in range(4):
                        aidx = (h * TB + qb) * 4 + qt
                        sy.wait_ge(_aev, aidx + 1)
                        sy.dma_start(
                            out=attnout[qb * 512 + qt * 128:qb * 512 + qt * 128 + 128,
                                        h * 256:(h + 1) * 256],
                            in_=aostage[:, (aidx % 4) * 256:(aidx % 4) * 256 + 256],
                        ).then_inc(_asl[aidx % 4], 16)
            for k in range(4):
                sy.wait_ge(_asl[k], 16 * 32)
            for H in range(2):
                for qb in range(TB):
                    g8 = H * 4 + qb
                    if g8 >= 2:
                        sy.wait_ge(_pmm, 8 * (g8 - 1))
                    for c8 in range(8):
                        cc = H * 8 + c8
                        sy.dma_start(
                            out=aoT[:, (g8 % 2) * 16 * 512 + c8 * 512:(g8 % 2) * 16 * 512 + c8 * 512 + 512],
                            in_=attnout[qb * 512:(qb + 1) * 512, cc * 128:(cc + 1) * 128],
                            transpose=True,
                        ).then_inc(_ao0 if g8 % 2 == 0 else _ao1, 16)
            for tb in range(TB):
                sy.wait_ge(_ln, LNS * (tb + 1))
                sv = streamout[:, :].rearrange("p (c t) -> p c t", t=T)[:, :, tb * 512:tb * 512 + 512]
                sy.dma_start(out=sv,
                             in_=stream[:, :].rearrange("p (c t) -> p c t", t=T)[:, :, tb * 512:tb * 512 + 512]
                             ).then_inc(_so, 16)
            sy.wait_ge(_mx, 10)
            sy.dma_start(out=maxx[0:1, 0:1], in_=mxo[0:1, 0:1]).then_inc(_so, 16)

        @block.scalar
        def _(s):
            for tb in range(TB):
                s.wait_ge(_lat, tb + 1)
                s.activation(latbf[0:64, tb * 512:(tb + 1) * 512],
                             psum[0:64, 0:512], AF.Copy).then_inc(_late, 1)
            for h in range(NH):
                s.wait_ge(_ch, h * 4 + 2)
                s.activation(csc[0:1, 3:4], csc[0:1, 2:3], AF.Ln).then_inc(_cha, 1)
                s.wait_ge(_cha, h * 2 + 1)
                s.activation(csc[0:1, 4:5], csc[0:1, 3:4], AF.Exp, bias=0.0,
                             scale=0.5).then_inc(_cha, 1)
                for qb in range(TB):
                    for pr in range(8):
                        s.wait_ge(_smm, (h * TB + qb) * 16 + 2 * pr + 2)
                        s.wait_ge(_ch, h * 4 + 4)
                        if pr == 0 and (qb > 0 or h > 0):
                            s.wait_ge(_amm, h * TB + qb)
                        off = 2048 if pr % 2 == 0 else 0
                        s.activation(probsT[:, pr * 1024:(pr + 1) * 1024],
                                     psum[:, off:off + 1024],
                                     AF.Exp, bias=negcb[:, 0:1],
                                     scale=1.0).then_inc(_sexp, 1)
            for tb in range(TB):
                b = LNS * tb
                s.wait_ge(_ln, b + 1)
                s.activation(xbf[:, DC * 512:2 * DC * 512], xbf[:, 0:DC * 512],
                             AF.Square).then_inc(_lna, 1)
                s.wait_ge(_ln, b + 5)
                s.activation(lrow[0:1, 1536:2048], lrow[0:1, 1024:1536],
                             AF.Ln).then_inc(_lna, 1)
                s.wait_ge(_lna, 3 * tb + 2)
                s.activation(lrow[0:1, 1536:2048], lrow[0:1, 1536:2048], AF.Exp,
                             bias=0.0, scale=-0.5).then_inc(_lna, 1)

        @block.vector
        def _(v):
            for tc in range(16):
                v.wait_ge(_ltr, tc + 1)
                pb = psum[:, 2048 + (tc % 2) * 512:2048 + (tc % 2) * 512 + 32].bitcast(BF16)
                v.tensor_copy(lattok[:, tc * 65:tc * 65 + 64],
                              pb[:, 0:64]).then_inc(_lte, 1)
            for i in range(4):
                v.wait_ge(_late, i + 1)
                v.tensor_tensor(sqs[0:64, i * 512:(i + 1) * 512],
                                latbf[0:64, i * 512:(i + 1) * 512],
                                latbf[0:64, i * 512:(i + 1) * 512],
                                ALU.mult).then_inc(_sq, 1)
            for i in range(4):
                v.wait_ge(_nmm, i + 1)
                v.tensor_copy(latn2[0:1, i * 512:(i + 1) * 512],
                              B2[0:1, :]).then_inc(_nev, 1)
            v.wait_ge(_nev, 4)
            v.tensor_reduce(csc[0:1, 1:2], latn2[0:1, :], AX,
                            ALU.max).then_inc(_lm, 1)
            for h in range(NH):
                v.wait_ge(_mmq, h * 5 + 1)
                v.tensor_copy(wqk[:, :], psum[:, 0:512]).then_inc(_ewq, 1)
                for i in range(4):
                    v.wait_ge(_mmq, h * 5 + 2 + i)
                    v.tensor_scalar(qhat[0:64, i * 512:(i + 1) * 512],
                                    psum[0:64, (i % 2) * 512:(i % 2) * 512 + 512],
                                    1.0 / 16.0, None, ALU.mult).then_inc(_eqh, 1)
                    v.wait_ge(_eqh, h * 4 + i + 1)
                    v.tensor_tensor(sqs[0:64, i * 512:(i + 1) * 512],
                                    qhat[0:64, i * 512:(i + 1) * 512],
                                    qhat[0:64, i * 512:(i + 1) * 512],
                                    ALU.mult).then_inc(_sq, 1)
                for i in range(4):
                    v.wait_ge(_nmm, 4 + h * 4 + i + 1)
                    v.tensor_copy(qn2[0:1, i * 512:(i + 1) * 512],
                                  B2[0:1, :]).then_inc(_nev, 1)
                v.wait_ge(_nev, 4 + h * 4 + 4)
                v.tensor_reduce(csc[0:1, 0:1], qn2[0:1, :], AX,
                                ALU.max).then_inc(_ch, 1)
                v.wait_ge(_ch, h * 4 + 1)
                v.wait_ge(_lm, 1)
                v.tensor_tensor(csc[0:1, 2:3], csc[0:1, 0:1], csc[0:1, 1:2],
                                ALU.mult).then_inc(_ch, 1)
                v.wait_ge(_cha, h * 2 + 2)
                v.tensor_scalar(csc[0:1, 5:6], csc[0:1, 4:5], -1.01, None,
                                ALU.mult).then_inc(_ch, 1)
                v.wait_ge(_chb, h + 1)
                if h > 0:
                    v.wait_ge(_sexp, h * 32)
                v.tensor_copy(negcb[:, 0:1], B3c[:, 0:1]).then_inc(_ch, 1)
                for qb in range(TB):
                    gq = h * TB + qb
                    v.wait_ge(_amm, gq + 1)
                    v.tensor_copy(obar[0:65, :],
                                  psum[0:65, 1024:1536]).then_inc(_oev, 1)
                    v.wait_ge(_otr, gq + 1)
                    if gq > 0:
                        v.wait_ge(_aev, gq * 4)
                    v.reciprocal(rcp[:, :],
                                 psum[:, 1540:1544].bitcast(BF16)[:, 0:8:2]
                                 ).then_inc(_arc, 1)
                    for qt in range(4):
                        aidx = gq * 4 + qt
                        v.wait_ge(_avm, aidx + 1)
                        v.wait_ge(_arc, gq + 1)
                        if aidx >= 4:
                            v.wait_ge(_asl[aidx % 4], 16 * (aidx // 4))
                        v.tensor_scalar(
                            aostage[:, (aidx % 4) * 256:(aidx % 4) * 256 + 256],
                            psum[:, 3072 + (aidx % 2) * 512:3072 + (aidx % 2) * 512 + 256],
                            rcp[:, qt:qt + 1], None, ALU.mult).then_inc(_aev, 1)
            v.wait_ge(_xs, 16 * DC)
            for H in range(2):
                for qb in range(TB):
                    for dc in range(DC):
                        pidx = H * 32 + qb * 8 + dc
                        v.wait_ge(_pmm, pidx + 1)
                        sl = stream[:, dc * T + qb * 512:dc * T + qb * 512 + 512]
                        v.tensor_tensor(sl, sl,
                                        psum[:, (pidx % 2) * 512:(pidx % 2) * 512 + 512],
                                        ALU.add).then_inc(_pev, 1)
            for tb in range(TB):
                b = LNS * tb
                v.wait_ge(_pev, 32 + 8 * (tb + 1))
                sv = stream[:, :].rearrange("p (c t) -> p c t", t=T)[:, :, tb * 512:tb * 512 + 512]
                v.tensor_copy(xbf[:, 0:DC * 512].rearrange("p (c t) -> p c t", t=512),
                              sv).then_inc(_ln, 1)
                v.wait_ge(_lnp, 2 * tb + 1)
                v.tensor_scalar(lrow[0:1, 0:512], psum[0:1, 1024:1536], 1.0 / D,
                                None, ALU.mult)
                v.tensor_scalar(lrow[0:1, 512:1024], psum[0:1, 1536:2048],
                                1.0 / D, None, ALU.mult).then_inc(_ln, 1)
                v.wait_ge(_ln, b + 2)
                v.tensor_tensor(lrow[0:1, 1024:1536], lrow[0:1, 0:512],
                                lrow[0:1, 0:512], ALU.mult).then_inc(_ln, 1)
                v.wait_ge(_ln, b + 3)
                v.tensor_tensor(lrow[0:1, 1024:1536], lrow[0:1, 512:1024],
                                lrow[0:1, 1024:1536], ALU.subtract).then_inc(_ln, 1)
                v.wait_ge(_ln, b + 4)
                v.tensor_scalar(lrow[0:1, 1024:1536], lrow[0:1, 1024:1536],
                                EPS_LN, None, ALU.add).then_inc(_ln, 1)
                v.wait_ge(_lna, 3 * tb + 3)
                v.tensor_tensor(lrow[0:1, 2048:2560], lrow[0:1, 0:512],
                                lrow[0:1, 1536:2048], ALU.mult).then_inc(_ln, 1)
                v.wait_ge(_lnp, 2 * tb + 2)
                for dcn in range(DC):
                    sl2 = stream[:, dcn * T + tb * 512:dcn * T + tb * 512 + 512]
                    v.tensor_tensor(sl2, sl2, psum[:, 3072:3584], ALU.mult).then_inc(_ln, 1)
                    v.wait_ge(_ln, b + 6 + 2 * dcn + 1)
                    v.tensor_tensor(sl2, sl2, psum[:, 3584:4096],
                                    ALU.subtract).then_inc(_ln, 1)
            v.wait_ge(_ln, TB * LNS)
            for dc in range(DC):
                v.tensor_reduce(maxcol[:, dc:dc + 1], stream[:, dc * T:(dc + 1) * T],
                                AX, ALU.max, apply_absolute_value=True).then_inc(_mx, 1)
            v.wait_ge(_mx, 8)
            v.tensor_reduce(mfold[:, 0:1], maxcol[:, 0:8], AX,
                            ALU.max).then_inc(_mx, 1)
            v.wait_ge(_mxp, 1)
            v.tensor_reduce(mxo[0:1, 0:1], psum[0:1, 2048:2176], AX,
                            ALU.max).then_inc(_mx, 1)

        @block.tensor
        def _(t_):
            t_.wait_ge(_cfg, 112)
            t_.wait_ge(_vb, 1)
            t_.wait_ge(_xb, 16 * DC)
            for tb in range(TB):
                if tb > 0:
                    t_.wait_ge(_late, tb)
                ins = None
                for dc in range(DC):
                    ins = t_.matmul(psum[0:64, 0:512], kvw[:, dc * LR:(dc + 1) * LR],
                                    xbf[:, dc * T + tb * 512:dc * T + tb * 512 + 512],
                                    start=(dc == 0), stop=(dc == DC - 1))
                ins.then_inc(_lat, 1)
            t_.wait_ge(_late, TB)
            for tc in range(16):
                if tc >= 2:
                    t_.wait_ge(_lte, tc - 1)
                pb = psum[:, 2048 + (tc % 2) * 512:2048 + (tc % 2) * 512 + 32].bitcast(BF16)
                t_.transpose(pb[:, 0:64], latbf[0:64, tc * 128:(tc + 1) * 128],
                             idb[0:64, 0:64]).then_inc(_ltr, 1)
            for i in range(4):
                t_.wait_ge(_sq, i + 1)
                if i > 0:
                    t_.wait_ge(_nev, i)
                t_.matmul(B2[:, :], onemb[0:64, :], sqs[0:64, i * 512:(i + 1) * 512],
                          start=True, stop=True).then_inc(_nmm, 1)
            for h in range(NH):
                t_.wait_ge(_wq0 if h % 2 == 0 else _wq1, 16 * (h // 2 + 1))
                t_.wait_ge(_eqh, h * 4)
                if h > 0:
                    t_.wait_ge(_ewq, h)
                ins = None
                for dc in range(DC):
                    for jj in range(2):
                        ins = t_.matmul(
                            psum[:, dc * 64:dc * 64 + 64],
                            wqn_sb[:, (h % 2) * 2 * D + jj * D + dc * 128:(h % 2) * 2 * D + jj * D + dc * 128 + 128],
                            wku_sb[:, (h * 2 + jj) * LR:(h * 2 + jj) * LR + LR],
                            start=(jj == 0), stop=(jj == 1))
                ins.then_inc(_mmq, 1)
                for i in range(4):
                    t_.wait_ge(_ewq, h + 1)
                    if i >= 2:
                        t_.wait_ge(_eqh, h * 4 + i - 1)
                    ins = None
                    for dc in range(DC):
                        ins = t_.matmul(
                            psum[0:64, (i % 2) * 512:(i % 2) * 512 + 512],
                            wqk[:, dc * LR:(dc + 1) * LR],
                            xbf[:, dc * T + i * 512:dc * T + i * 512 + 512],
                            start=(dc == 0), stop=(dc == DC - 1))
                    ins.then_inc(_mmq, 1)
                for i in range(4):
                    t_.wait_ge(_sq, 4 + h * 4 + i + 1)
                    t_.wait_ge(_nev, 4 + h * 4 + i)
                    t_.matmul(B2[:, :], onemb[0:64, :],
                              sqs[0:64, i * 512:(i + 1) * 512],
                              start=True, stop=True).then_inc(_nmm, 1)
                t_.wait_ge(_ch, h * 4 + 3)
                t_.matmul(B3c[:, 0:1], onesr[0:1, :], csc[0:1, 5:6],
                          start=True, stop=True).then_inc(_chb, 1)
                for qb in range(TB):
                    if qb == 0:
                        t_.wait_ge(_eqh, h * 4 + 4)
                    for kt in range(16):
                        sidx = (h * TB + qb) * 16 + kt
                        prg = (h * TB + qb) * 8 + kt // 2
                        if prg >= 2:
                            t_.wait_ge(_sexp, prg - 1)
                        off = (2048 if (kt // 2) % 2 == 0 else 0) + (kt % 2) * 512
                        t_.matmul(psum[:, off:off + 512],
                                  latbf[0:64, kt * 128:(kt + 1) * 128],
                                  qhat[0:64, qb * 512:(qb + 1) * 512],
                                  start=True, stop=True).then_inc(_smm, 1)
                    gq = h * TB + qb
                    t_.wait_ge(_sexp, gq * 8 + 8)
                    t_.wait_ge(_oev, gq)
                    if qb == 0:
                        t_.wait_ge(_nev, 4 + h * 4 + 4)
                    ins = None
                    for kc in range(16):
                        ins = t_.matmul(psum[0:65, 1024:1536],
                                        lattok[:, kc * 65:(kc + 1) * 65],
                                        probsT[:, kc * 512:(kc + 1) * 512],
                                        start=(kc == 0), stop=(kc == 15))
                    ins.then_inc(_amm, 1)
                    t_.wait_ge(_oev, gq + 1)
                    if gq > 0:
                        t_.wait_ge(_arc, gq)
                    ins = None
                    for qt in range(4):
                        pb2 = psum[:, 1540 + qt:1541 + qt].bitcast(BF16)
                        ins = t_.transpose(pb2[:, 0:1],
                                           obar[64:65, qt * 128:(qt + 1) * 128],
                                           idb[64:65, 64:65])
                    ins.then_inc(_otr, 1)
                    for qt in range(4):
                        aidx = gq * 4 + qt
                        if aidx >= 2:
                            t_.wait_ge(_aev, aidx - 1)
                        ins = t_.matmul(
                            psum[:, 3072 + (aidx % 2) * 512:3072 + (aidx % 2) * 512 + 256],
                            obar[0:64, qt * 128:(qt + 1) * 128],
                            vup[0:64, h * 256:(h + 1) * 256],
                            start=True, stop=True)
                        ins.then_inc(_avm, 1)
            for H in range(2):
                t_.wait_ge(_wo0 if H == 0 else _wo1, 16 * 8)
                for qb in range(TB):
                    g8 = H * 4 + qb
                    t_.wait_ge(_ao0 if g8 % 2 == 0 else _ao1, 16 * 8 * (g8 // 2 + 1))
                    for dc in range(DC):
                        pidx = H * 32 + qb * 8 + dc
                        if pidx >= 2:
                            t_.wait_ge(_pev, pidx - 1)
                        ins = None
                        for c8 in range(8):
                            ins = t_.matmul(
                                psum[:, (pidx % 2) * 512:(pidx % 2) * 512 + 512],
                                wo_sb[:, c8 * 1024 + dc * 128:c8 * 1024 + dc * 128 + 128],
                                aoT[:, (g8 % 2) * 16 * 512 + c8 * 512:(g8 % 2) * 16 * 512 + c8 * 512 + 512],
                                start=(c8 == 0), stop=(c8 == 7))
                        ins.then_inc(_pmm, 1)
            for tb in range(TB):
                b = LNS * tb
                t_.wait_ge(_ln, b + 1)
                t_.wait_ge(_lna, 3 * tb + 1)
                for dcn in range(DC):
                    t_.matmul(psum[:, 1024:1536], onemb[:, :],
                              xbf[:, dcn * 512:(dcn + 1) * 512],
                              start=(dcn == 0), stop=(dcn == DC - 1))
                ins = None
                for dcn in range(DC):
                    ins = t_.matmul(psum[:, 1536:2048], onemb[:, :],
                                    xbf[:, DC * 512 + dcn * 512:DC * 512 + (dcn + 1) * 512],
                                    start=(dcn == 0), stop=(dcn == DC - 1))
                ins.then_inc(_lnp, 1)
                t_.wait_ge(_ln, b + 6)
                t_.matmul(psum[:, 3072:3584], onesr[0:1, :], lrow[0:1, 1536:2048],
                          start=True, stop=True)
                ins = t_.matmul(psum[:, 3584:4096], onesr[0:1, :],
                                lrow[0:1, 2048:2560], start=True, stop=True)
                ins.then_inc(_lnp, 1)
            t_.wait_ge(_mx, 9)
            t_.transpose(psum[0:1, 2048:2176], mfold[0:128, 0:1],
                         id128[:, :]).then_inc(_mxp, 1)
    return nc


_NC_CACHE = {}


def _nc(name):
    if name not in _NC_CACHE:
        _NC_CACHE[name] = globals()["build_" + name]()
    return _NC_CACHE[name]


def _to_fm(a):
    return np.ascontiguousarray(
        a.T.reshape(DC, 128, T).transpose(1, 0, 2).reshape(128, DC * T))


def _from_fm(a):
    return a.reshape(128, DC, T).transpose(1, 0, 2).reshape(D, T).T


def kernel(L0_attn_x, L0_attn_Wq, L0_attn_Wkv_down, L0_attn_Wk_up,
           L0_attn_Wv_up, L0_attn_Wo, L0_ffn_w_gate, L0_ffn_w_up,
           L0_ffn_w_down):
    cores = list(range(8))
    x = np.ascontiguousarray(np.asarray(L0_attn_x, np.float32))
    w_up = np.ascontiguousarray(np.asarray(L0_ffn_w_up, np.float32))
    w_gate = np.ascontiguousarray(np.asarray(L0_ffn_w_gate, np.float32))
    w_down = np.ascontiguousarray(np.asarray(L0_ffn_w_down, np.float32))

    in0 = [{"wu": np.ascontiguousarray(w_up[c * 384:(c + 1) * 384]),
            "wg": np.ascontiguousarray(w_gate[c * 384:(c + 1) * 384]),
            "wd": np.ascontiguousarray(w_down[c * 128:(c + 1) * 128])}
           for c in cores]
    r0 = run_bass_kernel_spmd(_nc("l0"), in0, core_ids=cores).results
    wupdq = np.concatenate([np.asarray(r0[c]["wuo"]) for c in cores], 0)
    wgatedq = np.concatenate([np.asarray(r0[c]["wgo"]) for c in cores], 0)
    wdowndq = np.concatenate([np.asarray(r0[c]["wdo"]) for c in cores], 0)

    wqn = np.ascontiguousarray(np.asarray(L0_attn_Wq, np.float32))
    wkvT = np.ascontiguousarray(np.asarray(L0_attn_Wkv_down, np.float32).T)
    wku = np.ascontiguousarray(np.asarray(L0_attn_Wk_up, np.float32))
    wvuT = np.ascontiguousarray(np.asarray(L0_attn_Wv_up, np.float32).T)
    woT = np.ascontiguousarray(np.asarray(L0_attn_Wo, np.float32).T)
    in1 = [{"xT": _to_fm(x[c]), "wqn": wqn, "wkvT": wkvT, "wku": wku,
            "wvuT": wvuT, "woT": woT} for c in cores]
    r1 = run_bass_kernel_spmd(_nc("l1"), in1, core_ids=cores).results
    streams = [np.asarray(r1[c]["streamout"]) for c in cores]
    maxx8 = np.concatenate([np.asarray(r1[c]["maxx"]) for c in cores],
                           axis=1)

    in2 = [{"stream": streams[c], "maxx": maxx8, "wup": wupdq,
            "wgate": wgatedq} for c in cores]
    r2 = run_bass_kernel_spmd(_nc("l2"), in2, core_ids=cores).results
    maxug8 = np.concatenate(
        [np.asarray(r2[c]["maxug"])[0:1, :] for c in cores] +
        [np.asarray(r2[c]["maxug"])[1:2, :] for c in cores], axis=0).reshape(1, 16)

    in3 = [{"gatein": np.asarray(r2[c]["gateout"]), "maxug8": maxug8}
           for c in cores]
    r3 = run_bass_kernel_spmd(_nc("l3"), in3, core_ids=cores).results
    maxg28 = np.concatenate([np.asarray(r3[c]["maxg2"]) for c in cores],
                            axis=1)

    in4 = [{"g2in": np.asarray(r3[c]["g2out"]),
            "upin": np.asarray(r2[c]["upout"]),
            "maxg28": maxg28, "maxug8": maxug8} for c in cores]
    r4 = run_bass_kernel_spmd(_nc("l4"), in4, core_ids=cores).results
    maxo8 = np.concatenate([np.asarray(r4[c]["maxo"]) for c in cores], axis=1)

    maxo8r = np.ascontiguousarray(np.broadcast_to(maxo8, (128, 8)))
    wP = np.ascontiguousarray(
        wdowndq.reshape(8, 128, 24, 128).transpose(3, 0, 2, 1).reshape(128, 24576))
    in5 = [{"oin": np.asarray(r4[c]["oout"]), "maxo8": maxo8r,
            "streamin": streams[c], "wdown": wP} for c in cores]
    r5 = run_bass_kernel_spmd(_nc("l5"), in5, core_ids=cores).results

    out = np.empty((8, T, D), np.float32)
    for c in cores:
        out[c] = _from_fm(np.asarray(r5[c]["final"]))
    return out


# version 08
# speedup vs baseline: 1.3020x; 1.0905x over previous
from contextlib import ExitStack

import numpy as np
import concourse.bass as bass
import concourse.mybir as mybir
from concourse.bass_utils import run_bass_kernel_spmd

F32 = mybir.dt.float32
BF16 = mybir.dt.bfloat16
I16 = mybir.dt.int16
U16 = mybir.dt.uint16
I8 = mybir.dt.int8
AF = mybir.ActivationFunctionType
ALU = mybir.AluOpType
AX = mybir.AxisListType.X

T = 2048
D = 1024
NH = 8
DH = 256
LR = 64
FF = 3072
DC = D // 128
FC = FF // 128
TB = T // 512

QMAX = 32767.0
WQMAX = 7.0
EPS_Q = 1e-12
EPS_LN = 1e-5


class Ctr:

    def __init__(self, sem):
        self.sem = sem
        self.v = 0

    def inc(self, n=1):
        self.v += n
        return self.v



def build_l2():
    nc = bass.Bass("TRN2", target_bir_lowering=False, debug=False)
    stream = nc.dram_tensor("stream", [128, DC * T], F32, kind="ExternalInput")
    maxx = nc.dram_tensor("maxx", [1, 8], F32, kind="ExternalInput")
    wup = nc.dram_tensor("wup", [FF, D], BF16, kind="ExternalInput")
    wgate = nc.dram_tensor("wgate", [FF, D], BF16, kind="ExternalInput")
    upout = nc.dram_tensor("upout", [FF, T], BF16, kind="ExternalOutput")
    gateout = nc.dram_tensor("gateout", [FF, T], BF16, kind="ExternalOutput")
    maxug = nc.dram_tensor("maxug", [2, 1], F32, kind="ExternalOutput")
    ones_f32 = nc.inline_tensor(np.ones((1, 128), np.float32), name="ones_f32")
    ident128 = nc.inline_tensor(np.eye(128, dtype=np.float32), name="ident128")

    NT = 2 * FC * TB

    with ExitStack() as es:
        sb = lambda *a: es.enter_context(nc.sbuf_tensor(*a))
        ps = lambda *a: es.enter_context(nc.psum_tensor(*a))
        sem = lambda n: es.enter_context(nc.semaphore(n))
        xq = sb("xq", [128, DC * T], BF16)
        sstage = sb("sstage", [128, 2 * T], F32)
        i16s = sb("i16s", [128, T], I16)
        warena = sb("warena", [128, 2 * DC * 512], BF16)
        stage = sb("stage", [128, 4 * 512], BF16)
        maxcol = sb("maxcol", [128, NT], F32)
        maxin = sb("maxin", [1, 8], F32)
        scal = sb("scal", [1, 2], F32)
        sxb = sb("sxb", [128, 2], F32)
        onesr = sb("onesr", [1, 128], F32)
        id128 = sb("id128", [128, 128], F32)
        mfold = sb("mfold", [128, 2], F32)
        mout = sb("mout", [2, 1], F32)
        ps_mm = ps("ps_mm", [128, 4 * 512], F32)
        ps_bc = ps("ps_bc", [128, 2], F32)
        ps_tr = ps("ps_tr", [2, 128], F32)
        _cfg, _scal, _bc, _q = (sem(n) for n in
                                ("s_cfg", "s_scal", "s_bc", "s_q"))
        _i0, _i1 = sem("s_i0"), sem("s_i1")
        _w0, _w1 = sem("s_w0"), sem("s_w1")
        _mm, _ev, _red = (sem(n) for n in ("s_mm", "s_ev", "s_red"))
        _stl = [sem(f"s_st{k}") for k in range(4)]
        _stf = sem("s_stf")
        block = es.enter_context(nc.Block())
        @block.gpsimd
        def _(g):
            g.dma_start(out=maxin[:, :], in_=maxx[:, :]).then_inc(_cfg, 16)
            g.dma_start(out=onesr[:, :], in_=ones_f32[:, :]).then_inc(_cfg, 16)
            g.dma_start(out=id128[:, :], in_=ident128[:, :]).then_inc(_cfg, 16)
            for c in range(DC):
                if c >= 2:
                    g.wait_ge(_q, 2 * (c - 1))
                g.dma_start(out=sstage[:, (c % 2) * T:(c % 2) * T + T],
                            in_=stream[:, c * T:(c + 1) * T]
                            ).then_inc(_i0 if c % 2 == 0 else _i1, 16)
            for k in range(4):
                g.wait_ge(_stl[k], 16 * (NT // 4))
            g.wait_ge(_stf, 16)

        @block.sync
        def _(sy):
            for gidx in range(12):
                ub, W = gidx // 2, gidx % 2
                wt = wup if W == 0 else wgate
                half = gidx % 2
                if gidx >= 2:
                    sy.wait_ge(_mm, 16 * (gidx - 1))
                for dc in range(DC):
                    base = half * DC * 512 + dc * 512
                    sy.dma_start(
                        out=warena[:, base:base + 512],
                        in_=wt[ub * 512:(ub + 1) * 512, dc * 128:(dc + 1) * 128],
                        transpose=True,
                    ).then_inc(_w0 if half == 0 else _w1, 16)

        @block.vector
        def _(v):
            v.wait_ge(_cfg, 48)
            v.tensor_reduce(scal[0:1, 0:1], maxin[0:1, 0:8], AX,
                            ALU.max).then_inc(_scal, 1)
            v.wait_ge(_scal, 1)
            v.tensor_scalar(scal[0:1, 0:1], scal[0:1, 0:1], 1.0 / QMAX, EPS_Q,
                            ALU.mult, ALU.max).then_inc(_scal, 1)
            v.wait_ge(_scal, 2)
            v.reciprocal(scal[0:1, 1:2], scal[0:1, 0:1]).then_inc(_scal, 1)
            v.wait_ge(_bc, 1)
            v.tensor_copy(sxb[:, :], ps_bc[:, :]).then_inc(_scal, 1)
            v.wait_ge(_scal, 4)
            for c in range(DC):
                v.wait_ge(_i0 if c % 2 == 0 else _i1, 16 * (c // 2 + 1))
                if c > 0:
                    v.wait_ge(_q, 2 * c)
                st_sl = sstage[:, (c % 2) * T:(c % 2) * T + T]
                v.tensor_scalar(i16s[:, :], st_sl, sxb[:, 1:2], None,
                                ALU.mult).then_inc(_q, 1)
                v.wait_ge(_q, 2 * c + 1)
                v.tensor_scalar(xq[:, c * T:(c + 1) * T], i16s[:, :],
                                sxb[:, 0:1], None, ALU.mult).then_inc(_q, 1)
            for t in range(NT):
                v.wait_ge(_ev, t + 1)
                gidx, sub = t // 16, t % 16
                col = (gidx % 2) * (NT // 2) + (gidx // 2) * 16 + sub
                v.tensor_reduce(maxcol[:, col:col + 1],
                                ps_mm[:, (t % 4) * 512:(t % 4) * 512 + 512],
                                AX, ALU.max,
                                apply_absolute_value=True).then_inc(_red, 1)
            v.wait_ge(_red, NT)
            v.tensor_reduce(mfold[:, 0:1], maxcol[:, 0:NT // 2], AX, ALU.max)
            v.tensor_reduce(mfold[:, 1:2], maxcol[:, NT // 2:NT], AX,
                            ALU.max).then_inc(_red, 1)
            v.wait_ge(_bc, 2)
            v.tensor_reduce(mout[0:2, 0:1], ps_tr[0:2, 0:128], AX,
                            ALU.max).then_inc(_red, 1)

        @block.tensor
        def _(t_):
            t_.wait_ge(_cfg, 48)
            t_.wait_ge(_scal, 3)
            t_.matmul(ps_bc[:, :], onesr[0:1, :], scal[0:1, 0:2], start=True,
                      stop=True).then_inc(_bc, 1)
            t_.wait_ge(_q, 2 * DC)
            for t in range(NT):
                gidx, sub = t // 16, t % 16
                half = gidx % 2
                tb, uc4 = sub // 4, sub % 4
                if sub == 0:
                    t_.wait_ge(_w0 if half == 0 else _w1,
                               16 * DC * (gidx // 2 + 1))
                if t >= 4:
                    t_.wait_ge(_ev, t - 3)
                    t_.wait_ge(_red, t - 3)
                ins = None
                for dc in range(DC):
                    base = half * DC * 512 + dc * 512 + uc4 * 128
                    ins = t_.matmul(
                        ps_mm[:, (t % 4) * 512:(t % 4) * 512 + 512],
                        warena[:, base:base + 128],
                        xq[:, dc * T + tb * 512:dc * T + tb * 512 + 512],
                        start=(dc == 0), stop=(dc == DC - 1),
                    )
                ins.then_inc(_mm, 1)
            t_.wait_ge(_red, NT + 1)
            t_.transpose(ps_tr[0:2, 0:128], mfold[0:128, 0:2],
                         id128[:, :]).then_inc(_bc, 1)

        @block.scalar
        def _(s):
            for t in range(NT):
                s.wait_ge(_mm, t + 1)
                if t >= 4:
                    s.wait_ge(_stl[t % 4], 16 * (t // 4))
                s.activation(stage[:, (t % 4) * 512:(t % 4) * 512 + 512],
                             ps_mm[:, (t % 4) * 512:(t % 4) * 512 + 512],
                             AF.Copy).then_inc(_ev, 1)
                s.wait_ge(_ev, t + 1)
                gidx, sub = t // 16, t % 16
                ub, W = gidx // 2, gidx % 2
                tb, uc4 = sub // 4, sub % 4
                uc = ub * 4 + uc4
                dst = upout if W == 0 else gateout
                s.dma_start(
                    out=dst[uc * 128:(uc + 1) * 128, tb * 512:(tb + 1) * 512],
                    in_=stage[:, (t % 4) * 512:(t % 4) * 512 + 512],
                ).then_inc(_stl[t % 4], 16)
            s.wait_ge(_red, NT + 2)
            s.dma_start(out=maxug[0:2, 0:1], in_=mout[0:2, 0:1]).then_inc(_stf, 16)

    return nc



def build_l0():
    nc = bass.Bass("TRN2", target_bir_lowering=False, debug=False)
    wu = nc.dram_tensor("wu", [384, D], F32, kind="ExternalInput")
    wg = nc.dram_tensor("wg", [384, D], F32, kind="ExternalInput")
    wd = nc.dram_tensor("wd", [128, FF], F32, kind="ExternalInput")
    wuo = nc.dram_tensor("wuo", [384, D], BF16, kind="ExternalOutput")
    wgo = nc.dram_tensor("wgo", [384, D], BF16, kind="ExternalOutput")
    wdo = nc.dram_tensor("wdo", [128, FF], BF16, kind="ExternalOutput")
    tiles = [(wu, wuo, 0, D), (wu, wuo, 1, D), (wu, wuo, 2, D),
             (wg, wgo, 0, D), (wg, wgo, 1, D), (wg, wgo, 2, D),
             (wd, wdo, 0, FF)]
    with ExitStack() as es:
        sb = lambda *a: es.enter_context(nc.sbuf_tensor(*a))
        sem = lambda n: es.enter_context(nc.semaphore(n))
        wstage = sb("wstage", [128, 2 * FF], F32)
        t32 = sb("t32", [128, FF], F32)
        q8 = sb("q8", [128, FF], I8)
        sraw = sb("sraw", [128, FF // 32], F32)
        sinv = sb("sinv", [128, FF // 32], F32)
        ostage = sb("ostage", [128, 2 * FF], BF16)
        _i0, _i1 = sem("s_i0"), sem("s_i1")
        _c = sem("s_c")
        _o0, _o1 = sem("s_o0"), sem("s_o1")
        block = es.enter_context(nc.Block())

        @block.gpsimd
        def _(g):
            for k, (src_t, _, r, I) in enumerate(tiles):
                par, cnt = (_i0, k // 2 + 1) if k % 2 == 0 else (_i1, k // 2 + 1)
                if k >= 2:
                    g.wait_ge(_c, 6 * (k - 1))
                g.dma_start(out=wstage[:, (k % 2) * FF:(k % 2) * FF + I],
                            in_=src_t[r * 128:(r + 1) * 128, 0:I]).then_inc(par, 16)
            g.wait_ge(_o0, 16 * 4)
            g.wait_ge(_o1, 16 * 3)

        @block.vector
        def _(v):
            for k, (_, _, r, I) in enumerate(tiles):
                par, cnt = (_i0, 16 * (k // 2 + 1)) if k % 2 == 0 else (_i1, 16 * (k // 2 + 1))
                v.wait_ge(par, cnt)
                if k >= 2:
                    spar, scnt = (_o0, 16 * (k // 2)) if k % 2 == 0 else (_o1, 16 * (k // 2))
                    v.wait_ge(spar, scnt)
                if k > 0:
                    v.wait_ge(_c, 6 * k)
                w_sl = wstage[:, (k % 2) * FF:(k % 2) * FF + I]
                nb = I // 32
                ww = w_sl.rearrange("p (b w) -> p b w", w=32)
                v.tensor_reduce(sraw[:, 0:nb], ww, AX, ALU.max,
                                apply_absolute_value=True).then_inc(_c, 1)
                v.wait_ge(_c, 6 * k + 1)
                v.tensor_scalar(sraw[:, 0:nb], sraw[:, 0:nb], 1.0 / WQMAX, EPS_Q,
                                ALU.mult, ALU.max).then_inc(_c, 1)
                v.wait_ge(_c, 6 * k + 2)
                v.reciprocal(sinv[:, 0:nb], sraw[:, 0:nb]).then_inc(_c, 1)
                v.wait_ge(_c, 6 * k + 3)
                sb_b = sraw[:, 0:nb].unsqueeze(2).broadcast_to([128, nb, 32])
                si_b = sinv[:, 0:nb].unsqueeze(2).broadcast_to([128, nb, 32])
                tw = t32[:, 0:I].rearrange("p (b w) -> p b w", w=32)
                v.tensor_tensor(tw, ww, si_b, ALU.mult).then_inc(_c, 1)
                v.wait_ge(_c, 6 * k + 4)
                v.tensor_copy(q8[:, 0:I], t32[:, 0:I]).then_inc(_c, 1)
                v.wait_ge(_c, 6 * k + 5)
                o_sl = ostage[:, (k % 2) * FF:(k % 2) * FF + I]
                ow = o_sl.rearrange("p (b w) -> p b w", w=32)
                qw = q8[:, 0:I].rearrange("p (b w) -> p b w", w=32)
                v.tensor_tensor(ow, qw, sb_b, ALU.mult).then_inc(_c, 1)

        @block.sync
        def _(sy):
            for k, (_, dst_t, r, I) in enumerate(tiles):
                sy.wait_ge(_c, 6 * (k + 1))
                par = _o0 if k % 2 == 0 else _o1
                sy.dma_start(out=dst_t[r * 128:(r + 1) * 128, 0:I],
                             in_=ostage[:, (k % 2) * FF:(k % 2) * FF + I]
                             ).then_inc(par, 16)
    return nc



def build_l3():
    nc = bass.Bass("TRN2", target_bir_lowering=False, debug=False)
    gatein = nc.dram_tensor("gatein", [FF, T], BF16, kind="ExternalInput")
    maxug8 = nc.dram_tensor("maxug8", [1, 16], F32, kind="ExternalInput")
    g2out = nc.dram_tensor("g2out", [FF, T], BF16, kind="ExternalOutput")
    maxg2 = nc.dram_tensor("maxg2", [1, 1], F32, kind="ExternalOutput")
    ones_f32 = nc.inline_tensor(np.ones((1, 128), np.float32), name="ones_f32")
    ident128 = nc.inline_tensor(np.eye(128, dtype=np.float32), name="ident128")
    with ExitStack() as es:
        sb = lambda *a: es.enter_context(nc.sbuf_tensor(*a))
        ps = lambda *a: es.enter_context(nc.psum_tensor(*a))
        sem = lambda n: es.enter_context(nc.semaphore(n))
        gstage = sb("gstage", [128, 2 * T], BF16)
        i16g = sb("i16g", [128, T], I16)
        sig = sb("sig", [128, T], F32)
        u16s = sb("u16s", [128, T], U16)
        g2f = sb("g2f", [128, T], F32)
        ostage = sb("ostage", [128, 2 * T], BF16)
        maxin = sb("maxin", [1, 16], F32)
        scal = sb("scal", [1, 4], F32)
        scb = sb("scb", [128, 4], F32)
        onesr = sb("onesr", [1, 128], F32)
        id128 = sb("id128", [128, 128], F32)
        maxcol = sb("maxcol", [128, FC], F32)
        mfold = sb("mfold", [128, 1], F32)
        mo = sb("mo", [1, 1], F32)
        ps_bc = ps("ps_bc", [128, 4], F32)
        ps_tr = ps("ps_tr", [1, 128], F32)
        _cfg, _scal, _bc = sem("s_cfg"), sem("s_scal"), sem("s_bc")
        _i0, _i1 = sem("s_i0"), sem("s_i1")
        _vq = sem("s_vq")
        _sg = sem("s_sg")
        _red = sem("s_red")
        _o0, _o1 = sem("s_o0"), sem("s_o1")
        _stf = sem("s_stf")
        block = es.enter_context(nc.Block())

        @block.gpsimd
        def _(g):
            g.dma_start(out=maxin[:, :], in_=maxug8[:, :]).then_inc(_cfg, 16)
            g.dma_start(out=onesr[:, :], in_=ones_f32[:, :]).then_inc(_cfg, 16)
            g.dma_start(out=id128[:, :], in_=ident128[:, :]).then_inc(_cfg, 16)
            for c in range(FC):
                if c >= 2:
                    g.wait_ge(_vq, 5 * (c - 1))
                g.dma_start(out=gstage[:, (c % 2) * T:(c % 2) * T + T],
                            in_=gatein[c * 128:(c + 1) * 128, :]
                            ).then_inc(_i0 if c % 2 == 0 else _i1, 16)
            g.wait_ge(_o0, 16 * (FC // 2))
            g.wait_ge(_o1, 16 * (FC // 2))
            g.wait_ge(_stf, 16)

        @block.vector
        def _(v):
            v.wait_ge(_cfg, 48)
            v.tensor_reduce(scal[0:1, 0:1], maxin[0:1, 8:16], AX,
                            ALU.max).then_inc(_scal, 1)
            v.wait_ge(_scal, 1)
            v.tensor_scalar(scal[0:1, 0:1], scal[0:1, 0:1], 1.0 / QMAX, EPS_Q,
                            ALU.mult, ALU.max).then_inc(_scal, 1)
            v.wait_ge(_scal, 2)
            v.reciprocal(scal[0:1, 1:2], scal[0:1, 0:1]).then_inc(_scal, 1)
            v.wait_ge(_scal, 3)
            v.tensor_scalar(scal[0:1, 2:3], scal[0:1, 0:1], 1.0 / 65536.0, None,
                            ALU.mult).then_inc(_scal, 1)
            v.wait_ge(_bc, 1)
            v.tensor_copy(scb[:, :], ps_bc[:, :]).then_inc(_scal, 1)
            v.wait_ge(_scal, 5)
            for c in range(FC):
                v.wait_ge(_i0 if c % 2 == 0 else _i1, 16 * (c // 2 + 1))
                if c >= 2:
                    v.wait_ge(_o0 if c % 2 == 0 else _o1, 16 * (c // 2))
                if c > 0:
                    v.wait_ge(_red, c)
                g_sl = gstage[:, (c % 2) * T:(c % 2) * T + T]
                v.tensor_scalar(i16g[:, :], g_sl, scb[:, 1:2], None,
                                ALU.mult).then_inc(_vq, 1)
                v.wait_ge(_sg, c + 1)
                v.tensor_scalar(u16s[:, :], sig[:, :], 65536.0, None,
                                ALU.mult).then_inc(_vq, 1)
                v.wait_ge(_vq, 5 * c + 2)
                v.scalar_tensor_tensor(g2f[:, :], i16g[:, :], scb[:, 2:3],
                                       u16s[:, :], ALU.mult,
                                       ALU.mult).then_inc(_vq, 1)
                v.wait_ge(_vq, 5 * c + 3)
                v.tensor_reduce(maxcol[:, c:c + 1], g2f[:, :], AX, ALU.max,
                                apply_absolute_value=True).then_inc(_red, 1)
                o_sl = ostage[:, (c % 2) * T:(c % 2) * T + T]
                v.tensor_copy(o_sl, g2f[:, :]).then_inc(_vq, 2)
            v.wait_ge(_red, FC)
            v.tensor_reduce(mfold[:, 0:1], maxcol[:, :], AX,
                            ALU.max).then_inc(_red, 1)
            v.wait_ge(_bc, 2)
            v.tensor_reduce(mo[0:1, 0:1], ps_tr[0:1, 0:128], AX,
                            ALU.max).then_inc(_red, 1)

        @block.scalar
        def _(s):
            s.wait_ge(_scal, 5)
            for c in range(FC):
                s.wait_ge(_vq, 5 * c + 1)
                s.activation(sig[:, :], i16g[:, :], AF.Sigmoid, bias=0.0,
                             scale=scb[:, 0:1]).then_inc(_sg, 1)
                if c < FC - 1:
                    s.wait_ge(_vq, 5 * c + 2)

        @block.tensor
        def _(t_):
            t_.wait_ge(_cfg, 48)
            t_.wait_ge(_scal, 4)
            t_.matmul(ps_bc[:, :], onesr[0:1, :], scal[0:1, 0:4], start=True,
                      stop=True).then_inc(_bc, 1)
            t_.wait_ge(_red, FC + 1)
            t_.transpose(ps_tr[0:1, 0:128], mfold[0:128, 0:1],
                         id128[:, :]).then_inc(_bc, 1)

        @block.sync
        def _(sy):
            for c in range(FC):
                sy.wait_ge(_vq, 5 * c + 5)
                sy.dma_start(out=g2out[c * 128:(c + 1) * 128, :],
                             in_=ostage[:, (c % 2) * T:(c % 2) * T + T]
                             ).then_inc(_o0 if c % 2 == 0 else _o1, 16)
            sy.wait_ge(_red, FC + 2)
            sy.dma_start(out=maxg2[0:1, 0:1], in_=mo[0:1, 0:1]).then_inc(_stf, 16)
    return nc



def build_l4():
    nc = bass.Bass("TRN2", target_bir_lowering=False, debug=False)
    g2in = nc.dram_tensor("g2in", [FF, T], BF16, kind="ExternalInput")
    upin = nc.dram_tensor("upin", [FF, T], BF16, kind="ExternalInput")
    maxg28 = nc.dram_tensor("maxg28", [1, 8], F32, kind="ExternalInput")
    maxug8 = nc.dram_tensor("maxug8", [1, 16], F32, kind="ExternalInput")
    oout = nc.dram_tensor("oout", [FF, T], BF16, kind="ExternalOutput")
    maxo = nc.dram_tensor("maxo", [1, 1], F32, kind="ExternalOutput")
    ones_f32 = nc.inline_tensor(np.ones((1, 128), np.float32), name="ones_f32")
    ident128 = nc.inline_tensor(np.eye(128, dtype=np.float32), name="ident128")
    with ExitStack() as es:
        sb = lambda *a: es.enter_context(nc.sbuf_tensor(*a))
        ps = lambda *a: es.enter_context(nc.psum_tensor(*a))
        sem = lambda n: es.enter_context(nc.semaphore(n))
        gstage = sb("gstage", [128, 2 * T], BF16)
        ustage = sb("ustage", [128, 2 * T], BF16)
        i16g = sb("i16g", [128, T], I16)
        i16u = sb("i16u", [128, T], I16)
        of = sb("of", [128, T], F32)
        ostage = sb("ostage", [128, 2 * T], BF16)
        maxing = sb("maxing", [1, 8], F32)
        maxinu = sb("maxinu", [1, 16], F32)
        scal = sb("scal", [1, 4], F32)
        scb = sb("scb", [128, 4], F32)
        onesr = sb("onesr", [1, 128], F32)
        id128 = sb("id128", [128, 128], F32)
        maxcol = sb("maxcol", [128, FC], F32)
        mfold = sb("mfold", [128, 1], F32)
        mo = sb("mo", [1, 1], F32)
        ps_bc = ps("ps_bc", [128, 4], F32)
        ps_tr = ps("ps_tr", [1, 128], F32)
        _cfg, _scal, _bc = sem("s_cfg"), sem("s_scal"), sem("s_bc")
        _ig0, _ig1 = sem("s_ig0"), sem("s_ig1")
        _iu0, _iu1 = sem("s_iu0"), sem("s_iu1")
        _vq = sem("s_vq")
        _red = sem("s_red")
        _o0, _o1 = sem("s_o0"), sem("s_o1")
        _stf = sem("s_stf")
        block = es.enter_context(nc.Block())

        @block.gpsimd
        def _(g):
            g.dma_start(out=maxing[:, :], in_=maxg28[:, :]).then_inc(_cfg, 16)
            g.dma_start(out=maxinu[:, :], in_=maxug8[:, :]).then_inc(_cfg, 16)
            g.dma_start(out=onesr[:, :], in_=ones_f32[:, :]).then_inc(_cfg, 16)
            g.dma_start(out=id128[:, :], in_=ident128[:, :]).then_inc(_cfg, 16)
            for c in range(FC):
                if c >= 2:
                    g.wait_ge(_vq, 5 * (c - 1))
                g.dma_start(out=gstage[:, (c % 2) * T:(c % 2) * T + T],
                            in_=g2in[c * 128:(c + 1) * 128, :]
                            ).then_inc(_ig0 if c % 2 == 0 else _ig1, 16)
            g.wait_ge(_o0, 16 * (FC // 2))
            g.wait_ge(_o1, 16 * (FC // 2))
            g.wait_ge(_stf, 16)

        @block.sync
        def _(sy):
            for c in range(FC):
                if c >= 2:
                    sy.wait_ge(_vq, 5 * (c - 1))
                sy.dma_start(out=ustage[:, (c % 2) * T:(c % 2) * T + T],
                             in_=upin[c * 128:(c + 1) * 128, :]
                             ).then_inc(_iu0 if c % 2 == 0 else _iu1, 16)

        @block.scalar
        def _(s):
            for c in range(FC):
                s.wait_ge(_vq, 5 * c + 5)
                s.dma_start(out=oout[c * 128:(c + 1) * 128, :],
                            in_=ostage[:, (c % 2) * T:(c % 2) * T + T]
                            ).then_inc(_o0 if c % 2 == 0 else _o1, 16)
            s.wait_ge(_red, FC + 2)
            s.dma_start(out=maxo[0:1, 0:1], in_=mo[0:1, 0:1]).then_inc(_stf, 16)

        @block.vector
        def _(v):
            v.wait_ge(_cfg, 64)
            v.tensor_reduce(scal[0:1, 0:1], maxing[0:1, 0:8], AX,
                            ALU.max).then_inc(_scal, 1)
            v.tensor_reduce(scal[0:1, 1:2], maxinu[0:1, 0:8], AX,
                            ALU.max).then_inc(_scal, 1)
            v.wait_ge(_scal, 2)
            v.tensor_scalar(scal[0:1, 0:1], scal[0:1, 0:1], 1.0 / QMAX, EPS_Q,
                            ALU.mult, ALU.max)
            v.tensor_scalar(scal[0:1, 1:2], scal[0:1, 1:2], 1.0 / QMAX, EPS_Q,
                            ALU.mult, ALU.max).then_inc(_scal, 1)
            v.wait_ge(_scal, 3)
            v.tensor_tensor(scal[0:1, 2:3], scal[0:1, 0:1], scal[0:1, 1:2],
                            ALU.mult)
            v.reciprocal(scal[0:1, 3:4], scal[0:1, 1:2]).then_inc(_scal, 1)
            v.wait_ge(_scal, 4)
            v.reciprocal(scal[0:1, 0:1], scal[0:1, 0:1]).then_inc(_scal, 1)
            v.wait_ge(_bc, 1)
            v.tensor_copy(scb[:, :], ps_bc[:, :]).then_inc(_scal, 1)
            v.wait_ge(_scal, 6)
            for c in range(FC):
                v.wait_ge(_ig0 if c % 2 == 0 else _ig1, 16 * (c // 2 + 1))
                v.wait_ge(_iu0 if c % 2 == 0 else _iu1, 16 * (c // 2 + 1))
                if c >= 2:
                    v.wait_ge(_o0 if c % 2 == 0 else _o1, 16 * (c // 2))
                if c > 0:
                    v.wait_ge(_red, c)
                g_sl = gstage[:, (c % 2) * T:(c % 2) * T + T]
                u_sl = ustage[:, (c % 2) * T:(c % 2) * T + T]
                v.tensor_scalar(i16g[:, :], g_sl, scb[:, 0:1], None, ALU.mult)
                v.tensor_scalar(i16u[:, :], u_sl, scb[:, 3:4], None,
                                ALU.mult).then_inc(_vq, 2)
                v.wait_ge(_vq, 5 * c + 2)
                v.scalar_tensor_tensor(of[:, :], i16g[:, :], scb[:, 2:3],
                                       i16u[:, :], ALU.mult,
                                       ALU.mult).then_inc(_vq, 1)
                v.wait_ge(_vq, 5 * c + 3)
                v.tensor_reduce(maxcol[:, c:c + 1], of[:, :], AX, ALU.max,
                                apply_absolute_value=True).then_inc(_red, 1)
                o_sl = ostage[:, (c % 2) * T:(c % 2) * T + T]
                v.tensor_copy(o_sl, of[:, :]).then_inc(_vq, 2)
            v.wait_ge(_red, FC)
            v.tensor_reduce(mfold[:, 0:1], maxcol[:, :], AX,
                            ALU.max).then_inc(_red, 1)
            v.wait_ge(_bc, 2)
            v.tensor_reduce(mo[0:1, 0:1], ps_tr[0:1, 0:128], AX,
                            ALU.max).then_inc(_red, 1)

        @block.tensor
        def _(t_):
            t_.wait_ge(_cfg, 64)
            t_.wait_ge(_scal, 5)
            t_.matmul(ps_bc[:, :], onesr[0:1, :], scal[0:1, 0:4], start=True,
                      stop=True).then_inc(_bc, 1)
            t_.wait_ge(_red, FC + 1)
            t_.transpose(ps_tr[0:1, 0:128], mfold[0:128, 0:1],
                         id128[:, :]).then_inc(_bc, 1)
    return nc



def build_l5():
    import ml_dtypes as _mld
    nc = bass.Bass("TRN2", target_bir_lowering=False, debug=False)
    oin = nc.dram_tensor("oin", [FF, T], BF16, kind="ExternalInput")
    maxo8 = nc.dram_tensor("maxo8", [128, 8], F32, kind="ExternalInput")
    streamin = nc.dram_tensor("streamin", [128, DC * T], F32, kind="ExternalInput")
    wdown = nc.dram_tensor("wdown", [128, FC * 1024], BF16, kind="ExternalInput")
    final = nc.dram_tensor("final", [128, DC * T], F32, kind="ExternalOutput")
    ones_f32 = nc.inline_tensor(np.ones((1, 128), np.float32), name="ones_f32")
    onescol_b = nc.inline_tensor(np.ones((128, 1), dtype=_mld.bfloat16),
                                 name="onescol_b")
    SUMC = [1, 3, 5, 7]
    BCC = [2, 4, 6, 8]

    TAILSEQ = [("m", 0), ("m", 1), ("s", 0), ("m", 2), ("s", 1), ("m", 3),
               ("s", 2), ("m", 4), ("s", 3), ("m", 5), ("s", 4), ("m", 6),
               ("s", 5), ("m", 7), ("s", 6), ("s", 7)]
    TAILV = {}
    for _i, (_o, _d) in enumerate(TAILSEQ):
        if _o == "s":
            TAILV[_d] = _i + 1
    TSTORE = [max(TAILV[2 * j], TAILV[2 * j + 1]) for j in range(4)]

    with ExitStack() as es:
        sb = lambda *a: es.enter_context(nc.sbuf_tensor(*a))
        ps = lambda *a: es.enter_context(nc.psum_tensor(*a))
        sem = lambda n: es.enter_context(nc.semaphore(n))
        stream = sb("stream", [128, DC * T], F32)
        oq = sb("oq", [128, 2 * FC * 512], BF16)
        warena = sb("warena", [128, FC * 1024], BF16)
        i16o = sb("i16o", [128, 2 * 512], I16)
        maxin = sb("maxin", [128, 8], F32)
        scb = sb("scb", [128, 2], F32)
        onesr = sb("onesr", [1, 128], F32)
        onecb = sb("onecb", [128, 1], BF16)
        sbf = sb("sbf", [128, DC * 512], BF16)
        sqbf = sb("sqbf", [128, DC * 512], BF16)
        lrow = sb("lrow", [1, 5 * 512], F32)
        ps_mm = ps("ps_mm", [128, 4 * 512], F32)
        ps_ln = ps("ps_ln", [1, 1024], F32)
        ps_r = ps("ps_r", [128, 512], F32)
        ps_m = ps("ps_m", [128, 512], F32)
        _cfgm, _cfgo = sem("s_cfgm"), sem("s_cfgo")
        _scal = sem("s_scal")
        _q, _qi = sem("s_q"), sem("s_qi")
        _ldg = [sem(f"s_ldg{k}") for k in range(24)]
        _wd = [sem(f"s_wd{k}") for k in range(DC)]
        _sid = [sem(f"s_sid{k}") for k in range(DC)]
        _mm, _ev, _sb = sem("s_mm"), sem("s_ev"), sem("s_sb")
        _ln, _lnp, _lna = sem("s_ln"), sem("s_lnp"), sem("s_lna")
        _lnn = sem("s_lnn")
        _so = sem("s_so")
        block = es.enter_context(nc.Block())

        def _oqsl(k):
            tb, c = k // FC, k % FC
            return oq[:, (tb % 2) * FC * 512 + c * 512:(tb % 2) * FC * 512 + c * 512 + 512]

        def _oin_grp(eng, tb, g):
            if tb >= 2:
                eng.wait_ge(_mm, 8 * (tb - 2) + 8)
            base = (tb % 2) * FC * 512 + g * 4 * 512
            eng.dma_start(
                out=oq[:, base:base + 2048].rearrange("p (c t) -> p c t", t=512),
                in_=oin[g * 512:(g + 1) * 512,
                        tb * 512:(tb + 1) * 512].rearrange(
                            "(c p) t -> p c t", p=128)
            ).then_inc(_ldg[tb * 6 + g], 16)

        def _wd_dma(eng, dc):
            eng.dma_start(
                out=warena[:, dc * FC * 128:(dc + 1) * FC * 128],
                in_=wdown[:, dc * FC * 128:(dc + 1) * FC * 128],
            ).then_inc(_wd[dc], 16)

        def _sin_dma(eng, c):
            eng.dma_start(out=stream[:, c * T:(c + 1) * T],
                          in_=streamin[:, c * T:(c + 1) * T]
                          ).then_inc(_sid[c], 16)

        def _q_i16(s, k):
            if k % 4 == 0:
                s.wait_ge(_ldg[k // 4], 16)
            if k >= 2:
                s.wait_ge(_q, k - 1)
            s.activation(i16o[:, (k % 2) * 512:(k % 2) * 512 + 512],
                         _oqsl(k), AF.Copy, scale=scb[:, 1:2]).then_inc(_qi, 1)

        def _q_oq(v, k):
            v.wait_ge(_qi, k + 1)
            v.tensor_scalar(_oqsl(k), i16o[:, (k % 2) * 512:(k % 2) * 512 + 512],
                            scb[:, 0:1], None, ALU.mult).then_inc(_q, 1)

        def _store(eng, tbn, j):
            if tbn == TB - 1:
                eng.wait_ge(_lnn, 16 * tbn + TSTORE[j])
            else:
                eng.wait_ge(_lnn, 16 * tbn + 4 * (j + 1))
            sv = final[:, :].rearrange("p (c t) -> p c t", t=T)[
                :, 2 * j:2 * j + 2, tbn * 512:tbn * 512 + 512]
            eng.dma_start(out=sv,
                          in_=stream[:, :].rearrange("p (c t) -> p c t", t=T)[
                              :, 2 * j:2 * j + 2, tbn * 512:tbn * 512 + 512]
                          ).then_inc(_so, 16)

        @block.gpsimd
        def _(g):
            _sin_dma(g, 0)
            _sin_dma(g, 4)
            _sin_dma(g, 6)
            for grp in (1, 3, 5):
                _oin_grp(g, 1, grp)
            g.wait_ge(_so, 16 * 16)

        @block.sync
        def _(sy):
            sy.dma_start(out=onesr[:, :], in_=ones_f32[:, :]).then_inc(_cfgo, 16)
            sy.dma_start(out=onecb[:, :], in_=onescol_b[:, :]).then_inc(_cfgo, 16)
            _wd_dma(sy, 0)
            for grp in (0, 2, 4):
                _oin_grp(sy, 0, grp)
            _wd_dma(sy, 1)
            _sin_dma(sy, 2)
            _wd_dma(sy, 3)
            for grp in (0, 2, 4):
                _oin_grp(sy, 1, grp)
            _wd_dma(sy, 5)
            _sin_dma(sy, 5)
            _wd_dma(sy, 7)
            _sin_dma(sy, 7)
            for grp in range(6):
                _oin_grp(sy, 2, grp)
            _store(sy, 0, 1)
            _store(sy, 0, 3)
            for grp in range(6):
                _oin_grp(sy, 3, grp)
            _store(sy, 1, 1)
            _store(sy, 1, 3)
            for tbn in range(2, TB):
                _store(sy, tbn, 1)
                _store(sy, tbn, 3)

        @block.scalar
        def _(s):
            s.dma_start(out=maxin[:, :], in_=maxo8[:, :]).then_inc(_cfgm, 16)
            _wd_dma(s, 2)
            for grp in (1, 3, 5):
                _oin_grp(s, 0, grp)
            _wd_dma(s, 4)
            _sin_dma(s, 1)
            _wd_dma(s, 6)
            _sin_dma(s, 3)

            s.wait_ge(_scal, 3)
            for k in range(2 * FC):
                _q_i16(s, k)
            for tt in range(32):
                tb, dc = tt // 8, tt % 8
                s.wait_ge(_sb, tt + 1)
                s.activation(sqbf[:, dc * 512:dc * 512 + 512],
                             sbf[:, dc * 512:dc * 512 + 512],
                             AF.Square).then_inc(_lna, 1)
                if 1 <= dc <= 6 and tb + 1 in (2, 3):
                    for j in range(4):
                        _q_i16(s, (tb + 1) * FC + 4 * (dc - 1) + j)
                if tb > 0 and dc == 3:
                    _store(s, tb - 1, 0)
                if tb > 0 and dc == 6:
                    _store(s, tb - 1, 2)
                if dc == 7:
                    s.wait_ge(_ln, 5 * tb + 4)
                    if tb > 0:
                        s.wait_ge(_lnp, BCC[tb - 1])
                    s.activation(lrow[0:1, 1536:2048], lrow[0:1, 1024:1536],
                                 AF.Ln).then_inc(_lna, 1)
                    s.wait_ge(_lna, 10 * tb + 9)
                    s.activation(lrow[0:1, 1536:2048], lrow[0:1, 1536:2048],
                                 AF.Exp, bias=0.0,
                                 scale=-0.5).then_inc(_lna, 1)
            _store(s, TB - 1, 0)
            _store(s, TB - 1, 2)

        @block.vector
        def _(v):
            v.wait_ge(_cfgm, 16)
            v.tensor_reduce(scb[:, 0:1], maxin[:, :], AX,
                            ALU.max).then_inc(_scal, 1)
            v.wait_ge(_scal, 1)
            v.tensor_scalar(scb[:, 0:1], scb[:, 0:1], 1.0 / QMAX, EPS_Q,
                            ALU.mult, ALU.max).then_inc(_scal, 1)
            v.wait_ge(_scal, 2)
            v.reciprocal(scb[:, 1:2], scb[:, 0:1]).then_inc(_scal, 1)
            v.wait_ge(_scal, 3)
            for k in range(2 * FC):
                _q_oq(v, k)
            for tt in range(32):
                tb, dc = tt // 8, tt % 8
                if tb == 0:
                    v.wait_ge(_sid[dc], 16)
                v.wait_ge(_mm, tt + 1)
                sl = stream[:, dc * T + tb * 512:dc * T + tb * 512 + 512]
                v.tensor_tensor(sl, sl,
                                ps_mm[:, (tt % 4) * 512:(tt % 4) * 512 + 512],
                                ALU.add).then_inc(_ev, 1)
                v.wait_ge(_ev, tt + 1)
                v.tensor_copy(sbf[:, dc * 512:dc * 512 + 512],
                              sl).then_inc(_sb, 1)
                if 1 <= dc <= 6 and tb + 1 in (2, 3):
                    for j in range(4):
                        _q_oq(v, (tb + 1) * FC + 4 * (dc - 1) + j)

                def _norm_chunk(tbn, dcn, first):
                    if first:
                        v.wait_ge(_lnp, BCC[tbn])
                    sl2 = stream[:, dcn * T + tbn * 512:dcn * T + tbn * 512 + 512]
                    v.tensor_tensor(sl2, sl2, ps_r[:, 0:512],
                                    ALU.mult).then_inc(_lnn, 1)
                    v.wait_ge(_lnn, 16 * tbn + 2 * dcn + 1)
                    v.tensor_tensor(sl2, sl2, ps_m[:, 0:512],
                                    ALU.subtract).then_inc(_lnn, 1)

                if tb > 0 and dc >= 1:
                    if dc == 1:
                        _norm_chunk(tb - 1, 0, True)
                        _norm_chunk(tb - 1, 1, False)
                    else:
                        _norm_chunk(tb - 1, dc, False)
                if dc == 7:
                    b = 5 * tb
                    v.wait_ge(_lnp, SUMC[tb])
                    v.tensor_scalar(lrow[0:1, 0:512], ps_ln[0:1, 0:512],
                                    1.0 / D, None, ALU.mult)
                    v.tensor_scalar(lrow[0:1, 512:1024], ps_ln[0:1, 512:1024],
                                    1.0 / D, None, ALU.mult).then_inc(_ln, 1)
                    v.wait_ge(_ln, b + 1)
                    v.tensor_tensor(lrow[0:1, 1024:1536], lrow[0:1, 0:512],
                                    lrow[0:1, 0:512], ALU.mult).then_inc(_ln, 1)
                    v.wait_ge(_ln, b + 2)
                    v.tensor_tensor(lrow[0:1, 1024:1536], lrow[0:1, 512:1024],
                                    lrow[0:1, 1024:1536],
                                    ALU.subtract).then_inc(_ln, 1)
                    v.wait_ge(_ln, b + 3)
                    v.tensor_scalar(lrow[0:1, 1024:1536], lrow[0:1, 1024:1536],
                                    EPS_LN, None, ALU.add).then_inc(_ln, 1)
                    v.wait_ge(_lna, 10 * tb + 10)
                    if tb > 0:
                        v.wait_ge(_lnp, BCC[tb - 1])
                    v.tensor_tensor(lrow[0:1, 2048:2560], lrow[0:1, 0:512],
                                    lrow[0:1, 1536:2048], ALU.mult).then_inc(_ln, 1)
                    if tb == TB - 1:
                        v.wait_ge(_lnp, BCC[tb])
                        for op, dcn in TAILSEQ:
                            sl2 = stream[:, dcn * T + tb * 512:dcn * T + tb * 512 + 512]
                            if op == "m":
                                v.tensor_tensor(sl2, sl2, ps_r[:, 0:512],
                                                ALU.mult).then_inc(_lnn, 1)
                            else:
                                v.wait_ge(_lnn, 16 * tb + TAILSEQ.index(("m", dcn)) + 1)
                                v.tensor_tensor(sl2, sl2, ps_m[:, 0:512],
                                                ALU.subtract).then_inc(_lnn, 1)

        @block.tensor
        def _(t_):
            for tt in range(32):
                tb, dc = tt // 8, tt % 8
                if tb == 0:
                    t_.wait_ge(_wd[dc], 16)
                if dc != 0:
                    t_.wait_ge(_q, FC * (tb + 1))
                if tt >= 4:
                    t_.wait_ge(_ev, tt - 3)
                ins = None
                for ic in range(FC):
                    base = (dc * FC + ic) * 128
                    if dc == 0:
                        t_.wait_ge(_q, tb * FC + ic + 1)
                    ins = t_.matmul(
                        ps_mm[:, (tt % 4) * 512:(tt % 4) * 512 + 512],
                        warena[:, base:base + 128],
                        _oqsl(tb * FC + ic),
                        start=(ic == 0), stop=(ic == FC - 1))
                ins.then_inc(_mm, 1)

                def _sum_slice(dcn, stop):
                    t_.wait_ge(_sb, 8 * tb + dcn + 1)
                    t_.wait_ge(_lna, 10 * tb + dcn + 1)
                    if dcn == 0:
                        t_.wait_ge(_cfgo, 32)
                        if tb > 0:
                            t_.wait_ge(_ln, 5 * (tb - 1) + 1)
                    t_.matmul(ps_ln[0:1, 0:512], onecb[:, :],
                              sbf[:, dcn * 512:(dcn + 1) * 512],
                              start=(dcn == 0), stop=stop)
                    ins2 = t_.matmul(ps_ln[0:1, 512:1024], onecb[:, :],
                                     sqbf[:, dcn * 512:(dcn + 1) * 512],
                                     start=(dcn == 0), stop=stop)
                    if stop:
                        ins2.then_inc(_lnp, 1)

                def _bcast(tbn):
                    t_.wait_ge(_ln, 5 * tbn + 5)
                    if tbn > 0:
                        t_.wait_ge(_lnn, 16 * tbn)
                    t_.matmul(ps_r[:, 0:512], onesr[0:1, :],
                              lrow[0:1, 1536:2048], start=True, stop=True)
                    insb = t_.matmul(ps_m[:, 0:512], onesr[0:1, :],
                                     lrow[0:1, 2048:2560], start=True,
                                     stop=True)
                    insb.then_inc(_lnp, 1)

                if tb > 0 and dc == 0:
                    _bcast(tb - 1)
                if dc >= 1:
                    _sum_slice(dc - 1, False)
                if dc == 7:
                    _sum_slice(7, True)
                    if tt == 31:
                        _bcast(tb)
    return nc



def build_l1():
    import ml_dtypes as _mld
    nc = bass.Bass("TRN2", target_bir_lowering=False, debug=False)
    xT = nc.dram_tensor("xT", [128, DC * T], F32, kind="ExternalInput")
    wqn = nc.dram_tensor("wqn", [NH * DH, D], F32, kind="ExternalInput")
    wkvT = nc.dram_tensor("wkvT", [128, DC * LR], F32, kind="ExternalInput")
    wku = nc.dram_tensor("wku", [NH * DH, LR], F32, kind="ExternalInput")
    wvuT = nc.dram_tensor("wvuT", [LR, NH * DH], F32, kind="ExternalInput")
    woT = nc.dram_tensor("woT", [NH * DH, D], F32, kind="ExternalInput")
    streamout = nc.dram_tensor("streamout", [128, DC * T], F32, kind="ExternalOutput")
    maxx = nc.dram_tensor("maxx", [1, 1], F32, kind="ExternalOutput")
    attnout = nc.dram_tensor("attnout", [T, NH * DH], BF16)
    ones_f32 = nc.inline_tensor(np.ones((1, 128), np.float32), name="ones_f32")
    onesm_b = nc.inline_tensor(np.ones((128, 128), dtype=_mld.bfloat16),
                               name="onesm_b")
    identb = nc.inline_tensor(np.eye(128, dtype=_mld.bfloat16), name="identb")
    ident128 = nc.inline_tensor(np.eye(128, dtype=np.float32), name="ident128")
    SUMC = [1, 3, 5, 7]
    BCC = [2, 4, 6, 8]
    TAILSEQ = [("m", 0), ("m", 1), ("s", 0), ("m", 2), ("s", 1), ("m", 3),
               ("s", 2), ("m", 4), ("s", 3), ("m", 5), ("s", 4), ("m", 6),
               ("s", 5), ("m", 7), ("s", 6), ("s", 7)]
    TAILV = {}
    for _i, (_o, _d) in enumerate(TAILSEQ):
        if _o == "s":
            TAILV[_d] = _i + 1
    with ExitStack() as es:
        sb = lambda *a: es.enter_context(nc.sbuf_tensor(*a))
        sem = lambda n: es.enter_context(nc.semaphore(n))
        stream = sb("stream", [128, DC * T], F32)
        tmpk = sb("tmpk", [128, DC * LR], F32)
        xbf = sb("xbf", [128, DC * T], BF16)
        latbf = sb("latbf", [64, T], BF16)
        lattok = sb("lattok", [128, 16 * 65], BF16)
        kvw = sb("kvw", [128, DC * LR], BF16)
        wku_sb = sb("wku_sb", [128, 16 * LR], BF16)
        vup = sb("vup", [64, NH * DH], BF16)
        wqn_sb = sb("wqn_sb", [128, 2 * 2 * D], BF16)
        wqk = sb("wqk", [128, DC * LR], BF16)
        qhat = sb("qhat", [64, T], BF16)
        sqs = sb("sqs", [64, T], BF16)
        qn2 = sb("qn2", [1, T], F32)
        latn2 = sb("latn2", [1, T], F32)
        csc = sb("csc", [1, 8], F32)
        negcb = sb("negcb", [128, 1], F32)
        rcp = sb("rcp", [128, 4], F32)
        aostage = sb("aostage", [128, 4 * 256], BF16)
        u1 = sb("u1", [128, 16384], BF16)
        probsT = u1[:, 0:8192]
        obar = u1[:, 8192:8704]
        aoT = u1[:, 0:16384]
        wo_sb = sb("wo_sb", [128, 8 * 1024], BF16)
        onesr = sb("onesr", [1, 128], F32)
        onemb = sb("onemb", [128, 128], BF16)
        idb = sb("idb", [128, 128], BF16)
        id128 = sb("id128", [128, 128], F32)
        lrow = sb("lrow", [1, 5 * 512], F32)
        maxcol = sb("maxcol", [128, 32], F32)
        mfold = sb("mfold", [128, 1], F32)
        mxo = sb("mxo", [1, 1], F32)
        psum = es.enter_context(nc.psum_tensor("psum", [128, 4096], F32))
        B2 = psum[:, 1024:1536]
        B3c = psum[:, 1536:1537]
        _cfg = sem("s_cfg")
        _cfgk, _cfgk2 = sem("s_cfgk"), sem("s_cfgk2")
        _xsd = [sem(f"s_xsd{k}") for k in range(DC)]
        _xbd = [sem(f"s_xbd{k}") for k in range(DC)]
        _vb = sem("s_vb")
        _xs, _xb = sem("s_xs"), sem("s_xb")
        _wq0, _wq1 = sem("s_wq0"), sem("s_wq1")
        _lat, _late = sem("s_lat"), sem("s_late")
        _ltr, _lte = sem("s_ltr"), sem("s_lte")
        _mmq, _ewq = sem("s_mmq"), sem("s_ewq")
        _eqh, _sq = sem("s_eqh"), sem("s_sq")
        _nmm, _nev = sem("s_nmm"), sem("s_nev")
        _lm = sem("s_lm")
        _ch, _cha, _chb = sem("s_ch"), sem("s_cha"), sem("s_chb")
        _smm, _sexp = sem("s_smm"), sem("s_sexp")
        _amm, _oev = sem("s_amm"), sem("s_oev")
        _otr, _arc = sem("s_otr"), sem("s_arc")
        _avm, _aev = sem("s_avm"), sem("s_aev")
        _asl = [sem(f"s_as{k}") for k in range(4)]
        _ao0, _ao1 = sem("s_ao0"), sem("s_ao1")
        _wo0, _wo1 = sem("s_wo0"), sem("s_wo1")
        _pmm, _pev = sem("s_pmm"), sem("s_pev")
        _ln, _lnp, _lna = sem("s_ln"), sem("s_lnp"), sem("s_lna")
        _lnn, _sbc = sem("s_lnn"), sem("s_sbc")
        _mx, _mxp = sem("s_mx"), sem("s_mxp")
        _so = sem("s_so")
        block = es.enter_context(nc.Block())

        @block.gpsimd
        def _(g):
            g.dma_start(out=onesr[:, :], in_=ones_f32[:, :]).then_inc(_cfg, 16)
            g.dma_start(out=onemb[:, :], in_=onesm_b[:, :]).then_inc(_cfg, 16)
            g.dma_start(out=idb[:, :], in_=identb[:, :]).then_inc(_cfg, 16)
            g.dma_start(out=id128[:, :], in_=ident128[:, :]).then_inc(_cfg, 16)
            g.dma_start(out=wku_sb[:, :].rearrange("p (c n) -> p c n", n=LR),
                        in_=wku[:, :].rearrange("(c p) n -> p c n", p=128)
                        ).then_inc(_cfg, 16)
            g.dma_start(out=vup[:, :], in_=wvuT[:, :]).then_inc(_cfg, 16)
            for tc in range(16):
                g.memset(lattok[:, tc * 65 + 64:tc * 65 + 65], 1.0)
            g.sem_inc(_vb, 1)
            for h in range(NH):
                if h >= 2:
                    g.wait_ge(_mmq, (h - 2) * 5 + 1)
                g.dma_start(
                    out=wqn_sb[:, (h % 2) * 2 * D:(h % 2) * 2 * D + 2 * D]
                        .rearrange("p (c n) -> p c n", n=D),
                    in_=wqn[h * 256:(h + 1) * 256, :]
                        .rearrange("(c p) n -> p c n", p=128)
                ).then_inc(_wq0 if h % 2 == 0 else _wq1, 16)
            for c8 in range(8):
                g.dma_start(out=wo_sb[:, c8 * 1024:(c8 + 1) * 1024],
                            in_=woT[c8 * 128:(c8 + 1) * 128, :]
                            ).then_inc(_wo0, 16)
            g.wait_ge(_mmq, NH * 5)
            for c8 in range(8):
                g.dma_start(out=xbf[:, DC * 512 * 2 + c8 * 1024:DC * 512 * 2 + (c8 + 1) * 1024],
                            in_=woT[(8 + c8) * 128:(9 + c8) * 128, :]
                            ).then_inc(_wo1, 16)
            g.wait_ge(_so, 16 * (TB + 1))

        @block.sync
        def _(sy):
            sy.dma_start(out=tmpk[:, :], in_=wkvT[:, :]).then_inc(_cfgk, 16)
            for c in range(4):
                sy.dma_start(out=stream[:, c * T:(c + 1) * T],
                             in_=xT[:, c * T:(c + 1) * T]).then_inc(_xsd[c], 16)
            for h in range(NH):
                for qb in range(TB):
                    for qt in range(4):
                        aidx = (h * TB + qb) * 4 + qt
                        sy.wait_ge(_aev, aidx + 1)
                        sy.dma_start(
                            out=attnout[qb * 512 + qt * 128:qb * 512 + qt * 128 + 128,
                                        h * 256:(h + 1) * 256],
                            in_=aostage[:, (aidx % 4) * 256:(aidx % 4) * 256 + 256],
                        ).then_inc(_asl[aidx % 4], 16)
            for k in range(4):
                sy.wait_ge(_asl[k], 16 * 32)
            for g8 in range(8):
                H, qb = g8 // 4, g8 % 4
                if g8 >= 2:
                    sy.wait_ge(_pmm, 8 * (g8 - 1))
                for c8 in range(8):
                    cc = H * 8 + c8
                    sy.dma_start(
                        out=aoT[:, (g8 % 2) * 16 * 512 + c8 * 512:(g8 % 2) * 16 * 512 + c8 * 512 + 512],
                        in_=attnout[qb * 512:(qb + 1) * 512, cc * 128:(cc + 1) * 128],
                        transpose=True,
                    ).then_inc(_ao0 if g8 % 2 == 0 else _ao1, 16)
            for tb in range(TB):
                sy.wait_ge(_lnn, 16 * (tb + 1))
                sv = streamout[:, :].rearrange("p (c t) -> p c t", t=T)[:, :, tb * 512:tb * 512 + 512]
                sy.dma_start(out=sv,
                             in_=stream[:, :].rearrange("p (c t) -> p c t", t=T)[:, :, tb * 512:tb * 512 + 512]
                             ).then_inc(_so, 16)
            sy.wait_ge(_mx, 34)
            sy.dma_start(out=maxx[0:1, 0:1], in_=mxo[0:1, 0:1]).then_inc(_so, 16)

        @block.scalar
        def _(s):
            for c in range(4, DC):
                s.dma_start(out=stream[:, c * T:(c + 1) * T],
                            in_=xT[:, c * T:(c + 1) * T]).then_inc(_xsd[c], 16)
            for tb in range(TB):
                s.wait_ge(_lat, tb + 1)
                s.activation(latbf[0:64, tb * 512:(tb + 1) * 512],
                             psum[0:64, 0:512], AF.Copy).then_inc(_late, 1)
            for h in range(NH):
                s.wait_ge(_ch, h * 4 + 2)
                s.activation(csc[0:1, 3:4], csc[0:1, 2:3], AF.Ln).then_inc(_cha, 1)
                s.wait_ge(_cha, h * 2 + 1)
                s.activation(csc[0:1, 4:5], csc[0:1, 3:4], AF.Exp, bias=0.0,
                             scale=0.5).then_inc(_cha, 1)
                for qb in range(TB):
                    for pr in range(8):
                        s.wait_ge(_smm, (h * TB + qb) * 16 + 2 * pr + 2)
                        s.wait_ge(_ch, h * 4 + 4)
                        if pr == 0 and (qb > 0 or h > 0):
                            s.wait_ge(_amm, h * TB + qb)
                        off = 2048 if pr % 2 == 0 else 0
                        s.activation(probsT[:, pr * 1024:(pr + 1) * 1024],
                                     psum[:, off:off + 1024],
                                     AF.Exp, bias=negcb[:, 0:1],
                                     scale=1.0).then_inc(_sexp, 1)
            for tb in range(TB):
                for dcn in range(DC):
                    s.wait_ge(_sbc, 8 * tb + dcn + 1)
                    if tb > 0 and dcn == 0:
                        s.wait_ge(_lnp, SUMC[tb - 1])
                    s.activation(xbf[:, DC * 512 + dcn * 512:DC * 512 + dcn * 512 + 512],
                                 xbf[:, dcn * 512:dcn * 512 + 512],
                                 AF.Square).then_inc(_lna, 1)
                s.wait_ge(_ln, 5 * tb + 4)
                if tb > 0:
                    s.wait_ge(_lnp, BCC[tb - 1])
                s.activation(lrow[0:1, 1536:2048], lrow[0:1, 1024:1536],
                             AF.Ln).then_inc(_lna, 1)
                s.wait_ge(_lna, 10 * tb + 9)
                s.activation(lrow[0:1, 1536:2048], lrow[0:1, 1536:2048], AF.Exp,
                             bias=0.0, scale=-0.5).then_inc(_lna, 1)

        @block.vector
        def _(v):
            v.wait_ge(_cfgk, 16)
            v.tensor_copy(kvw[:, :], tmpk[:, :]).then_inc(_cfgk2, 1)
            for c in range(DC):
                v.wait_ge(_xsd[c], 16)
                v.tensor_copy(xbf[:, c * T:(c + 1) * T],
                              stream[:, c * T:(c + 1) * T]).then_inc(_xbd[c], 1)
            for tc in range(16):
                v.wait_ge(_ltr, tc + 1)
                pb = psum[:, 2048 + (tc % 2) * 512:2048 + (tc % 2) * 512 + 32].bitcast(BF16)
                v.tensor_copy(lattok[:, tc * 65:tc * 65 + 64],
                              pb[:, 0:64]).then_inc(_lte, 1)
            for i in range(4):
                v.wait_ge(_late, i + 1)
                v.tensor_tensor(sqs[0:64, i * 512:(i + 1) * 512],
                                latbf[0:64, i * 512:(i + 1) * 512],
                                latbf[0:64, i * 512:(i + 1) * 512],
                                ALU.mult).then_inc(_sq, 1)
            for i in range(4):
                v.wait_ge(_nmm, i + 1)
                v.tensor_copy(latn2[0:1, i * 512:(i + 1) * 512],
                              B2[0:1, :]).then_inc(_nev, 1)
            v.wait_ge(_nev, 4)
            v.tensor_reduce(csc[0:1, 1:2], latn2[0:1, :], AX,
                            ALU.max).then_inc(_lm, 1)
            for h in range(NH):
                v.wait_ge(_mmq, h * 5 + 1)
                v.tensor_copy(wqk[:, :], psum[:, 0:512]).then_inc(_ewq, 1)
                for i in range(4):
                    v.wait_ge(_mmq, h * 5 + 2 + i)
                    v.tensor_scalar(qhat[0:64, i * 512:(i + 1) * 512],
                                    psum[0:64, (i % 2) * 512:(i % 2) * 512 + 512],
                                    1.0 / 16.0, None, ALU.mult).then_inc(_eqh, 1)
                    v.wait_ge(_eqh, h * 4 + i + 1)
                    v.tensor_tensor(sqs[0:64, i * 512:(i + 1) * 512],
                                    qhat[0:64, i * 512:(i + 1) * 512],
                                    qhat[0:64, i * 512:(i + 1) * 512],
                                    ALU.mult).then_inc(_sq, 1)
                for i in range(4):
                    v.wait_ge(_nmm, 4 + h * 4 + i + 1)
                    v.tensor_copy(qn2[0:1, i * 512:(i + 1) * 512],
                                  B2[0:1, :]).then_inc(_nev, 1)
                v.wait_ge(_nev, 4 + h * 4 + 4)
                v.tensor_reduce(csc[0:1, 0:1], qn2[0:1, :], AX,
                                ALU.max).then_inc(_ch, 1)
                v.wait_ge(_ch, h * 4 + 1)
                v.wait_ge(_lm, 1)
                v.tensor_tensor(csc[0:1, 2:3], csc[0:1, 0:1], csc[0:1, 1:2],
                                ALU.mult).then_inc(_ch, 1)
                v.wait_ge(_cha, h * 2 + 2)
                v.tensor_scalar(csc[0:1, 5:6], csc[0:1, 4:5], -1.01, None,
                                ALU.mult).then_inc(_ch, 1)
                v.wait_ge(_chb, h + 1)
                if h > 0:
                    v.wait_ge(_sexp, h * 32)
                v.tensor_copy(negcb[:, 0:1], B3c[:, 0:1]).then_inc(_ch, 1)
                for qb in range(TB):
                    gq = h * TB + qb
                    v.wait_ge(_amm, gq + 1)
                    v.tensor_copy(obar[0:65, :],
                                  psum[0:65, 1024:1536]).then_inc(_oev, 1)
                    v.wait_ge(_otr, gq + 1)
                    if gq > 0:
                        v.wait_ge(_aev, gq * 4)
                    v.reciprocal(rcp[:, :],
                                 psum[:, 1540:1544].bitcast(BF16)[:, 0:8:2]
                                 ).then_inc(_arc, 1)
                    for qt in range(4):
                        aidx = gq * 4 + qt
                        v.wait_ge(_avm, aidx + 1)
                        v.wait_ge(_arc, gq + 1)
                        if aidx >= 4:
                            v.wait_ge(_asl[aidx % 4], 16 * (aidx // 4))
                        v.tensor_scalar(
                            aostage[:, (aidx % 4) * 256:(aidx % 4) * 256 + 256],
                            psum[:, 3072 + (aidx % 2) * 512:3072 + (aidx % 2) * 512 + 256],
                            rcp[:, qt:qt + 1], None, ALU.mult).then_inc(_aev, 1)
            for c in range(DC):
                v.wait_ge(_xsd[c], 16)
            for qb in range(TB):
                for dc in range(DC):
                    pidx = qb * 8 + dc
                    v.wait_ge(_pmm, pidx + 1)
                    sl = stream[:, dc * T + qb * 512:dc * T + qb * 512 + 512]
                    v.tensor_tensor(sl, sl,
                                    psum[:, (pidx % 2) * 512:(pidx % 2) * 512 + 512],
                                    ALU.add).then_inc(_pev, 1)

            def _norm_ops(tbn, ops):
                for op, dcn in ops:
                    sl2 = stream[:, dcn * T + tbn * 512:dcn * T + tbn * 512 + 512]
                    if op == "m":
                        if dcn == 0:
                            v.wait_ge(_lnp, BCC[tbn])
                        v.tensor_tensor(sl2, sl2, psum[:, 3072:3584],
                                        ALU.mult).then_inc(_lnn, 1)
                    else:
                        v.wait_ge(_lnn, 16 * tbn + TAILSEQ.index(("m", dcn)) + 1)
                        v.tensor_tensor(sl2, sl2, psum[:, 3584:4096],
                                        ALU.subtract).then_inc(_lnn, 1)
                        v.wait_ge(_lnn, 16 * tbn + TAILV[dcn])
                        v.tensor_reduce(maxcol[:, tbn * 8 + dcn:tbn * 8 + dcn + 1],
                                        stream[:, dcn * T + tbn * 512:dcn * T + tbn * 512 + 512],
                                        AX, ALU.max,
                                        apply_absolute_value=True).then_inc(_mx, 1)

            for qb in range(TB):
                for dc in range(DC):
                    pidx = 32 + qb * 8 + dc
                    v.wait_ge(_pmm, pidx + 1)
                    sl = stream[:, dc * T + qb * 512:dc * T + qb * 512 + 512]
                    v.tensor_tensor(sl, sl,
                                    psum[:, (pidx % 2) * 512:(pidx % 2) * 512 + 512],
                                    ALU.add).then_inc(_pev, 1)
                    v.wait_ge(_pev, pidx + 1)
                    v.tensor_copy(xbf[:, dc * 512:dc * 512 + 512],
                                  sl).then_inc(_sbc, 1)
                    if qb > 0:
                        _norm_ops(qb - 1, TAILSEQ[2 * dc:2 * dc + 2])
                b = 5 * qb
                v.wait_ge(_lnp, SUMC[qb])
                v.tensor_scalar(lrow[0:1, 0:512], psum[0:1, 1024:1536], 1.0 / D,
                                None, ALU.mult)
                v.tensor_scalar(lrow[0:1, 512:1024], psum[0:1, 1536:2048],
                                1.0 / D, None, ALU.mult).then_inc(_ln, 1)
                v.wait_ge(_ln, b + 1)
                v.tensor_tensor(lrow[0:1, 1024:1536], lrow[0:1, 0:512],
                                lrow[0:1, 0:512], ALU.mult).then_inc(_ln, 1)
                v.wait_ge(_ln, b + 2)
                v.tensor_tensor(lrow[0:1, 1024:1536], lrow[0:1, 512:1024],
                                lrow[0:1, 1024:1536], ALU.subtract).then_inc(_ln, 1)
                v.wait_ge(_ln, b + 3)
                v.tensor_scalar(lrow[0:1, 1024:1536], lrow[0:1, 1024:1536],
                                EPS_LN, None, ALU.add).then_inc(_ln, 1)
                v.wait_ge(_lna, 10 * qb + 10)
                if qb > 0:
                    v.wait_ge(_lnp, BCC[qb - 1])
                v.tensor_tensor(lrow[0:1, 2048:2560], lrow[0:1, 0:512],
                                lrow[0:1, 1536:2048], ALU.mult).then_inc(_ln, 1)
                if qb == TB - 1:
                    _norm_ops(qb, TAILSEQ)
            v.wait_ge(_mx, 32)
            v.tensor_reduce(mfold[:, 0:1], maxcol[:, 0:32], AX,
                            ALU.max).then_inc(_mx, 1)
            v.wait_ge(_mxp, 1)
            v.tensor_reduce(mxo[0:1, 0:1], psum[0:1, 2048:2176], AX,
                            ALU.max).then_inc(_mx, 1)

        @block.tensor
        def _(t_):
            t_.wait_ge(_cfgk2, 1)
            for tb in range(TB):
                if tb > 0:
                    t_.wait_ge(_late, tb)
                ins = None
                for dc in range(DC):
                    if tb == 0:
                        t_.wait_ge(_xbd[dc], 1)
                    ins = t_.matmul(psum[0:64, 0:512], kvw[:, dc * LR:(dc + 1) * LR],
                                    xbf[:, dc * T + tb * 512:dc * T + tb * 512 + 512],
                                    start=(dc == 0), stop=(dc == DC - 1))
                ins.then_inc(_lat, 1)
            t_.wait_ge(_cfg, 96)
            t_.wait_ge(_vb, 1)
            t_.wait_ge(_late, TB)
            for tc in range(16):
                if tc >= 2:
                    t_.wait_ge(_lte, tc - 1)
                pb = psum[:, 2048 + (tc % 2) * 512:2048 + (tc % 2) * 512 + 32].bitcast(BF16)
                t_.transpose(pb[:, 0:64], latbf[0:64, tc * 128:(tc + 1) * 128],
                             idb[0:64, 0:64]).then_inc(_ltr, 1)
            for i in range(4):
                t_.wait_ge(_sq, i + 1)
                if i > 0:
                    t_.wait_ge(_nev, i)
                t_.matmul(B2[:, :], onemb[0:64, :], sqs[0:64, i * 512:(i + 1) * 512],
                          start=True, stop=True).then_inc(_nmm, 1)
            for h in range(NH):
                t_.wait_ge(_wq0 if h % 2 == 0 else _wq1, 16 * (h // 2 + 1))
                t_.wait_ge(_eqh, h * 4)
                if h > 0:
                    t_.wait_ge(_ewq, h)
                ins = None
                for dc in range(DC):
                    for jj in range(2):
                        ins = t_.matmul(
                            psum[:, dc * 64:dc * 64 + 64],
                            wqn_sb[:, (h % 2) * 2 * D + jj * D + dc * 128:(h % 2) * 2 * D + jj * D + dc * 128 + 128],
                            wku_sb[:, (h * 2 + jj) * LR:(h * 2 + jj) * LR + LR],
                            start=(jj == 0), stop=(jj == 1))
                ins.then_inc(_mmq, 1)
                for i in range(4):
                    t_.wait_ge(_ewq, h + 1)
                    if i >= 2:
                        t_.wait_ge(_eqh, h * 4 + i - 1)
                    ins = None
                    for dc in range(DC):
                        ins = t_.matmul(
                            psum[0:64, (i % 2) * 512:(i % 2) * 512 + 512],
                            wqk[:, dc * LR:(dc + 1) * LR],
                            xbf[:, dc * T + i * 512:dc * T + i * 512 + 512],
                            start=(dc == 0), stop=(dc == DC - 1))
                    ins.then_inc(_mmq, 1)
                for i in range(4):
                    t_.wait_ge(_sq, 4 + h * 4 + i + 1)
                    t_.wait_ge(_nev, 4 + h * 4 + i)
                    t_.matmul(B2[:, :], onemb[0:64, :],
                              sqs[0:64, i * 512:(i + 1) * 512],
                              start=True, stop=True).then_inc(_nmm, 1)
                for qb in range(TB):
                    if qb == 0:
                        t_.wait_ge(_eqh, h * 4 + 4)
                    for kt in range(16):
                        sidx = (h * TB + qb) * 16 + kt
                        prg = (h * TB + qb) * 8 + kt // 2
                        if prg >= 2:
                            t_.wait_ge(_sexp, prg - 1)
                        off = (2048 if (kt // 2) % 2 == 0 else 0) + (kt % 2) * 512
                        t_.matmul(psum[:, off:off + 512],
                                  latbf[0:64, kt * 128:(kt + 1) * 128],
                                  qhat[0:64, qb * 512:(qb + 1) * 512],
                                  start=True, stop=True).then_inc(_smm, 1)
                        if qb == 0 and kt == 3:
                            t_.wait_ge(_ch, h * 4 + 3)
                            t_.matmul(B3c[:, 0:1], onesr[0:1, :], csc[0:1, 5:6],
                                      start=True, stop=True).then_inc(_chb, 1)
                    gq = h * TB + qb
                    t_.wait_ge(_oev, gq)
                    if qb == 0:
                        t_.wait_ge(_nev, 4 + h * 4 + 4)
                    ins = None
                    for kc in range(16):
                        t_.wait_ge(_sexp, gq * 8 + kc // 2 + 1)
                        ins = t_.matmul(psum[0:65, 1024:1536],
                                        lattok[:, kc * 65:(kc + 1) * 65],
                                        probsT[:, kc * 512:(kc + 1) * 512],
                                        start=(kc == 0), stop=(kc == 15))
                    ins.then_inc(_amm, 1)
                    t_.wait_ge(_oev, gq + 1)
                    if gq > 0:
                        t_.wait_ge(_arc, gq)
                    ins = None
                    for qt in range(4):
                        pb2 = psum[:, 1540 + qt:1541 + qt].bitcast(BF16)
                        ins = t_.transpose(pb2[:, 0:1],
                                           obar[64:65, qt * 128:(qt + 1) * 128],
                                           idb[64:65, 64:65])
                    ins.then_inc(_otr, 1)
                    for qt in range(4):
                        aidx = gq * 4 + qt
                        if aidx >= 2:
                            t_.wait_ge(_aev, aidx - 1)
                        ins = t_.matmul(
                            psum[:, 3072 + (aidx % 2) * 512:3072 + (aidx % 2) * 512 + 256],
                            obar[0:64, qt * 128:(qt + 1) * 128],
                            vup[0:64, h * 256:(h + 1) * 256],
                            start=True, stop=True)
                        ins.then_inc(_avm, 1)
            t_.wait_ge(_wo0, 16 * 8)
            for qb in range(TB):
                t_.wait_ge(_ao0 if qb % 2 == 0 else _ao1, 16 * 8 * (qb // 2 + 1))
                for dc in range(DC):
                    pidx = qb * 8 + dc
                    if pidx >= 2:
                        t_.wait_ge(_pev, pidx - 1)
                    ins = None
                    for c8 in range(8):
                        ins = t_.matmul(
                            psum[:, (pidx % 2) * 512:(pidx % 2) * 512 + 512],
                            wo_sb[:, c8 * 1024 + dc * 128:c8 * 1024 + dc * 128 + 128],
                            aoT[:, (qb % 2) * 16 * 512 + c8 * 512:(qb % 2) * 16 * 512 + c8 * 512 + 512],
                            start=(c8 == 0), stop=(c8 == 7))
                    ins.then_inc(_pmm, 1)
            t_.wait_ge(_wo1, 16 * 8)

            def _sums(tbn):
                t_.wait_ge(_sbc, 8 * tbn + 8)
                t_.wait_ge(_lna, 10 * tbn + 8)
                for dcn in range(DC):
                    t_.matmul(psum[:, 1024:1536], onemb[:, :],
                              xbf[:, dcn * 512:(dcn + 1) * 512],
                              start=(dcn == 0), stop=(dcn == DC - 1))
                ins2 = None
                for dcn in range(DC):
                    ins2 = t_.matmul(psum[:, 1536:2048], onemb[:, :],
                                     xbf[:, DC * 512 + dcn * 512:DC * 512 + (dcn + 1) * 512],
                                     start=(dcn == 0), stop=(dcn == DC - 1))
                ins2.then_inc(_lnp, 1)

            def _bcast(tbn):
                t_.wait_ge(_ln, 5 * tbn + 5)
                if tbn == 0:
                    t_.wait_ge(_aev, 128)
                else:
                    t_.wait_ge(_lnn, 16 * tbn)
                t_.matmul(psum[:, 3072:3584], onesr[0:1, :], lrow[0:1, 1536:2048],
                          start=True, stop=True)
                ins2 = t_.matmul(psum[:, 3584:4096], onesr[0:1, :],
                                 lrow[0:1, 2048:2560], start=True, stop=True)
                ins2.then_inc(_lnp, 1)

            for qb in range(TB):
                g8 = 4 + qb
                t_.wait_ge(_ao0 if g8 % 2 == 0 else _ao1, 16 * 8 * (g8 // 2 + 1))
                for dc in range(DC):
                    pidx = 32 + qb * 8 + dc
                    if pidx >= 2:
                        t_.wait_ge(_pev, pidx - 1)
                    ins = None
                    for c8 in range(8):
                        ins = t_.matmul(
                            psum[:, (pidx % 2) * 512:(pidx % 2) * 512 + 512],
                            xbf[:, DC * 512 * 2 + c8 * 1024 + dc * 128:DC * 512 * 2 + c8 * 1024 + dc * 128 + 128],
                            aoT[:, (g8 % 2) * 16 * 512 + c8 * 512:(g8 % 2) * 16 * 512 + c8 * 512 + 512],
                            start=(c8 == 0), stop=(c8 == 7))
                    ins.then_inc(_pmm, 1)
                    if qb > 0 and dc == 1:
                        _bcast(qb - 1)
                _sums(qb)
            _bcast(TB - 1)
            t_.wait_ge(_mx, 33)
            t_.transpose(psum[0:1, 2048:2176], mfold[0:128, 0:1],
                         id128[:, :]).then_inc(_mxp, 1)
    return nc


_NC_CACHE = {}


def _nc(name):
    if name not in _NC_CACHE:
        _NC_CACHE[name] = globals()["build_" + name]()
    return _NC_CACHE[name]


def _to_fm(a):
    return np.ascontiguousarray(
        a.T.reshape(DC, 128, T).transpose(1, 0, 2).reshape(128, DC * T))


def _from_fm(a):
    return a.reshape(128, DC, T).transpose(1, 0, 2).reshape(D, T).T


def kernel(L0_attn_x, L0_attn_Wq, L0_attn_Wkv_down, L0_attn_Wk_up,
           L0_attn_Wv_up, L0_attn_Wo, L0_ffn_w_gate, L0_ffn_w_up,
           L0_ffn_w_down):
    cores = list(range(8))
    x = np.ascontiguousarray(np.asarray(L0_attn_x, np.float32))
    w_up = np.ascontiguousarray(np.asarray(L0_ffn_w_up, np.float32))
    w_gate = np.ascontiguousarray(np.asarray(L0_ffn_w_gate, np.float32))
    w_down = np.ascontiguousarray(np.asarray(L0_ffn_w_down, np.float32))

    in0 = [{"wu": np.ascontiguousarray(w_up[c * 384:(c + 1) * 384]),
            "wg": np.ascontiguousarray(w_gate[c * 384:(c + 1) * 384]),
            "wd": np.ascontiguousarray(w_down[c * 128:(c + 1) * 128])}
           for c in cores]
    r0 = run_bass_kernel_spmd(_nc("l0"), in0, core_ids=cores).results
    wupdq = np.concatenate([np.asarray(r0[c]["wuo"]) for c in cores], 0)
    wgatedq = np.concatenate([np.asarray(r0[c]["wgo"]) for c in cores], 0)
    wdowndq = np.concatenate([np.asarray(r0[c]["wdo"]) for c in cores], 0)

    wqn = np.ascontiguousarray(np.asarray(L0_attn_Wq, np.float32))
    wkvT = np.ascontiguousarray(np.asarray(L0_attn_Wkv_down, np.float32)
                                .T.reshape(8, 128, 64).transpose(1, 0, 2)
                                .reshape(128, 512))
    wku = np.ascontiguousarray(np.asarray(L0_attn_Wk_up, np.float32))
    wvuT = np.ascontiguousarray(np.asarray(L0_attn_Wv_up, np.float32).T)
    woT = np.ascontiguousarray(np.asarray(L0_attn_Wo, np.float32).T)
    in1 = [{"xT": _to_fm(x[c]), "wqn": wqn, "wkvT": wkvT, "wku": wku,
            "wvuT": wvuT, "woT": woT} for c in cores]
    r1 = run_bass_kernel_spmd(_nc("l1"), in1, core_ids=cores).results
    streams = [np.asarray(r1[c]["streamout"]) for c in cores]
    maxx8 = np.concatenate([np.asarray(r1[c]["maxx"]) for c in cores],
                           axis=1)

    in2 = [{"stream": streams[c], "maxx": maxx8, "wup": wupdq,
            "wgate": wgatedq} for c in cores]
    r2 = run_bass_kernel_spmd(_nc("l2"), in2, core_ids=cores).results
    maxug8 = np.concatenate(
        [np.asarray(r2[c]["maxug"])[0:1, :] for c in cores] +
        [np.asarray(r2[c]["maxug"])[1:2, :] for c in cores], axis=0).reshape(1, 16)

    in3 = [{"gatein": np.asarray(r2[c]["gateout"]), "maxug8": maxug8}
           for c in cores]
    r3 = run_bass_kernel_spmd(_nc("l3"), in3, core_ids=cores).results
    maxg28 = np.concatenate([np.asarray(r3[c]["maxg2"]) for c in cores],
                            axis=1)

    in4 = [{"g2in": np.asarray(r3[c]["g2out"]),
            "upin": np.asarray(r2[c]["upout"]),
            "maxg28": maxg28, "maxug8": maxug8} for c in cores]
    r4 = run_bass_kernel_spmd(_nc("l4"), in4, core_ids=cores).results
    maxo8 = np.concatenate([np.asarray(r4[c]["maxo"]) for c in cores], axis=1)

    maxo8r = np.ascontiguousarray(np.broadcast_to(maxo8, (128, 8)))
    wP = np.ascontiguousarray(
        wdowndq.reshape(8, 128, 24, 128).transpose(3, 0, 2, 1).reshape(128, 24576))
    in5 = [{"oin": np.asarray(r4[c]["oout"]), "maxo8": maxo8r,
            "streamin": streams[c], "wdown": wP} for c in cores]
    r5 = run_bass_kernel_spmd(_nc("l5"), in5, core_ids=cores).results

    out = np.empty((8, T, D), np.float32)
    for c in cores:
        out[c] = _from_fm(np.asarray(r5[c]["final"]))
    return out


# version 09
# speedup vs baseline: 1.3489x; 1.0360x over previous
from contextlib import ExitStack

import numpy as np
import concourse.bass as bass
import concourse.mybir as mybir
from concourse.bass_utils import run_bass_kernel_spmd

F32 = mybir.dt.float32
BF16 = mybir.dt.bfloat16
I16 = mybir.dt.int16
U16 = mybir.dt.uint16
I8 = mybir.dt.int8
AF = mybir.ActivationFunctionType
ALU = mybir.AluOpType
AX = mybir.AxisListType.X

T = 2048
D = 1024
NH = 8
DH = 256
LR = 64
FF = 3072
DC = D // 128
FC = FF // 128
TB = T // 512

QMAX = 32767.0
WQMAX = 7.0
EPS_Q = 1e-12
EPS_LN = 1e-5


class Ctr:

    def __init__(self, sem):
        self.sem = sem
        self.v = 0

    def inc(self, n=1):
        self.v += n
        return self.v



def build_l2():
    nc = bass.Bass("TRN2", target_bir_lowering=False, debug=False)
    stream = nc.dram_tensor("stream", [128, DC * T], F32, kind="ExternalInput")
    maxx = nc.dram_tensor("maxx", [1, 8], F32, kind="ExternalInput")
    wup = nc.dram_tensor("wup", [FF, D], BF16, kind="ExternalInput")
    wgate = nc.dram_tensor("wgate", [FF, D], BF16, kind="ExternalInput")
    upout = nc.dram_tensor("upout", [FF, T], BF16, kind="ExternalOutput")
    gateout = nc.dram_tensor("gateout", [FF, T], BF16, kind="ExternalOutput")
    maxug = nc.dram_tensor("maxug", [2, 1], F32, kind="ExternalOutput")
    ones_f32 = nc.inline_tensor(np.ones((1, 128), np.float32), name="ones_f32")
    ident128 = nc.inline_tensor(np.eye(128, dtype=np.float32), name="ident128")

    NT = 2 * FC * TB

    with ExitStack() as es:
        sb = lambda *a: es.enter_context(nc.sbuf_tensor(*a))
        ps = lambda *a: es.enter_context(nc.psum_tensor(*a))
        sem = lambda n: es.enter_context(nc.semaphore(n))
        xq = sb("xq", [128, DC * T], BF16)
        sstage = sb("sstage", [128, 2 * T], F32)
        i16s = sb("i16s", [128, T], I16)
        warena = sb("warena", [128, 2 * DC * 512], BF16)
        stage = sb("stage", [128, 4 * 512], BF16)
        maxcol = sb("maxcol", [128, NT], F32)
        maxin = sb("maxin", [1, 8], F32)
        scal = sb("scal", [1, 2], F32)
        sxb = sb("sxb", [128, 2], F32)
        onesr = sb("onesr", [1, 128], F32)
        id128 = sb("id128", [128, 128], F32)
        mfold = sb("mfold", [128, 2], F32)
        mout = sb("mout", [2, 1], F32)
        ps_mm = ps("ps_mm", [128, 4 * 512], F32)
        ps_bc = ps("ps_bc", [128, 2], F32)
        ps_tr = ps("ps_tr", [2, 128], F32)
        _cfg, _scal, _bc, _q = (sem(n) for n in
                                ("s_cfg", "s_scal", "s_bc", "s_q"))
        _i0, _i1 = sem("s_i0"), sem("s_i1")
        _w0, _w1 = sem("s_w0"), sem("s_w1")
        _mm, _ev, _red = (sem(n) for n in ("s_mm", "s_ev", "s_red"))
        _stl = [sem(f"s_st{k}") for k in range(4)]
        _stf = sem("s_stf")
        block = es.enter_context(nc.Block())
        @block.gpsimd
        def _(g):
            g.dma_start(out=maxin[:, :], in_=maxx[:, :]).then_inc(_cfg, 16)
            g.dma_start(out=onesr[:, :], in_=ones_f32[:, :]).then_inc(_cfg, 16)
            g.dma_start(out=id128[:, :], in_=ident128[:, :]).then_inc(_cfg, 16)
            for c in range(DC):
                if c >= 2:
                    g.wait_ge(_q, 2 * (c - 1))
                g.dma_start(out=sstage[:, (c % 2) * T:(c % 2) * T + T],
                            in_=stream[:, c * T:(c + 1) * T]
                            ).then_inc(_i0 if c % 2 == 0 else _i1, 16)
            for k in range(4):
                g.wait_ge(_stl[k], 16 * (NT // 4))
            g.wait_ge(_stf, 16)

        @block.sync
        def _(sy):
            for gidx in range(12):
                ub, W = gidx // 2, gidx % 2
                wt = wup if W == 0 else wgate
                half = gidx % 2
                if gidx >= 2:
                    sy.wait_ge(_mm, 16 * (gidx - 1))
                for dc in range(DC):
                    base = half * DC * 512 + dc * 512
                    sy.dma_start(
                        out=warena[:, base:base + 512],
                        in_=wt[ub * 512:(ub + 1) * 512, dc * 128:(dc + 1) * 128],
                        transpose=True,
                    ).then_inc(_w0 if half == 0 else _w1, 16)

        @block.vector
        def _(v):
            v.wait_ge(_cfg, 48)
            v.tensor_reduce(scal[0:1, 0:1], maxin[0:1, 0:8], AX,
                            ALU.max).then_inc(_scal, 1)
            v.wait_ge(_scal, 1)
            v.tensor_scalar(scal[0:1, 0:1], scal[0:1, 0:1], 1.0 / QMAX, EPS_Q,
                            ALU.mult, ALU.max).then_inc(_scal, 1)
            v.wait_ge(_scal, 2)
            v.reciprocal(scal[0:1, 1:2], scal[0:1, 0:1]).then_inc(_scal, 1)
            v.wait_ge(_bc, 1)
            v.tensor_copy(sxb[:, :], ps_bc[:, :]).then_inc(_scal, 1)
            v.wait_ge(_scal, 4)
            for c in range(DC):
                v.wait_ge(_i0 if c % 2 == 0 else _i1, 16 * (c // 2 + 1))
                if c > 0:
                    v.wait_ge(_q, 2 * c)
                st_sl = sstage[:, (c % 2) * T:(c % 2) * T + T]
                v.tensor_scalar(i16s[:, :], st_sl, sxb[:, 1:2], None,
                                ALU.mult).then_inc(_q, 1)
                v.wait_ge(_q, 2 * c + 1)
                v.tensor_scalar(xq[:, c * T:(c + 1) * T], i16s[:, :],
                                sxb[:, 0:1], None, ALU.mult).then_inc(_q, 1)
            for t in range(NT):
                v.wait_ge(_ev, t + 1)
                gidx, sub = t // 16, t % 16
                col = (gidx % 2) * (NT // 2) + (gidx // 2) * 16 + sub
                v.tensor_reduce(maxcol[:, col:col + 1],
                                ps_mm[:, (t % 4) * 512:(t % 4) * 512 + 512],
                                AX, ALU.max,
                                apply_absolute_value=True).then_inc(_red, 1)
            v.wait_ge(_red, NT)
            v.tensor_reduce(mfold[:, 0:1], maxcol[:, 0:NT // 2], AX, ALU.max)
            v.tensor_reduce(mfold[:, 1:2], maxcol[:, NT // 2:NT], AX,
                            ALU.max).then_inc(_red, 1)
            v.wait_ge(_bc, 2)
            v.tensor_reduce(mout[0:2, 0:1], ps_tr[0:2, 0:128], AX,
                            ALU.max).then_inc(_red, 1)

        @block.tensor
        def _(t_):
            t_.wait_ge(_cfg, 48)
            t_.wait_ge(_scal, 3)
            t_.matmul(ps_bc[:, :], onesr[0:1, :], scal[0:1, 0:2], start=True,
                      stop=True).then_inc(_bc, 1)
            t_.wait_ge(_q, 2 * DC)
            for t in range(NT):
                gidx, sub = t // 16, t % 16
                half = gidx % 2
                tb, uc4 = sub // 4, sub % 4
                if sub == 0:
                    t_.wait_ge(_w0 if half == 0 else _w1,
                               16 * DC * (gidx // 2 + 1))
                if t >= 4:
                    t_.wait_ge(_ev, t - 3)
                    t_.wait_ge(_red, t - 3)
                ins = None
                for dc in range(DC):
                    base = half * DC * 512 + dc * 512 + uc4 * 128
                    ins = t_.matmul(
                        ps_mm[:, (t % 4) * 512:(t % 4) * 512 + 512],
                        warena[:, base:base + 128],
                        xq[:, dc * T + tb * 512:dc * T + tb * 512 + 512],
                        start=(dc == 0), stop=(dc == DC - 1),
                    )
                ins.then_inc(_mm, 1)
            t_.wait_ge(_red, NT + 1)
            t_.transpose(ps_tr[0:2, 0:128], mfold[0:128, 0:2],
                         id128[:, :]).then_inc(_bc, 1)

        @block.scalar
        def _(s):
            for t in range(NT):
                s.wait_ge(_mm, t + 1)
                if t >= 4:
                    s.wait_ge(_stl[t % 4], 16 * (t // 4))
                s.activation(stage[:, (t % 4) * 512:(t % 4) * 512 + 512],
                             ps_mm[:, (t % 4) * 512:(t % 4) * 512 + 512],
                             AF.Copy).then_inc(_ev, 1)
                s.wait_ge(_ev, t + 1)
                gidx, sub = t // 16, t % 16
                ub, W = gidx // 2, gidx % 2
                tb, uc4 = sub // 4, sub % 4
                uc = ub * 4 + uc4
                dst = upout if W == 0 else gateout
                s.dma_start(
                    out=dst[uc * 128:(uc + 1) * 128, tb * 512:(tb + 1) * 512],
                    in_=stage[:, (t % 4) * 512:(t % 4) * 512 + 512],
                ).then_inc(_stl[t % 4], 16)
            s.wait_ge(_red, NT + 2)
            s.dma_start(out=maxug[0:2, 0:1], in_=mout[0:2, 0:1]).then_inc(_stf, 16)

    return nc



def build_l0():
    nc = bass.Bass("TRN2", target_bir_lowering=False, debug=False)
    wu = nc.dram_tensor("wu", [384, D], F32, kind="ExternalInput")
    wg = nc.dram_tensor("wg", [384, D], F32, kind="ExternalInput")
    wd = nc.dram_tensor("wd", [128, FF], F32, kind="ExternalInput")
    wuo = nc.dram_tensor("wuo", [384, D], BF16, kind="ExternalOutput")
    wgo = nc.dram_tensor("wgo", [384, D], BF16, kind="ExternalOutput")
    wdo = nc.dram_tensor("wdo", [128, FF], BF16, kind="ExternalOutput")
    tiles = [(wu, wuo, 0, D), (wu, wuo, 1, D), (wu, wuo, 2, D),
             (wg, wgo, 0, D), (wg, wgo, 1, D), (wg, wgo, 2, D),
             (wd, wdo, 0, FF)]
    with ExitStack() as es:
        sb = lambda *a: es.enter_context(nc.sbuf_tensor(*a))
        sem = lambda n: es.enter_context(nc.semaphore(n))
        wstage = sb("wstage", [128, 2 * FF], F32)
        t32 = sb("t32", [128, FF], F32)
        q8 = sb("q8", [128, FF], I8)
        sraw = sb("sraw", [128, FF // 32], F32)
        sinv = sb("sinv", [128, FF // 32], F32)
        ostage = sb("ostage", [128, 2 * FF], BF16)
        _i0, _i1 = sem("s_i0"), sem("s_i1")
        _c = sem("s_c")
        _o0, _o1 = sem("s_o0"), sem("s_o1")
        block = es.enter_context(nc.Block())

        @block.gpsimd
        def _(g):
            for k, (src_t, _, r, I) in enumerate(tiles):
                par, cnt = (_i0, k // 2 + 1) if k % 2 == 0 else (_i1, k // 2 + 1)
                if k >= 2:
                    g.wait_ge(_c, 6 * (k - 1))
                g.dma_start(out=wstage[:, (k % 2) * FF:(k % 2) * FF + I],
                            in_=src_t[r * 128:(r + 1) * 128, 0:I]).then_inc(par, 16)
            g.wait_ge(_o0, 16 * 4)
            g.wait_ge(_o1, 16 * 3)

        @block.vector
        def _(v):
            for k, (_, _, r, I) in enumerate(tiles):
                par, cnt = (_i0, 16 * (k // 2 + 1)) if k % 2 == 0 else (_i1, 16 * (k // 2 + 1))
                v.wait_ge(par, cnt)
                if k >= 2:
                    spar, scnt = (_o0, 16 * (k // 2)) if k % 2 == 0 else (_o1, 16 * (k // 2))
                    v.wait_ge(spar, scnt)
                if k > 0:
                    v.wait_ge(_c, 6 * k)
                w_sl = wstage[:, (k % 2) * FF:(k % 2) * FF + I]
                nb = I // 32
                ww = w_sl.rearrange("p (b w) -> p b w", w=32)
                v.tensor_reduce(sraw[:, 0:nb], ww, AX, ALU.max,
                                apply_absolute_value=True).then_inc(_c, 1)
                v.wait_ge(_c, 6 * k + 1)
                v.tensor_scalar(sraw[:, 0:nb], sraw[:, 0:nb], 1.0 / WQMAX, EPS_Q,
                                ALU.mult, ALU.max).then_inc(_c, 1)
                v.wait_ge(_c, 6 * k + 2)
                v.reciprocal(sinv[:, 0:nb], sraw[:, 0:nb]).then_inc(_c, 1)
                v.wait_ge(_c, 6 * k + 3)
                sb_b = sraw[:, 0:nb].unsqueeze(2).broadcast_to([128, nb, 32])
                si_b = sinv[:, 0:nb].unsqueeze(2).broadcast_to([128, nb, 32])
                tw = t32[:, 0:I].rearrange("p (b w) -> p b w", w=32)
                v.tensor_tensor(tw, ww, si_b, ALU.mult).then_inc(_c, 1)
                v.wait_ge(_c, 6 * k + 4)
                v.tensor_copy(q8[:, 0:I], t32[:, 0:I]).then_inc(_c, 1)
                v.wait_ge(_c, 6 * k + 5)
                o_sl = ostage[:, (k % 2) * FF:(k % 2) * FF + I]
                ow = o_sl.rearrange("p (b w) -> p b w", w=32)
                qw = q8[:, 0:I].rearrange("p (b w) -> p b w", w=32)
                v.tensor_tensor(ow, qw, sb_b, ALU.mult).then_inc(_c, 1)

        @block.sync
        def _(sy):
            for k, (_, dst_t, r, I) in enumerate(tiles):
                sy.wait_ge(_c, 6 * (k + 1))
                par = _o0 if k % 2 == 0 else _o1
                sy.dma_start(out=dst_t[r * 128:(r + 1) * 128, 0:I],
                             in_=ostage[:, (k % 2) * FF:(k % 2) * FF + I]
                             ).then_inc(par, 16)
    return nc



def build_l3():
    nc = bass.Bass("TRN2", target_bir_lowering=False, debug=False)
    gatein = nc.dram_tensor("gatein", [FF, T], BF16, kind="ExternalInput")
    maxug8 = nc.dram_tensor("maxug8", [1, 16], F32, kind="ExternalInput")
    g2out = nc.dram_tensor("g2out", [FF, T], BF16, kind="ExternalOutput")
    maxg2 = nc.dram_tensor("maxg2", [1, 1], F32, kind="ExternalOutput")
    ones_f32 = nc.inline_tensor(np.ones((1, 128), np.float32), name="ones_f32")
    ident128 = nc.inline_tensor(np.eye(128, dtype=np.float32), name="ident128")
    with ExitStack() as es:
        sb = lambda *a: es.enter_context(nc.sbuf_tensor(*a))
        ps = lambda *a: es.enter_context(nc.psum_tensor(*a))
        sem = lambda n: es.enter_context(nc.semaphore(n))
        gstage = sb("gstage", [128, 2 * T], BF16)
        i16g = sb("i16g", [128, 2 * T], I16)
        sig = sb("sig", [128, T], F32)
        u16s = sb("u16s", [128, 2 * T], U16)
        g2f = sb("g2f", [128, T], F32)
        ostage = sb("ostage", [128, 2 * T], BF16)
        maxin = sb("maxin", [1, 16], F32)
        scal = sb("scal", [1, 4], F32)
        scb = sb("scb", [128, 4], F32)
        onesr = sb("onesr", [1, 128], F32)
        id128 = sb("id128", [128, 128], F32)
        maxcol = sb("maxcol", [128, FC], F32)
        mfold = sb("mfold", [128, 1], F32)
        mo = sb("mo", [1, 1], F32)
        ps_bc = ps("ps_bc", [128, 4], F32)
        ps_tr = ps("ps_tr", [1, 128], F32)
        _cfg, _scal, _bc = sem("s_cfg"), sem("s_scal"), sem("s_bc")
        _i0, _i1 = sem("s_i0"), sem("s_i1")
        _qa = sem("s_qa")
        _dv = sem("s_dv")
        _red = sem("s_red")
        _o0, _o1 = sem("s_o0"), sem("s_o1")
        _stf = sem("s_stf")
        block = es.enter_context(nc.Block())

        @block.gpsimd
        def _(g):
            g.dma_start(out=maxin[:, :], in_=maxug8[:, :]).then_inc(_cfg, 16)
            g.dma_start(out=onesr[:, :], in_=ones_f32[:, :]).then_inc(_cfg, 16)
            g.dma_start(out=id128[:, :], in_=ident128[:, :]).then_inc(_cfg, 16)
            for c in range(FC):
                if c >= 2:
                    g.wait_ge(_qa, 3 * (c - 2) + 1)
                g.dma_start(out=gstage[:, (c % 2) * T:(c % 2) * T + T],
                            in_=gatein[c * 128:(c + 1) * 128, :]
                            ).then_inc(_i0 if c % 2 == 0 else _i1, 16)
            g.wait_ge(_o0, 16 * (FC // 2))
            g.wait_ge(_o1, 16 * (FC // 2))
            g.wait_ge(_stf, 16)

        @block.scalar
        def _(s):
            s.wait_ge(_scal, 5)
            for c in range(FC):
                sl = (c % 2) * T
                s.wait_ge(_i0 if c % 2 == 0 else _i1, 16 * (c // 2 + 1))
                if c >= 2:
                    s.wait_ge(_dv, 3 * (c - 2) + 1)
                s.activation(i16g[:, sl:sl + T], gstage[:, sl:sl + T],
                             AF.Copy, scale=scb[:, 1:2]).then_inc(_qa, 1)
                s.wait_ge(_qa, 3 * c + 1)
                s.activation(sig[:, :], i16g[:, sl:sl + T], AF.Sigmoid,
                             bias=0.0, scale=scb[:, 0:1]).then_inc(_qa, 1)
                s.wait_ge(_qa, 3 * c + 2)
                s.activation(u16s[:, sl:sl + T], sig[:, :], AF.Copy,
                             scale=65536.0).then_inc(_qa, 1)

        @block.vector
        def _(v):
            v.wait_ge(_cfg, 48)
            v.tensor_reduce(scal[0:1, 0:1], maxin[0:1, 8:16], AX,
                            ALU.max).then_inc(_scal, 1)
            v.wait_ge(_scal, 1)
            v.tensor_scalar(scal[0:1, 0:1], scal[0:1, 0:1], 1.0 / QMAX, EPS_Q,
                            ALU.mult, ALU.max).then_inc(_scal, 1)
            v.wait_ge(_scal, 2)
            v.reciprocal(scal[0:1, 1:2], scal[0:1, 0:1]).then_inc(_scal, 1)
            v.wait_ge(_scal, 3)
            v.tensor_scalar(scal[0:1, 2:3], scal[0:1, 0:1], 1.0 / 65536.0, None,
                            ALU.mult).then_inc(_scal, 1)
            v.wait_ge(_bc, 1)
            v.tensor_copy(scb[:, :], ps_bc[:, :]).then_inc(_scal, 1)
            for c in range(FC):
                sl = (c % 2) * T
                v.wait_ge(_qa, 3 * c + 3)
                if c > 0:
                    v.wait_ge(_dv, 3 * (c - 1) + 3)
                v.scalar_tensor_tensor(g2f[:, :], i16g[:, sl:sl + T],
                                       scb[:, 2:3], u16s[:, sl:sl + T],
                                       ALU.mult, ALU.mult).then_inc(_dv, 1)
                v.wait_ge(_dv, 3 * c + 1)
                v.tensor_reduce(maxcol[:, c:c + 1], g2f[:, :], AX, ALU.max,
                                apply_absolute_value=True).then_inc(_red, 1)
                v.wait_ge(_dv, 3 * c + 1)
                if c >= 2:
                    v.wait_ge(_o0 if c % 2 == 0 else _o1, 16 * (c // 2))
                v.tensor_copy(ostage[:, sl:sl + T], g2f[:, :]).then_inc(_dv, 2)
            v.wait_ge(_red, FC)
            v.tensor_reduce(mfold[:, 0:1], maxcol[:, :], AX,
                            ALU.max).then_inc(_red, 1)
            v.wait_ge(_bc, 2)
            v.tensor_reduce(mo[0:1, 0:1], ps_tr[0:1, 0:128], AX,
                            ALU.max).then_inc(_red, 1)

        @block.tensor
        def _(t_):
            t_.wait_ge(_cfg, 48)
            t_.wait_ge(_scal, 4)
            t_.matmul(ps_bc[:, :], onesr[0:1, :], scal[0:1, 0:4], start=True,
                      stop=True).then_inc(_bc, 1)
            t_.wait_ge(_red, FC + 1)
            t_.transpose(ps_tr[0:1, 0:128], mfold[0:128, 0:1],
                         id128[:, :]).then_inc(_bc, 1)

        @block.sync
        def _(sy):
            for c in range(FC):
                sy.wait_ge(_dv, 3 * c + 3)
                sy.dma_start(out=g2out[c * 128:(c + 1) * 128, :],
                             in_=ostage[:, (c % 2) * T:(c % 2) * T + T]
                             ).then_inc(_o0 if c % 2 == 0 else _o1, 16)
            sy.wait_ge(_red, FC + 2)
            sy.dma_start(out=maxg2[0:1, 0:1], in_=mo[0:1, 0:1]).then_inc(_stf, 16)
    return nc



def build_l4():
    nc = bass.Bass("TRN2", target_bir_lowering=False, debug=False)
    g2in = nc.dram_tensor("g2in", [FF, T], BF16, kind="ExternalInput")
    upin = nc.dram_tensor("upin", [FF, T], BF16, kind="ExternalInput")
    maxg28 = nc.dram_tensor("maxg28", [1, 8], F32, kind="ExternalInput")
    maxug8 = nc.dram_tensor("maxug8", [1, 16], F32, kind="ExternalInput")
    oout = nc.dram_tensor("oout", [FF, T], BF16, kind="ExternalOutput")
    maxo = nc.dram_tensor("maxo", [1, 1], F32, kind="ExternalOutput")
    ones_f32 = nc.inline_tensor(np.ones((1, 128), np.float32), name="ones_f32")
    ident128 = nc.inline_tensor(np.eye(128, dtype=np.float32), name="ident128")
    with ExitStack() as es:
        sb = lambda *a: es.enter_context(nc.sbuf_tensor(*a))
        ps = lambda *a: es.enter_context(nc.psum_tensor(*a))
        sem = lambda n: es.enter_context(nc.semaphore(n))
        gstage = sb("gstage", [128, 2 * T], BF16)
        ustage = sb("ustage", [128, 2 * T], BF16)
        i16g = sb("i16g", [128, 2 * T], I16)
        i16u = sb("i16u", [128, 2 * T], I16)
        of = sb("of", [128, T], F32)
        ostage = sb("ostage", [128, 2 * T], BF16)
        maxing = sb("maxing", [1, 8], F32)
        maxinu = sb("maxinu", [1, 16], F32)
        scal = sb("scal", [1, 4], F32)
        scb = sb("scb", [128, 4], F32)
        onesr = sb("onesr", [1, 128], F32)
        id128 = sb("id128", [128, 128], F32)
        maxcol = sb("maxcol", [128, FC], F32)
        mfold = sb("mfold", [128, 1], F32)
        mo = sb("mo", [1, 1], F32)
        ps_bc = ps("ps_bc", [128, 4], F32)
        ps_tr = ps("ps_tr", [1, 128], F32)
        _cfg, _scal, _bc = sem("s_cfg"), sem("s_scal"), sem("s_bc")
        _ig0, _ig1 = sem("s_ig0"), sem("s_ig1")
        _iu0, _iu1 = sem("s_iu0"), sem("s_iu1")
        _qa = sem("s_qa")
        _dv = sem("s_dv")
        _red = sem("s_red")
        _o0, _o1 = sem("s_o0"), sem("s_o1")
        _stf = sem("s_stf")
        block = es.enter_context(nc.Block())

        @block.gpsimd
        def _(g):
            g.dma_start(out=maxing[:, :], in_=maxg28[:, :]).then_inc(_cfg, 16)
            g.dma_start(out=maxinu[:, :], in_=maxug8[:, :]).then_inc(_cfg, 16)
            g.dma_start(out=onesr[:, :], in_=ones_f32[:, :]).then_inc(_cfg, 16)
            g.dma_start(out=id128[:, :], in_=ident128[:, :]).then_inc(_cfg, 16)
            for c in range(FC):
                if c >= 2:
                    g.wait_ge(_qa, 2 * (c - 2) + 1)
                g.dma_start(out=gstage[:, (c % 2) * T:(c % 2) * T + T],
                            in_=g2in[c * 128:(c + 1) * 128, :]
                            ).then_inc(_ig0 if c % 2 == 0 else _ig1, 16)
            g.wait_ge(_o0, 16 * (FC // 2))
            g.wait_ge(_o1, 16 * (FC // 2))
            g.wait_ge(_stf, 16)

        @block.sync
        def _(sy):
            for c in range(FC):
                if c >= 2:
                    sy.wait_ge(_qa, 2 * (c - 2) + 2)
                sy.dma_start(out=ustage[:, (c % 2) * T:(c % 2) * T + T],
                             in_=upin[c * 128:(c + 1) * 128, :]
                             ).then_inc(_iu0 if c % 2 == 0 else _iu1, 16)
                if c >= 1:
                    cp = c - 1
                    sy.wait_ge(_dv, 3 * cp + 3)
                    sy.dma_start(out=oout[cp * 128:(cp + 1) * 128, :],
                                 in_=ostage[:, (cp % 2) * T:(cp % 2) * T + T]
                                 ).then_inc(_o0 if cp % 2 == 0 else _o1, 16)
            cp = FC - 1
            sy.wait_ge(_dv, 3 * cp + 3)
            sy.dma_start(out=oout[cp * 128:(cp + 1) * 128, :],
                         in_=ostage[:, (cp % 2) * T:(cp % 2) * T + T]
                         ).then_inc(_o0 if cp % 2 == 0 else _o1, 16)
            sy.wait_ge(_red, FC + 2)
            sy.dma_start(out=maxo[0:1, 0:1], in_=mo[0:1, 0:1]).then_inc(_stf, 16)

        @block.scalar
        def _(s):
            s.wait_ge(_scal, 6)
            for c in range(FC):
                sl = (c % 2) * T
                s.wait_ge(_ig0 if c % 2 == 0 else _ig1, 16 * (c // 2 + 1))
                s.wait_ge(_iu0 if c % 2 == 0 else _iu1, 16 * (c // 2 + 1))
                if c >= 2:
                    s.wait_ge(_dv, 3 * (c - 2) + 1)
                s.activation(i16g[:, sl:sl + T], gstage[:, sl:sl + T],
                             AF.Copy, scale=scb[:, 0:1]).then_inc(_qa, 1)
                s.activation(i16u[:, sl:sl + T], ustage[:, sl:sl + T],
                             AF.Copy, scale=scb[:, 3:4]).then_inc(_qa, 1)

        @block.vector
        def _(v):
            v.wait_ge(_cfg, 64)
            v.tensor_reduce(scal[0:1, 0:1], maxing[0:1, 0:8], AX,
                            ALU.max).then_inc(_scal, 1)
            v.tensor_reduce(scal[0:1, 1:2], maxinu[0:1, 0:8], AX,
                            ALU.max).then_inc(_scal, 1)
            v.wait_ge(_scal, 2)
            v.tensor_scalar(scal[0:1, 0:1], scal[0:1, 0:1], 1.0 / QMAX, EPS_Q,
                            ALU.mult, ALU.max)
            v.tensor_scalar(scal[0:1, 1:2], scal[0:1, 1:2], 1.0 / QMAX, EPS_Q,
                            ALU.mult, ALU.max).then_inc(_scal, 1)
            v.wait_ge(_scal, 3)
            v.tensor_tensor(scal[0:1, 2:3], scal[0:1, 0:1], scal[0:1, 1:2],
                            ALU.mult)
            v.reciprocal(scal[0:1, 3:4], scal[0:1, 1:2]).then_inc(_scal, 1)
            v.wait_ge(_scal, 4)
            v.reciprocal(scal[0:1, 0:1], scal[0:1, 0:1]).then_inc(_scal, 1)
            v.wait_ge(_bc, 1)
            v.tensor_copy(scb[:, :], ps_bc[:, :]).then_inc(_scal, 1)
            for c in range(FC):
                sl = (c % 2) * T
                v.wait_ge(_qa, 2 * c + 2)
                if c > 0:
                    v.wait_ge(_dv, 3 * (c - 1) + 3)
                v.scalar_tensor_tensor(of[:, :], i16g[:, sl:sl + T],
                                       scb[:, 2:3], i16u[:, sl:sl + T],
                                       ALU.mult, ALU.mult).then_inc(_dv, 1)
                v.wait_ge(_dv, 3 * c + 1)
                v.tensor_reduce(maxcol[:, c:c + 1], of[:, :], AX, ALU.max,
                                apply_absolute_value=True).then_inc(_red, 1)
                v.wait_ge(_dv, 3 * c + 1)
                if c >= 2:
                    v.wait_ge(_o0 if c % 2 == 0 else _o1, 16 * (c // 2))
                v.tensor_copy(ostage[:, sl:sl + T], of[:, :]).then_inc(_dv, 2)
            v.wait_ge(_red, FC)
            v.tensor_reduce(mfold[:, 0:1], maxcol[:, :], AX,
                            ALU.max).then_inc(_red, 1)
            v.wait_ge(_bc, 2)
            v.tensor_reduce(mo[0:1, 0:1], ps_tr[0:1, 0:128], AX,
                            ALU.max).then_inc(_red, 1)

        @block.tensor
        def _(t_):
            t_.wait_ge(_cfg, 64)
            t_.wait_ge(_scal, 5)
            t_.matmul(ps_bc[:, :], onesr[0:1, :], scal[0:1, 0:4], start=True,
                      stop=True).then_inc(_bc, 1)
            t_.wait_ge(_red, FC + 1)
            t_.transpose(ps_tr[0:1, 0:128], mfold[0:128, 0:1],
                         id128[:, :]).then_inc(_bc, 1)
    return nc



def build_l5():
    import ml_dtypes as _mld
    nc = bass.Bass("TRN2", target_bir_lowering=False, debug=False)
    oin = nc.dram_tensor("oin", [FF, T], BF16, kind="ExternalInput")
    maxo8 = nc.dram_tensor("maxo8", [128, 8], F32, kind="ExternalInput")
    streamin = nc.dram_tensor("streamin", [128, DC * T], F32, kind="ExternalInput")
    wdown = nc.dram_tensor("wdown", [128, FC * 1024], BF16, kind="ExternalInput")
    final = nc.dram_tensor("final", [128, DC * T], F32, kind="ExternalOutput")
    ones_f32 = nc.inline_tensor(np.ones((1, 128), np.float32), name="ones_f32")
    onescol_b = nc.inline_tensor(np.ones((128, 1), dtype=_mld.bfloat16),
                                 name="onescol_b")
    SUMC = [1, 3, 5, 7]
    BCC = [2, 4, 6, 8]

    TAILSEQ = [("m", 0), ("m", 1), ("s", 0), ("m", 2), ("s", 1), ("m", 3),
               ("s", 2), ("m", 4), ("s", 3), ("m", 5), ("s", 4), ("m", 6),
               ("s", 5), ("m", 7), ("s", 6), ("s", 7)]
    TAILV = {}
    for _i, (_o, _d) in enumerate(TAILSEQ):
        if _o == "s":
            TAILV[_d] = _i + 1
    TSTORE = [max(TAILV[2 * j], TAILV[2 * j + 1]) for j in range(4)]

    with ExitStack() as es:
        sb = lambda *a: es.enter_context(nc.sbuf_tensor(*a))
        ps = lambda *a: es.enter_context(nc.psum_tensor(*a))
        sem = lambda n: es.enter_context(nc.semaphore(n))
        stream = sb("stream", [128, DC * T], F32)
        oq = sb("oq", [128, 2 * FC * 512], BF16)
        warena = sb("warena", [128, FC * 1024], BF16)
        i16o = sb("i16o", [128, 2 * 512], I16)
        maxin = sb("maxin", [128, 8], F32)
        scb = sb("scb", [128, 2], F32)
        onesr = sb("onesr", [1, 128], F32)
        onecb = sb("onecb", [128, 1], BF16)
        sbf = sb("sbf", [128, DC * 512], BF16)
        sqbf = sb("sqbf", [128, DC * 512], BF16)
        lrow = sb("lrow", [1, 5 * 512], F32)
        ps_mm = ps("ps_mm", [128, 4 * 512], F32)
        ps_ln = ps("ps_ln", [1, 1024], F32)
        ps_r = ps("ps_r", [128, 512], F32)
        ps_m = ps("ps_m", [128, 512], F32)
        _cfgm, _cfgo = sem("s_cfgm"), sem("s_cfgo")
        _scal = sem("s_scal")
        _q, _qi = sem("s_q"), sem("s_qi")
        _ldg = [sem(f"s_ldg{k}") for k in range(24)]
        _wd = [sem(f"s_wd{k}") for k in range(DC)]
        _sid = [sem(f"s_sid{k}") for k in range(DC)]
        _mm, _ev, _sb = sem("s_mm"), sem("s_ev"), sem("s_sb")
        _ln, _lnp, _lna = sem("s_ln"), sem("s_lnp"), sem("s_lna")
        _lnn = sem("s_lnn")
        _so = sem("s_so")
        block = es.enter_context(nc.Block())

        def _oqsl(k):
            tb, c = k // FC, k % FC
            return oq[:, (tb % 2) * FC * 512 + c * 512:(tb % 2) * FC * 512 + c * 512 + 512]

        def _oin_grp(eng, tb, g):
            if tb >= 2:
                eng.wait_ge(_mm, 8 * (tb - 2) + 8)
            base = (tb % 2) * FC * 512 + g * 4 * 512
            eng.dma_start(
                out=oq[:, base:base + 2048].rearrange("p (c t) -> p c t", t=512),
                in_=oin[g * 512:(g + 1) * 512,
                        tb * 512:(tb + 1) * 512].rearrange(
                            "(c p) t -> p c t", p=128)
            ).then_inc(_ldg[tb * 6 + g], 16)

        def _wd_dma(eng, dc):
            eng.dma_start(
                out=warena[:, dc * FC * 128:(dc + 1) * FC * 128],
                in_=wdown[:, dc * FC * 128:(dc + 1) * FC * 128],
            ).then_inc(_wd[dc], 16)

        def _sin_dma(eng, c):
            eng.dma_start(out=stream[:, c * T:(c + 1) * T],
                          in_=streamin[:, c * T:(c + 1) * T]
                          ).then_inc(_sid[c], 16)

        def _q_i16(s, k):
            if k % 4 == 0:
                s.wait_ge(_ldg[k // 4], 16)
            if k >= 2:
                s.wait_ge(_q, k - 1)
            s.activation(i16o[:, (k % 2) * 512:(k % 2) * 512 + 512],
                         _oqsl(k), AF.Copy, scale=scb[:, 1:2]).then_inc(_qi, 1)

        def _q_oq(v, k):
            v.wait_ge(_qi, k + 1)
            v.tensor_scalar(_oqsl(k), i16o[:, (k % 2) * 512:(k % 2) * 512 + 512],
                            scb[:, 0:1], None, ALU.mult).then_inc(_q, 1)

        def _store(eng, tbn, j):
            if tbn == TB - 1:
                eng.wait_ge(_lnn, 16 * tbn + TSTORE[j])
            else:
                eng.wait_ge(_lnn, 16 * tbn + 4 * (j + 1))
            sv = final[:, :].rearrange("p (c t) -> p c t", t=T)[
                :, 2 * j:2 * j + 2, tbn * 512:tbn * 512 + 512]
            eng.dma_start(out=sv,
                          in_=stream[:, :].rearrange("p (c t) -> p c t", t=T)[
                              :, 2 * j:2 * j + 2, tbn * 512:tbn * 512 + 512]
                          ).then_inc(_so, 16)

        @block.gpsimd
        def _(g):
            _sin_dma(g, 0)
            _sin_dma(g, 4)
            _sin_dma(g, 6)
            for grp in (1, 3, 5):
                _oin_grp(g, 1, grp)
            g.wait_ge(_so, 16 * 16)

        @block.sync
        def _(sy):
            sy.dma_start(out=onesr[:, :], in_=ones_f32[:, :]).then_inc(_cfgo, 16)
            sy.dma_start(out=onecb[:, :], in_=onescol_b[:, :]).then_inc(_cfgo, 16)
            _wd_dma(sy, 0)
            for grp in (0, 2, 4):
                _oin_grp(sy, 0, grp)
            _wd_dma(sy, 1)
            _sin_dma(sy, 2)
            _wd_dma(sy, 3)
            for grp in (0, 2, 4):
                _oin_grp(sy, 1, grp)
            _wd_dma(sy, 5)
            _sin_dma(sy, 5)
            _wd_dma(sy, 7)
            _sin_dma(sy, 7)
            for grp in range(6):
                _oin_grp(sy, 2, grp)
            _store(sy, 0, 1)
            _store(sy, 0, 3)
            for grp in range(6):
                _oin_grp(sy, 3, grp)
            _store(sy, 1, 1)
            _store(sy, 1, 3)
            for tbn in range(2, TB):
                _store(sy, tbn, 1)
                _store(sy, tbn, 3)

        @block.scalar
        def _(s):
            s.dma_start(out=maxin[:, :], in_=maxo8[:, :]).then_inc(_cfgm, 16)
            _wd_dma(s, 2)
            for grp in (1, 3, 5):
                _oin_grp(s, 0, grp)
            _wd_dma(s, 4)
            _sin_dma(s, 1)
            _wd_dma(s, 6)
            _sin_dma(s, 3)

            s.wait_ge(_scal, 3)
            for k in range(2 * FC):
                _q_i16(s, k)
            for tt in range(32):
                tb, dc = tt // 8, tt % 8
                s.wait_ge(_sb, tt + 1)
                s.activation(sqbf[:, dc * 512:dc * 512 + 512],
                             sbf[:, dc * 512:dc * 512 + 512],
                             AF.Square).then_inc(_lna, 1)
                if 1 <= dc <= 6 and tb + 1 in (2, 3):
                    for j in range(4):
                        _q_i16(s, (tb + 1) * FC + 4 * (dc - 1) + j)
                if tb > 0 and dc == 3:
                    _store(s, tb - 1, 0)
                if tb > 0 and dc == 6:
                    _store(s, tb - 1, 2)
                if dc == 7:
                    s.wait_ge(_ln, 5 * tb + 4)
                    if tb > 0:
                        s.wait_ge(_lnp, BCC[tb - 1])
                    s.activation(lrow[0:1, 1536:2048], lrow[0:1, 1024:1536],
                                 AF.Ln).then_inc(_lna, 1)
                    s.wait_ge(_lna, 10 * tb + 9)
                    s.activation(lrow[0:1, 1536:2048], lrow[0:1, 1536:2048],
                                 AF.Exp, bias=0.0,
                                 scale=-0.5).then_inc(_lna, 1)
            _store(s, TB - 1, 0)
            _store(s, TB - 1, 2)

        @block.vector
        def _(v):
            v.wait_ge(_cfgm, 16)
            v.tensor_reduce(scb[:, 0:1], maxin[:, :], AX,
                            ALU.max).then_inc(_scal, 1)
            v.wait_ge(_scal, 1)
            v.tensor_scalar(scb[:, 0:1], scb[:, 0:1], 1.0 / QMAX, EPS_Q,
                            ALU.mult, ALU.max).then_inc(_scal, 1)
            v.wait_ge(_scal, 2)
            v.reciprocal(scb[:, 1:2], scb[:, 0:1]).then_inc(_scal, 1)
            v.wait_ge(_scal, 3)
            for k in range(2 * FC):
                _q_oq(v, k)
            for tt in range(32):
                tb, dc = tt // 8, tt % 8
                if tb == 0:
                    v.wait_ge(_sid[dc], 16)
                v.wait_ge(_mm, tt + 1)
                sl = stream[:, dc * T + tb * 512:dc * T + tb * 512 + 512]
                v.tensor_tensor(sl, sl,
                                ps_mm[:, (tt % 4) * 512:(tt % 4) * 512 + 512],
                                ALU.add).then_inc(_ev, 1)
                v.wait_ge(_ev, tt + 1)
                v.tensor_copy(sbf[:, dc * 512:dc * 512 + 512],
                              sl).then_inc(_sb, 1)
                if 1 <= dc <= 6 and tb + 1 in (2, 3):
                    for j in range(4):
                        _q_oq(v, (tb + 1) * FC + 4 * (dc - 1) + j)

                def _norm_chunk(tbn, dcn, first):
                    if first:
                        v.wait_ge(_lnp, BCC[tbn])
                    sl2 = stream[:, dcn * T + tbn * 512:dcn * T + tbn * 512 + 512]
                    v.tensor_tensor(sl2, sl2, ps_r[:, 0:512],
                                    ALU.mult).then_inc(_lnn, 1)
                    v.wait_ge(_lnn, 16 * tbn + 2 * dcn + 1)
                    v.tensor_tensor(sl2, sl2, ps_m[:, 0:512],
                                    ALU.subtract).then_inc(_lnn, 1)

                if tb > 0 and dc >= 1:
                    if dc == 1:
                        _norm_chunk(tb - 1, 0, True)
                        _norm_chunk(tb - 1, 1, False)
                    else:
                        _norm_chunk(tb - 1, dc, False)
                if dc == 7:
                    b = 5 * tb
                    v.wait_ge(_lnp, SUMC[tb])
                    v.tensor_scalar(lrow[0:1, 0:512], ps_ln[0:1, 0:512],
                                    1.0 / D, None, ALU.mult)
                    v.tensor_scalar(lrow[0:1, 512:1024], ps_ln[0:1, 512:1024],
                                    1.0 / D, None, ALU.mult).then_inc(_ln, 1)
                    v.wait_ge(_ln, b + 1)
                    v.tensor_tensor(lrow[0:1, 1024:1536], lrow[0:1, 0:512],
                                    lrow[0:1, 0:512], ALU.mult).then_inc(_ln, 1)
                    v.wait_ge(_ln, b + 2)
                    v.tensor_tensor(lrow[0:1, 1024:1536], lrow[0:1, 512:1024],
                                    lrow[0:1, 1024:1536],
                                    ALU.subtract).then_inc(_ln, 1)
                    v.wait_ge(_ln, b + 3)
                    v.tensor_scalar(lrow[0:1, 1024:1536], lrow[0:1, 1024:1536],
                                    EPS_LN, None, ALU.add).then_inc(_ln, 1)
                    v.wait_ge(_lna, 10 * tb + 10)
                    if tb > 0:
                        v.wait_ge(_lnp, BCC[tb - 1])
                    v.tensor_tensor(lrow[0:1, 2048:2560], lrow[0:1, 0:512],
                                    lrow[0:1, 1536:2048], ALU.mult).then_inc(_ln, 1)
                    if tb == TB - 1:
                        v.wait_ge(_lnp, BCC[tb])
                        for op, dcn in TAILSEQ:
                            sl2 = stream[:, dcn * T + tb * 512:dcn * T + tb * 512 + 512]
                            if op == "m":
                                v.tensor_tensor(sl2, sl2, ps_r[:, 0:512],
                                                ALU.mult).then_inc(_lnn, 1)
                            else:
                                v.wait_ge(_lnn, 16 * tb + TAILSEQ.index(("m", dcn)) + 1)
                                v.tensor_tensor(sl2, sl2, ps_m[:, 0:512],
                                                ALU.subtract).then_inc(_lnn, 1)

        @block.tensor
        def _(t_):
            for tt in range(32):
                tb, dc = tt // 8, tt % 8
                if tb == 0:
                    t_.wait_ge(_wd[dc], 16)
                if dc != 0:
                    t_.wait_ge(_q, FC * (tb + 1))
                if tt >= 4:
                    t_.wait_ge(_ev, tt - 3)
                ins = None
                for ic in range(FC):
                    base = (dc * FC + ic) * 128
                    if dc == 0:
                        t_.wait_ge(_q, tb * FC + ic + 1)
                    ins = t_.matmul(
                        ps_mm[:, (tt % 4) * 512:(tt % 4) * 512 + 512],
                        warena[:, base:base + 128],
                        _oqsl(tb * FC + ic),
                        start=(ic == 0), stop=(ic == FC - 1))
                ins.then_inc(_mm, 1)

                def _sum_slice(dcn, stop):
                    t_.wait_ge(_sb, 8 * tb + dcn + 1)
                    t_.wait_ge(_lna, 10 * tb + dcn + 1)
                    if dcn == 0:
                        t_.wait_ge(_cfgo, 32)
                        if tb > 0:
                            t_.wait_ge(_ln, 5 * (tb - 1) + 1)
                    t_.matmul(ps_ln[0:1, 0:512], onecb[:, :],
                              sbf[:, dcn * 512:(dcn + 1) * 512],
                              start=(dcn == 0), stop=stop)
                    ins2 = t_.matmul(ps_ln[0:1, 512:1024], onecb[:, :],
                                     sqbf[:, dcn * 512:(dcn + 1) * 512],
                                     start=(dcn == 0), stop=stop)
                    if stop:
                        ins2.then_inc(_lnp, 1)

                def _bcast(tbn):
                    t_.wait_ge(_ln, 5 * tbn + 5)
                    if tbn > 0:
                        t_.wait_ge(_lnn, 16 * tbn)
                    t_.matmul(ps_r[:, 0:512], onesr[0:1, :],
                              lrow[0:1, 1536:2048], start=True, stop=True)
                    insb = t_.matmul(ps_m[:, 0:512], onesr[0:1, :],
                                     lrow[0:1, 2048:2560], start=True,
                                     stop=True)
                    insb.then_inc(_lnp, 1)

                if tb > 0 and dc == 0:
                    _bcast(tb - 1)
                if dc >= 1:
                    _sum_slice(dc - 1, False)
                if dc == 7:
                    _sum_slice(7, True)
                    if tt == 31:
                        _bcast(tb)
    return nc



def build_l1():
    import ml_dtypes as _mld
    nc = bass.Bass("TRN2", target_bir_lowering=False, debug=False)
    xT = nc.dram_tensor("xT", [128, DC * T], F32, kind="ExternalInput")
    wqn = nc.dram_tensor("wqn", [NH * DH, D], F32, kind="ExternalInput")
    wkvT = nc.dram_tensor("wkvT", [128, DC * LR], F32, kind="ExternalInput")
    wku = nc.dram_tensor("wku", [NH * DH, LR], F32, kind="ExternalInput")
    wvuT = nc.dram_tensor("wvuT", [LR, NH * DH], F32, kind="ExternalInput")
    woT = nc.dram_tensor("woT", [NH * DH, D], F32, kind="ExternalInput")
    streamout = nc.dram_tensor("streamout", [128, DC * T], F32, kind="ExternalOutput")
    maxx = nc.dram_tensor("maxx", [1, 1], F32, kind="ExternalOutput")
    attnout = nc.dram_tensor("attnout", [T, NH * DH], BF16)
    ones_f32 = nc.inline_tensor(np.ones((1, 128), np.float32), name="ones_f32")
    onesm_b = nc.inline_tensor(np.ones((128, 128), dtype=_mld.bfloat16),
                               name="onesm_b")
    identb = nc.inline_tensor(np.eye(128, dtype=_mld.bfloat16), name="identb")
    ident128 = nc.inline_tensor(np.eye(128, dtype=np.float32), name="ident128")
    SUMC = [1, 3, 5, 7]
    BCC = [2, 4, 6, 8]
    TAILSEQ = [("m", 0), ("m", 1), ("s", 0), ("m", 2), ("s", 1), ("m", 3),
               ("s", 2), ("m", 4), ("s", 3), ("m", 5), ("s", 4), ("m", 6),
               ("s", 5), ("m", 7), ("s", 6), ("s", 7)]
    TAILV = {}
    for _i, (_o, _d) in enumerate(TAILSEQ):
        if _o == "s":
            TAILV[_d] = _i + 1
    with ExitStack() as es:
        sb = lambda *a: es.enter_context(nc.sbuf_tensor(*a))
        sem = lambda n: es.enter_context(nc.semaphore(n))
        stream = sb("stream", [128, DC * T], F32)
        tmpk = sb("tmpk", [128, DC * LR], F32)
        xbf = sb("xbf", [128, DC * T], BF16)
        latbf = sb("latbf", [64, T], BF16)
        lattok = sb("lattok", [128, 16 * 65], BF16)
        kvw = sb("kvw", [128, DC * LR], BF16)
        wku_sb = sb("wku_sb", [128, 16 * LR], BF16)
        vup = sb("vup", [64, NH * DH], BF16)
        wqn_sb = sb("wqn_sb", [128, 2 * 2 * D], BF16)
        wqk = sb("wqk", [128, DC * LR], BF16)
        qhat = sb("qhat", [64, T], BF16)
        sqs = sb("sqs", [64, T], BF16)
        qn2 = sb("qn2", [1, T], F32)
        latn2 = sb("latn2", [1, T], F32)
        csc = sb("csc", [1, 8], F32)
        negcb = sb("negcb", [128, 1], F32)
        rcp = sb("rcp", [128, 4], F32)
        aostage = sb("aostage", [128, 4 * 256], BF16)
        u1 = sb("u1", [128, 16384], BF16)
        probsT = u1[:, 0:8192]
        obar = u1[:, 8192:8704]
        aoT = u1[:, 0:16384]
        wo_sb = sb("wo_sb", [128, 8 * 1024], BF16)
        onesr = sb("onesr", [1, 128], F32)
        onemb = sb("onemb", [128, 128], BF16)
        idb = sb("idb", [128, 128], BF16)
        id128 = sb("id128", [128, 128], F32)
        lrow = sb("lrow", [1, 5 * 512], F32)
        maxcol = sb("maxcol", [128, 32], F32)
        mfold = sb("mfold", [128, 1], F32)
        mxo = sb("mxo", [1, 1], F32)
        psum = es.enter_context(nc.psum_tensor("psum", [128, 4096], F32))
        B2 = psum[:, 1024:1536]
        B3c = psum[:, 1536:1537]
        _cfg = sem("s_cfg")
        _cfgk, _cfgk2 = sem("s_cfgk"), sem("s_cfgk2")
        _xsd = [sem(f"s_xsd{k}") for k in range(DC)]
        _xbd = [sem(f"s_xbd{k}") for k in range(DC)]
        _vb = sem("s_vb")
        _xs, _xb = sem("s_xs"), sem("s_xb")
        _wq0, _wq1 = sem("s_wq0"), sem("s_wq1")
        _lat, _late = sem("s_lat"), sem("s_late")
        _ltr, _lte = sem("s_ltr"), sem("s_lte")
        _mmq, _ewq = sem("s_mmq"), sem("s_ewq")
        _eqh, _sq = sem("s_eqh"), sem("s_sq")
        _nmm, _nev = sem("s_nmm"), sem("s_nev")
        _lm = sem("s_lm")
        _ch, _cha, _chb = sem("s_ch"), sem("s_cha"), sem("s_chb")
        _smm, _sexp = sem("s_smm"), sem("s_sexp")
        _amm, _oev = sem("s_amm"), sem("s_oev")
        _otr, _arc = sem("s_otr"), sem("s_arc")
        _avm, _aev = sem("s_avm"), sem("s_aev")
        _asl = [sem(f"s_as{k}") for k in range(4)]
        _ao0, _ao1 = sem("s_ao0"), sem("s_ao1")
        _wo0, _wo1 = sem("s_wo0"), sem("s_wo1")
        _pmm, _pev = sem("s_pmm"), sem("s_pev")
        _ln, _lnp, _lna = sem("s_ln"), sem("s_lnp"), sem("s_lna")
        _lnn, _sbc = sem("s_lnn"), sem("s_sbc")
        _mx, _mxp = sem("s_mx"), sem("s_mxp")
        _so = sem("s_so")
        block = es.enter_context(nc.Block())

        @block.gpsimd
        def _(g):
            g.dma_start(out=onesr[:, :], in_=ones_f32[:, :]).then_inc(_cfg, 16)
            g.dma_start(out=onemb[:, :], in_=onesm_b[:, :]).then_inc(_cfg, 16)
            g.dma_start(out=idb[:, :], in_=identb[:, :]).then_inc(_cfg, 16)
            g.dma_start(out=id128[:, :], in_=ident128[:, :]).then_inc(_cfg, 16)
            g.dma_start(out=wku_sb[:, :].rearrange("p (c n) -> p c n", n=LR),
                        in_=wku[:, :].rearrange("(c p) n -> p c n", p=128)
                        ).then_inc(_cfg, 16)
            g.dma_start(out=vup[:, :], in_=wvuT[:, :]).then_inc(_cfg, 16)
            for tc in range(16):
                g.memset(lattok[:, tc * 65 + 64:tc * 65 + 65], 1.0)
            g.sem_inc(_vb, 1)
            for h in range(NH):
                if h >= 2:
                    g.wait_ge(_mmq, (h - 2) * 5 + 1)
                g.dma_start(
                    out=wqn_sb[:, (h % 2) * 2 * D:(h % 2) * 2 * D + 2 * D]
                        .rearrange("p (c n) -> p c n", n=D),
                    in_=wqn[h * 256:(h + 1) * 256, :]
                        .rearrange("(c p) n -> p c n", p=128)
                ).then_inc(_wq0 if h % 2 == 0 else _wq1, 16)
            for c8 in range(8):
                g.dma_start(out=wo_sb[:, c8 * 1024:(c8 + 1) * 1024],
                            in_=woT[c8 * 128:(c8 + 1) * 128, :]
                            ).then_inc(_wo0, 16)
            g.wait_ge(_mmq, NH * 5)
            for c8 in range(8):
                g.dma_start(out=xbf[:, DC * 512 * 2 + c8 * 1024:DC * 512 * 2 + (c8 + 1) * 1024],
                            in_=woT[(8 + c8) * 128:(9 + c8) * 128, :]
                            ).then_inc(_wo1, 16)
            g.wait_ge(_so, 16 * (TB + 1))

        @block.sync
        def _(sy):
            sy.dma_start(out=tmpk[:, :], in_=wkvT[:, :]).then_inc(_cfgk, 16)
            for c in range(4):
                sy.dma_start(out=stream[:, c * T:(c + 1) * T],
                             in_=xT[:, c * T:(c + 1) * T]).then_inc(_xsd[c], 16)
            for h in range(NH):
                for qb in range(TB):
                    for qt in range(4):
                        aidx = (h * TB + qb) * 4 + qt
                        sy.wait_ge(_aev, aidx + 1)
                        sy.dma_start(
                            out=attnout[qb * 512 + qt * 128:qb * 512 + qt * 128 + 128,
                                        h * 256:(h + 1) * 256],
                            in_=aostage[:, (aidx % 4) * 256:(aidx % 4) * 256 + 256],
                        ).then_inc(_asl[aidx % 4], 16)
            for k in range(4):
                sy.wait_ge(_asl[k], 16 * 32)
            for g8 in range(8):
                H, qb = g8 // 4, g8 % 4
                if g8 >= 2:
                    sy.wait_ge(_pmm, 8 * (g8 - 1))
                for c8 in range(8):
                    cc = H * 8 + c8
                    sy.dma_start(
                        out=aoT[:, (g8 % 2) * 16 * 512 + c8 * 512:(g8 % 2) * 16 * 512 + c8 * 512 + 512],
                        in_=attnout[qb * 512:(qb + 1) * 512, cc * 128:(cc + 1) * 128],
                        transpose=True,
                    ).then_inc(_ao0 if g8 % 2 == 0 else _ao1, 16)
            for tb in range(TB):
                sy.wait_ge(_lnn, 16 * (tb + 1))
                sv = streamout[:, :].rearrange("p (c t) -> p c t", t=T)[:, :, tb * 512:tb * 512 + 512]
                sy.dma_start(out=sv,
                             in_=stream[:, :].rearrange("p (c t) -> p c t", t=T)[:, :, tb * 512:tb * 512 + 512]
                             ).then_inc(_so, 16)
            sy.wait_ge(_mx, 34)
            sy.dma_start(out=maxx[0:1, 0:1], in_=mxo[0:1, 0:1]).then_inc(_so, 16)

        @block.scalar
        def _(s):
            for c in range(4, DC):
                s.dma_start(out=stream[:, c * T:(c + 1) * T],
                            in_=xT[:, c * T:(c + 1) * T]).then_inc(_xsd[c], 16)
            for tb in range(TB):
                s.wait_ge(_lat, tb + 1)
                s.activation(latbf[0:64, tb * 512:(tb + 1) * 512],
                             psum[0:64, 0:512], AF.Copy).then_inc(_late, 1)
            for h in range(NH):
                s.wait_ge(_ch, h * 4 + 2)
                s.activation(csc[0:1, 3:4], csc[0:1, 2:3], AF.Ln).then_inc(_cha, 1)
                s.wait_ge(_cha, h * 2 + 1)
                s.activation(csc[0:1, 4:5], csc[0:1, 3:4], AF.Exp, bias=0.0,
                             scale=0.5).then_inc(_cha, 1)
                for qb in range(TB):
                    for pr in range(8):
                        s.wait_ge(_smm, (h * TB + qb) * 16 + 2 * pr + 2)
                        s.wait_ge(_ch, h * 4 + 4)
                        if pr == 0 and (qb > 0 or h > 0):
                            s.wait_ge(_amm, h * TB + qb)
                        off = 2048 if pr % 2 == 0 else 0
                        s.activation(probsT[:, pr * 1024:(pr + 1) * 1024],
                                     psum[:, off:off + 1024],
                                     AF.Exp, bias=negcb[:, 0:1],
                                     scale=1.0).then_inc(_sexp, 1)
            for tb in range(TB):
                for dcn in range(DC):
                    s.wait_ge(_sbc, 8 * tb + dcn + 1)
                    if tb > 0 and dcn == 0:
                        s.wait_ge(_lnp, SUMC[tb - 1])
                    s.activation(xbf[:, DC * 512 + dcn * 512:DC * 512 + dcn * 512 + 512],
                                 xbf[:, dcn * 512:dcn * 512 + 512],
                                 AF.Square).then_inc(_lna, 1)
                s.wait_ge(_ln, 5 * tb + 4)
                if tb > 0:
                    s.wait_ge(_lnp, BCC[tb - 1])
                s.activation(lrow[0:1, 1536:2048], lrow[0:1, 1024:1536],
                             AF.Ln).then_inc(_lna, 1)
                s.wait_ge(_lna, 10 * tb + 9)
                s.activation(lrow[0:1, 1536:2048], lrow[0:1, 1536:2048], AF.Exp,
                             bias=0.0, scale=-0.5).then_inc(_lna, 1)

        @block.vector
        def _(v):
            v.wait_ge(_cfgk, 16)
            v.tensor_copy(kvw[:, :], tmpk[:, :]).then_inc(_cfgk2, 1)
            for c in range(DC):
                v.wait_ge(_xsd[c], 16)
                v.tensor_copy(xbf[:, c * T:(c + 1) * T],
                              stream[:, c * T:(c + 1) * T]).then_inc(_xbd[c], 1)
            for tc in range(16):
                v.wait_ge(_ltr, tc + 1)
                pb = psum[:, 2048 + (tc % 2) * 512:2048 + (tc % 2) * 512 + 32].bitcast(BF16)
                v.tensor_copy(lattok[:, tc * 65:tc * 65 + 64],
                              pb[:, 0:64]).then_inc(_lte, 1)
            for i in range(4):
                v.wait_ge(_late, i + 1)
                v.tensor_tensor(sqs[0:64, i * 512:(i + 1) * 512],
                                latbf[0:64, i * 512:(i + 1) * 512],
                                latbf[0:64, i * 512:(i + 1) * 512],
                                ALU.mult).then_inc(_sq, 1)
            for i in range(4):
                v.wait_ge(_nmm, i + 1)
                v.tensor_copy(latn2[0:1, i * 512:(i + 1) * 512],
                              B2[0:1, :]).then_inc(_nev, 1)
            v.wait_ge(_nev, 4)
            v.tensor_reduce(csc[0:1, 1:2], latn2[0:1, :], AX,
                            ALU.max).then_inc(_lm, 1)
            for h in range(NH):
                v.wait_ge(_mmq, h * 5 + 1)
                v.tensor_copy(wqk[:, :], psum[:, 0:512]).then_inc(_ewq, 1)
                for i in range(4):
                    v.wait_ge(_mmq, h * 5 + 2 + i)
                    v.tensor_scalar(qhat[0:64, i * 512:(i + 1) * 512],
                                    psum[0:64, (i % 2) * 512:(i % 2) * 512 + 512],
                                    1.0 / 16.0, None, ALU.mult).then_inc(_eqh, 1)
                    v.wait_ge(_eqh, h * 4 + i + 1)
                    v.tensor_tensor(sqs[0:64, i * 512:(i + 1) * 512],
                                    qhat[0:64, i * 512:(i + 1) * 512],
                                    qhat[0:64, i * 512:(i + 1) * 512],
                                    ALU.mult).then_inc(_sq, 1)
                for i in range(4):
                    v.wait_ge(_nmm, 4 + h * 4 + i + 1)
                    v.tensor_copy(qn2[0:1, i * 512:(i + 1) * 512],
                                  B2[0:1, :]).then_inc(_nev, 1)
                v.wait_ge(_nev, 4 + h * 4 + 4)
                v.tensor_reduce(csc[0:1, 0:1], qn2[0:1, :], AX,
                                ALU.max).then_inc(_ch, 1)
                v.wait_ge(_ch, h * 4 + 1)
                v.wait_ge(_lm, 1)
                v.tensor_tensor(csc[0:1, 2:3], csc[0:1, 0:1], csc[0:1, 1:2],
                                ALU.mult).then_inc(_ch, 1)
                v.wait_ge(_cha, h * 2 + 2)
                v.tensor_scalar(csc[0:1, 5:6], csc[0:1, 4:5], -1.01, None,
                                ALU.mult).then_inc(_ch, 1)
                v.wait_ge(_chb, h + 1)
                if h > 0:
                    v.wait_ge(_sexp, h * 32)
                v.tensor_copy(negcb[:, 0:1], B3c[:, 0:1]).then_inc(_ch, 1)
                for qb in range(TB):
                    gq = h * TB + qb
                    v.wait_ge(_amm, gq + 1)
                    v.tensor_copy(obar[0:65, :],
                                  psum[0:65, 1024:1536]).then_inc(_oev, 1)
                    v.wait_ge(_otr, gq + 1)
                    if gq > 0:
                        v.wait_ge(_aev, gq * 4)
                    v.reciprocal(rcp[:, :],
                                 psum[:, 1540:1544].bitcast(BF16)[:, 0:8:2]
                                 ).then_inc(_arc, 1)
                    for qt in range(4):
                        aidx = gq * 4 + qt
                        v.wait_ge(_avm, aidx + 1)
                        v.wait_ge(_arc, gq + 1)
                        if aidx >= 4:
                            v.wait_ge(_asl[aidx % 4], 16 * (aidx // 4))
                        v.tensor_scalar(
                            aostage[:, (aidx % 4) * 256:(aidx % 4) * 256 + 256],
                            psum[:, 3072 + (aidx % 2) * 512:3072 + (aidx % 2) * 512 + 256],
                            rcp[:, qt:qt + 1], None, ALU.mult).then_inc(_aev, 1)
            for c in range(DC):
                v.wait_ge(_xsd[c], 16)
            for qb in range(TB):
                for dc in range(DC):
                    pidx = qb * 8 + dc
                    v.wait_ge(_pmm, pidx + 1)
                    sl = stream[:, dc * T + qb * 512:dc * T + qb * 512 + 512]
                    v.tensor_tensor(sl, sl,
                                    psum[:, (pidx % 2) * 512:(pidx % 2) * 512 + 512],
                                    ALU.add).then_inc(_pev, 1)

            def _norm_ops(tbn, ops):
                for op, dcn in ops:
                    sl2 = stream[:, dcn * T + tbn * 512:dcn * T + tbn * 512 + 512]
                    if op == "m":
                        if dcn == 0:
                            v.wait_ge(_lnp, BCC[tbn])
                        v.tensor_tensor(sl2, sl2, psum[:, 3072:3584],
                                        ALU.mult).then_inc(_lnn, 1)
                    else:
                        v.wait_ge(_lnn, 16 * tbn + TAILSEQ.index(("m", dcn)) + 1)
                        v.tensor_tensor(sl2, sl2, psum[:, 3584:4096],
                                        ALU.subtract).then_inc(_lnn, 1)
                        v.wait_ge(_lnn, 16 * tbn + TAILV[dcn])
                        v.tensor_reduce(maxcol[:, tbn * 8 + dcn:tbn * 8 + dcn + 1],
                                        stream[:, dcn * T + tbn * 512:dcn * T + tbn * 512 + 512],
                                        AX, ALU.max,
                                        apply_absolute_value=True).then_inc(_mx, 1)

            for qb in range(TB):
                for dc in range(DC):
                    pidx = 32 + qb * 8 + dc
                    v.wait_ge(_pmm, pidx + 1)
                    sl = stream[:, dc * T + qb * 512:dc * T + qb * 512 + 512]
                    v.tensor_tensor(sl, sl,
                                    psum[:, (pidx % 2) * 512:(pidx % 2) * 512 + 512],
                                    ALU.add).then_inc(_pev, 1)
                    v.wait_ge(_pev, pidx + 1)
                    v.tensor_copy(xbf[:, dc * 512:dc * 512 + 512],
                                  sl).then_inc(_sbc, 1)
                    if qb > 0:
                        _norm_ops(qb - 1, TAILSEQ[2 * dc:2 * dc + 2])
                b = 5 * qb
                v.wait_ge(_lnp, SUMC[qb])
                v.tensor_scalar(lrow[0:1, 0:512], psum[0:1, 1024:1536], 1.0 / D,
                                None, ALU.mult)
                v.tensor_scalar(lrow[0:1, 512:1024], psum[0:1, 1536:2048],
                                1.0 / D, None, ALU.mult).then_inc(_ln, 1)
                v.wait_ge(_ln, b + 1)
                v.tensor_tensor(lrow[0:1, 1024:1536], lrow[0:1, 0:512],
                                lrow[0:1, 0:512], ALU.mult).then_inc(_ln, 1)
                v.wait_ge(_ln, b + 2)
                v.tensor_tensor(lrow[0:1, 1024:1536], lrow[0:1, 512:1024],
                                lrow[0:1, 1024:1536], ALU.subtract).then_inc(_ln, 1)
                v.wait_ge(_ln, b + 3)
                v.tensor_scalar(lrow[0:1, 1024:1536], lrow[0:1, 1024:1536],
                                EPS_LN, None, ALU.add).then_inc(_ln, 1)
                v.wait_ge(_lna, 10 * qb + 10)
                if qb > 0:
                    v.wait_ge(_lnp, BCC[qb - 1])
                v.tensor_tensor(lrow[0:1, 2048:2560], lrow[0:1, 0:512],
                                lrow[0:1, 1536:2048], ALU.mult).then_inc(_ln, 1)
                if qb == TB - 1:
                    _norm_ops(qb, TAILSEQ)
            v.wait_ge(_mx, 32)
            v.tensor_reduce(mfold[:, 0:1], maxcol[:, 0:32], AX,
                            ALU.max).then_inc(_mx, 1)
            v.wait_ge(_mxp, 1)
            v.tensor_reduce(mxo[0:1, 0:1], psum[0:1, 2048:2176], AX,
                            ALU.max).then_inc(_mx, 1)

        @block.tensor
        def _(t_):
            t_.wait_ge(_cfgk2, 1)
            for tb in range(TB):
                if tb > 0:
                    t_.wait_ge(_late, tb)
                ins = None
                for dc in range(DC):
                    if tb == 0:
                        t_.wait_ge(_xbd[dc], 1)
                    ins = t_.matmul(psum[0:64, 0:512], kvw[:, dc * LR:(dc + 1) * LR],
                                    xbf[:, dc * T + tb * 512:dc * T + tb * 512 + 512],
                                    start=(dc == 0), stop=(dc == DC - 1))
                ins.then_inc(_lat, 1)
            t_.wait_ge(_cfg, 96)
            t_.wait_ge(_vb, 1)
            t_.wait_ge(_late, TB)
            for tc in range(16):
                if tc >= 2:
                    t_.wait_ge(_lte, tc - 1)
                pb = psum[:, 2048 + (tc % 2) * 512:2048 + (tc % 2) * 512 + 32].bitcast(BF16)
                t_.transpose(pb[:, 0:64], latbf[0:64, tc * 128:(tc + 1) * 128],
                             idb[0:64, 0:64]).then_inc(_ltr, 1)
            for i in range(4):
                t_.wait_ge(_sq, i + 1)
                if i > 0:
                    t_.wait_ge(_nev, i)
                t_.matmul(B2[:, :], onemb[0:64, :], sqs[0:64, i * 512:(i + 1) * 512],
                          start=True, stop=True).then_inc(_nmm, 1)
            for h in range(NH):
                t_.wait_ge(_wq0 if h % 2 == 0 else _wq1, 16 * (h // 2 + 1))
                t_.wait_ge(_eqh, h * 4)
                if h > 0:
                    t_.wait_ge(_ewq, h)
                ins = None
                for dc in range(DC):
                    for jj in range(2):
                        ins = t_.matmul(
                            psum[:, dc * 64:dc * 64 + 64],
                            wqn_sb[:, (h % 2) * 2 * D + jj * D + dc * 128:(h % 2) * 2 * D + jj * D + dc * 128 + 128],
                            wku_sb[:, (h * 2 + jj) * LR:(h * 2 + jj) * LR + LR],
                            start=(jj == 0), stop=(jj == 1))
                ins.then_inc(_mmq, 1)
                for i in range(4):
                    t_.wait_ge(_ewq, h + 1)
                    if i >= 2:
                        t_.wait_ge(_eqh, h * 4 + i - 1)
                    ins = None
                    for dc in range(DC):
                        ins = t_.matmul(
                            psum[0:64, (i % 2) * 512:(i % 2) * 512 + 512],
                            wqk[:, dc * LR:(dc + 1) * LR],
                            xbf[:, dc * T + i * 512:dc * T + i * 512 + 512],
                            start=(dc == 0), stop=(dc == DC - 1))
                    ins.then_inc(_mmq, 1)
                for i in range(4):
                    t_.wait_ge(_sq, 4 + h * 4 + i + 1)
                    t_.wait_ge(_nev, 4 + h * 4 + i)
                    t_.matmul(B2[:, :], onemb[0:64, :],
                              sqs[0:64, i * 512:(i + 1) * 512],
                              start=True, stop=True).then_inc(_nmm, 1)
                for qb in range(TB):
                    if qb == 0:
                        t_.wait_ge(_eqh, h * 4 + 4)
                    for kt in range(16):
                        sidx = (h * TB + qb) * 16 + kt
                        prg = (h * TB + qb) * 8 + kt // 2
                        if prg >= 2:
                            t_.wait_ge(_sexp, prg - 1)
                        off = (2048 if (kt // 2) % 2 == 0 else 0) + (kt % 2) * 512
                        t_.matmul(psum[:, off:off + 512],
                                  latbf[0:64, kt * 128:(kt + 1) * 128],
                                  qhat[0:64, qb * 512:(qb + 1) * 512],
                                  start=True, stop=True).then_inc(_smm, 1)
                        if qb == 0 and kt == 3:
                            t_.wait_ge(_ch, h * 4 + 3)
                            t_.matmul(B3c[:, 0:1], onesr[0:1, :], csc[0:1, 5:6],
                                      start=True, stop=True).then_inc(_chb, 1)
                    gq = h * TB + qb
                    t_.wait_ge(_oev, gq)
                    if qb == 0:
                        t_.wait_ge(_nev, 4 + h * 4 + 4)
                    ins = None
                    for kc in range(16):
                        t_.wait_ge(_sexp, gq * 8 + kc // 2 + 1)
                        ins = t_.matmul(psum[0:65, 1024:1536],
                                        lattok[:, kc * 65:(kc + 1) * 65],
                                        probsT[:, kc * 512:(kc + 1) * 512],
                                        start=(kc == 0), stop=(kc == 15))
                    ins.then_inc(_amm, 1)
                    t_.wait_ge(_oev, gq + 1)
                    if gq > 0:
                        t_.wait_ge(_arc, gq)
                    ins = None
                    for qt in range(4):
                        pb2 = psum[:, 1540 + qt:1541 + qt].bitcast(BF16)
                        ins = t_.transpose(pb2[:, 0:1],
                                           obar[64:65, qt * 128:(qt + 1) * 128],
                                           idb[64:65, 64:65])
                    ins.then_inc(_otr, 1)
                    for qt in range(4):
                        aidx = gq * 4 + qt
                        if aidx >= 2:
                            t_.wait_ge(_aev, aidx - 1)
                        ins = t_.matmul(
                            psum[:, 3072 + (aidx % 2) * 512:3072 + (aidx % 2) * 512 + 256],
                            obar[0:64, qt * 128:(qt + 1) * 128],
                            vup[0:64, h * 256:(h + 1) * 256],
                            start=True, stop=True)
                        ins.then_inc(_avm, 1)
            t_.wait_ge(_wo0, 16 * 8)
            for qb in range(TB):
                t_.wait_ge(_ao0 if qb % 2 == 0 else _ao1, 16 * 8 * (qb // 2 + 1))
                for dc in range(DC):
                    pidx = qb * 8 + dc
                    if pidx >= 2:
                        t_.wait_ge(_pev, pidx - 1)
                    ins = None
                    for c8 in range(8):
                        ins = t_.matmul(
                            psum[:, (pidx % 2) * 512:(pidx % 2) * 512 + 512],
                            wo_sb[:, c8 * 1024 + dc * 128:c8 * 1024 + dc * 128 + 128],
                            aoT[:, (qb % 2) * 16 * 512 + c8 * 512:(qb % 2) * 16 * 512 + c8 * 512 + 512],
                            start=(c8 == 0), stop=(c8 == 7))
                    ins.then_inc(_pmm, 1)
            t_.wait_ge(_wo1, 16 * 8)

            def _sums(tbn):
                t_.wait_ge(_sbc, 8 * tbn + 8)
                t_.wait_ge(_lna, 10 * tbn + 8)
                for dcn in range(DC):
                    t_.matmul(psum[:, 1024:1536], onemb[:, :],
                              xbf[:, dcn * 512:(dcn + 1) * 512],
                              start=(dcn == 0), stop=(dcn == DC - 1))
                ins2 = None
                for dcn in range(DC):
                    ins2 = t_.matmul(psum[:, 1536:2048], onemb[:, :],
                                     xbf[:, DC * 512 + dcn * 512:DC * 512 + (dcn + 1) * 512],
                                     start=(dcn == 0), stop=(dcn == DC - 1))
                ins2.then_inc(_lnp, 1)

            def _bcast(tbn):
                t_.wait_ge(_ln, 5 * tbn + 5)
                if tbn == 0:
                    t_.wait_ge(_aev, 128)
                else:
                    t_.wait_ge(_lnn, 16 * tbn)
                t_.matmul(psum[:, 3072:3584], onesr[0:1, :], lrow[0:1, 1536:2048],
                          start=True, stop=True)
                ins2 = t_.matmul(psum[:, 3584:4096], onesr[0:1, :],
                                 lrow[0:1, 2048:2560], start=True, stop=True)
                ins2.then_inc(_lnp, 1)

            for qb in range(TB):
                g8 = 4 + qb
                t_.wait_ge(_ao0 if g8 % 2 == 0 else _ao1, 16 * 8 * (g8 // 2 + 1))
                for dc in range(DC):
                    pidx = 32 + qb * 8 + dc
                    if pidx >= 2:
                        t_.wait_ge(_pev, pidx - 1)
                    ins = None
                    for c8 in range(8):
                        ins = t_.matmul(
                            psum[:, (pidx % 2) * 512:(pidx % 2) * 512 + 512],
                            xbf[:, DC * 512 * 2 + c8 * 1024 + dc * 128:DC * 512 * 2 + c8 * 1024 + dc * 128 + 128],
                            aoT[:, (g8 % 2) * 16 * 512 + c8 * 512:(g8 % 2) * 16 * 512 + c8 * 512 + 512],
                            start=(c8 == 0), stop=(c8 == 7))
                    ins.then_inc(_pmm, 1)
                    if qb > 0 and dc == 1:
                        _bcast(qb - 1)
                _sums(qb)
            _bcast(TB - 1)
            t_.wait_ge(_mx, 33)
            t_.transpose(psum[0:1, 2048:2176], mfold[0:128, 0:1],
                         id128[:, :]).then_inc(_mxp, 1)
    return nc


_NC_CACHE = {}


def _nc(name):
    if name not in _NC_CACHE:
        _NC_CACHE[name] = globals()["build_" + name]()
    return _NC_CACHE[name]


def _to_fm(a):
    return np.ascontiguousarray(
        a.T.reshape(DC, 128, T).transpose(1, 0, 2).reshape(128, DC * T))


def _from_fm(a):
    return a.reshape(128, DC, T).transpose(1, 0, 2).reshape(D, T).T


def kernel(L0_attn_x, L0_attn_Wq, L0_attn_Wkv_down, L0_attn_Wk_up,
           L0_attn_Wv_up, L0_attn_Wo, L0_ffn_w_gate, L0_ffn_w_up,
           L0_ffn_w_down):
    cores = list(range(8))
    x = np.ascontiguousarray(np.asarray(L0_attn_x, np.float32))
    w_up = np.ascontiguousarray(np.asarray(L0_ffn_w_up, np.float32))
    w_gate = np.ascontiguousarray(np.asarray(L0_ffn_w_gate, np.float32))
    w_down = np.ascontiguousarray(np.asarray(L0_ffn_w_down, np.float32))

    in0 = [{"wu": np.ascontiguousarray(w_up[c * 384:(c + 1) * 384]),
            "wg": np.ascontiguousarray(w_gate[c * 384:(c + 1) * 384]),
            "wd": np.ascontiguousarray(w_down[c * 128:(c + 1) * 128])}
           for c in cores]
    r0 = run_bass_kernel_spmd(_nc("l0"), in0, core_ids=cores).results
    wupdq = np.concatenate([np.asarray(r0[c]["wuo"]) for c in cores], 0)
    wgatedq = np.concatenate([np.asarray(r0[c]["wgo"]) for c in cores], 0)
    wdowndq = np.concatenate([np.asarray(r0[c]["wdo"]) for c in cores], 0)

    wqn = np.ascontiguousarray(np.asarray(L0_attn_Wq, np.float32))
    wkvT = np.ascontiguousarray(np.asarray(L0_attn_Wkv_down, np.float32)
                                .T.reshape(8, 128, 64).transpose(1, 0, 2)
                                .reshape(128, 512))
    wku = np.ascontiguousarray(np.asarray(L0_attn_Wk_up, np.float32))
    wvuT = np.ascontiguousarray(np.asarray(L0_attn_Wv_up, np.float32).T)
    woT = np.ascontiguousarray(np.asarray(L0_attn_Wo, np.float32).T)
    in1 = [{"xT": _to_fm(x[c]), "wqn": wqn, "wkvT": wkvT, "wku": wku,
            "wvuT": wvuT, "woT": woT} for c in cores]
    r1 = run_bass_kernel_spmd(_nc("l1"), in1, core_ids=cores).results
    streams = [np.asarray(r1[c]["streamout"]) for c in cores]
    maxx8 = np.concatenate([np.asarray(r1[c]["maxx"]) for c in cores],
                           axis=1)

    in2 = [{"stream": streams[c], "maxx": maxx8, "wup": wupdq,
            "wgate": wgatedq} for c in cores]
    r2 = run_bass_kernel_spmd(_nc("l2"), in2, core_ids=cores).results
    maxug8 = np.concatenate(
        [np.asarray(r2[c]["maxug"])[0:1, :] for c in cores] +
        [np.asarray(r2[c]["maxug"])[1:2, :] for c in cores], axis=0).reshape(1, 16)

    in3 = [{"gatein": np.asarray(r2[c]["gateout"]), "maxug8": maxug8}
           for c in cores]
    r3 = run_bass_kernel_spmd(_nc("l3"), in3, core_ids=cores).results
    maxg28 = np.concatenate([np.asarray(r3[c]["maxg2"]) for c in cores],
                            axis=1)

    in4 = [{"g2in": np.asarray(r3[c]["g2out"]),
            "upin": np.asarray(r2[c]["upout"]),
            "maxg28": maxg28, "maxug8": maxug8} for c in cores]
    r4 = run_bass_kernel_spmd(_nc("l4"), in4, core_ids=cores).results
    maxo8 = np.concatenate([np.asarray(r4[c]["maxo"]) for c in cores], axis=1)

    maxo8r = np.ascontiguousarray(np.broadcast_to(maxo8, (128, 8)))
    wP = np.ascontiguousarray(
        wdowndq.reshape(8, 128, 24, 128).transpose(3, 0, 2, 1).reshape(128, 24576))
    in5 = [{"oin": np.asarray(r4[c]["oout"]), "maxo8": maxo8r,
            "streamin": streams[c], "wdown": wP} for c in cores]
    r5 = run_bass_kernel_spmd(_nc("l5"), in5, core_ids=cores).results

    out = np.empty((8, T, D), np.float32)
    for c in cores:
        out[c] = _from_fm(np.asarray(r5[c]["final"]))
    return out
